# version 05
# baseline (speedup 1.0000x reference)
import contextlib
import numpy as np
import ml_dtypes
import concourse.bass as bass
import concourse.mybir as mybir
from concourse.bass_utils import run_bass_kernel_spmd

F32 = mybir.dt.float32
BF16 = mybir.dt.bfloat16
AF = mybir.ActivationFunctionType
ALU = mybir.AluOpType
AX = mybir.AxisListType
BF16NP = np.dtype(ml_dtypes.bfloat16)

D = 1024
T = 2048
NCH = 8
TBW = 512
NTB = T // TBW
CORES = 8

ENGS = ("tensor", "vector", "scalar", "gpsimd", "sync")
HOSTBF16 = True
DEBUG = False
PHASE = 99


class Prog:
    def __init__(self, nc):
        self.nc = nc
        self.ops = {e: [] for e in ENGS}
        self.cnt = {}
        self.waited = {e: {} for e in ENGS}
        self.chkeys = []
        self.n_cc = 0

    def _key(self, eng, dma):
        k = ("d" if dma else "c", eng)
        if k not in self.cnt:
            self.cnt[k] = 0
            self.chkeys.append(k)
        return k

    def op(self, eng, emit, deps=(), dma=False):
        key = self._key(eng, dma)
        deps = list(deps)
        if dma and self.cnt[key] > 0:
            deps.append((key, self.cnt[key]))
        wl = {}
        for tok in deps:
            if tok is None:
                continue
            k, c = tok
            if self.waited[eng].get(k, -1) >= c:
                continue
            if wl.get(k, -1) < c:
                wl[k] = c
        for k, c in wl.items():
            self.waited[eng][k] = c
        amt = 16 if dma else 1
        self.cnt[key] += amt
        self.ops[eng].append((wl, emit, key, amt))
        return (key, self.cnt[key])

    def collective(self, emit, deps=()):
        self.n_cc += 1
        key = ("cc", self.n_cc)
        self.cnt[key] = 1
        self.chkeys.append(key)
        eng = "gpsimd"
        wl = {}
        for tok in deps:
            if tok is None:
                continue
            k, c = tok
            if self.waited[eng].get(k, -1) >= c:
                continue
            if wl.get(k, -1) < c:
                wl[k] = c
        for k, c in wl.items():
            self.waited[eng][k] = c
        self.ops[eng].append((wl, emit, key, 1))
        return (key, 1)

    def finalize(self, stack, block):
        sems = {}
        for k in self.chkeys:
            sems[k] = stack.enter_context(self.nc.semaphore(f"s_{k[0]}_{k[1]}"))
        for eng in ENGS:
            ops = self.ops[eng]
            if not ops:
                continue

            def make(eng, ops):
                def body(h):
                    for wl, emit, inckey, amt in ops:
                        for k, c in wl.items():
                            h.wait_ge(sems[k], c)
                        inst = emit(h)
                        inst.then_inc(sems[inckey], amt)
                return body
            getattr(block, eng)(make(eng, ops))


class Pool:
    def __init__(self, tensor, nelem):
        self.t = tensor
        self.n = nelem
        self.off = 0
        self.marks = []

    def alloc_bf16(self, cols):
        cols = (cols + 15) // 16 * 16
        assert self.off + cols <= self.n, f"pool overflow {self.off}+{cols}>{self.n}"
        ap = self.t.ap()[:, self.off:self.off + cols]
        self.off += cols
        return ap

    def alloc_f32(self, cols):
        return self.alloc_bf16(2 * cols).bitcast(F32)

    def at(self, off, cols):
        return self.t.ap()[:, off:off + cols]

    def push(self):
        self.marks.append(self.off)

    def pop(self):
        self.off = self.marks.pop()


def _rope_tables():
    d = 64
    inv = 1.0 / (10000.0 ** (np.arange(0, d, 2, dtype=np.float64) / d))
    t = np.arange(T, dtype=np.float64)
    fr = np.outer(t, inv)
    cos = np.cos(fr).T.astype(np.float32)
    sin = np.sin(fr).T.astype(np.float32)
    C = np.concatenate([cos, cos, cos, cos], 0)
    S = np.concatenate([-sin, sin, -sin, sin], 0)
    return C.astype(BF16NP), S.astype(BF16NP)


def _perm_rope_cols():
    return np.concatenate([np.arange(0, 64, 2), np.arange(1, 64, 2)])


def prep_inputs(I):
    g = {}
    X = np.asarray(I["L0_attn_a_X"])
    per_core = [{"xin": np.ascontiguousarray(X[b].T)} for b in range(CORES)]

    g["a0_wq"] = np.ascontiguousarray(I["L0_attn_a_W_Q"])
    g["a0_wk"] = np.ascontiguousarray(I["L0_attn_a_W_K"])
    g["a0_wv"] = np.ascontiguousarray(I["L0_attn_a_W_V"])

    for nm in ("wf", "wi", "wc", "wo"):
        g[f"f0_{nm}"] = np.ascontiguousarray(I[f"L0_ffn_{nm}"].T)
    for nm in ("uf", "ui", "uc", "uo"):
        g[f"f0_{nm}"] = np.ascontiguousarray(I[f"L0_ffn_{nm}"].T)
    for nm in ("bf", "bi", "bc", "bo", "in_c"):
        g[f"f0_{nm}"] = np.ascontiguousarray(I[f"L0_ffn_{nm}"].reshape(8, 128).T)

    perm = _perm_rope_cols()
    Wqb = np.asarray(I["L0_attn_b_Wqb"])
    Wqb_p = Wqb.reshape(64, 16, 64)[:, :, perm].reshape(64, 16, 64)
    hperm = [0] * 16
    for h in range(16):
        slot = 2 * ((h % 4) + 4 * (h // 8)) + ((h // 4) % 2)
        hperm[slot] = h
    Wqb_p = Wqb_p[:, hperm].reshape(64, 1024)
    g["b0_hperm"] = np.asarray(hperm)
    g["b0_wqa"] = np.ascontiguousarray(I["L0_attn_b_Wqa"])
    g["b0_wqb"] = np.ascontiguousarray(Wqb_p)
    Wkvb = np.asarray(I["L0_attn_b_Wkvb"])
    Wk_part = Wkvb.reshape(32, 4, 64, 2)[:, :, perm, 0].reshape(32, 256)
    Wv_part = Wkvb.reshape(32, 4, 64, 2)[:, :, :, 1].reshape(32, 256)
    g["b0_wkva"] = np.ascontiguousarray(I["L0_attn_b_Wkva"])
    g["b0_wkvb"] = np.ascontiguousarray(np.concatenate([Wk_part, Wv_part], 1))
    Wout = np.asarray(I["L0_attn_b_Wout"]).reshape(16, 64, 1024)[hperm].reshape(1024, 1024)
    g["b0_wout"] = np.ascontiguousarray(Wout)
    g["b0_bout"] = np.ascontiguousarray(I["L0_attn_b_bout"].reshape(1, 1024))
    C, S = _rope_tables()
    g["b0_ropec"] = C
    g["b0_ropes"] = S
    masks = np.zeros((4, 128, TBW), dtype=np.float32)
    kk = np.arange(128)[:, None]
    qq = np.arange(TBW)[None, :]
    for r in range(4):
        masks[r] = np.where(128 * r + kk <= qq, 0.0, -1e9)
    g["b0_masks"] = np.ascontiguousarray(masks.transpose(1, 0, 2).reshape(128, 4 * TBW).astype(BF16NP))

    g["d1_wq"] = np.ascontiguousarray(I["L1_attn_a_Wq"].T)
    g["d1_wk"] = np.ascontiguousarray(I["L1_attn_a_Wk"].T)
    g["d1_wv"] = np.ascontiguousarray(I["L1_attn_a_Wv"].T)
    wbp = np.zeros((1024, 32), dtype=np.float32)
    wbp[:, 0:4] = I["L1_attn_a_Wb"].T
    g["d1_wb"] = np.ascontiguousarray(wbp)
    for nm in ("qconv_w", "kconv_w", "vconv_w"):
        w = np.asarray(I[f"L1_attn_a_{nm}"])
        g[f"d1_{nm}"] = np.ascontiguousarray(w.reshape(8, 128, 4).transpose(1, 0, 2).reshape(128, 32))
    fs = np.asarray(I["L1_attn_a_fir_short_w"]).reshape(1024, 3)
    fl = np.asarray(I["L1_attn_a_fir_long_w"]).reshape(1024, 63)
    g["d1_fir_s"] = np.ascontiguousarray(fs.reshape(8, 128, 3).transpose(1, 0, 2).reshape(128, 24))
    g["d1_fir_l"] = np.ascontiguousarray(fl.reshape(8, 128, 63).transpose(1, 0, 2).reshape(128, 8 * 63))
    g1p = np.zeros((1152, 2048), dtype=np.float32)
    w1t = I["L1_attn_a_gate_w1"].T
    g1p[0:1024] = w1t[0:1024]
    for mv in range(2):
        for path in range(4):
            for hh in range(4):
                g1p[1024 + mv * 16 + path * 4 + hh] = w1t[1024 + path * 8 + hh * 2 + mv]
    g["d1_g1"] = np.ascontiguousarray(g1p)
    g["d1_g1b"] = np.ascontiguousarray(I["L1_attn_a_gate_b1"].reshape(16, 128).T)
    g["d1_g2"] = np.ascontiguousarray(I["L1_attn_a_gate_w2"].T)
    g["d1_bb"] = np.ascontiguousarray(I["L1_attn_a_base_bias"].reshape(16, 1))
    g["d1_logt"] = np.ascontiguousarray(np.repeat(np.asarray(I["L1_attn_a_log_temp"]).reshape(4), 4).reshape(16, 1).astype(np.float32))
    g["d1_floor"] = np.ascontiguousarray(np.asarray(I["L1_attn_a_floor_raw"]).reshape(1, 16).astype(np.float32))
    g["d1_onw"] = np.ascontiguousarray(np.tile(I["L1_attn_a_onorm_w"].reshape(2, 128).T, (1, 4)))
    g["d1_wo"] = np.ascontiguousarray(I["L1_attn_a_Wo"].T)

    wg = np.zeros((1024, 32), dtype=np.float32)
    wg[:, 0:8] = I["L1_ffn_Wg"]
    g["m1_wg"] = np.ascontiguousarray(wg)
    g["m1_w1"] = np.ascontiguousarray(I["L1_ffn_W1"])
    g["m1_w2"] = np.ascontiguousarray(I["L1_ffn_W2"])
    g["m1_b1"] = np.ascontiguousarray(np.asarray(I["L1_ffn_b1"]).reshape(8, 32, 128).transpose(0, 2, 1).reshape(8, 128 * 32))
    g["m1_b2"] = np.ascontiguousarray(I["L1_ffn_b2"])

    g["s1_ns"] = np.ascontiguousarray(I["L1_attn_b_norm_scale"].reshape(8, 128).T)
    w_in = np.asarray(I["L1_attn_b_w_in"])
    DIN = 2048
    gate = w_in[:, 0:DIN]
    ssm = w_in[:, DIN:2 * DIN]
    dtr = w_in[:, 2 * DIN:2 * DIN + 32]
    Bc = w_in[:, 2 * DIN + 32:2 * DIN + 96]
    Cc = w_in[:, 2 * DIN + 96:2 * DIN + 160]
    zeros = np.zeros((1024, 96), dtype=np.float32)
    g["s1_win"] = np.ascontiguousarray(np.concatenate([Bc, Cc, dtr, zeros, gate, ssm], 1))
    cw = np.asarray(I["L1_attn_b_conv_weight"])
    g["s1_cw"] = np.ascontiguousarray(cw.T.reshape(16, 128, 4).transpose(1, 0, 2).reshape(128, 64))
    g["s1_cb"] = np.ascontiguousarray(I["L1_attn_b_conv_bias"].reshape(16, 128).T)
    g["s1_alog"] = np.ascontiguousarray(I["L1_attn_b_A_log"].reshape(32, 1))
    g["s1_dtb"] = np.ascontiguousarray(I["L1_attn_b_dt_bias"].reshape(32, 1))
    g["s1_wout"] = np.ascontiguousarray(I["L1_attn_b_w_out"])

    sel = np.zeros((8, 8, 128), dtype=BF16NP)
    for e_ in range(8):
        sel[e_, e_, :] = 1.0
    g["m1_sel"] = np.ascontiguousarray(sel.transpose(1, 0, 2).reshape(8, 1024))
    bgp = np.full((1, 32), -1e9, dtype=np.float32)
    bgp[0, 0:8] = I["L1_ffn_bg"]
    g["m1_bg"] = np.ascontiguousarray(bgp)
    g["c_ident"] = np.eye(128, dtype=BF16NP)
    g["c_ones"] = np.ones((128, 128), dtype=np.float32)
    i32 = np.arange(32)
    tril_s = (i32[:, None] > i32[None, :]).astype(np.float32)
    triu_i = (i32[:, None] <= i32[None, :]).astype(np.float32)
    bd = np.zeros((128, 128), dtype=np.float32)
    bdT = np.zeros((128, 128), dtype=np.float32)
    bdA = np.zeros((128, 128), dtype=np.float32)
    for h in range(4):
        bd[32 * h:32 * h + 32, 32 * h:32 * h + 32] = -tril_s
        bdT[32 * h:32 * h + 32, 32 * h:32 * h + 32] = -tril_s.T
        bdA[32 * h:32 * h + 32, 32 * h:32 * h + 32] = triu_i
    cd = np.zeros((128, 64, 32), dtype=np.float32)
    for path in range(4):
        for fc in range(8):
            cd[:, (path * 8 + fc) * 2 + 0, path * 4 + fc // 2] = 1.0 / 256
            cd[:, (path * 8 + fc) * 2 + 1, 16 + path * 4 + fc // 2] = 1.0 / 256
    g["c_dsel"] = np.ascontiguousarray(cd.reshape(128, 2048).astype(BF16NP))
    cdh = np.zeros((128, 8, 4), dtype=np.float32)
    for fc in range(8):
        cdh[:, fc, fc // 2] = 1.0 / 256
    g["c_dselh"] = np.ascontiguousarray(cdh.reshape(128, 32).astype(BF16NP))
    g["c_bd"] = bd
    g["c_bdT"] = bdT
    g["c_bdA"] = bdA
    st = (np.arange(128)[:, None] <= np.arange(128)[None, :])
    g["c_mst"] = np.tile(st.astype(BF16NP), (1, 4))
    g["c_utri"] = np.ascontiguousarray(np.triu(np.ones((128, 128), dtype=np.float32)).T)
    for k, v in list(g.items()):
        if HOSTBF16 and v.dtype == np.float32 and v.size > 65536:
            g[k] = np.ascontiguousarray(v.astype(BF16NP))
    return g, per_core


class Env:
    pass


def _mmgroup(E, ps_ap, pairs, deps, extra=None, tile_position=None, start_clear=True,
             extra_tile_position=None, stop_group=True):
    def emit(h):
        last = None
        n = len(pairs)
        for i, (l, r) in enumerate(pairs):
            last = h.matmul(ps_ap, l, r, start=(i == 0 and start_clear),
                            stop=(i == n - 1 and extra is None and stop_group),
                            tile_position=tile_position)
        if extra is not None:
            for j, (l, r) in enumerate(extra):
                last = h.matmul(ps_ap, l, r, start=False,
                                stop=(j == len(extra) - 1 and stop_group),
                                tile_position=extra_tile_position)
        return last
    return E.p.op("tensor", emit, deps)


def _mmgroup_tp(E, ps_fn, triples, deps, start_clear=True, stop_group=True, pairwise=False):
    def emit(h):
        last = None
        n = len(triples)
        for i, (ps_ap, l, r, tp) in enumerate(triples):
            if pairwise:
                st, sp = (i % 2 == 0), (i % 2 == 1)
            else:
                st, sp = (i == 0 and start_clear), (i == n - 1 and stop_group)
            last = h.matmul(ps_ap, l, r, start=st, stop=sp, tile_position=tp)
        return last
    return E.p.op("tensor", emit, deps)


def declare_input(E, name):
    if name in E.innames:
        return E.innames[name]
    arr = E.g[name]
    dt = BF16 if arr.dtype == BF16NP else F32
    ap = E.nc.dram_tensor(name, list(arr.shape), dt, kind="ExternalInput").ap()
    E.innames[name] = ap
    return ap


def barrier(E):
    snap = dict(E.p.cnt)
    toks = [(k, c) for k, c in snap.items() if c > 0]
    new = {}
    for eng in ENGS:
        new[eng] = E.p.op(eng, lambda h: h.nop(), deps=toks)
    return new


def load_weight_bf16(E, name, dst_ap, deps=(), cols=None):
    src = declare_input(E, name)
    R, C = E.g[name].shape
    nch = R // 128
    if cols is None:
        cols = C

    def emit(h):
        return h.dma_start(
            out=dst_ap.rearrange("p (c h) -> p c h", c=nch),
            in_=src.rearrange("(c p) h -> p c h", p=128)[:, :, 0:cols])
    return E.p.op("gpsimd", emit, deps, dma=True)


def emit_ln(E, xsrc=None, xsrc_tok=None, out=None, rms=False):
    p = E.p
    out = out if out is not None else E.xln
    if rms:
        nsc = E.pool.alloc_f32(8)
        ns_in = declare_input(E, "s1_ns")
        t_ns = p.op("gpsimd", (lambda h: h.dma_start(out=nsc, in_=ns_in)), dma=True)
    E.pool.push()
    E.lnbf = E.pool.alloc_bf16(NCH * TBW)
    E.lnsq = E.pool.alloc_bf16(NCH * TBW)
    E.lnt1 = E.lnsq
    E.lnbf_war = E.ln_scratch_war
    E.lnsq_war = E.ln_scratch_war
    E.lnt1_war = E.ln_scratch_war
    E.lntok = {}
    for tb in range(NTB):
        q0 = tb * TBW
        cps, sqs = [], []
        for c in range(NCH):
            if xsrc is None:
                xs = E.x[:, c * T + q0: c * T + q0 + TBW]
                d = [E.xtok.get((c, tb))]
            else:
                xs = xsrc[:, c * T + q0: c * T + q0 + TBW]
                d = [xsrc_tok]
            cps.append(p.op("vector", (lambda h, c=c, xs=xs, b_=E.lnbf: h.tensor_copy(
                b_[:, c * TBW:(c + 1) * TBW], xs)), d + [E.lnbf_war]))
            sqs.append(p.op("scalar", (lambda h, c=c, xs=xs, b_=E.lnsq: h.activation(
                b_[:, c * TBW:(c + 1) * TBW], xs, AF.Square)), d + [E.lnsq_war]))
        t_s1 = None
        if not rms:
            t_s1 = _mmgroup(E, E.ps[0][0:1, :],
                            [(E.ones_bf[:, 0:1], E.lnbf[:, c * TBW:(c + 1) * TBW]) for c in range(NCH)],
                            cps + [E.ps_war[0]])
        t_s2 = _mmgroup(E, E.ps[1][0:1, :],
                        [(E.ones_bf[:, 0:1], E.lnsq[:, c * TBW:(c + 1) * TBW]) for c in range(NCH)],
                        sqs + [E.ps_war[1]])
        rm, rv, rr, rs, rt = E.row_m, E.row_v, E.row_r, E.row_s, E.row_t
        if rms:
            t2 = p.op("vector", lambda h: h.tensor_scalar_mul(rv, E.ps[1][0:1, :], 1.0 / D),
                      [t_s2, E.row_war])
            t5 = p.op("vector", lambda h: h.tensor_scalar_add(rv, rv, 1e-6), [t2])
            t1 = t5
        else:
            t1 = p.op("vector", lambda h: h.tensor_scalar_mul(rm, E.ps[0][0:1, :], 1.0 / D), [t_s1, E.row_war])
            t2 = p.op("vector", lambda h: h.tensor_scalar_mul(rv, E.ps[1][0:1, :], 1.0 / D), [t_s2, E.row_war])
            t3 = p.op("vector", lambda h: h.tensor_tensor(rt, rm, rm, ALU.mult), [t1])
            t4 = p.op("vector", lambda h: h.tensor_sub(rv, rv, rt), [t2, t3])
            t5 = p.op("vector", lambda h: h.tensor_scalar_add(rv, rv, 1e-5), [t4])
        t6 = p.op("scalar", lambda h: h.activation(rt, rv, AF.Ln), [t5])
        t7 = p.op("scalar", lambda h: h.activation(rr, rt, AF.Exp, scale=-0.5), [t6])
        if rms:
            t8b = p.op("vector", lambda h: h.tensor_copy(E.row_b1, rr), [t7, E.row_b_war])
            t_rb = _mmgroup(E, E.ps[2][:, :], [(E.ones_bf[0:1, :], E.row_b1)], [t8b, E.ps_war[2]])
            tas = []
            t1w = E.lnt1_war
            for c in range(NCH):
                tas.append(p.op("vector", (lambda h, c=c, t_=E.lnt1, b_=E.lnbf: h.tensor_tensor(
                    t_[:, c * TBW:(c + 1) * TBW],
                    b_[:, c * TBW:(c + 1) * TBW], E.ps[2][:, :], ALU.mult)), [t_rb, cps[c], t1w]))
            lastv = None
            for c in range(NCH):
                lastv = p.op("vector", (lambda h, c=c, q0=q0, t_=E.lnt1: h.tensor_scalar(
                    out[:, c * T + q0: c * T + q0 + TBW],
                    t_[:, c * TBW:(c + 1) * TBW], nsc[:, c:c + 1], None, ALU.mult)),
                    [t_ns, tas[c]])
                E.lntok[(c, tb)] = lastv
            E.row_b_war = lastv
            E.ps_war[2] = lastv
        else:
            t8 = p.op("vector", lambda h: h.scalar_tensor_tensor(rs, rm, -1.0, rr, ALU.mult, ALU.mult), [t7])
            t8b = p.op("vector", lambda h: h.tensor_copy(E.row_b1, rr), [t7, E.row_b_war])
            t8c = p.op("vector", lambda h: h.tensor_copy(E.row_b2, rs), [t8, E.row_b_war])
            t_rb = _mmgroup(E, E.ps[2][:, :], [(E.ones_bf[0:1, :], E.row_b1)], [t8b, E.ps_war[2]])
            t_sb = _mmgroup(E, E.ps[3][:, :], [(E.ones_bf[0:1, :], E.row_b2)], [t8c, E.ps_war[3]])
            tas = []
            t1w = E.lnt1_war
            for c in range(NCH):
                tas.append(p.op("vector", (lambda h, c=c, t_=E.lnt1, b_=E.lnbf: h.tensor_tensor(
                    t_[:, c * TBW:(c + 1) * TBW],
                    b_[:, c * TBW:(c + 1) * TBW], E.ps[2][:, :], ALU.mult)), [t_rb, cps[c], t1w]))
            lastv = None
            for c in range(NCH):
                lastv = p.op("vector", (lambda h, c=c, q0=q0, t_=E.lnt1: h.tensor_tensor(
                    out[:, c * T + q0: c * T + q0 + TBW],
                    t_[:, c * TBW:(c + 1) * TBW], E.ps[3][:, :], ALU.add)), [t_sb, tas[c]])
                E.lntok[(c, tb)] = lastv
            E.row_b_war = lastv
        if not rms:
            E.ps_war[0] = t1
            E.ps_war[3] = lastv
        E.ps_war[1] = t2
        E.ps_war[2] = lastv
        E.row_war = lastv
        E.lnbf_war = lastv
        E.lnsq_war = lastv
        E.lnt1_war = lastv
    E.lntoks = list(E.lntok.values())
    E.ln_scratch_war = lastv
    E.pool.pop()


def emit_attn_a(E):
    p = E.p
    QB = 256
    NQB = T // QB
    E.pool.push()
    E.xln = E.pool.alloc_bf16(NCH * T)
    emit_ln(E)
    if DEBUG:
        E.dbg["xln"] = (E.xln, E.lntoks)
    KT = E.pool.alloc_bf16(NCH * T)
    VT = E.pool.alloc_bf16(16 * D)
    sqs = E.pool.alloc_bf16(2 * TBW)
    rowk = E.pool.alloc_bf16(T)
    kmax = E.pool.alloc_f32(8)
    E.pool.push()
    wbuf = E.pool.alloc_bf16(NCH * D)
    prev_sq = None

    t_w = load_weight_bf16(E, "a0_wk", wbuf, [E.wload_war])
    ktoks = []
    sq_mm_toks = []
    for ec in range(NCH):
        for tb in range(NTB):
            i = (ec * NTB + tb) % 2
            pairs = [(wbuf[:, c * D + ec * 128: c * D + ec * 128 + 128],
                      E.xln[:, c * T + tb * TBW: c * T + tb * TBW + TBW]) for c in range(NCH)]
            tm = _mmgroup(E, E.ps[i][:, :], pairs, E.lntoks + [t_w, E.ps_war[i]])
            tv = p.op("vector", (lambda h, ec=ec, tb=tb, i=i: h.tensor_copy(
                KT[:, ec * T + tb * TBW: ec * T + tb * TBW + TBW], E.ps[i][:, :])), [tm])
            E.ps_war[i] = tv
            ktoks.append(tv)
    prev_sq = None
    for tb in range(NTB):
        mmdeps = [E.ps_war[2]]
        for ec in range(NCH):
            half = ec % 2
            tsq = p.op("scalar", (lambda h, ec=ec, tb=tb, half=half: h.activation(
                sqs[:, half * TBW:(half + 1) * TBW],
                KT[:, ec * T + tb * TBW: ec * T + tb * TBW + TBW], AF.Square)),
                [ktoks[ec * NTB + tb], prev_sq] + sq_mm_toks[-2:])
            tm = _mmgroup(E, E.ps[2][0:1, :], [(E.ones_bf[:, 0:1], sqs[:, half * TBW:(half + 1) * TBW])],
                          [tsq] + mmdeps, start_clear=(ec == 0), stop_group=(ec == NCH - 1))
            sq_mm_toks.append(tm)
            prev_sq = tsq
        tv = p.op("vector", (lambda h, tb=tb: h.tensor_copy(
            rowk[0:1, tb * TBW:(tb + 1) * TBW], E.ps[2][0:1, :])), [sq_mm_toks[-1]])
        E.ps_war[2] = tv
        ktoks.append(tv)
    t_km = p.op("vector", lambda h: h.reduce_max(kmax[0:1, 0:1], rowk[0:1, :], axis=AX.X),
                [ktoks[-1], ktoks[-2], ktoks[-3], ktoks[-4]])
    t_km2 = p.op("scalar", lambda h: h.activation(kmax[0:1, 1:2], kmax[0:1, 0:1], AF.Sqrt), [t_km])
    t_km3 = p.op("scalar", lambda h: h.mul(kmax[0:1, 2:3], kmax[0:1, 1:2], -1.0), [t_km2])

    t_wv = load_weight_bf16(E, "a0_wv", wbuf, ktoks + sq_mm_toks)
    vtoks = []
    for tc in range(16):
        for eb in range(2):
            i = (tc * 2 + eb) % 2
            pairs = [(E.xln[:, c * T + tc * 128: c * T + tc * 128 + 128],
                      wbuf[:, c * D + eb * TBW: c * D + eb * TBW + TBW]) for c in range(NCH)]
            tm = _mmgroup(E, E.ps[i][:, :], pairs, E.lntoks + [t_wv, E.ps_war[i]])
            tv = p.op("vector", (lambda h, tc=tc, eb=eb, i=i: h.tensor_copy(
                VT[:, tc * D + eb * TBW: tc * D + eb * TBW + TBW], E.ps[i][:, :])), [tm])
            E.ps_war[i] = tv
            vtoks.append(tv)

    if DEBUG:
        E.dbg["KT"] = (KT, ktoks)
        E.dbg["VT"] = (VT, vtoks)
        E.dbg["rowk"] = (rowk, ktoks[-4:])
    E.pool.pop()
    QTb = E.pool.alloc_bf16(NCH * QB)
    PT = E.pool.alloc_bf16(QB)
    rowq = E.pool.alloc_f32(QB)
    nbound = E.pool.alloc_bf16(QB)
    RB = E.pool.alloc_bf16(QB)
    ot1 = E.pool.alloc_bf16(QB)
    wq2 = E.pool.alloc_bf16(2 * D)
    wqin = declare_input(E, "a0_wq")
    wq_consumed = {0: vtoks, 1: vtoks}

    for qb in range(NQB):
        q0 = qb * QB
        tb = q0 // TBW
        qtoks = []
        for ec in range(NCH):
            i = ec % 2
            slot = ec % 2
            wsl = wq2[:, slot * D:(slot + 1) * D]
            t_wq = E.p.op("gpsimd", (lambda h, ec=ec, wsl=wsl: h.dma_start(
                out=wsl.rearrange("p (c e) -> p c e", c=NCH),
                in_=wqin.rearrange("(c p) e -> p c e", p=128)[:, :, ec * 128:(ec + 1) * 128])),
                wq_consumed[slot], dma=True)
            pairs = [(wsl[:, c * 128:(c + 1) * 128],
                      E.xln[:, c * T + q0: c * T + q0 + QB]) for c in range(NCH)]
            tm = _mmgroup(E, E.ps[i][:, 0:QB], pairs, E.lntoks + [t_wq, E.ps_war[i]])
            wq_consumed[slot] = [tm]
            tv = p.op("vector", (lambda h, ec=ec, i=i: h.tensor_scalar_mul(
                QTb[:, ec * QB:(ec + 1) * QB], E.ps[i][:, 0:QB], 1.0 / 32.0)), [tm])
            E.ps_war[i] = tv
            qtoks.append(tv)
        sqq = []
        for ec in range(NCH):
            half = ec % 2
            tsq = p.op("scalar", (lambda h, ec=ec, half=half: h.activation(
                sqs[:, half * TBW: half * TBW + QB], QTb[:, ec * QB:(ec + 1) * QB], AF.Square)),
                [qtoks[ec], prev_sq] + sq_mm_toks[-2:])
            tm = _mmgroup(E, E.ps[2][0:1, 0:QB], [(E.ones_bf[:, 0:1], sqs[:, half * TBW: half * TBW + QB])],
                          [tsq, E.ps_war[2]], start_clear=(ec == 0), stop_group=(ec == NCH - 1))
            sq_mm_toks.append(tm)
            prev_sq = tsq
        t_qr = p.op("scalar", lambda h: h.activation(rowq[0:1, :], E.ps[2][0:1, 0:QB], AF.Sqrt),
                    [sq_mm_toks[-1]])
        E.ps_war[2] = t_qr
        t_nb = p.op("vector", lambda h: h.tensor_scalar(nbound[0:1, :], rowq[0:1, :],
                                                        kmax[0:1, 2:3], None, ALU.mult),
                    [t_qr, t_km3])

        o_mms = []
        exps = []
        for kc in range(16):
            i = kc % 2
            pairs = [(KT[:, ec * T + kc * 128: ec * T + kc * 128 + 128],
                      QTb[:, ec * QB:(ec + 1) * QB]) for ec in range(NCH)]
            tm = _mmgroup(E, E.ps[i][:, 0:QB], pairs, [t_nb, E.ps_war[i]],
                          extra=[(E.ones_bf[0:1, :], nbound[0:1, :])])
            te = p.op("scalar", (lambda h, i=i: h.activation(PT, E.ps[i][:, 0:QB], AF.Exp)),
                      [tm] + o_mms[-8:])
            E.ps_war[i] = te
            exps.append(te)
            tden = _mmgroup(E, E.ps[6][0:1, 0:QB], [(E.ones_bf[:, 0:1], PT)],
                            [te, E.ps_war[6]], start_clear=(kc == 0), stop_group=(kc == 15))
            if kc == 15:
                E.ps_war[6] = tden
            o_mms = []
            for ec in range(NCH):
                bank = 2 + ec // 2
                off = (ec % 2) * QB
                tm2 = _mmgroup(E, E.ps[bank][:, off:off + QB],
                               [(VT[:, kc * D + ec * 128: kc * D + ec * 128 + 128], PT)],
                               [te, E.ps_war[bank]], start_clear=(kc == 0), stop_group=(kc == 15))
                o_mms.append(tm2)
            tdl = tden
        t_e = p.op("vector", lambda h: h.tensor_copy(E.row_t[0:1, 0:QB], E.ps[6][0:1, 0:QB]),
                   [tdl, E.row_war])
        E.ps_war[6] = t_e
        t_rc = p.op("vector", lambda h: h.reciprocal(E.row_m[0:1, 0:QB], E.row_t[0:1, 0:QB]), [t_e])
        t_rb = _mmgroup(E, E.ps[0][:, 0:QB], [(E.ones_f32[0:1, :], E.row_m[0:1, 0:QB])],
                        [t_rc, exps[-1], E.ps_war[0]])
        t_rbs = p.op("vector", lambda h: h.tensor_copy(RB, E.ps[0][:, 0:QB]), [t_rb])
        E.ps_war[0] = t_rbs
        E.row_war = t_rbs
        ot_war = None
        for ec in range(NCH):
            bank = 2 + ec // 2
            off = (ec % 2) * QB
            t1 = p.op("vector", (lambda h, bank=bank, off=off: h.tensor_tensor(
                ot1, E.ps[bank][:, off:off + QB], RB, ALU.mult)), [o_mms[ec], t_rbs, ot_war])
            tx = p.op("vector", (lambda h, ec=ec, q0=q0: h.tensor_tensor(
                E.x[:, ec * T + q0: ec * T + q0 + QB],
                E.x[:, ec * T + q0: ec * T + q0 + QB], ot1, ALU.add)),
                [t1, E.xtok.get((ec, tb))])
            ot_war = tx
            E.xtok[(ec, tb)] = tx
            E.ps_war[bank] = tx
    E.pool.pop()


SUBLAYERS = ("attn_a", "ffn0a", "attn_b", "ffn0b", "delta", "moe0", "mamba", "moe1")


def build_program(subs):
    nc = bass.Bass("TRN2", target_bir_lowering=False, debug=False)
    E = Env()
    E.nc = nc
    E.g = build_program.g
    E.innames = {}
    E.p = Prog(nc)
    p = E.p
    xin = nc.dram_tensor("xin", [D, T], F32, kind="ExternalInput").ap()
    xout = nc.dram_tensor("xout", [D, T], F32, kind="ExternalOutput").ap()

    with contextlib.ExitStack() as stack:
        e = stack.enter_context
        xsb = e(nc.sbuf_tensor("x_sb", [128, NCH * T], F32))
        E.x = xsb.ap()
        poolt = e(nc.sbuf_tensor("pool", [128, 73400], BF16))
        E.pool = Pool(poolt, 73400)
        E.ps = [e(nc.psum_tensor(f"psum{i}", [128, 512], F32)).ap() for i in range(7)]
        E.psb = e(nc.psum_tensor("psumb", [128, 1024], BF16)).ap()
        E.ps_war = {i: None for i in range(7)}
        E.psb_war = None
        E.ln_scratch_war = None
        E.row_m = E.pool.alloc_f32(TBW)[0:1, :]
        E.row_v = E.pool.alloc_f32(TBW)[0:1, :]
        E.row_r = E.pool.alloc_f32(TBW)[0:1, :]
        E.row_s = E.pool.alloc_f32(TBW)[0:1, :]
        E.row_t = E.pool.alloc_f32(TBW)[0:1, :]
        E.ones_bf = E.pool.alloc_bf16(128)
        E.onesr = E.pool.alloc_bf16(TBW)
        E.row_b1 = E.pool.alloc_bf16(TBW)[0:1, :]
        E.row_b2 = E.pool.alloc_bf16(TBW)[0:1, :]
        E.row_b_war = None
        cident = E.pool.alloc_bf16(128)
        E.ident = cident
        E.ones_f32 = E.pool.alloc_f32(128)
        E.row_war = None
        E.wload_war = None
        E.hsc_war = {}
        E.qsw_war = None
        E.rp1_war = None
        E.nb_war = None
        E.obuf_war = None
        E.msc_war = None
        E.mwb_war = None
        E.ot1_war = None
        E.gsc_war = None
        E.gall_war = None
        E.b2h_war = None
        E.selam_war = None
        E.sstmp_war = None
        E.ystg_war = None
        E.yint_war = None
        E.cbm_war = None
        E.sq_war = None
        E.rr_war = None
        E.vtk_war = None
        E.bcsc_war = None
        E.pmb_war = None
        E.zz_war = None
        E.at_war = None
        E.dstg2_war = None
        E.xtok = {}
        E.lntok = {}

        idin = declare_input(E, "c_ident")
        onin = declare_input(E, "c_ones")
        t_c1 = p.op("gpsimd", lambda h: h.dma_start(out=E.ones_bf, in_=onin[:, 0:128]), dma=True)
        t_c2 = p.op("gpsimd", lambda h: h.dma_start(out=cident, in_=idin[:, :]), dma=True)
        t_c3 = p.op("gpsimd", lambda h: h.dma_start(out=E.ones_f32, in_=onin[:, :]), dma=True)
        t_c4 = p.op("gpsimd", lambda h: h.memset(E.onesr[0:1, :], 1.0))
        E.const_tok = [t_c1, t_c2, t_c3, t_c4]

        t_x = p.op("sync", lambda h: h.dma_start(
            out=E.x.rearrange("p (c t) -> p c t", c=NCH),
            in_=xin.rearrange("(c p) t -> p c t", p=128)), dma=True)
        for c in range(NCH):
            for tb in range(NTB):
                E.xtok[(c, tb)] = t_x
        barrier(E)

        E.dbg = {}
        first = True
        for s in subs:
            if not first:
                barrier(E)
            first = False
            if s == 0:
                emit_attn_a(E)
            elif s == 1 or s == 3:
                emit_ffn0(E, s)
            elif s == 2:
                emit_attn_b(E)
            elif s == 4:
                emit_delta(E)
            elif s in (5, 7):
                emit_moe(E, s)
            elif s == 6:
                emit_mamba(E)

        for nm, (ap, toks) in E.dbg.items():
            od = nc.dram_tensor("dbg_" + nm, list(ap.shape), ap.dtype, kind="ExternalOutput").ap()
            E.p.op("sync", (lambda h, od=od, ap=ap: h.dma_start(out=od, in_=ap)), toks, dma=True)
        xdeps = [E.xtok[(c, tb)] for c in range(NCH) for tb in range(NTB)]
        t_o = p.op("sync", lambda h: h.dma_start(
            out=xout.rearrange("(c p) t -> p c t", p=128),
            in_=E.x.rearrange("p (c t) -> p c t", c=NCH)), xdeps, dma=True)
        p.op("sync", lambda h: h.nop(), [t_o])

        block = e(nc.Block())
        p.finalize(stack, block)
    return nc, set(E.innames.keys())


GROUPS = [[0, 1, 2, 3, 4, 5, 6, 7]]


def kernel(**inputs):
    g, per_core = prep_inputs(inputs)
    build_program.g = g
    x_cur = [pc["xin"] for pc in per_core]
    for subs in GROUPS:
        nc, names = build_program(subs)
        in_maps = []
        for b in range(CORES):
            m = {"xin": np.ascontiguousarray(x_cur[b], dtype=np.float32)}
            for nm in names:
                if nm != "xin":
                    m[nm] = g[nm]
            in_maps.append(m)
        res = run_bass_kernel_spmd(nc, in_maps, core_ids=list(range(CORES)))
        x_cur = [res.results[b]["xout"] for b in range(CORES)]
    return np.stack([x.T for x in x_cur]).astype(np.float32)


def emit_ffn0(E, s):
    p = E.p
    nc = E.nc
    R = 256
    agin = nc.dram_tensor(f"ffn_agin{s}", [D, T], BF16).ap()
    agout = nc.dram_tensor(f"ffn_agout{s}", [8 * D, T], BF16).ap()
    zin = nc.dram_tensor(f"ffn_zin{s}", [D, R], F32).ap()
    zout = nc.dram_tensor(f"ffn_zout{s}", [8 * D, R], F32).ap()
    pid = nc.partition_id()

    E.pool.push()
    E.pool.push()
    E.xln = E.pool.alloc_bf16(NCH * T)
    emit_ln(E)
    t_w = p.op("sync", (lambda h, x_=E.xln: h.dma_start(
        out=agin.rearrange("(c p) t -> p c t", p=128),
        in_=x_.rearrange("p (c t) -> p c t", c=NCH))), E.lntoks, dma=True)
    t_ag = p.collective(lambda h: h.collective_compute(
        "AllGather", ALU.bypass, replica_groups=[list(range(CORES))],
        ins=[agin[:, :]], outs=[agout[:, :]]), [t_w])
    E.pool.pop()

    wbase = E.pool.off
    wbufs, tw = {}, {}
    for nm in ("wf", "wi", "wc", "uf", "ui", "uc"):
        wbufs[nm] = E.pool.alloc_bf16(NCH * D)
        tw[nm] = load_weight_bf16(E, f"f0_{nm}", wbufs[nm], [t_w])
    bias, tb_ = {}, {}
    for nm in ("bf", "bi", "bc", "in_c", "bo"):
        bias[nm] = E.pool.alloc_f32(8)
        bin_ = declare_input(E, f"f0_{nm}")
        tb_[nm] = p.op("sync", (lambda h, b=bias[nm], bin_=bin_: h.dma_start(out=b, in_=bin_)), [t_w], dma=True)
    cbase = E.pool.off
    cst = E.pool.alloc_f32(2 * NCH * R)
    cbf = E.pool.alloc_bf16(2 * NCH * R)
    xgp = E.pool.alloc_bf16(NCH * R)
    gsc = E.pool.alloc_f32(3 * R)
    t1s = E.pool.alloc_f32(2 * R)

    def cs(sl, mc):
        return cst[:, (sl * NCH + mc) * R:(sl * NCH + mc) * R + R]

    def cb(sl, mc):
        return cbf[:, (sl * NCH + mc) * R:(sl * NCH + mc) * R + R]

    t_z = p.op("gpsimd", lambda h: h.memset(cst[:, 0:NCH * R], 0.0))
    ctoks = {}
    cbtoks = {}
    for mc in range(NCH):
        t0 = p.op("vector", (lambda h, mc=mc: h.tensor_scalar(
            cs(0, mc), cs(0, mc), bias["in_c"][:, mc:mc + 1], None, ALU.add)),
            [tb_["in_c"], t_z])
        ctoks[mc] = t0
        cbtoks[mc] = p.op("vector", (lambda h, mc=mc: h.tensor_copy(cb(0, mc), cs(0, mc))), [t0])

    ag4 = agout.rearrange("(b c p) t -> b p c t", p=128, c=NCH)
    xg_consumed = {0: [], 1: []}
    GATES = (("wf", "uf", "bf", AF.Sigmoid), ("wi", "ui", "bi", AF.Sigmoid), ("wc", "uc", "bc", AF.Tanh))
    for t in range(8):
        slot = 0
        rd, wr = t % 2, (t + 1) % 2
        xg = xgp
        t_xg = p.op("gpsimd", (lambda h, t=t, xg=xg: h.dma_start(
            out=xg.rearrange("p (c r) -> p c r", c=NCH),
            in_=ag4[t][:, :, bass.ts(pid, R)])), [t_ag] + xg_consumed[slot], dma=True)
        xg_consumed[slot] = []
        new_ctoks, new_cbtoks = {}, {}
        for mc in range(NCH):
            gtoks = []
            for gi, (wn, un, bn, fn) in enumerate(GATES):
                bank = gi
                pairs = [(wbufs[wn][:, cc * D + mc * 128: cc * D + mc * 128 + 128],
                          xg[:, cc * R:(cc + 1) * R]) for cc in range(NCH)]
                pairs += [(wbufs[un][:, cc * D + mc * 128: cc * D + mc * 128 + 128],
                           cb(rd, cc)) for cc in range(NCH)]
                deps = [t_xg, tw[wn], tw[un], E.ps_war[bank]] + [cbtoks[cc] for cc in range(NCH)]
                tm = _mmgroup(E, E.ps[bank][:, 0:R], pairs, deps)
                xg_consumed[slot].append(tm)
                ta = p.op("scalar", (lambda h, gi=gi, bank=bank, bn=bn, fn=fn, mc=mc: h.activation(
                    gsc[:, gi * R:(gi + 1) * R], E.ps[bank][:, 0:R], fn,
                    bias=bias[bn][:, mc:mc + 1])), [tm, tb_[bn]])
                E.ps_war[bank] = ta
                gtoks.append(ta)
            tm1 = p.op("vector", (lambda h, rd=rd, mc=mc: h.tensor_tensor(
                t1s[:, 0:R], gsc[:, 0:R], cs(rd, mc), ALU.mult)), [gtoks[0], ctoks[mc]])
            tm2 = p.op("vector", lambda h: h.tensor_tensor(
                t1s[:, R:2 * R], gsc[:, R:2 * R], gsc[:, 2 * R:3 * R], ALU.mult),
                [gtoks[1], gtoks[2], tm1])
            tc = p.op("vector", (lambda h, wr=wr, mc=mc: h.tensor_tensor(
                cs(wr, mc), t1s[:, 0:R], t1s[:, R:2 * R], ALU.add)),
                [tm2, new_ctoks.get(("war", mc))])
            new_ctoks[mc] = tc
            new_cbtoks[mc] = p.op("vector", (lambda h, wr=wr, mc=mc: h.tensor_copy(
                cb(wr, mc), cs(wr, mc))), [tc])
        ctoks, cbtoks = new_ctoks, new_cbtoks

    fin = 8 % 2
    uo = wbufs["uf"]
    wo = wbufs["ui"]
    t_uo = load_weight_bf16(E, "f0_uo", uo, list(xg_consumed[0]) + list(xg_consumed[1]))
    t_wo = load_weight_bf16(E, "f0_wo", wo, list(xg_consumed[0]) + list(xg_consumed[1]))
    ztoks = []
    prev_ts = None
    for oc in range(NCH):
        bank = oc % 2
        sl = oc % 2
        pairs = [(uo[:, cc * D + oc * 128: cc * D + oc * 128 + 128], cb(fin, cc)) for cc in range(NCH)]
        tm = _mmgroup(E, E.ps[bank][:, 0:R], pairs,
                      [t_uo, E.ps_war[bank]] + list(cbtoks.values()))
        tv = p.op("vector", (lambda h, sl=sl, bank=bank: h.tensor_copy(
            t1s[:, sl * R:(sl + 1) * R], E.ps[bank][:, 0:R])), [tm] + ztoks[-2:])
        ts_ = p.op("sync", (lambda h, oc=oc, sl=sl: h.dma_start(
            out=zin.rearrange("(c p) r -> p c r", p=128)[:, oc], in_=t1s[:, sl * R:(sl + 1) * R])),
            [tv], dma=True)
        E.ps_war[bank] = tv
        ztoks.append(ts_)
    t_agz = p.collective(lambda h: h.collective_compute(
        "AllGather", ALU.bypass, replica_groups=[list(range(CORES))],
        ins=[zin[:, :]], outs=[zout[:, :]]), ztoks)

    allmm = list(xg_consumed[0]) + list(xg_consumed[1]) + ztoks + [t_agz]
    zbf = E.pool.at(wbase, NCH * T)
    zbf4 = zbf.rearrange("p (c r t) -> p c r t", c=NCH, r=CORES)
    zout4 = zout.rearrange("(r c p) t -> r p c t", p=128, c=NCH)
    t_zls = []
    for r in range(CORES):
        t_zls.append(p.op("gpsimd", (lambda h, r=r: h.dma_start(
            out=zbf4[:, :, r], in_=zout4[r])), allmm, dma=True))
    t_zl = t_zls[-1]
    xlb = E.pool.at(wbase + 2 * NCH * D, 2 * NCH * TBW)
    hsc = E.pool.at(cbase, 4 * TBW).bitcast(F32)
    xl_consumed = {0: [], 1: []}
    agin4 = agin.rearrange("(c p) t -> p c t", p=128)
    for tb in range(NTB):
        slot = tb % 2
        xl = xlb[:, slot * NCH * TBW:(slot + 1) * NCH * TBW]
        t_xl = p.op("gpsimd", (lambda h, tb=tb, xl=xl: h.dma_start(
            out=xl.rearrange("p (c t) -> p c t", c=NCH),
            in_=agin4[:, :, tb * TBW:(tb + 1) * TBW])), xl_consumed[slot] + allmm, dma=True)
        xl_consumed[slot] = []
        for ec in range(NCH):
            bank = 2 + ec % 2
            hslot = ec % 2
            pairs = [(wo[:, cc * D + ec * 128: cc * D + ec * 128 + 128],
                      xl[:, cc * TBW:(cc + 1) * TBW]) for cc in range(NCH)]
            tm = _mmgroup(E, E.ps[bank][:, :], pairs, [t_xl, t_wo, E.ps_war[bank]])
            xl_consumed[slot].append(tm)
            ta = p.op("vector", (lambda h, ec=ec, tb=tb, bank=bank, hslot=hslot: h.tensor_tensor(
                hsc[:, hslot * TBW:(hslot + 1) * TBW], E.ps[bank][:, :],
                zbf[:, ec * T + tb * TBW: ec * T + tb * TBW + TBW], ALU.add)),
                [tm] + t_zls + [E.hsc_war.get(hslot)])
            th = p.op("scalar", (lambda h, ec=ec, hslot=hslot: h.activation(
                hsc[:, hslot * TBW:(hslot + 1) * TBW],
                hsc[:, hslot * TBW:(hslot + 1) * TBW], AF.Tanh,
                bias=bias["bo"][:, ec:ec + 1])), [ta, tb_["bo"]])
            E.ps_war[bank] = th
            tx = p.op("vector", (lambda h, ec=ec, tb=tb, hslot=hslot: h.tensor_tensor(
                E.x[:, ec * T + tb * TBW: ec * T + tb * TBW + TBW],
                E.x[:, ec * T + tb * TBW: ec * T + tb * TBW + TBW],
                hsc[:, hslot * TBW:(hslot + 1) * TBW], ALU.add)),
                [th, E.xtok.get((ec, tb))])
            E.hsc_war[hslot] = tx
            E.xtok[(ec, tb)] = tx
    E.pool.pop()


def emit_attn_b(E):
    p = E.p
    nc = E.nc
    E.pool.push()
    t1b = E.pool.alloc_bf16(T)
    t2b = E.pool.alloc_bf16(T)
    t_ms1 = p.op("gpsimd", lambda h: h.memset(t1b[64:128, :], 0.0))
    t_ms2a = p.op("gpsimd", lambda h: h.memset(t2b[32:64, :], 0.0))
    t_ms2b = p.op("gpsimd", lambda h: h.memset(t2b[64:128, :], 0.0))
    t_ms2 = t_ms2b
    E.pool.push()
    E.xln = E.pool.alloc_bf16(NCH * T)
    emit_ln(E)
    if DEBUG:
        _od = nc.dram_tensor("dbg_xlnb", [128, NCH * T], BF16, kind="ExternalOutput").ap()
        p.op("sync", (lambda h, _od=_od: h.dma_start(out=_od, in_=E.xln)), E.lntoks, dma=True)
    wqa = E.pool.alloc_bf16(NCH * 64)
    wkva = E.pool.alloc_bf16(NCH * 32)
    t_wqa = load_weight_bf16(E, "b0_wqa", wqa, E.lntoks)
    t_wkva = load_weight_bf16(E, "b0_wkva", wkva, E.lntoks)
    t1toks, t2toks = [], []
    for (dst, wsrc, wtok, w_, toks) in ((t1b, wqa, t_wqa, 64, t1toks), (t2b, wkva, t_wkva, 32, t2toks)):
        for tb in range(NTB):
            i = tb % 2
            pairs = [(wsrc[:, c * w_:(c + 1) * w_],
                      E.xln[:, c * T + tb * TBW: c * T + tb * TBW + TBW]) for c in range(NCH)]
            tm = _mmgroup(E, E.ps[i][0:w_, :], pairs,
                          E.lntoks + [wtok, E.ps_war[i], t_ms1, t_ms2a, t_ms2b])
            tv = p.op("vector", (lambda h, dst=dst, tb=tb, i=i, w_=w_: h.tensor_copy(
                dst[0:w_, tb * TBW:(tb + 1) * TBW], E.ps[i][0:w_, :])), [tm])
            E.ps_war[i] = tv
            toks.append(tv)
    if DEBUG:
        E.dbg["t1b"] = (t1b, t1toks)
        E.dbg["t2b"] = (t2b, t2toks)
    E.pool.pop()

    kT = E.pool.alloc_bf16(2 * T)
    vtok = E.pool.alloc_bf16(16 * 256)
    qT = E.pool.alloc_bf16(NCH * T)
    SQF = E.pool.alloc_bf16(TBW)
    rp1 = E.pool.alloc_bf16(T)
    qsw = E.pool.alloc_bf16(T)
    ctab = E.pool.alloc_bf16(T)
    stab = E.pool.alloc_bf16(T)
    masks = E.pool.alloc_bf16(4 * TBW)
    wkvb = E.pool.alloc_bf16(512)
    wqb = E.pool.alloc_bf16(D)
    wout = E.pool.alloc_bf16(NCH * D)
    boutb = E.pool.alloc_bf16(D)
    obuf = E.pool.alloc_bf16(NCH * TBW)
    PT = E.pool.alloc_bf16(2 * TBW)
    rkb = E.pool.alloc_f32(T)
    rowq = E.pool.alloc_f32(TBW)
    kmax = E.pool.alloc_f32(16)
    RBb = E.pool.alloc_bf16(TBW)
    oadd = E.pool.alloc_bf16(TBW)
    E.oadd_war = None
    E.rbb_war = None
    E.sqf_war = None

    prj = t1toks + t2toks
    cin = declare_input(E, "b0_ropec")
    sin_ = declare_input(E, "b0_ropes")
    min_ = declare_input(E, "b0_masks")
    t_ct = p.op("gpsimd", lambda h: h.dma_start(out=ctab, in_=cin[:, :]), prj, dma=True)
    t_st = p.op("gpsimd", lambda h: h.dma_start(out=stab, in_=sin_[:, :]), prj, dma=True)
    t_mk = p.op("gpsimd", lambda h: h.dma_start(out=masks, in_=min_[:, :]), prj, dma=True)
    wkvb_in = declare_input(E, "b0_wkvb")
    t_wkvb = p.op("gpsimd", lambda h: h.dma_start(out=wkvb[0:32, :], in_=wkvb_in[:, :]), prj, dma=True)
    wqb_in = declare_input(E, "b0_wqb")
    t_wqb = p.op("gpsimd", lambda h: h.dma_start(out=wqb[0:64, :], in_=wqb_in[:, :]), prj, dma=True)
    t_wout = load_weight_bf16(E, "b0_wout", wout, prj)
    bout_in = declare_input(E, "b0_bout")
    t_bout = p.op("gpsimd", lambda h: h.dma_start(out=boutb[0:1, :], in_=bout_in[:, :]), prj, dma=True)

    ktoks, vtoks = [], []
    for mc in range(2):
        for tb in range(NTB):
            i = (mc * NTB + tb) % 2
            tm = _mmgroup(E, E.ps[i][:, :],
                          [(wkvb[0:32, mc * 128:(mc + 1) * 128],
                            t2b[0:32, tb * TBW:(tb + 1) * TBW])],
                          t2toks + [t_wkvb, E.ps_war[i], t_ms2a, t_ms2b])
            tv = p.op("vector", (lambda h, mc=mc, tb=tb, i=i: h.tensor_copy(
                kT[:, mc * T + tb * TBW: mc * T + tb * TBW + TBW], E.ps[i][:, :])), [tm])
            E.ps_war[i] = tv
            ktoks.append(tv)
    for tc in range(16):
        i = tc % 2
        tm = _mmgroup(E, E.ps[i][:, 0:256],
                      [(t2b[0:32, tc * 128:(tc + 1) * 128], wkvb[0:32, 256:512])],
                      t2toks + [t_wkvb, E.ps_war[i], t_ms2a, t_ms2b])
        tv = p.op("vector", (lambda h, tc=tc, i=i: h.tensor_copy(
            vtok[:, tc * 256:(tc + 1) * 256], E.ps[i][:, 0:256])), [tm])
        E.ps_war[i] = tv
        vtoks.append(tv)
    qtoks = []
    for ec in range(NCH):
        for tb in range(NTB):
            i = (ec * NTB + tb) % 2
            tm = _mmgroup(E, E.ps[i][:, :],
                          [(wqb[0:64, ec * 128:(ec + 1) * 128],
                            t1b[0:64, tb * TBW:(tb + 1) * TBW])],
                          t1toks + [t_wqb, E.ps_war[i], t_ms1])
            tv = p.op("vector", (lambda h, ec=ec, tb=tb, i=i: h.tensor_scalar_mul(
                qT[:, ec * T + tb * TBW: ec * T + tb * TBW + TBW], E.ps[i][:, :], 0.125)), [tm])
            E.ps_war[i] = tv
            qtoks.append(tv)

    def rope_chunk(buf, c, deps):
        base = buf[:, c * T:(c + 1) * T]
        sw = []
        for (dst, srow) in ((0, 32), (32, 0), (64, 96), (96, 64)):
            sw.append(p.op("sync", (lambda h, dst=dst, srow=srow, base=base: h.dma_start(
                out=qsw[dst:dst + 32, :], in_=base[srow:srow + 32, :])), deps + [E.qsw_war], dma=True))
        tr1 = p.op("vector", (lambda h, base=base: h.tensor_tensor(rp1, base, ctab, ALU.mult)),
                   deps + [t_ct, E.rp1_war])
        tr2 = p.op("vector", lambda h: h.tensor_tensor(qsw, qsw, stab, ALU.mult), sw + [t_st])
        tr3 = p.op("vector", (lambda h, base=base: h.tensor_tensor(base, rp1, qsw, ALU.add)),
                   [tr1, tr2])
        E.qsw_war = tr3
        E.rp1_war = tr3
        return tr3
    E.qsw_war = None
    E.rp1_war = None
    rtoks = []
    for c in range(NCH):
        rtoks.append(rope_chunk(qT, c, qtoks))
    krtoks = []
    for c in range(2):
        krtoks.append(rope_chunk(kT, c, ktoks))
    if DEBUG:
        E.dbg["qT"] = (qT, rtoks)
        E.dbg["kT"] = (kT, krtoks)
        E.dbg["vtok"] = (vtok, vtoks)
    if PHASE <= 1:
        E.pool.pop()
        return

    sqk = []
    for c in range(2):
        sqk.append(p.op("scalar", (lambda h, c=c: h.activation(
            rp1 if c == 0 else qsw, kT[:, c * T:(c + 1) * T], AF.Square)),
            [krtoks[c], E.rp1_war]))
    kmtoks = []
    for g in range(4):
        b = (g % 2) * 64
        src_ = rp1 if g // 2 == 0 else qsw
        for tb in range(NTB):
            tm = _mmgroup(E, E.ps[5][0:1, :],
                          [(E.ones_bf[b:b + 64, 0:1], src_[b:b + 64, tb * TBW:(tb + 1) * TBW])],
                          [sqk[g // 2], E.ps_war[5]], tile_position=(b, 0), start_clear=True)
            tv = p.op("vector", (lambda h, tb=tb: h.tensor_copy(
                rkb[0:1, tb * TBW:(tb + 1) * TBW], E.ps[5][0:1, :])), [tm])
            E.ps_war[5] = tv
            kmtoks.append(tv)
        t_m = p.op("vector", (lambda h, g=g: h.reduce_max(
            kmax[0:1, g:g + 1], rkb[0:1, :], axis=AX.X)), kmtoks[-4:])
        kmtoks.append(p.op("scalar", (lambda h, g=g: h.activation(
            kmax[0:1, 4 + g:5 + g], kmax[0:1, g:g + 1], AF.Sqrt)), [t_m]))
        kmtoks.append(p.op("scalar", (lambda h, g=g: h.mul(
            kmax[0:1, 8 + g:9 + g], kmax[0:1, 4 + g:5 + g], -1.0)), [kmtoks[-1]]))

    if DEBUG:
        E.dbg["kmax"] = (kmax, kmtoks[-2:])
    if PHASE <= 2:
        E.pool.pop()
        return
    for qb in range(NTB):
        oproj_deps = []
        for hh in range(16):
            ch = (hh % 4) + 4 * (hh // 8)
            half = (hh // 4) % 2
            b = half * 64
            g = hh // 4
            tsq = p.op("scalar", (lambda h, ch=ch, qb=qb, b=b: h.activation(
                SQF[b:b + 64, :], qT[:, ch * T + qb * TBW: ch * T + qb * TBW + TBW][b:b + 64, :],
                AF.Square)), [rtoks[ch], E.sqf_war])
            tqm = _mmgroup(E, E.ps[5][0:1, :],
                           [(E.ones_bf[b:b + 64, 0:1], SQF[b:b + 64, :])],
                           [tsq, E.ps_war[5]], tile_position=(b, 0))
            E.sqf_war = tqm
            tqr = p.op("scalar", lambda h: h.activation(rowq[0:1, :], E.ps[5][0:1, :], AF.Sqrt), [tqm])
            E.ps_war[5] = tqr
            tnb = p.op("vector", (lambda h, g=g: h.tensor_scalar(
                PT[0:1, TBW:TBW + TBW], rowq[0:1, :], kmax[0:1, 8 + g:9 + g], None, ALU.mult)),
                [tqr, kmtoks[-1], E.nb_war])
            nbrow = PT[0:1, TBW:2 * TBW]
            nkc = 4 * qb + 4
            omms = []
            for kc in range(nkc):
                i = kc % 2
                pairs = [(kT[:, g * T // 2 + kc * 128: g * T // 2 + kc * 128 + 128][b:b + 64, :]
                          if False else
                          kT[:, (g // 2) * T + kc * 128:(g // 2) * T + kc * 128 + 128][b:b + 64, :],
                          qT[:, ch * T + qb * TBW: ch * T + qb * TBW + TBW][b:b + 64, :])]
                tm = _mmgroup(E, E.ps[i][:, :], pairs, [rtoks[ch], krtoks[g // 2], tnb, E.ps_war[i]],
                              extra=[(E.ones_bf[0:1, :], nbrow)], tile_position=(b, 0))
                if kc >= 4 * qb:
                    r = kc - 4 * qb
                    tm = p.op("vector", (lambda h, i=i, r=r: h.tensor_tensor(
                        E.ps[i][:, :], E.ps[i][:, :], masks[:, r * TBW:(r + 1) * TBW], ALU.add)),
                        [tm, t_mk])
                te = p.op("scalar", (lambda h, i=i, kc=kc: h.activation(
                    PT[:, (kc % 2) * TBW:(kc % 2) * TBW + TBW] if False else PT[:, 0:TBW],
                    E.ps[i][:, :], AF.Exp)), [tm] + omms)
                E.ps_war[i] = te
                tden = _mmgroup(E, E.ps[6][0:1, :], [(E.ones_bf[:, 0:1], PT[:, 0:TBW])],
                                [te, E.ps_war[6]], start_clear=(kc == 0), stop_group=(kc == nkc - 1))
                tO = _mmgroup(E, E.ps[2][0:64, :],
                              [(vtok[:, kc * 256 + g * 64: kc * 256 + g * 64 + 64], PT[:, 0:TBW])],
                              [te, E.ps_war[2]], start_clear=(kc == 0), stop_group=(kc == nkc - 1))
                omms = [tO, tden]
            E.ps_war[6] = omms[1]
            t_e = p.op("vector", lambda h: h.tensor_copy(E.row_t, E.ps[6][0:1, :]),
                       [omms[1], E.row_war])
            E.ps_war[6] = t_e
            t_rc = p.op("vector", lambda h: h.reciprocal(E.row_m, E.row_t), [t_e])
            t_rb = _mmgroup(E, E.ps[4][0:64, :], [(E.ones_f32[0:1, 0:64], E.row_m)],
                            [t_rc, E.ps_war[4]])
            t_rbs = p.op("vector", lambda h: h.tensor_copy(RBb[0:64, :], E.ps[4][0:64, :]),
                         [t_rb, E.rbb_war])
            t_o = p.op("vector", (lambda h, ch=ch, b=b: h.tensor_tensor(
                obuf[:, ch * TBW:(ch + 1) * TBW][b:b + 64, :],
                E.ps[2][0:64, :], RBb[0:64, :], ALU.mult)), [t_rbs, omms[0], E.obuf_war])
            E.rbb_war = t_o
            E.ps_war[4] = t_o
            E.ps_war[2] = t_o
            E.row_war = t_o
            E.nb_war = t_o
            oproj_deps.append(t_o)
        if PHASE in (21, 22, 23):
            break
        for ec in range(NCH):
            bank = 3 + ec % 2
            pairs = [(wout[:, cc * D + ec * 128: cc * D + ec * 128 + 128],
                      obuf[:, cc * TBW:(cc + 1) * TBW]) for cc in range(NCH)]
            tm = _mmgroup(E, E.ps[bank][:, :], pairs,
                          oproj_deps + [t_wout, t_bout, E.const_tok[-1], E.ps_war[bank]],
                          extra=[(boutb[0:1, ec * 128:(ec + 1) * 128], E.onesr[0:1, :])])
            tcp = p.op("vector", (lambda h, bank=bank: h.tensor_copy(oadd, E.ps[bank][:, :])),
                       [tm, E.oadd_war])
            tx = p.op("vector", (lambda h, ec=ec, qb=qb: h.tensor_tensor(
                E.x[:, ec * T + qb * TBW: ec * T + qb * TBW + TBW],
                E.x[:, ec * T + qb * TBW: ec * T + qb * TBW + TBW],
                oadd, ALU.add)), [tcp, E.xtok.get((ec, qb))])
            E.oadd_war = tx
            E.ps_war[bank] = tcp
            E.xtok[(ec, qb)] = tx
            E.obuf_war = tx
    E.pool.pop()


def emit_moe(E, s):
    p = E.p
    nc = E.nc
    xld = nc.dram_tensor(f"moe_xln{s}", [D, T], BF16).ap()
    E.pool.push()
    mw8 = E.pool.alloc_bf16(T)
    E.pool.push()
    E.xln = E.pool.alloc_bf16(NCH * T)
    emit_ln(E)
    wgb = E.pool.alloc_bf16(NCH * 32)
    lg = E.pool.alloc_bf16(T)
    ltok = E.pool.alloc_bf16(16 * 32)
    msc = E.pool.alloc_f32(4 * 16)
    prb = E.pool.alloc_f32(16 * 32)
    prbb = E.pool.alloc_bf16(16 * 32)
    bgr = E.pool.alloc_bf16(32)
    t_wg = load_weight_bf16(E, "m1_wg", wgb, E.lntoks)
    bg_in = declare_input(E, "m1_bg")
    t_bg = p.op("gpsimd", lambda h: h.dma_start(out=bgr[0:1, 0:32], in_=bg_in[:, :]), E.lntoks, dma=True)
    lgt = []
    for tb in range(NTB):
        i = tb % 2
        pairs = [(wgb[:, c * 32:(c + 1) * 32],
                  E.xln[:, c * T + tb * TBW: c * T + tb * TBW + TBW]) for c in range(NCH)]
        tm = _mmgroup(E, E.ps[i][0:32, :], pairs, E.lntoks + [t_wg, t_bg, E.ps_war[i]],
                      extra=[(bgr[0:1, 0:32], E.onesr[0:1, :])])
        tv = p.op("vector", (lambda h, tb=tb, i=i: h.tensor_copy(
            lg[0:32, tb * TBW:(tb + 1) * TBW], E.ps[i][0:32, :])), [tm])
        E.ps_war[i] = tv
        lgt.append(tv)
    if PHASE == 411:
        E.pool.pop()
        E.pool.pop()
        return
    def all_tr(h):
        last = None
        for tc in range(16):
            last = h.transpose(E.psb[:, tc * 32:(tc + 1) * 32],
                               lg[0:32, tc * 128:(tc + 1) * 128], E.ident[0:32, 0:32])
        return last
    ttr = p.op("tensor", all_tr, lgt + [E.const_tok[1], E.psb_war])
    tte = p.op("vector", lambda h: h.tensor_copy(ltok[:, 0:512], E.psb[:, 0:512]), [ttr])
    ttk = [tte] * 16
    E.psb_war = tte
    if PHASE == 412:
        E.pool.pop()
        E.pool.pop()
        return
    mw_toks = []
    for tc in range(16):
        sl = ltok[:, tc * 32:(tc + 1) * 32]
        po = prb[:, tc * 32:(tc + 1) * 32]
        pb = prbb[:, tc * 32:(tc + 1) * 32]
        m1_ = msc[:, tc:tc + 1]
        t0 = p.op("vector", (lambda h, sl=sl, m1_=m1_: h.reduce_max(m1_, sl, axis=AX.X, negate=True)),
                  [ttk[tc], E.msc_war])
        t1 = p.op("scalar", (lambda h, sl=sl, po=po, m1_=m1_: h.activation(
            po, sl, AF.Exp, bias=m1_)), [t0])
        t2 = p.op("vector", (lambda h, po=po, tc=tc: h.reduce_sum(
            msc[:, 16 + tc:17 + tc], po, axis=AX.X)), [t1])
        t3 = p.op("vector", (lambda h, tc=tc: h.reciprocal(
            msc[:, 32 + tc:33 + tc], msc[:, 16 + tc:17 + tc])), [t2])
        t6 = p.op("vector", (lambda h, po=po, m1_=m1_: h.reduce_max(m1_, po, axis=AX.X)), [t1, t0])
        t7 = p.op("scalar", (lambda h, tc=tc, m1_=m1_: h.copy(msc[:, 48 + tc:49 + tc], m1_)), [t6])
        t8 = p.op("vector", (lambda h, po=po, tc=tc: h.tensor_scalar(
            po, po, msc[:, 48 + tc:49 + tc], None, ALU.is_ge)), [t7, t3])
        t9 = p.op("scalar", (lambda h, tc=tc: h.mul(msc[:, tc:tc + 1],
                                                    msc[:, 32 + tc:33 + tc], 1.0)), [t3, t6, t7])
        ta_ = p.op("vector", (lambda h, tc=tc: h.tensor_tensor(
            msc[:, 16 + tc:17 + tc], msc[:, tc:tc + 1], msc[:, 48 + tc:49 + tc], ALU.mult)),
            [t9, t7])
        tb2 = p.op("scalar", (lambda h, tc=tc: h.copy(msc[:, 32 + tc:33 + tc],
                                                      msc[:, 16 + tc:17 + tc])), [ta_])
        tc_ = p.op("vector", (lambda h, po=po, pb=pb, tc=tc: h.tensor_scalar(
            pb, po, msc[:, 32 + tc:33 + tc], None, ALU.mult)), [tb2, t8])
        mw_toks.append(tc_)
    E.msc_war = mw_toks[-1]
    if PHASE == 413:
        E.pool.pop()
        E.pool.pop()
        return
    mw8t = []
    for rc in range(2):
        def all_tr2(h, rc=rc):
            last = None
            for j in range(8):
                tc = rc * 8 + j
                last = h.transpose(E.psb[0:32, j * 128:(j + 1) * 128],
                                   prbb[:, tc * 32:(tc + 1) * 32], E.ident)
            return last
        ttr = p.op("tensor", all_tr2, mw_toks + [E.const_tok[1], E.psb_war])
        tv = p.op("vector", (lambda h, rc=rc: h.tensor_copy(
            mw8[0:32, rc * 1024:(rc + 1) * 1024], E.psb[0:32, 0:1024])), [ttr])
        E.psb_war = tv
        mw8t.append(tv)
    t_xd = p.op("sync", (lambda h, x_=E.xln: h.dma_start(
        out=xld.rearrange("(c p) t -> p c t", p=128),
        in_=x_.rearrange("p (c t) -> p c t", c=NCH))), E.lntoks, dma=True)
    if DEBUG:
        E.dbg["mw8"] = (mw8[0:32, :], mw8t)
    E.pool.pop()
    if PHASE == 41:
        E.pool.pop()
        return

    xlr = E.pool.alloc_bf16(NCH * TBW)
    hbuf = E.pool.alloc_bf16(32 * TBW)
    w1b = E.pool.alloc_bf16(2 * NCH * 2048)
    w2b = E.pool.alloc_bf16(2 * NCH * 256)
    mwb = E.pool.alloc_bf16(TBW)
    ot1 = E.pool.alloc_bf16(TBW)
    selb = E.pool.alloc_bf16(1024)
    b1b = E.pool.alloc_f32(8 * 32)
    b2b = E.pool.alloc_bf16(1024)
    sel_in = declare_input(E, "m1_sel")
    b1_in = declare_input(E, "m1_b1")
    b2_in = declare_input(E, "m1_b2")
    gdeps = mw8t + [t_xd]
    t_sel = p.op("gpsimd", lambda h: h.dma_start(out=selb[0:8, :], in_=sel_in[:, :]), gdeps, dma=True)
    t_b1 = p.op("gpsimd", lambda h: h.dma_start(
        out=b1b.rearrange("p (e c) -> p e c", e=8), in_=b1_in.rearrange("e (p c) -> p e c", p=128)),
        gdeps, dma=True)
    t_b2 = p.op("gpsimd", lambda h: h.dma_start(out=b2b[0:8, :], in_=b2_in[:, :]), gdeps, dma=True)
    w1_in = declare_input(E, "m1_w1")
    w2_in = declare_input(E, "m1_w2")
    w1_4 = w1_in.rearrange("e (c p) m -> e p c m", p=128)
    w2_4 = w2_in.rearrange("e (c p) m -> e p c m", p=128)
    xld4 = xld.rearrange("(c p) t -> p c t", p=128)

    w1tok = {0: gdeps, 1: gdeps}
    w2tok = {0: gdeps, 1: gdeps}
    xltok = {0: gdeps, 1: gdeps}
    for e in range(8):
        t_w1 = {}
        for hf in range(2):
            t_w1[hf] = p.op("gpsimd", (lambda h, e=e, hf=hf: h.dma_start(
                out=w1b[:, hf * NCH * 2048:(hf + 1) * NCH * 2048].rearrange("p (c m) -> p c m", c=NCH),
                in_=w1_4[e][:, :, hf * 2048:(hf + 1) * 2048])), w1tok[hf], dma=True)
        w1reads = {0: [], 1: []}
        for tb in range(NTB):
            slot = 0
            xl = xlr
            t_xl = p.op("sync", (lambda h, tb=tb, xl=xl: h.dma_start(
                out=xl.rearrange("p (c t) -> p c t", c=NCH),
                in_=xld4[:, :, tb * TBW:(tb + 1) * TBW])), xltok[slot] + [t_xd], dma=True)
            xltok[slot] = []
            htoks = []
            for hc in range(32):
                hf = hc // 16
                i = hc % 2
                wsl = w1b[:, hf * NCH * 2048:(hf + 1) * NCH * 2048]
                hc_ = hc % 16
                pairs = [(wsl[:, c * 2048 + hc_ * 128: c * 2048 + hc_ * 128 + 128],
                          xl[:, c * TBW:(c + 1) * TBW]) for c in range(NCH)]
                tm = _mmgroup(E, E.ps[i][:, :], pairs, [t_xl, t_w1[hf], t_b1, E.ps_war[i]])
                xltok[slot].append(tm)
                w1reads[hf].append(tm)
                th = p.op("scalar", (lambda h, hc=hc, i=i, e=e: h.activation(
                    hbuf[:, hc * TBW:(hc + 1) * TBW], E.ps[i][:, :], AF.Silu,
                    bias=b1b[:, e * 32 + hc: e * 32 + hc + 1])), [tm])
                E.ps_war[i] = th
                htoks.append(th)
            mb = _mmgroup(E, E.ps[6][:, :], [(selb[0:8, e * 128:(e + 1) * 128],
                                              mw8[0:8, tb * TBW:(tb + 1) * TBW])],
                          [t_sel, E.ps_war[6]] + mw8t)
            mvb = p.op("vector", lambda h: h.tensor_copy(mwb, E.ps[6][:, :]), [mb, E.mwb_war])
            E.ps_war[6] = mvb
            for grp in range(4):
                otoks = []
                for Kq in range(4):
                    wslot = (grp * 4 + Kq) % 2
                    t_w2 = p.op("gpsimd", (lambda h, e=e, grp=grp, Kq=Kq, wslot=wslot: h.dma_start(
                        out=w2b[:, wslot * NCH * 256:(wslot + 1) * NCH * 256].rearrange(
                            "p (c m) -> p c m", c=NCH),
                        in_=w2_4[e][:, Kq * 8:(Kq + 1) * 8, grp * 256:(grp + 1) * 256])),
                        w2tok[wslot], dma=True)
                    w2tok[wslot] = []
                    for dh in range(2):
                        dc = grp * 2 + dh
                        bank = 2 + dh
                        wsl = w2b[:, wslot * NCH * 256:(wslot + 1) * NCH * 256]
                        pairs = [(wsl[:, kc * 256 + dh * 128: kc * 256 + dh * 128 + 128],
                                  hbuf[:, (Kq * 8 + kc) * TBW:(Kq * 8 + kc + 1) * TBW])
                                 for kc in range(8)]
                        tm = _mmgroup(E, E.ps[bank][:, :], pairs,
                                      htoks + [t_w2, E.ps_war[bank]], start_clear=(Kq == 0),
                                      stop_group=(Kq == 3))
                        w2tok[wslot].append(tm)
                        otoks.append(tm)
                for dh in range(2):
                    dc = grp * 2 + dh
                    bank = 2 + dh
                    t1 = p.op("vector", (lambda h, bank=bank: h.tensor_tensor(
                        ot1, E.ps[bank][:, :], mwb, ALU.mult)),
                        [otoks[-2 + dh], mvb, E.ot1_war])
                    tx = p.op("vector", (lambda h, dc=dc, tb=tb: h.tensor_tensor(
                        E.x[:, dc * T + tb * TBW: dc * T + tb * TBW + TBW],
                        E.x[:, dc * T + tb * TBW: dc * T + tb * TBW + TBW], ot1, ALU.add)),
                        [t1, E.xtok.get((dc, tb))])
                    E.ot1_war = tx
                    E.ps_war[bank] = tx
                    E.xtok[(dc, tb)] = tx
                E.mwb_war = E.ot1_war
        w1tok = {0: w1reads[0], 1: w1reads[1]}
        if PHASE == 42:
            break
    for tb in range(NTB):
        for dc in range(NCH):
            bank = 2 + dc % 2
            tm = _mmgroup(E, E.ps[bank][:, :],
                          [(b2b[0:8, dc * 128:(dc + 1) * 128], mw8[0:8, tb * TBW:(tb + 1) * TBW])],
                          [t_b2, E.ps_war[bank]] + mw8t)
            t1 = p.op("vector", (lambda h, bank=bank: h.tensor_copy(ot1, E.ps[bank][:, :])),
                      [tm, E.ot1_war])
            tx = p.op("vector", (lambda h, dc=dc, tb=tb: h.tensor_tensor(
                E.x[:, dc * T + tb * TBW: dc * T + tb * TBW + TBW],
                E.x[:, dc * T + tb * TBW: dc * T + tb * TBW + TBW], ot1, ALU.add)),
                [t1, E.xtok.get((dc, tb))])
            E.ot1_war = tx
            E.ps_war[bank] = tx
            E.xtok[(dc, tb)] = tx
    E.pool.pop()


def emit_mamba(E):
    p = E.p
    nc = E.nc
    NH, PP, NN = 32, 64, 64
    DIN = 2048
    gated = nc.dram_tensor("mb_gate", [DIN, T], BF16).ap()
    ssmd = nc.dram_tensor("mb_ssm", [DIN, T], BF16).ap()
    htd = nc.dram_tensor("mb_htok", [T, DIN], BF16).ap()
    ytd = nc.dram_tensor("mb_yfm", [DIN, T], BF16).ap()
    ymd = nc.dram_tensor("mb_ymix", [DIN, T], BF16).ap()

    E.pool.push()
    bcT = E.pool.alloc_bf16(T)
    dtb = E.pool.alloc_bf16(T)
    lcb = E.pool.alloc_bf16(T)
    mlcb = E.pool.alloc_bf16(T)
    elamb = E.pool.alloc_bf16(16 * 32)
    lamr = E.pool.alloc_f32(16 * 32)
    lamb16 = E.pool.alloc_bf16(16 * 32)
    E.pool.push()
    xln1 = E.pool.alloc_bf16(NCH * T)
    E.xln = xln1
    emit_ln(E)
    ln1toks = list(E.lntoks)
    xnd = nc.dram_tensor("mb_ln1", [D, T], BF16).ap()
    t_x1 = p.op("sync", (lambda h, x_=xln1: h.dma_start(
        out=xnd.rearrange("(c p) t -> p c t", p=128),
        in_=x_.rearrange("p (c t) -> p c t", c=NCH))), ln1toks, dma=True)
    xn = E.pool.alloc_bf16(NCH * T)
    E.xln = xn
    emit_ln(E, xsrc=xln1, xsrc_tok=ln1toks[-1], out=xn, rms=True)
    dtrF = E.pool.alloc_f32(T)
    wrng = E.pool.alloc_bf16(2 * NCH * TBW)
    stg = E.pool.alloc_bf16(TBW)
    win_in = declare_input(E, "s1_win")
    win4 = win_in.rearrange("(c p) m -> p c m", p=128)
    wtok = {0: E.lntoks, 1: E.lntoks}
    bct = []
    dtt = []
    stg_war = None
    for blk in range(9):
        ncols = 512 if blk < 8 else 256
        slot = blk % 2
        wsl = wrng[:, slot * NCH * TBW: slot * NCH * TBW + NCH * ncols]
        t_w = p.op("gpsimd", (lambda h, blk=blk, wsl=wsl, ncols=ncols: h.dma_start(
            out=wsl.rearrange("p (c m) -> p c m", c=NCH),
            in_=win4[:, :, blk * 512: blk * 512 + ncols])), wtok[slot], dma=True)
        wtok[slot] = []
        for j in range(ncols // 128):
            oc = blk * 4 + j
            for tb in range(NTB):
                i = tb % 2
                pairs = [(wsl[:, c * ncols + j * 128: c * ncols + j * 128 + 128],
                          xn[:, c * T + tb * TBW: c * T + tb * TBW + TBW]) for c in range(NCH)]
                tm = _mmgroup(E, E.ps[i][:, :], pairs, E.lntoks + [t_w, E.ps_war[i]])
                wtok[slot].append(tm)
                if oc == 0:
                    tv = p.op("vector", (lambda h, tb=tb, i=i: h.tensor_copy(
                        bcT[:, tb * TBW:(tb + 1) * TBW], E.ps[i][:, :])), [tm])
                    bct.append(tv)
                elif oc == 1:
                    tv = p.op("vector", (lambda h, tb=tb, i=i: h.tensor_copy(
                        dtrF[0:32, tb * TBW:(tb + 1) * TBW], E.ps[i][0:32, :])), [tm])
                    dtt.append(tv)
                else:
                    dst = gated if oc < 18 else ssmd
                    fc = oc - 2 if oc < 18 else oc - 18
                    tv = p.op("vector", (lambda h, i=i: h.tensor_copy(stg, E.ps[i][:, :])),
                              [tm, stg_war])
                    ts_ = p.op("sync", (lambda h, dst=dst, fc=fc, tb=tb: h.dma_start(
                        out=dst.rearrange("(c p) t -> p c t", p=128)[:, fc, tb * TBW:(tb + 1) * TBW],
                        in_=stg)), [tv], dma=True)
                    stg_war = ts_
                E.ps_war[i] = tv
    sc32 = E.pool.alloc_f32(T)
    sc32b = E.pool.alloc_f32(T)
    dtb_in = declare_input(E, "s1_dtb")
    al_in = declare_input(E, "s1_alog")
    cols = E.pool.alloc_f32(8)
    t_c = p.op("gpsimd", lambda h: h.dma_start(out=cols[0:32, 0:1], in_=dtb_in[:, :]), dma=True)
    t_a = p.op("gpsimd", lambda h: h.dma_start(out=cols[0:32, 1:2], in_=al_in[:, :]), dma=True)
    dtv = dtrF[0:32, :]
    s_a = sc32[0:32, :]
    s_b = sc32b[0:32, :]
    t1 = p.op("vector", lambda h: h.tensor_scalar(dtv, dtv, cols[0:32, 0:1], None, ALU.add),
              dtt + [t_c])
    t2 = p.op("scalar", lambda h: h.activation(s_a, dtv, AF.Abs), [t1])
    t3 = p.op("scalar", lambda h: h.activation(s_b, s_a, AF.Exp, scale=-1.0), [t2])
    t4 = p.op("vector", lambda h: h.tensor_scalar_add(s_b, s_b, 1.0), [t3])
    t5 = p.op("scalar", lambda h: h.activation(s_b, s_b, AF.Ln), [t4])
    t6 = p.op("vector", lambda h: h.tensor_relu(s_a, dtv), [t2, t5])
    t7 = p.op("vector", lambda h: h.tensor_tensor(dtv, s_a, s_b, ALU.add), [t6, t5])
    t15 = p.op("vector", lambda h: h.tensor_copy(dtb[0:32, :], dtv), [t7])
    t9 = p.op("scalar", lambda h: h.activation(cols[0:32, 2:3], cols[0:32, 1:2], AF.Exp), [t_a])
    t10 = p.op("scalar", lambda h: h.mul(cols[0:32, 3:4], cols[0:32, 2:3], -1.0), [t9])
    t11 = p.op("vector", lambda h: h.tensor_scalar(s_a, dtv, cols[0:32, 3:4], None, ALU.mult),
               [t10, t6, t15])
    lct = []
    lam_t = []
    for c_ in range(16):
        lct.append(p.op("vector", (lambda h, c_=c_: h.tensor_tensor_scan(
            s_b[:, c_ * 128:(c_ + 1) * 128], E.ones_f32[0:32, :],
            s_a[:, c_ * 128:(c_ + 1) * 128], 0.0, ALU.mult, ALU.add)), [t11, E.const_tok[2], t5]))
        lam_t.append(p.op("vector", (lambda h, c_=c_: h.reduce_sum(
            dtrF[32:64, c_:c_ + 1], s_a[:, c_ * 128:(c_ + 1) * 128], axis=AX.X)), [t11]))
    t13 = p.op("vector", lambda h: h.tensor_copy(lcb[0:32, :], s_b[0:32, :]), lct)
    t14 = p.op("vector", lambda h: h.tensor_scalar_mul(mlcb[0:32, :], s_b[0:32, :], -1.0), lct)
    tds = []
    for c_ in range(16):
        tds.append(p.op("sync", (lambda h, c_=c_: h.dma_start(
            out=lamr[0:1, c_ * 32:(c_ + 1) * 32], in_=dtrF[32:64, c_:c_ + 1])), lam_t, dma=True))
    te_ = p.op("scalar", lambda h: h.activation(elamb[0:1, 0:512], lamr[0:1, 0:512], AF.Exp), tds)
    tlb16 = p.op("vector", lambda h: h.tensor_copy(lamb16[0:1, 0:512], lamr[0:1, 0:512]), tds)
    E.mbtok = dict(rows=[t13, t14, t15, te_], proj=list(wtok[0]) + list(wtok[1]) + [stg_war],
                   bct=bct, xnd=t_x1, lamb16=tlb16)
    E.mbtok["xnd_ap"] = xnd
    E.mb = (bcT, dtb, lcb, mlcb, elamb, lamb16)
    E.pool.pop()

    barrier(E)
    E.pool.push()
    cwcol = E.pool.alloc_f32(64)
    cbcol = E.pool.alloc_f32(16)
    cw_in = declare_input(E, "s1_cw")
    cb_in = declare_input(E, "s1_cb")
    prj = E.mbtok["proj"]
    t_cw = p.op("gpsimd", lambda h: h.dma_start(out=cwcol, in_=cw_in), prj, dma=True)
    t_cb = p.op("gpsimd", lambda h: h.dma_start(out=cbcol, in_=cb_in), prj, dma=True)
    spad = E.pool.alloc_bf16(T + 16)
    diag = E.pool.alloc_bf16(4 * 128)
    hTc = E.pool.alloc_bf16(T)
    hstg = E.pool.alloc_bf16(1024)
    sp_war, dg_war, hs_war = None, None, None
    ssm4 = ssmd.rearrange("(c p) t -> p c t", p=128)
    ht3 = htd.rearrange("(tc p) f -> p tc f", p=128)
    ht_done = []
    for fc in range(16):
        t_z = p.op("gpsimd", (lambda h: h.memset(spad[:, 0:16], 0.0)), [sp_war])
        t_l = p.op("sync", (lambda h, fc=fc: h.dma_start(
            out=spad[:, 13:13 + T], in_=ssm4[:, fc])), [t_z, sp_war] + prj, dma=True)
        dts = []
        for tap in range(4):
            dts.append(p.op("vector", (lambda h, fc=fc, tap=tap: h.tensor_scalar(
                diag[:, tap * 128:(tap + 1) * 128], E.ident,
                cwcol[:, fc * 4 + tap: fc * 4 + tap + 1], None, ALU.mult)),
                [t_cw, E.const_tok[1], dg_war]))
        httoks = []
        tm = None
        for tb in range(NTB):
            i = tb % 2
            pairs = [(diag[:, tap * 128:(tap + 1) * 128],
                      spad[:, 10 + tap + tb * TBW: 10 + tap + tb * TBW + TBW]) for tap in range(4)]
            tm = _mmgroup(E, E.ps[i][:, :], pairs, dts + [t_l, E.ps_war[i], t_cb])
            th = p.op("scalar", (lambda h, tb=tb, i=i, fc=fc: h.activation(
                hTc[:, tb * TBW:(tb + 1) * TBW], E.ps[i][:, :], AF.Silu,
                bias=cbcol[:, fc:fc + 1])), [tm, hs_war])
            E.ps_war[i] = th
            httoks.append(th)
        sp_war = tm
        dg_war = tm
        for rnd in range(2):
            def trs(h, rnd=rnd):
                last = None
                for j in range(8):
                    tc = rnd * 8 + j
                    last = h.transpose(E.psb[:, j * 128:(j + 1) * 128],
                                       hTc[:, tc * 128:(tc + 1) * 128], E.ident)
                return last
            ttr = p.op("tensor", trs, httoks + [E.psb_war, E.const_tok[1]])
            tev = p.op("vector", (lambda h: h.tensor_copy(hstg, E.psb[:, 0:1024])), [ttr, hs_war])
            E.psb_war = tev
            tdm = p.op("sync", (lambda h, fc=fc, rnd=rnd: h.dma_start(
                out=ht3[:, rnd * 8:(rnd + 1) * 8, fc * 128:(fc + 1) * 128],
                in_=hstg.rearrange("p (tc f) -> p tc f", tc=8))), [tev], dma=True)
            hs_war = tdm
            ht_done.append(tdm)
    E.pool.pop()

    barrier(E)
    E.pool.push()
    (bcT, dtb, lcb, mlcb, elamb, lamb16) = E.mb
    btok = E.pool.alloc_bf16(16 * 64)
    xhb = E.pool.alloc_bf16(2 * DIN)
    Gall = E.pool.alloc_bf16(2 * DIN)
    Gsc = E.pool.alloc_f32(DIN)
    lrow = E.pool.alloc_bf16(4096)
    mrow = E.pool.alloc_bf16(4096)
    dtk = E.pool.alloc_bf16(32)
    ltk = E.pool.alloc_f32(32)
    exl = E.pool.alloc_f32(32)
    e2s = E.pool.alloc_f32(64)
    ftk = E.pool.alloc_bf16(32)
    xht = E.pool.alloc_bf16(DIN)
    scat = E.pool.alloc_f32(DIN)
    scbf = E.pool.alloc_bf16(DIN)
    selam = E.pool.alloc_bf16(DIN)
    b2h = E.pool.alloc_bf16(DIN)
    yint = E.pool.alloc_bf16(DIN)
    ytok = E.pool.alloc_bf16(DIN)
    sstmp = E.pool.alloc_f32(TBW)
    ystg = E.pool.alloc_bf16(1024)
    umask = E.pool.alloc_bf16(128)
    rowst = E.mbtok["rows"]
    bct = E.mbtok["bct"]
    um_in = declare_input(E, "c_mst")
    t_um = p.op("gpsimd", lambda h: h.dma_start(out=umask, in_=um_in[:, 0:128]), prj, dma=True)
    bt_war = None
    btt = []
    bt3 = btok.rearrange("p (tc n) -> p tc n", tc=16)
    for j in range(16):
        tb_j = p.op("sync", (lambda h, j=j: h.dma_start_transpose(
            bt3[:, j], bcT[0:64, j * 128:(j + 1) * 128])), bct + [bt_war], dma=True)
        bt_war = tb_j
        btt.append(tb_j)
    cb0 = E.pool.alloc_bf16(T)
    cbm = E.pool.alloc_bf16(128)
    t_cb0 = p.op("sync", (lambda h: h.dma_start(out=cb0[0:64, :], in_=bcT[64:128, :])),
                 bct, dma=True)
    t_s0a = p.op("gpsimd", (lambda h: h.memset(scat[64:128, :], 0.0)), prj)
    t_s0b = p.op("gpsimd", (lambda h: h.memset(scbf[64:128, :], 0.0)), prj)
    sbf_tok = [t_s0a, t_s0b]
    yt4 = ytd.rearrange("(fc p) t -> p fc t", p=128)
    E.yint_war = None
    ring_war = {0: [], 1: []}
    row_war = []
    y_done = []
    ytok_war = [None] * 4
    for c in range(16):
        sl = c % 2
        xh = xhb[:, sl * DIN:(sl + 1) * DIN]
        t_xh = p.op("sync", (lambda h, c=c, xh=xh: h.dma_start(
            out=xh, in_=htd[c * 128:(c + 1) * 128, :])), ring_war[sl] + ht_done, dma=True)
        ring_war[sl] = []
        t_lr = p.op("gpsimd", (lambda h, c=c: h.dma_start(
            out=lrow[0:1, :], in_=lcb[0:32, c * 128:(c + 1) * 128])), rowst + row_war, dma=True)
        t_mr = p.op("gpsimd", (lambda h, c=c: h.dma_start(
            out=mrow[0:1, :], in_=mlcb[0:32, c * 128:(c + 1) * 128])), rowst + row_war, dma=True)

        def trdt(h, c=c):
            h.transpose(E.psb[:, 0:32], dtb[0:32, c * 128:(c + 1) * 128], E.ident[0:32, 0:32])
            return h.transpose(E.psb[:, 32:64], lcb[0:32, c * 128:(c + 1) * 128],
                               E.ident[0:32, 0:32])
        t_tr = p.op("tensor", trdt, rowst + [E.psb_war, E.const_tok[1]] + row_war)
        t_ev1 = p.op("vector", lambda h: h.tensor_copy(dtk[:, 0:32], E.psb[:, 0:32]),
                     [t_tr] + row_war)
        t_ev2 = p.op("vector", lambda h: h.tensor_copy(ltk[:, 0:32], E.psb[:, 32:64]),
                     [t_tr] + row_war)
        E.psb_war = t_ev2
        t_ex = p.op("scalar", lambda h: h.activation(exl[:, 0:32], ltk[:, 0:32], AF.Exp),
                    [t_ev2] + row_war)
        t_cbm = _mmgroup(E, E.ps[6][:, 0:128],
                         [(bcT[0:64, c * 128:(c + 1) * 128], cb0[0:64, c * 128:(c + 1) * 128])],
                         bct + [t_cb0, E.ps_war[6]])
        t_cbe = p.op("vector", (lambda h: h.tensor_tensor(cbm, E.ps[6][:, 0:128], umask,
                                                          ALU.mult)), [t_cbm, t_um, E.cbm_war])
        E.ps_war[6] = t_cbe
        E.cbm_war = t_cbe
        t_xhat = p.op("vector", (lambda h, xh=xh: h.tensor_tensor(
            xht.rearrange("p (hh pp) -> p hh pp", hh=32),
            xh.rearrange("p (hh pp) -> p hh pp", hh=32),
            dtk[:, 0:32].unsqueeze(2).broadcast_to([128, 32, 64]), ALU.mult)),
            [t_xh, t_ev1] + row_war)
        w1t = []
        for blk in range(4):
            w1t.append(_mmgroup(E, E.ps[blk][:, :],
                                [(bcT[64:128, c * 128:(c + 1) * 128],
                                  scbf[64:128, blk * TBW:(blk + 1) * TBW])],
                                bct + sbf_tok + [E.ps_war[blk]], tile_position=(64, 0)))
        yintt = []
        for blk in range(4):
            yintt.append(p.op("vector", (lambda h, blk=blk: h.tensor_tensor(
                yint[:, blk * TBW:(blk + 1) * TBW].rearrange("p (hh pp) -> p hh pp", hh=8),
                E.ps[blk][:, :].rearrange("p (hh pp) -> p hh pp", hh=8),
                exl[:, blk * 8:(blk + 1) * 8].unsqueeze(2).broadcast_to([128, 8, 64]),
                ALU.mult)), [w1t[blk], t_ex, E.yint_war]))
        E.yint_war = yintt[-1]
        gtoks = []
        for hf in range(2):
            emms = []
            for j in range(16):
                hh = hf * 16 + j
                bank = j // 4
                off = (j % 4) * 128
                emms.append(_mmgroup(
                    E, E.ps[bank][:, off:off + 128],
                    [(E.ones_bf[0:1, :], lrow[0:1, hh * 128:(hh + 1) * 128])],
                    [t_lr, t_mr, E.ps_war[bank]] + yintt,
                    extra=[(mrow[0:1, hh * 128:(hh + 1) * 128], E.onesr[0:1, 0:128])]))
                E.ps_war[bank] = emms[-1]
            tss = []
            for bank in range(4):
                tss.append(p.op("vector", (lambda h, hf=hf, bank=bank: h.tensor_scalar(
                    Gsc[:, bank * TBW:(bank + 1) * TBW], E.ps[bank][:, :], 20.0, None, ALU.min)),
                    emms + [E.gsc_war]))
                E.ps_war[bank] = tss[-1]
            t_ge = p.op("scalar", (lambda h, hf=hf: h.activation(
                Gsc[:, 0:DIN], Gsc[:, 0:DIN], AF.Exp)), tss)
            t_gm = p.op("vector", (lambda h, hf=hf: h.tensor_tensor(
                Gall[:, hf * DIN:(hf + 1) * DIN].rearrange("p (hh t) -> p hh t", hh=16),
                Gsc[:, 0:DIN].rearrange("p (hh t) -> p hh t", hh=16),
                cbm.unsqueeze(1).broadcast_to([128, 16, 128]), ALU.mult)),
                [t_ge, t_cbe, E.gall_war])
            E.gsc_war = t_gm
            E.gall_war = t_gm
            gtoks.append(t_gm)
        yev = []
        for w in range(4):
            bank = 4 + w % 2
            imms = []
            for j in range(8):
                hh = w * 8 + j
                imms.append(_mmgroup(E, E.ps[bank][:, j * 64:(j + 1) * 64],
                                     [(Gall[:, hh * 128:(hh + 1) * 128],
                                       xht[:, hh * 64:(hh + 1) * 64])],
                                     [gtoks[hh // 16], t_xhat, E.ps_war[bank]]))
                E.ps_war[bank] = imms[-1]
            tye = p.op("vector", (lambda h, w=w, bank=bank: h.tensor_tensor(
                ytok[:, w * TBW:(w + 1) * TBW], E.ps[bank][:, :],
                yint[:, w * TBW:(w + 1) * TBW], ALU.add)), imms + yintt + [ytok_war[w]])
            E.ps_war[bank] = tye
            yev.append(tye)
        t_f1 = _mmgroup(E, E.ps[6][:, 0:32], [(E.ones_bf[0:1, :], lamb16[0:1, c * 32:(c + 1) * 32])],
                        [E.ps_war[6], E.mbtok["lamb16"]])
        t_f2 = p.op("vector", lambda h: h.tensor_tensor(e2s[:, 0:32], E.ps[6][:, 0:32],
                                                        ltk[:, 0:32], ALU.subtract), [t_f1, t_ev2])
        E.ps_war[6] = t_f2
        t_f3 = p.op("scalar", lambda h: h.activation(e2s[:, 32:64], e2s[:, 0:32], AF.Exp), [t_f2])
        t_f4 = p.op("vector", lambda h: h.tensor_copy(ftk[:, 0:32], e2s[:, 32:64]), [t_f3])
        t_bh = p.op("vector", (lambda h, c=c: h.tensor_tensor(
            b2h.rearrange("p (hh n) -> p hh n", hh=32),
            bt3[:, c].unsqueeze(1).broadcast_to([128, 32, 64]),
            ftk[:, 0:32].unsqueeze(2).broadcast_to([128, 32, 64]), ALU.mult)),
            btt + [t_f4, E.b2h_war])
        E.b2h_war = t_bh
        t_se = p.op("vector", (lambda h, c=c: h.tensor_copy(
            selam[0:1, :].rearrange("q (hh pp) -> q hh pp", hh=32),
            elamb[0:1, c * 32:(c + 1) * 32].unsqueeze(2).broadcast_to([1, 32, 64]))),
            [rowst[3], E.selam_war])
        for w in range(4):
            dmms = []
            for j in range(8):
                hh = w * 8 + j
                dmms.append(_mmgroup(E, E.ps[4][64:128, j * 64:(j + 1) * 64],
                                     [(b2h[:, hh * 64:(hh + 1) * 64], xht[:, hh * 64:(hh + 1) * 64])],
                                     [t_bh, t_xhat, E.ps_war[4]], tile_position=(0, 64)))
                E.ps_war[4] = dmms[-1]
            t_lb = _mmgroup(E, E.ps[5][64:128, :], [(E.ones_bf[0:1, 0:64],
                                                     selam[0:1, w * TBW:(w + 1) * TBW])],
                            [t_se, E.ps_war[5]], tile_position=(0, 64))
            t_u1 = p.op("vector", (lambda h, w=w: h.tensor_tensor(
                sstmp[64:128, :], scat[64:128, w * TBW:(w + 1) * TBW], E.ps[5][64:128, :],
                ALU.mult)), [t_lb, E.sstmp_war] + sbf_tok)
            E.ps_war[5] = t_u1
            t_u2 = p.op("vector", (lambda h, w=w: h.tensor_tensor(
                scat[64:128, w * TBW:(w + 1) * TBW], sstmp[64:128, :], E.ps[4][64:128, :],
                ALU.add)), dmms + [t_u1])
            E.ps_war[4] = t_u2
            E.sstmp_war = t_u2
            t_u3 = p.op("vector", (lambda h, w=w: h.tensor_copy(
                scbf[64:128, w * TBW:(w + 1) * TBW], scat[64:128, w * TBW:(w + 1) * TBW])), [t_u2])
            sbf_tok = [t_u3]
            ring_war[sl].append(t_u2)
        for rnd in range(2):
            def trs2(h, rnd=rnd):
                last = None
                for j in range(8):
                    fco = rnd * 8 + j
                    last = h.transpose(E.psb[:, j * 128:(j + 1) * 128],
                                      ytok[:, fco * 128:(fco + 1) * 128], E.ident)
                return last
            ttr = p.op("tensor", trs2, yev + [E.psb_war, E.const_tok[1]])
            tev = p.op("vector", (lambda h: h.tensor_copy(ystg, E.psb[:, 0:1024])),
                       [ttr, E.ystg_war])
            E.psb_war = tev
            tdm = p.op("sync", (lambda h, c=c, rnd=rnd: h.dma_start(
                out=yt4[:, rnd * 8:(rnd + 1) * 8, c * 128:(c + 1) * 128],
                in_=ystg.rearrange("p (fc t) -> p fc t", fc=8))), [tev], dma=True)
            E.ystg_war = tdm
            y_done.append(tdm)
        for w in range(4):
            ytok_war[w] = y_done[-1]
        row_war = gtoks + [t_f2, t_bh, t_xhat, t_ev1, t_ev2, t_ex]
    E.pool.pop()

    barrier(E)
    ychk = E.pool.alloc_bf16(DIN)
    gchk = E.pool.alloc_bf16(DIN)
    mchk = E.pool.alloc_bf16(DIN)
    ym_done = []
    yg_war = None
    gt4 = gated.rearrange("(c p) t -> p c t", p=128)
    ym4 = ymd.rearrange("(c p) t -> p c t", p=128)
    for fc in range(16):
        t_y = p.op("sync", (lambda h, fc=fc: h.dma_start(out=ychk, in_=yt4[:, fc])),
                   y_done + [yg_war], dma=True)
        t_g = p.op("gpsimd", (lambda h, fc=fc: h.dma_start(out=gchk, in_=gt4[:, fc])),
                   prj + [yg_war], dma=True)
        t_sg = p.op("scalar", (lambda h: h.activation(gchk, gchk, AF.Silu)), [t_g])
        t_m = p.op("vector", (lambda h: h.tensor_tensor(mchk, ychk, gchk, ALU.mult)),
                   [t_y, t_sg, yg_war])
        t_o = p.op("sync", (lambda h, fc=fc: h.dma_start(out=ym4[:, fc], in_=mchk)), [t_m], dma=True)
        yg_war = t_o
        ym_done.append(t_o)
    barrier(E)
    wob = E.pool.alloc_bf16(16 * D)
    t_wo = load_weight_bf16(E, "s1_wout", wob, prj + ym_done)
    ymr = E.pool.alloc_bf16(2 * TBW)
    xnr = E.pool.alloc_bf16(2 * NCH * TBW)
    oadd = E.pool.alloc_bf16(TBW)
    xnd_ap = E.mbtok["xnd_ap"]
    xn4 = xnd_ap.rearrange("(c p) t -> p c t", p=128)
    ymr_war = {0: [], 1: []}
    xnr_war = {0: [], 1: []}
    oadd_war = None
    for tb in range(NTB):
        xsl = tb % 2
        xnt = xnr[:, xsl * NCH * TBW:(xsl + 1) * NCH * TBW]
        t_xn = p.op("gpsimd", (lambda h, tb=tb, xnt=xnt: h.dma_start(
            out=xnt.rearrange("p (c t) -> p c t", c=NCH),
            in_=xn4[:, :, tb * TBW:(tb + 1) * TBW])), xnr_war[xsl] + [E.mbtok["xnd"]], dma=True)
        xnr_war[xsl] = []
        for g in range(2):
            mms = {}
            for fc in range(16):
                ysl = fc % 2
                ym_ = ymr[:, ysl * TBW:(ysl + 1) * TBW]
                t_ym = p.op("sync", (lambda h, tb=tb, fc=fc, ym_=ym_: h.dma_start(
                    out=ym_, in_=ym4[:, fc, tb * TBW:(tb + 1) * TBW])),
                    ymr_war[ysl] + ym_done, dma=True)
                ymr_war[ysl] = []
                for ei in range(4):
                    ec = g * 4 + ei
                    bank = 2 + ei
                    tm = _mmgroup(E, E.ps[bank][:, :],
                                  [(wob[:, fc * D + ec * 128: fc * D + ec * 128 + 128], ym_)],
                                  [t_ym, t_wo, E.ps_war[bank]] if fc == 0 else [t_ym, t_wo],
                                  start_clear=(fc == 0), stop_group=(fc == 15))
                    ymr_war[ysl].append(tm)
                    mms[ei] = tm
            for ei in range(4):
                ec = g * 4 + ei
                bank = 2 + ei
                t1 = p.op("vector", (lambda h, bank=bank, ec=ec, xnt=xnt: h.tensor_tensor(
                    oadd, E.ps[bank][:, :], xnt[:, ec * TBW:(ec + 1) * TBW], ALU.add)),
                    [mms[ei], t_xn, oadd_war])
                tx = p.op("vector", (lambda h, ec=ec, tb=tb: h.tensor_tensor(
                    E.x[:, ec * T + tb * TBW: ec * T + tb * TBW + TBW],
                    E.x[:, ec * T + tb * TBW: ec * T + tb * TBW + TBW], oadd, ALU.add)),
                    [t1, E.xtok.get((ec, tb))])
                oadd_war = tx
                E.ps_war[bank] = tx
                E.xtok[(ec, tb)] = tx
                xnr_war[xsl].append(t1)
    E.pool.pop()


def emit_delta(E):
    p = E.p
    nc = E.nc
    H, DV, CH = 4, 256, 32
    NC = T // CH
    vd = nc.dram_tensor("dl_v", [D, T], BF16).ap()
    xlnd = nc.dram_tensor("dl_xln", [D, T], BF16).ap()
    vstkd = nc.dram_tensor("dl_vstk", [128, NC * DV], BF16).ap()
    kstkd = nc.dram_tensor("dl_kstk", [128, NC * DV], BF16).ap()
    dd = nc.dram_tensor("dl_delta", [D, T], BF16).ap()
    mixd = nc.dram_tensor("dl_mix", [D, T], BF16).ap()

    qd = nc.dram_tensor("dl_q", [D, T], BF16).ap()
    kd = nc.dram_tensor("dl_k", [D, T], BF16).ap()
    vbd = nc.dram_tensor("dl_vb", [D, T], BF16).ap()
    betd = nc.dram_tensor("dl_bet", [4, T], BF16).ap()

    E.pool.push()
    E.pool.push()
    E.xln = E.pool.alloc_bf16(NCH * T)
    emit_ln(E)
    t_xs = p.op("sync", (lambda h, x_=E.xln: h.dma_start(
        out=xlnd.rearrange("(c p) t -> p c t", p=128),
        in_=x_.rearrange("p (c t) -> p c t", c=NCH))), E.lntoks, dma=True)
    wbuf = E.pool.alloc_bf16(NCH * D)
    wbb = E.pool.alloc_bf16(NCH * 32)
    bet = E.pool.alloc_bf16(T)
    betr = E.pool.alloc_bf16(4 * T)
    ccol = E.pool.alloc_f32(32)
    diag = [E.pool.alloc_bf16(4 * 128) for _ in range(2)]
    pp = [E.pool.alloc_bf16(T + 16) for _ in range(2)]
    stg = [E.pool.alloc_bf16(TBW) for _ in range(2)]
    stg2 = [E.pool.alloc_bf16(TBW) for _ in range(2)]
    t_wb = load_weight_bf16(E, "d1_wb", wbb, E.lntoks)
    btoks = []
    for tb in range(NTB):
        i = tb % 2
        pairs = [(wbb[:, c * 32:(c + 1) * 32],
                  E.xln[:, c * T + tb * TBW: c * T + tb * TBW + TBW]) for c in range(NCH)]
        tm = _mmgroup(E, E.ps[i][0:32, :], pairs, E.lntoks + [t_wb, E.ps_war[i]])
        ts_ = p.op("scalar", (lambda h, tb=tb, i=i: h.activation(
            bet[0:32, tb * TBW:(tb + 1) * TBW], E.ps[i][0:32, :], AF.Sigmoid)), [tm])
        E.ps_war[i] = ts_
        btoks.append(ts_)
    brt = []
    for hh in range(4):
        brt.append(p.op("sync", (lambda h, hh=hh: h.dma_start(
            out=betd[hh:hh + 1, :], in_=bet[hh:hh + 1, :])), btoks, dma=True))
    t_br = p.op("sync", lambda h: h.dma_start(
        out=betr[0:1, :], in_=betd.rearrange("hh t -> (hh t)").unsqueeze(0)), brt, dma=True)
    vtoks = []
    qtoks = []
    ktoks = []
    pmms = list(E.lntoks)
    ccw = []
    ppw = {0: [], 1: []}
    dgw = {0: [], 1: []}
    sgw = {0: [], 1: []}
    sg2w = {0: None, 1: None}
    for pi, (wn, cn) in enumerate((("d1_wv", "d1_vconv_w"), ("d1_wk", "d1_kconv_w"),
                                   ("d1_wq", "d1_qconv_w"))):
        outd = (vd, kd, qd)[pi]
        otk = (vtoks, ktoks, qtoks)[pi]
        t_w = load_weight_bf16(E, wn, wbuf, pmms)
        cin = declare_input(E, cn)
        t_cc = p.op("gpsimd", (lambda h, cin=cin: h.dma_start(out=ccol, in_=cin)),
                    pmms + ccw, dma=True)
        pmms = []
        ccw = []
        if pi == 0:
            t_pz = [p.op("gpsimd", (lambda h, s=s: h.memset(pp[s][:, 0:16], 0.0)))
                    for s in (0, 1)]
            ppw = {0: t_pz[0:1], 1: t_pz[1:2]}
        for ec in range(NCH):
            sl = ec % 2
            pptoks = []
            for tb in range(NTB):
                i = tb % 2
                pairs = [(wbuf[:, c * D + ec * 128: c * D + ec * 128 + 128],
                          E.xln[:, c * T + tb * TBW: c * T + tb * TBW + TBW]) for c in range(NCH)]
                tm = _mmgroup(E, E.ps[i][:, :], pairs, E.lntoks + [t_w, E.ps_war[i]])
                tv = p.op("vector", (lambda h, sl=sl, tb=tb, i=i: h.tensor_copy(
                    pp[sl][:, 16 + tb * TBW: 16 + tb * TBW + TBW],
                    E.ps[i][:, :])), [tm] + ppw[sl])
                E.ps_war[i] = tv
                pptoks.append(tv)
                pmms.append(tm)
            dts = []
            for tap in range(4):
                dts.append(p.op("vector", (lambda h, sl=sl, ec=ec, tap=tap: h.tensor_scalar(
                    diag[sl][:, tap * 128:(tap + 1) * 128], E.ident,
                    ccol[:, ec * 4 + tap: ec * 4 + tap + 1], None, ALU.mult)),
                    [t_cc] + dgw[sl] + [E.const_tok[1]]))
            ccw += dts
            dgw[sl] = []
            ppw[sl] = []
            for tb in range(NTB):
                i = 4 + tb % 2
                pairs = [(diag[sl][:, tap * 128:(tap + 1) * 128],
                          pp[sl][:, 13 + tap + tb * TBW: 13 + tap + tb * TBW + TBW])
                         for tap in range(4)]
                tm = _mmgroup(E, E.ps[i][:, :], pairs, dts + pptoks + [E.ps_war[i]])
                dgw[sl].append(tm)
                ppw[sl].append(tm)
                th = p.op("scalar", (lambda h, sl=sl, tb=tb, i=i: h.activation(
                    stg[tb % 2], E.ps[i][:, :], AF.Silu)), [tm] + sgw[tb % 2])
                E.ps_war[i] = th
                td = p.op("sync", (lambda h, outd=outd, ec=ec, tb=tb, s=tb % 2: h.dma_start(
                    out=outd[ec * 128:(ec + 1) * 128, tb * TBW:(tb + 1) * TBW], in_=stg[s])),
                    [th], dma=True)
                sgw[tb % 2] = [td]
                otk.append(td)
                if pi == 0:
                    hh = ec // 2
                    t_rb = _mmgroup(E, E.ps[6][:, :],
                                    [(E.ones_bf[0:1, 0:128],
                                      betr[0:1, hh * T + tb * TBW: hh * T + tb * TBW + TBW])],
                                    [t_br, E.ps_war[6]])
                    t_sc = p.op("vector", (lambda h, s=tb % 2: h.tensor_tensor(
                        stg2[s], stg[s], E.ps[6][:, :], ALU.mult)),
                        [th, t_rb, sg2w[tb % 2]])
                    E.ps_war[6] = t_sc
                    sgw[tb % 2].append(t_sc)
                    td2 = p.op("sync", (lambda h, ec=ec, tb=tb, s=tb % 2: h.dma_start(
                        out=vbd[ec * 128:(ec + 1) * 128, tb * TBW:(tb + 1) * TBW], in_=stg2[s])),
                        [t_sc], dma=True)
                    sg2w[tb % 2] = td2
                    vtoks.append(td2)
    E.pool.pop()

    qnT = E.pool.alloc_bf16(NCH * T)
    knT = E.pool.alloc_bf16(NCH * T)
    kbT = E.pool.alloc_bf16(NCH * T)
    E.pool.push()
    betr2 = E.pool.alloc_bf16(4 * T)
    sqsc = E.pool.alloc_bf16(TBW)
    bcsc = E.pool.alloc_bf16(TBW)
    rrow = E.pool.alloc_f32(TBW)
    t_br2 = p.op("sync", lambda h: h.dma_start(
        out=betr2[0:1, :], in_=betd.rearrange("hh t -> (hh t)").unsqueeze(0)), brt, dma=True)
    t_lq = p.op("sync", lambda h: h.dma_start(
        out=qnT.rearrange("p (c t) -> p c t", c=NCH),
        in_=qd.rearrange("(c p) t -> p c t", p=128)), qtoks[-1:] + pmms, dma=True)
    t_lk = p.op("sync", lambda h: h.dma_start(
        out=knT.rearrange("p (c t) -> p c t", c=NCH),
        in_=kd.rearrange("(c p) t -> p c t", p=128)), ktoks[-1:] + pmms, dma=True)
    l2toks = []
    for (buf, tld) in ((qnT, t_lq), (knT, t_lk)):
        for hh in range(H):
            for tb in range(NTB):
                sq2 = []
                for cc in range(2):
                    c = 2 * hh + cc
                    tq = p.op("scalar", (lambda h, buf=buf, c=c, tb=tb: h.activation(
                        sqsc, buf[:, c * T + tb * TBW: c * T + tb * TBW + TBW], AF.Square)),
                        [tld, E.sq_war] + l2toks[-2:])
                    tmm = _mmgroup(E, E.ps[6][0:1, :], [(E.ones_bf[:, 0:1], sqsc)],
                                   [tq, E.ps_war[6]], start_clear=(cc == 0), stop_group=(cc == 1))
                    E.ps_war[6] = tmm
                    E.sq_war = tmm
                    sq2.append(tmm)
                t1 = p.op("vector", lambda h: h.tensor_scalar_add(rrow[0:1, :], E.ps[6][0:1, :],
                                                                 1e-6), [sq2[-1], E.rr_war])
                E.ps_war[6] = t1
                t2 = p.op("scalar", lambda h: h.activation(rrow[0:1, :], rrow[0:1, :], AF.Ln), [t1])
                t3 = p.op("scalar", lambda h: h.activation(rrow[0:1, :], rrow[0:1, :], AF.Exp,
                                                           scale=-0.5), [t2])
                t4 = p.op("vector", lambda h: h.tensor_copy(E.row_b1, rrow[0:1, :]), [t3, E.row_b_war])
                t_rb = _mmgroup(E, E.ps[5][:, :], [(E.ones_bf[0:1, :], E.row_b1)],
                                [t4, E.ps_war[5]])
                t_rv = p.op("vector", lambda h: h.tensor_copy(bcsc, E.ps[5][:, :]),
                            [t_rb, E.bcsc_war])
                E.ps_war[5] = t_rv
                ap2 = []
                for cc in range(2):
                    c = 2 * hh + cc
                    ap2.append(p.op("vector", (lambda h, buf=buf, c=c, tb=tb: h.tensor_tensor(
                        buf[:, c * T + tb * TBW: c * T + tb * TBW + TBW],
                        buf[:, c * T + tb * TBW: c * T + tb * TBW + TBW], bcsc,
                        ALU.mult)), [t_rv]))
                E.bcsc_war = ap2[-1]
                E.row_b_war = ap2[-1]
                E.rr_war = t1
                l2toks += ap2
    kbtoks = []
    for hh in range(H):
        for tb in range(NTB):
            t_rb = _mmgroup(E, E.ps[5][:, :],
                            [(E.ones_bf[0:1, :], betr2[0:1, hh * T + tb * TBW: hh * T + tb * TBW + TBW])],
                            [t_br2, E.ps_war[5]])
            for cc in range(2):
                c = 2 * hh + cc
                tv = p.op("vector", (lambda h, c=c, tb=tb: h.tensor_tensor(
                    kbT[:, c * T + tb * TBW: c * T + tb * TBW + TBW],
                    knT[:, c * T + tb * TBW: c * T + tb * TBW + TBW], E.ps[5][:, :], ALU.mult)),
                    [t_rb] + l2toks)
                kbtoks.append(tv)
            E.ps_war[5] = kbtoks[-1]
    E.pool.pop()

    barrier(E)
    E.pool.push()
    vtmp = [E.pool.alloc_bf16(NCH * 128) for _ in range(2)]
    vtk1 = [E.pool.alloc_bf16(D) for _ in range(2)]
    sdt = []
    vbw = {0: [], 1: []}
    vkw = {0: [], 1: []}
    for src_is_v in (True, False):
        dstd = vstkd if src_is_v else kstkd
        for tc in range(16):
            sl = tc % 2
            if src_is_v:
                t_vt = p.op("sync", (lambda h, tc=tc, sl=sl: h.dma_start(
                    out=vtmp[sl].rearrange("p (c w) -> p c w", c=NCH),
                    in_=vbd.rearrange("(c p) t -> p c t", p=128)[:, :, tc * 128:(tc + 1) * 128])),
                    vbw[sl] + vtoks[-1:], dma=True)
                vbw[sl] = []
                srcb, sdep = vtmp[sl], [t_vt]
            else:
                srcb, sdep = knT, l2toks[-1:] + kbtoks[-1:]
            evs = []
            for rnd in range(2):
                def trs(h, tc=tc, rnd=rnd, srcb=srcb, src_is_v=src_is_v):
                    last = None
                    for j in range(4):
                        c = rnd * 4 + j
                        off = c * 128 if src_is_v else c * T + tc * 128
                        last = h.transpose(E.psb[:, j * 128:(j + 1) * 128],
                                           srcb[:, off:off + 128], E.ident)
                    return last
                ttr = p.op("tensor", trs, sdep + [E.psb_war, E.const_tok[1]])
                if src_is_v:
                    vbw[sl].append(ttr)
                tev = p.op("vector", (lambda h, sl=sl, rnd=rnd: h.tensor_copy(
                    vtk1[sl][:, rnd * 512:(rnd + 1) * 512], E.psb[:, 0:512])),
                    [ttr] + vkw[sl])
                E.psb_war = tev
                evs.append(tev)
            vkw[sl] = []
            for s in range(4):
                c = tc * 4 + s
                tsd = p.op("sync", (lambda h, sl=sl, s=s, c=c, dstd=dstd: h.dma_start(
                    out=dstd.rearrange("(hh i) (cc dv) -> i hh cc dv", hh=4, cc=NC)[:, :, c],
                    in_=vtk1[sl].rearrange("p (hh dv) -> p hh dv", hh=4)[32 * s:32 * s + 32])),
                    evs, dma=True)
                vkw[sl].append(tsd)
                sdt.append(tsd)
    E.delta_sdt = sdt[-1:]
    E.pool.pop()
    barrier(E)
    PM = E.pool.alloc_bf16(5 * 256)
    zz = E.pool.alloc_bf16(DV)
    at = E.pool.alloc_bf16(128)
    Sbf = E.pool.alloc_bf16(8 * DV)
    scat = [E.pool.alloc_f32(8 * DV) for _ in range(2)]
    dstg = E.pool.alloc_bf16(DV)
    vkr = E.pool.alloc_bf16(4 * DV)
    cbd_in = declare_input(E, "c_bd")
    cbdT_in = declare_input(E, "c_bdT")
    cbdA_in = declare_input(E, "c_bdA")
    cmsk = E.pool.alloc_bf16(3 * 256)
    t_m1 = p.op("gpsimd", lambda h: h.dma_start(out=cmsk[:, 0:256].bitcast(F32), in_=cbd_in), dma=True)
    t_m2 = p.op("gpsimd", lambda h: h.dma_start(out=cmsk[:, 256:512].bitcast(F32), in_=cbdT_in), dma=True)
    t_m3 = p.op("gpsimd", lambda h: h.dma_start(out=cmsk[:, 512:768].bitcast(F32), in_=cbdA_in), dma=True)
    mbd = cmsk[:, 0:256].bitcast(F32)
    mbdT = cmsk[:, 256:512].bitcast(F32)
    mbdA = cmsk[:, 512:768].bitcast(F32)
    t_sz = p.op("gpsimd", lambda h: h.memset(scat[1][:, :], 0.0), E.delta_sdt)
    t_pz0 = p.op("vector", lambda h: h.memset(E.ps[0][:, 0:128], 0.0), [E.ps_war[0]])
    t_pz3 = p.op("vector", lambda h: h.memset(E.ps[3][:, 0:128], 0.0), [E.ps_war[3]])
    E.ps_war[0] = t_pz0
    E.ps_war[3] = t_pz3
    t_szb = p.op("vector", lambda h: h.tensor_copy(Sbf[:, :], scat[1][:, :]), [t_sz])
    stoks = [t_szb]
    vk_war = {0: [], 1: []}
    prevst = [t_szb] * 8
    dstg_war = None
    ddone = []
    for c in range(NC):
        sl = c % 2
        vs = vkr[:, sl * DV:(sl + 1) * DV]
        ks = vkr[:, (2 + sl) * DV:(3 + sl) * DV]
        t_vl = p.op("sync", (lambda h, c=c, vs=vs: h.dma_start(
            out=vs, in_=vstkd[:, c * DV:(c + 1) * DV])), vk_war[sl] + E.delta_sdt, dma=True)
        t_kl = p.op("sync", (lambda h, c=c, ks=ks: h.dma_start(
            out=ks, in_=kstkd[:, c * DV:(c + 1) * DV])), vk_war[sl] + E.delta_sdt, dma=True)
        vk_war[sl] = []
        amms = []
        for rev in range(2):
            dst = PM[:, 0:128] if rev == 0 else PM[:, 128:256]
            msk = mbd if rev == 0 else mbdT
            triples = []
            for hh in range(H):
                for cc in range(2):
                    ch_ = 2 * hh + cc
                    a_ = kbT[:, ch_ * T + c * CH: ch_ * T + c * CH + CH]
                    b_ = knT[:, ch_ * T + c * CH: ch_ * T + c * CH + CH]
                    l_, r_ = (a_, b_) if rev == 0 else (b_, a_)
                    triples.append((E.ps[0][32 * hh:32 * hh + 32, 32 * hh:32 * hh + 32],
                                    l_, r_, (0, 32 * hh)))
            tmm = _mmgroup_tp(E, None, triples, [E.ps_war[0]] + stoks[-1:] + kbtoks[-1:] + l2toks[-1:],
                              pairwise=True)
            tev = p.op("vector", (lambda h, dst=dst, msk=msk: h.tensor_tensor(
                dst, E.ps[0][:, 0:128], msk, ALU.mult)), [tmm, t_m1, t_m2, E.pmb_war])
            E.ps_war[0] = tev
            E.pmb_war = tev
            amms.append(tev)
        utt = list(amms)
        for j in range(1, 5):
            def dbl(h, j=j):
                pp_ = PM[:, (j - 1) * 256:(j - 1) * 256 + 128]
                mm_ = PM[:, (j - 1) * 256 + 128: j * 256]
                last = None
                if j < 4:
                    last = h.matmul(E.ps[1][:, 0:128], mm_, pp_, start=True, stop=True)
                last = h.matmul(E.ps[1][:, 128:256], pp_, mm_, start=True, stop=True)
                return last
            t_pm = p.op("tensor", dbl, utt + [E.ps_war[1]])
            if j < 4:
                t_pe = p.op("vector", (lambda h, j=j: h.tensor_copy(
                    PM[:, j * 256:(j + 1) * 256], E.ps[1][:, 0:256])), [t_pm])
            else:
                t_pe = p.op("vector", (lambda h, j=j: h.tensor_copy(
                    PM[:, j * 256 + 128:(j + 1) * 256], E.ps[1][:, 128:256])), [t_pm])
            E.ps_war[1] = t_pe
            utt.append(t_pe)
        triples = []
        for hh in range(H):
            for cc in range(2):
                triples.append((E.ps[2][32 * hh:32 * hh + 32, 0:DV],
                                kbT[:, (2 * hh + cc) * T + c * CH: (2 * hh + cc) * T + c * CH + CH],
                                Sbf[:, (hh * 2 + cc) * DV:(hh * 2 + cc + 1) * DV],
                                (0, 32 * hh)))
        rmms = [_mmgroup_tp(E, None, triples, stoks + [E.ps_war[2]] + kbtoks[-1:], pairwise=True)]
        E.ps_war[2] = rmms[-1]
        t_r = p.op("vector", (lambda h, vs=vs: h.tensor_tensor(zz, vs, E.ps[2][:, 0:DV],
                                                               ALU.subtract)),
                   rmms + [t_vl, E.zz_war])
        E.ps_war[2] = t_r
        zt = t_r
        for j in range(4, -1, -1):
            t_a = _mmgroup(E, E.ps[2][:, 0:DV],
                           [(PM[:, j * 256 + 128:(j + 1) * 256], zz)],
                           utt + [zt, E.ps_war[2]])
            zt = p.op("vector", lambda h: h.tensor_tensor(zz, zz, E.ps[2][:, 0:DV], ALU.add),
                      [t_a])
            E.ps_war[2] = zt
        E.zz_war = zt
        triples = []
        for hh in range(H):
            for cc in range(2):
                triples.append((E.ps[3][32 * hh:32 * hh + 32, 32 * hh:32 * hh + 32],
                                knT[:, (2 * hh + cc) * T + c * CH: (2 * hh + cc) * T + c * CH + CH],
                                qnT[:, (2 * hh + cc) * T + c * CH: (2 * hh + cc) * T + c * CH + CH],
                                (0, 32 * hh)))
        prev = _mmgroup_tp(E, None, triples, [E.ps_war[3]] + l2toks[-1:], pairwise=True)
        t_at = p.op("vector", lambda h: h.tensor_tensor(at, E.ps[3][:, 0:128], mbdA, ALU.mult),
                    [prev, t_m3, E.at_war])
        E.ps_war[3] = t_at
        E.at_war = t_at
        omms = []
        first = {}
        for hh in range(H):
            for half in range(2):
                fo = (2 * hh + half) * 32
                pairs = [(Sbf[:, (hh * 2 + cc) * DV + half * 128:(hh * 2 + cc) * DV + half * 128 + 128],
                          qnT[:, (2 * hh + cc) * T + c * CH: (2 * hh + cc) * T + c * CH + CH])
                         for cc in range(2)]
                t_o1 = _mmgroup(E, E.ps[4][:, fo:fo + 32], pairs,
                                stoks + [E.ps_war[4], zt], stop_group=False)
                t_o2 = _mmgroup(E, E.ps[4][:, fo:fo + 32],
                                [(zz[32 * hh:32 * hh + 32, half * 128: half * 128 + 128],
                                  at[32 * hh:32 * hh + 32, 32 * hh:32 * hh + 32])],
                                [t_o1, zt, t_at], tile_position=(32 * hh, 0), start_clear=False)
                E.ps_war[4] = t_o2
                omms.append(t_o2)
        t_oe = p.op("vector", lambda h: h.tensor_copy(dstg, E.ps[4][:, 0:DV]),
                    omms + [dstg_war])
        E.ps_war[4] = t_oe
        t_od = p.op("sync", (lambda h, c=c: h.dma_start(
            out=dd.rearrange("(fc p) t -> p fc t", p=128)[:, :, c * CH:(c + 1) * CH],
            in_=dstg.rearrange("p (fc i) -> p fc i", fc=8))), [t_oe], dma=True)
        dstg_war = t_od
        ddone.append(t_od)
        so, si = scat[c % 2], scat[1 - c % 2]
        newst = []
        for hh in range(H):
            for cc in range(2):
                idx = hh * 2 + cc
                bank = 5 + idx % 2
                t_d = _mmgroup(E, E.ps[bank][:, 0:DV],
                               [(ks[32 * hh:32 * hh + 32, cc * 128: cc * 128 + 128],
                                 zz[32 * hh:32 * hh + 32, :])],
                               [t_kl, zt, E.ps_war[bank]], tile_position=(32 * hh, 0))
                t_su = p.op("vector", (lambda h, so=so, si=si, idx=idx, bank=bank: h.tensor_tensor(
                    so[:, idx * DV:(idx + 1) * DV], si[:, idx * DV:(idx + 1) * DV],
                    E.ps[bank][:, 0:DV], ALU.add)),
                    [t_d, prevst[idx]] + stoks[-1:])
                E.ps_war[bank] = t_su
                newst.append(t_su)
                vk_war[sl].append(t_d)
        prevst = newst
        t_sb2 = p.op("vector", (lambda h, so=so: h.tensor_copy(Sbf[:, :], so[:, :])), newst)
        stoks = [t_sb2]
        vk_war[sl].append(t_r)
    E.pool.pop()

    firs_d = nc.dram_tensor("dl_firs", [D, T], BF16).ap()
    firl_d = nc.dram_tensor("dl_firl", [D, T], BF16).ap()
    prd = nc.dram_tensor("dl_prd", [16, TBW], BF16).ap()
    ord_ = nc.dram_tensor("dl_ord", [4, TBW], F32).ap()

    barrier(E)
    E.pool.push()
    PWF = 64 + T
    vch = [E.pool.alloc_bf16(PWF) for _ in range(2)]
    diagl = E.pool.alloc_bf16(63 * 128)
    diags = E.pool.alloc_bf16(3 * 128)
    fstg = [E.pool.alloc_bf16(TBW) for _ in range(4)]
    flb = E.pool.alloc_f32(8 * 63)
    fsb = E.pool.alloc_f32(24)
    fl_in = declare_input(E, "d1_fir_l")
    fs_in = declare_input(E, "d1_fir_s")
    t_fl = p.op("gpsimd", lambda h: h.dma_start(out=flb, in_=fl_in), dma=True)
    t_fs = p.op("gpsimd", lambda h: h.dma_start(out=fsb, in_=fs_in), dma=True)
    t_pz = [p.op("gpsimd", (lambda h, s=s: h.memset(vch[s][:, 0:64], 0.0))) for s in (0, 1)]
    vw = {0: [], 1: []}
    dw = []
    fw = [None] * 4
    fir_done = []
    for fc in range(8):
        sl = fc % 2
        t_v = p.op("sync", (lambda h, fc=fc, sl=sl: h.dma_start(
            out=vch[sl][:, 64:PWF], in_=vd[fc * 128:(fc + 1) * 128, :])),
            vw[sl] + [t_pz[sl]] + vtoks[-1:], dma=True)
        vw[sl] = []
        dts = []
        for k in range(63):
            dts.append(p.op("vector", (lambda h, fc=fc, k=k: h.tensor_scalar(
                diagl[:, k * 128:(k + 1) * 128], E.ident, flb[:, fc * 63 + k: fc * 63 + k + 1],
                None, ALU.mult)), [t_fl, E.const_tok[1]] + dw))
        for k in range(3):
            dts.append(p.op("vector", (lambda h, fc=fc, k=k: h.tensor_scalar(
                diags[:, k * 128:(k + 1) * 128], E.ident, fsb[:, fc * 3 + k: fc * 3 + k + 1],
                None, ALU.mult)), [t_fs, E.const_tok[1]] + dw))
        dw = []
        for tb in range(NTB):
            si = (tb % 2) * 2
            pairs = [(diagl[:, k * 128:(k + 1) * 128],
                      vch[sl][:, 2 + tb * TBW + k: 2 + tb * TBW + k + TBW]) for k in range(63)]
            tm = _mmgroup(E, E.ps[6][:, :], pairs, dts + [t_v, E.ps_war[6]])
            te = p.op("vector", (lambda h, si=si: h.tensor_copy(fstg[si], E.ps[6][:, :])),
                      [tm, fw[si]])
            E.ps_war[6] = te
            td = p.op("sync", (lambda h, fc=fc, tb=tb, si=si: h.dma_start(
                out=firl_d[fc * 128:(fc + 1) * 128, tb * TBW:(tb + 1) * TBW], in_=fstg[si])),
                [te], dma=True)
            fw[si] = td
            fir_done.append(td)
            pairs = [(diags[:, k * 128:(k + 1) * 128],
                      vch[sl][:, 62 + tb * TBW + k: 62 + tb * TBW + k + TBW]) for k in range(3)]
            tm2 = _mmgroup(E, E.ps[5][:, :], pairs, dts + [t_v, E.ps_war[5]])
            te2 = p.op("vector", (lambda h, si=si: h.tensor_copy(fstg[si + 1], E.ps[5][:, :])),
                       [tm2, fw[si + 1]])
            E.ps_war[5] = te2
            td2 = p.op("sync", (lambda h, fc=fc, tb=tb, si=si: h.dma_start(
                out=firs_d[fc * 128:(fc + 1) * 128, tb * TBW:(tb + 1) * TBW], in_=fstg[si + 1])),
                [te2], dma=True)
            fw[si + 1] = td2
            fir_done.append(td2)
            vw[sl] += [tm, tm2]
            dw += [tm, tm2]
    E.pool.pop()

    barrier(E)
    E.pool.push()
    vt_b = E.pool.alloc_bf16(NCH * TBW)
    st_b = E.pool.alloc_bf16(NCH * TBW)
    lt_b = E.pool.alloc_bf16(NCH * TBW)
    dt_b = E.pool.alloc_bf16(NCH * TBW)
    xl_b = E.pool.alloc_bf16(NCH * TBW)
    pbuf = (st_b, lt_b, dt_b, vt_b)
    mixf = E.pool.alloc_bf16(NCH * TBW)
    mix3 = E.pool.alloc_bf16(NCH * TBW)
    hmsl = [E.pool.alloc_bf16(TBW) for _ in range(2)]
    g1db = [E.pool.alloc_bf16(1152) for _ in range(2)]
    wobuf = E.pool.alloc_bf16(8 * 1024)
    g2buf = E.pool.alloc_bf16(256)
    g1bb = E.pool.alloc_f32(16)
    gstat = E.pool.alloc_bf16(TBW)
    sqp = [E.pool.alloc_bf16(TBW) for _ in range(2)]
    s32 = E.pool.alloc_bf16(TBW)
    stT = E.pool.alloc_f32(128)
    msq = E.pool.alloc_f32(16)
    stTb = E.pool.alloc_bf16(128)
    lgf = E.pool.alloc_f32(TBW)
    lgb = E.pool.alloc_bf16(TBW)
    lgT = E.pool.alloc_f32(64)
    prf = E.pool.alloc_f32(64)
    prb_ = E.pool.alloc_bf16(64)
    sx2 = E.pool.alloc_f32(32)
    sx1 = E.pool.alloc_f32(16)
    prT = E.pool.alloc_bf16(TBW)
    prrow = E.pool.alloc_bf16(8 * TBW)
    prbcf = E.pool.alloc_f32(TBW)
    msc = [E.pool.alloc_f32(TBW) for _ in range(2)]
    ost = E.pool.alloc_f32(TBW)
    orow = E.pool.alloc_f32(4 * TBW)
    orb = E.pool.alloc_bf16(4 * TBW)
    oadd = E.pool.alloc_f32(TBW)
    flbb = E.pool.alloc_f32(16)
    r16 = E.pool.alloc_f32(8)
    flr = E.pool.alloc_f32(32)
    frb = E.pool.alloc_bf16(16)
    onwb = E.pool.alloc_f32(8)
    cdbuf = E.pool.alloc_bf16(2048)
    cdh = E.pool.alloc_bf16(32)

    cdsel_in = declare_input(E, "c_dsel")
    cdh_in = declare_input(E, "c_dselh")
    t_cd = p.op("gpsimd", lambda h: h.dma_start(out=cdbuf, in_=cdsel_in), dma=True)
    t_ch = p.op("gpsimd", lambda h: h.dma_start(out=cdh, in_=cdh_in), dma=True)
    t_wo = load_weight_bf16(E, "d1_wo", wobuf)
    t_g2 = load_weight_bf16(E, "d1_g2", g2buf)
    g1b_in = declare_input(E, "d1_g1b")
    t_g1b = p.op("sync", lambda h: h.dma_start(out=g1bb, in_=g1b_in), dma=True)
    onw_in = declare_input(E, "d1_onw")
    t_onw = p.op("sync", lambda h: h.dma_start(out=onwb, in_=onw_in), dma=True)
    bb_in = declare_input(E, "d1_bb")
    t_bb = p.op("sync", lambda h: h.dma_start(out=r16[0:16, 0:1], in_=bb_in), dma=True)
    lt_in = declare_input(E, "d1_logt")
    t_lt = p.op("sync", lambda h: h.dma_start(out=r16[0:16, 1:2], in_=lt_in), dma=True)
    fl2_in = declare_input(E, "d1_floor")
    t_fr = p.op("sync", lambda h: h.dma_start(out=flr[0:1, 0:16], in_=fl2_in), dma=True)
    g1_in = declare_input(E, "d1_g1")

    ta_ = p.op("scalar", lambda h: h.activation(r16[0:16, 2:3], r16[0:16, 1:2], AF.Relu), [t_lt])
    tb_ = p.op("scalar", lambda h: h.activation(r16[0:16, 3:4], r16[0:16, 1:2], AF.Abs), [t_lt])
    tc_ = p.op("scalar", lambda h: h.activation(r16[0:16, 3:4], r16[0:16, 3:4], AF.Exp, scale=-1.0), [tb_])
    td_ = p.op("vector", lambda h: h.tensor_scalar_add(r16[0:16, 3:4], r16[0:16, 3:4], 1.0), [tc_])
    te_ = p.op("scalar", lambda h: h.activation(r16[0:16, 3:4], r16[0:16, 3:4], AF.Ln), [td_])
    tf_ = p.op("vector", lambda h: h.tensor_tensor(r16[0:16, 4:5], r16[0:16, 2:3], r16[0:16, 3:4], ALU.add), [ta_, te_])
    tg_ = p.op("vector", lambda h: h.tensor_scalar_add(r16[0:16, 4:5], r16[0:16, 4:5], 1e-4), [tf_])
    th_ = p.op("vector", lambda h: h.reciprocal(r16[0:16, 5:6], r16[0:16, 4:5]), [tg_])
    rtmp = r16[0:16, 5:6]
    tfs = p.op("scalar", lambda h: h.activation(flr[0:1, 16:32], flr[0:1, 0:16], AF.Sigmoid), [t_fr])
    tfm = p.op("vector", lambda h: h.tensor_scalar_mul(flr[0:1, 16:32], flr[0:1, 16:32], 0.05), [tfs])
    tfc = p.op("vector", lambda h: h.tensor_copy(frb[0:1, :], flr[0:1, 16:32]), [tfm])
    tfb = _mmgroup(E, E.ps[4][:, 0:16], [(E.ones_bf[0:1, 0:128], frb[0:1, 0:16])], [tfc, E.ps_war[4]])
    t_flb = p.op("vector", lambda h: h.tensor_copy(flbb[:, 0:16], E.ps[4][:, 0:16]), [tfb])
    E.ps_war[4] = t_flb
    t_gz = p.op("gpsimd", lambda h: h.memset(gstat, 0.0))

    v3 = lgT.rearrange("p (g q) -> p g q", q=4)
    p3 = prf.rearrange("p (g q) -> p g q", q=4)
    p4 = prf.rearrange("p (q l) -> p q l", l=16)
    m2 = sx2.rearrange("p (g q) -> p g q", q=2)
    m1 = sx1.rearrange("p (g q) -> p g q", q=1)

    tbw = []
    gstw = [t_gz]
    sqw = [None, None]
    psbw = [E.psb_war]
    s32w = stTw = msqw = stTbw = lgfw = lgbw = lgTw = sxw = prfw = prbw = None
    prTw = prroww = ostw = ordw = oroww = None
    orbw = []
    prbcw = []
    mscw = {0: None, 1: None}
    oaddw = None
    mixtok = {}
    mix3tok = {}
    mfw = {fc: [] for fc in range(NCH)}
    mix3w = {fc: [] for fc in range(NCH)}
    hmw = [None, None]
    g1slw = {0: [], 1: []}
    prdw = []
    for tb in range(NTB):
        t0 = tb * TBW
        ld = {}
        for nm, buf, srcd, dps in (("v", vt_b, vd, vtoks[-1:]),
                                   ("s", st_b, firs_d, fir_done[-1:]),
                                   ("l", lt_b, firl_d, fir_done[-1:]),
                                   ("d", dt_b, dd, ddone[-1:]),
                                   ("x", xl_b, xlnd, [t_xs])):
            ld[nm] = p.op("sync", (lambda h, buf=buf, srcd=srcd, t0=t0: h.dma_start(
                out=buf.rearrange("p (c w) -> p c w", c=NCH),
                in_=srcd.rearrange("(c p) t -> p c t", p=128)[:, :, t0:t0 + TBW])),
                dps + tbw, dma=True)
        ldp = [ld["s"], ld["l"], ld["d"], ld["v"]]
        tbw = []
        pairs = []
        for path in range(4):
            for fc in range(NCH):
                b = (path * 8 + fc) * 2
                pairs.append((cdbuf[:, b * 32:(b + 1) * 32],
                              pbuf[path][:, fc * TBW:(fc + 1) * TBW]))
        t_raw = _mmgroup(E, E.ps[0][0:32, :], pairs, ldp + [t_cd, E.ps_war[0]], stop_group=False)
        prevmm = t_raw
        sq_mm = list(sqw)
        i = 0
        for path in range(4):
            for fc in range(NCH):
                b = (path * 8 + fc) * 2 + 1
                t_sq = p.op("scalar", (lambda h, path=path, fc=fc, i=i: h.activation(
                    sqp[i % 2], pbuf[path][:, fc * TBW:(fc + 1) * TBW], AF.Square)),
                    [ldp[path], sq_mm[i % 2]])
                t_mm = _mmgroup(E, E.ps[0][0:32, :], [(cdbuf[:, b * 32:(b + 1) * 32], sqp[i % 2])],
                                [t_sq, t_cd], start_clear=False, stop_group=(i == 31))
                sq_mm[i % 2] = t_mm
                prevmm = t_mm
                i += 1
        t_se = p.op("vector", (lambda h: h.tensor_copy(s32[0:32, :], E.ps[0][0:32, :])),
                    [prevmm, s32w])
        E.ps_war[0] = t_se

        def trf(h, t_se=t_se):
            last = None
            for q in range(4):
                last = h.transpose(E.psb[:, q * 32:(q + 1) * 32],
                                   s32[0:32, q * 128:(q + 1) * 128], E.ident[0:32, 0:32])
            return last
        t_tf = p.op("tensor", trf, [t_se, E.const_tok[1]] + psbw)
        s32w = t_tf
        t_st = p.op("vector", lambda h: h.tensor_copy(stT, E.psb[:, 0:128]), [t_tf, stTw])
        psbw = [t_st]
        vfix = []
        tprev = t_st
        for q in range(4):
            t_m2 = p.op("scalar", (lambda h, q=q: h.activation(
                msq[:, 0:16], stT[:, q * 32:q * 32 + 16], AF.Square)), [t_st, msqw, tprev])
            t_vf = p.op("vector", (lambda h, q=q: h.tensor_tensor(
                stT[:, q * 32 + 16:q * 32 + 32], stT[:, q * 32 + 16:q * 32 + 32],
                msq[:, 0:16], ALU.subtract)), [t_m2])
            msqw = t_vf
            tprev = t_vf
            vfix.append(t_vf)
        t_sb = p.op("vector", lambda h: h.tensor_copy(stTb, stT), vfix + [stTbw])
        stTw = t_sb

        def trb(h, t_sb=t_sb):
            last = None
            for q in range(4):
                last = h.transpose(E.psb[0:32, q * 128:(q + 1) * 128],
                                   stTb[:, q * 32:(q + 1) * 32], E.ident)
            return last
        t_tb2 = p.op("tensor", trb, [t_sb, E.const_tok[1]] + psbw)
        stTbw = t_tb2
        t_gs = p.op("vector", (lambda h: h.tensor_copy(gstat[0:32, :], E.psb[0:32, 0:512])),
                    [t_tb2] + gstw)
        psbw = [t_gs]
        gw = []
        t_lgmm = None
        for hbc in range(16):
            slt = hbc % 2
            bank = 2 + slt
            t_g = p.op("gpsimd", (lambda h, hbc=hbc, slt=slt: h.dma_start(
                out=g1db[slt].rearrange("p (c m) -> p c m", c=9),
                in_=g1_in[:, hbc * 128:(hbc + 1) * 128].rearrange("(c p) m -> p c m", p=128))),
                g1slw[slt], dma=True)
            pairs = [(g1db[slt][:, c * 128:(c + 1) * 128],
                      xl_b[:, c * TBW:(c + 1) * TBW]) for c in range(NCH)] +                     [(g1db[slt][:, 8 * 128:9 * 128], gstat)]
            t_mm = _mmgroup(E, E.ps[bank][:, :], pairs, [t_g, ld["x"], t_gs, E.ps_war[bank]])
            t_gel = p.op("scalar", (lambda h, slt=slt, bank=bank, hbc=hbc: h.activation(
                hmsl[slt], E.ps[bank][:, :], AF.Gelu, bias=g1bb[:, hbc:hbc + 1])),
                [t_mm, t_g1b, hmw[slt]])
            E.ps_war[bank] = t_gel
            t_lgmm = _mmgroup(E, E.ps[1][0:16, :], [(g2buf[:, hbc * 16:(hbc + 1) * 16], hmsl[slt])],
                              [t_gel, t_g2] + ([E.ps_war[1]] if hbc == 0 else []),
                              start_clear=(hbc == 0), stop_group=(hbc == 15))
            hmw[slt] = t_lgmm
            g1slw[slt] = [t_mm]
            gw.append(t_mm)
        gstw = gw
        t_l1 = p.op("vector", lambda h: h.tensor_scalar(lgf[0:16, :], E.ps[1][0:16, :],
                                                        r16[0:16, 0:1], None, ALU.add),
                    [t_lgmm, t_bb, lgfw])
        E.ps_war[1] = t_l1
        t_l2 = p.op("vector", lambda h: h.tensor_scalar(lgf[0:16, :], lgf[0:16, :], rtmp,
                                                        None, ALU.mult), [t_l1, th_])
        t_l3 = p.op("vector", lambda h: h.tensor_copy(lgb[0:16, :], lgf[0:16, :]), [t_l2, lgbw])
        lgfw = t_l3

        def trl(h, t_l3=t_l3):
            last = None
            for q in range(4):
                last = h.transpose(E.psb[:, q * 16:(q + 1) * 16],
                                   lgb[0:16, q * 128:(q + 1) * 128], E.ident[0:16, 0:16])
            return last
        t_tl = p.op("tensor", trl, [t_l3, E.const_tok[1]] + psbw)
        lgbw = t_tl
        t_lt2 = p.op("vector", lambda h: h.tensor_copy(lgT, E.psb[:, 0:64]), [t_tl, lgTw])
        psbw = [t_lt2]
        t_x1 = p.op("vector", lambda h: h.tensor_tensor(m2, v3[:, :, 0:2], v3[:, :, 2:4], ALU.max),
                    [t_lt2, sxw])
        t_x2 = p.op("vector", lambda h: h.tensor_tensor(m1, m2[:, :, 0:1], m2[:, :, 1:2], ALU.max),
                    [t_x1])
        t_x3 = p.op("vector", lambda h: h.tensor_tensor(
            v3, v3, m1.broadcast_to([128, 16, 4]), ALU.subtract), [t_x2])
        lgTw = t_x3
        t_x4 = p.op("scalar", lambda h: h.activation(prf, lgT, AF.Exp), [t_x3, prfw])
        t_s1_ = p.op("vector", lambda h: h.tensor_tensor(m2, p3[:, :, 0:2], p3[:, :, 2:4], ALU.add),
                     [t_x4])
        t_s2_ = p.op("vector", lambda h: h.tensor_tensor(m1, m2[:, :, 0:1], m2[:, :, 1:2], ALU.add),
                     [t_s1_])
        t_s3 = p.op("vector", lambda h: h.reciprocal(sx1, sx1), [t_s2_])
        t_s4 = p.op("vector", lambda h: h.tensor_tensor(
            p3, p3, m1.broadcast_to([128, 16, 4]), ALU.mult), [t_s3])
        t_f1 = p.op("vector", lambda h: h.tensor_tensor(
            p4, p4, flbb[:, 0:16].unsqueeze(1).broadcast_to([128, 4, 16]), ALU.max),
            [t_s4, t_flb])
        t_s5 = p.op("vector", lambda h: h.tensor_tensor(m2, p3[:, :, 0:2], p3[:, :, 2:4], ALU.add),
                    [t_f1])
        t_s6 = p.op("vector", lambda h: h.tensor_tensor(m1, m2[:, :, 0:1], m2[:, :, 1:2], ALU.add),
                    [t_s5])
        t_s7 = p.op("vector", lambda h: h.reciprocal(sx1, sx1), [t_s6])
        t_s8 = p.op("vector", lambda h: h.tensor_tensor(
            p3, p3, m1.broadcast_to([128, 16, 4]), ALU.mult), [t_s7])
        sxw = t_s8
        t_pc = p.op("vector", lambda h: h.tensor_copy(prb_, prf), [t_s8, prbw])
        prfw = t_pc

        def trp(h, t_pc=t_pc):
            last = None
            for q in range(4):
                last = h.transpose(E.psb[0:16, q * 128:(q + 1) * 128],
                                   prb_[:, q * 16:(q + 1) * 16], E.ident)
            return last
        t_tp = p.op("tensor", trp, [t_pc, E.const_tok[1]] + psbw)
        prbw = t_tp
        t_pe = p.op("vector", lambda h: h.tensor_copy(prT[0:16, :], E.psb[0:16, 0:512]),
                    [t_tp, prTw])
        psbw = [t_pe]
        t_d1 = p.op("sync", lambda h: h.dma_start(out=prd, in_=prT[0:16, :]),
                    [t_pe] + ([prdw] if prdw else []), dma=True)
        prTw = t_d1
        t_d2 = p.op("sync", lambda h: h.dma_start(
            out=prrow[0:1, :], in_=prd[0:8, :].rearrange("l w -> (l w)").unsqueeze(0)),
            [t_d1] + prbcw, dma=True)
        prdw = t_d2
        prroww = []
        prbcw = []
        for hh in range(4):
            if hh == 2:
                t_d2 = p.op("sync", lambda h: h.dma_start(
                    out=prrow[0:1, :], in_=prd[8:16, :].rearrange("l w -> (l w)").unsqueeze(0)),
                    [t_d1] + prroww, dma=True)
                prdw = t_d2
            for path in range(4):
                l = hh * 4 + path
                t_b1 = _mmgroup(E, E.ps[4][:, :],
                                [(E.ones_bf[0:1, 0:128], prrow[0:1, (l % 8) * TBW:(l % 8 + 1) * TBW])],
                                [t_d2, E.ps_war[4]])
                prroww.append(t_b1)
                t_be = p.op("vector", lambda h: h.tensor_copy(prbcf, E.ps[4][:, :]),
                            [t_b1] + prbcw)
                E.ps_war[4] = t_be
                prbcw = []
                for cc in range(2):
                    fc = 2 * hh + cc
                    if path == 0:
                        tt = p.op("vector", (lambda h, fc=fc, path=path: h.tensor_tensor(
                            mixf[:, fc * TBW:(fc + 1) * TBW],
                            pbuf[path][:, fc * TBW:(fc + 1) * TBW], prbcf, ALU.mult)),
                            [t_be, ldp[path]] + mfw[fc])
                    else:
                        t_t1 = p.op("vector", (lambda h, fc=fc, path=path, mi=cc: h.tensor_tensor(
                            msc[mi], pbuf[path][:, fc * TBW:(fc + 1) * TBW], prbcf, ALU.mult)),
                            [t_be, ldp[path], mscw[cc]])
                        tt = p.op("vector", (lambda h, fc=fc, mi=cc: h.tensor_tensor(
                            mixf[:, fc * TBW:(fc + 1) * TBW], mixf[:, fc * TBW:(fc + 1) * TBW],
                            msc[mi], ALU.add)), [t_t1, mixtok[fc]])
                        mscw[cc] = tt
                    mixtok[fc] = tt
                    prbcw.append(tt)
        sq_mm2 = list(sq_mm)
        rmms = None
        for fc in range(NCH):
            t_sq = p.op("scalar", (lambda h, fc=fc: h.activation(
                sqp[fc % 2], mixf[:, fc * TBW:(fc + 1) * TBW], AF.Square)),
                [mixtok[fc], sq_mm2[fc % 2]])
            rmms = _mmgroup(E, E.ps[5][0:4, :], [(cdh[:, fc * 4:(fc + 1) * 4], sqp[fc % 2])],
                            [t_sq, t_ch] + ([E.ps_war[5]] if fc == 0 else []),
                            start_clear=(fc == 0), stop_group=(fc == 7))
            sq_mm2[fc % 2] = rmms
        sqw = list(sq_mm2)
        t_oe2 = p.op("vector", lambda h: h.tensor_copy(ost[0:4, :], E.ps[5][0:4, :]),
                     [rmms, ostw])
        E.ps_war[5] = t_oe2
        t_od1 = p.op("sync", lambda h: h.dma_start(out=ord_, in_=ost[0:4, :]),
                     [t_oe2] + ([ordw] if ordw else []), dma=True)
        ostw = t_od1
        t_od2 = p.op("sync", lambda h: h.dma_start(
            out=orow[0:1, :], in_=ord_.rearrange("l w -> (l w)").unsqueeze(0)),
            [t_od1, oroww], dma=True)
        ordw = t_od2
        t_o1 = p.op("vector", lambda h: h.tensor_scalar_add(orow[0:1, :], orow[0:1, :], 1e-5),
                    [t_od2])
        t_o2 = p.op("scalar", lambda h: h.activation(orow[0:1, :], orow[0:1, :], AF.Ln), [t_o1])
        t_o3 = p.op("scalar", lambda h: h.activation(orow[0:1, :], orow[0:1, :], AF.Exp,
                                                     scale=-0.5), [t_o2])
        t_o4 = p.op("vector", lambda h: h.tensor_copy(orb[0:1, :], orow[0:1, :]),
                    [t_o3] + orbw)
        oroww = t_o4
        orbw = []
        for hh in range(4):
            t_b2 = _mmgroup(E, E.ps[4][:, :],
                            [(E.ones_bf[0:1, 0:128], orb[0:1, hh * TBW:(hh + 1) * TBW])],
                            [t_o4, E.ps_war[4]])
            orbw.append(t_b2)
            t_be2 = p.op("vector", lambda h: h.tensor_copy(prbcf, E.ps[4][:, :]),
                         [t_b2] + prbcw)
            E.ps_war[4] = t_be2
            prbcw = []
            for cc in range(2):
                fc = 2 * hh + cc
                t_mn = p.op("vector", (lambda h, fc=fc, mi=cc: h.tensor_tensor(
                    msc[mi], mixf[:, fc * TBW:(fc + 1) * TBW], prbcf, ALU.mult)),
                    [t_be2, mscw[cc]] + ([sq_mm2[fc % 2]] if True else []))
                t_m3 = p.op("vector", (lambda h, fc=fc, mi=cc: h.tensor_scalar(
                    mix3[:, fc * TBW:(fc + 1) * TBW], msc[mi], onwb[:, fc:fc + 1],
                    None, ALU.mult)), [t_mn, t_onw] + mix3w[fc])
                mscw[cc] = t_m3
                mix3tok[fc] = t_m3
                mfw[fc] = [t_mn]
                prbcw.append(t_mn)
        wol = []
        for oc in range(NCH):
            bank = 2 + oc % 2
            pairs = [(wobuf[:, fc * 1024 + oc * 128: fc * 1024 + oc * 128 + 128],
                      mix3[:, fc * TBW:(fc + 1) * TBW]) for fc in range(NCH)]
            t_wm = _mmgroup(E, E.ps[bank][:, :], pairs,
                            [t_wo, E.ps_war[bank]] + [mix3tok[fc] for fc in range(NCH)])
            t_ev = p.op("vector", (lambda h, bank=bank: h.tensor_copy(oadd, E.ps[bank][:, :])),
                        [t_wm, oaddw])
            t_x = p.op("vector", (lambda h, oc=oc, t0=t0: h.tensor_tensor(
                E.x[:, oc * T + t0: oc * T + t0 + TBW], E.x[:, oc * T + t0: oc * T + t0 + TBW],
                oadd, ALU.add)), [t_ev, E.xtok.get((oc, tb))])
            E.ps_war[bank] = t_x
            oaddw = t_x
            E.xtok[(oc, tb)] = t_x
            wol.append(t_wm)
        for fc in range(NCH):
            mix3w[fc] = wol
        tbw = [t_raw, t_lgmm, prevmm] + wol + [mixtok[fc] for fc in range(NCH)] + gw[-1:]
    E.pool.pop()


# version 06 (final)
# speedup vs baseline: 1.1040x; 1.1040x over previous
import contextlib
import numpy as np
import ml_dtypes
import concourse.bass as bass
import concourse.mybir as mybir
from concourse.bass_utils import run_bass_kernel_spmd

F32 = mybir.dt.float32
BF16 = mybir.dt.bfloat16
AF = mybir.ActivationFunctionType
ALU = mybir.AluOpType
AX = mybir.AxisListType
BF16NP = np.dtype(ml_dtypes.bfloat16)

D = 1024
T = 2048
NCH = 8
TBW = 512
NTB = T // TBW
CORES = 8

ENGS = ("tensor", "vector", "scalar", "gpsimd", "sync")
HOSTBF16 = True
DEBUG = False
PHASE = 99


class Prog:
    def __init__(self, nc):
        self.nc = nc
        self.ops = {e: [] for e in ENGS}
        self.cnt = {}
        self.waited = {e: {} for e in ENGS}
        self.chkeys = []
        self.n_cc = 0

    def _key(self, eng, dma):
        k = ("d" if dma else "c", eng)
        if k not in self.cnt:
            self.cnt[k] = 0
            self.chkeys.append(k)
        return k

    def op(self, eng, emit, deps=(), dma=False):
        key = self._key(eng, dma)
        deps = list(deps)
        if dma and self.cnt[key] > 0:
            deps.append((key, self.cnt[key]))
        wl = {}
        for tok in deps:
            if tok is None:
                continue
            k, c = tok
            if self.waited[eng].get(k, -1) >= c:
                continue
            if wl.get(k, -1) < c:
                wl[k] = c
        for k, c in wl.items():
            self.waited[eng][k] = c
        amt = 16 if dma else 1
        self.cnt[key] += amt
        self.ops[eng].append((wl, emit, key, amt))
        return (key, self.cnt[key])

    def collective(self, emit, deps=()):
        self.n_cc += 1
        key = ("cc", self.n_cc)
        self.cnt[key] = 1
        self.chkeys.append(key)
        eng = "gpsimd"
        wl = {}
        for tok in deps:
            if tok is None:
                continue
            k, c = tok
            if self.waited[eng].get(k, -1) >= c:
                continue
            if wl.get(k, -1) < c:
                wl[k] = c
        for k, c in wl.items():
            self.waited[eng][k] = c
        self.ops[eng].append((wl, emit, key, 1))
        return (key, 1)

    def finalize(self, stack, block):
        sems = {}
        for k in self.chkeys:
            sems[k] = stack.enter_context(self.nc.semaphore(f"s_{k[0]}_{k[1]}"))
        for eng in ENGS:
            ops = self.ops[eng]
            if not ops:
                continue

            def make(eng, ops):
                def body(h):
                    for wl, emit, inckey, amt in ops:
                        for k, c in wl.items():
                            h.wait_ge(sems[k], c)
                        inst = emit(h)
                        inst.then_inc(sems[inckey], amt)
                return body
            getattr(block, eng)(make(eng, ops))


class Pool:
    def __init__(self, tensor, nelem):
        self.t = tensor
        self.n = nelem
        self.off = 0
        self.marks = []

    def alloc_bf16(self, cols):
        cols = (cols + 15) // 16 * 16
        assert self.off + cols <= self.n, f"pool overflow {self.off}+{cols}>{self.n}"
        ap = self.t.ap()[:, self.off:self.off + cols]
        self.off += cols
        return ap

    def alloc_f32(self, cols):
        return self.alloc_bf16(2 * cols).bitcast(F32)

    def at(self, off, cols):
        return self.t.ap()[:, off:off + cols]

    def push(self):
        self.marks.append(self.off)

    def pop(self):
        self.off = self.marks.pop()


def _rope_tables():
    d = 64
    inv = 1.0 / (10000.0 ** (np.arange(0, d, 2, dtype=np.float64) / d))
    t = np.arange(T, dtype=np.float64)
    fr = np.outer(t, inv)
    cos = np.cos(fr).T.astype(np.float32)
    sin = np.sin(fr).T.astype(np.float32)
    C = np.concatenate([cos, cos, cos, cos], 0)
    S = np.concatenate([-sin, sin, -sin, sin], 0)
    return C.astype(BF16NP), S.astype(BF16NP)


def _perm_rope_cols():
    return np.concatenate([np.arange(0, 64, 2), np.arange(1, 64, 2)])


def prep_inputs(I):
    g = {}
    X = np.asarray(I["L0_attn_a_X"])
    per_core = [{"xin": np.ascontiguousarray(X[b].T)} for b in range(CORES)]

    g["a0_wq"] = np.ascontiguousarray(I["L0_attn_a_W_Q"])
    g["a0_wk"] = np.ascontiguousarray(I["L0_attn_a_W_K"])
    g["a0_wv"] = np.ascontiguousarray(I["L0_attn_a_W_V"])

    for nm in ("wf", "wi", "wc", "wo"):
        g[f"f0_{nm}"] = np.ascontiguousarray(I[f"L0_ffn_{nm}"].T)
    for nm in ("uf", "ui", "uc", "uo"):
        g[f"f0_{nm}"] = np.ascontiguousarray(I[f"L0_ffn_{nm}"].T)
    for nm in ("bf", "bi", "bc", "bo", "in_c"):
        g[f"f0_{nm}"] = np.ascontiguousarray(I[f"L0_ffn_{nm}"].reshape(8, 128).T)

    perm = _perm_rope_cols()
    Wqb = np.asarray(I["L0_attn_b_Wqb"])
    Wqb_p = Wqb.reshape(64, 16, 64)[:, :, perm].reshape(64, 16, 64)
    hperm = [0] * 16
    for h in range(16):
        slot = 2 * ((h % 4) + 4 * (h // 8)) + ((h // 4) % 2)
        hperm[slot] = h
    Wqb_p = Wqb_p[:, hperm].reshape(64, 1024)
    g["b0_hperm"] = np.asarray(hperm)
    g["b0_wqa"] = np.ascontiguousarray(I["L0_attn_b_Wqa"])
    g["b0_wqb"] = np.ascontiguousarray(Wqb_p)
    Wkvb = np.asarray(I["L0_attn_b_Wkvb"])
    Wk_part = Wkvb.reshape(32, 4, 64, 2)[:, :, perm, 0].reshape(32, 256)
    Wv_part = Wkvb.reshape(32, 4, 64, 2)[:, :, :, 1].reshape(32, 256)
    g["b0_wkva"] = np.ascontiguousarray(I["L0_attn_b_Wkva"])
    g["b0_wkvb"] = np.ascontiguousarray(np.concatenate([Wk_part, Wv_part], 1))
    Wout = np.asarray(I["L0_attn_b_Wout"]).reshape(16, 64, 1024)[hperm].reshape(1024, 1024)
    g["b0_wout"] = np.ascontiguousarray(Wout)
    g["b0_bout"] = np.ascontiguousarray(I["L0_attn_b_bout"].reshape(1, 1024))
    C, S = _rope_tables()
    g["b0_ropec"] = C
    g["b0_ropes"] = S
    masks = np.zeros((4, 128, TBW), dtype=np.float32)
    kk = np.arange(128)[:, None]
    qq = np.arange(TBW)[None, :]
    for r in range(4):
        masks[r] = np.where(128 * r + kk <= qq, 0.0, -1e9)
    g["b0_masks"] = np.ascontiguousarray(masks.transpose(1, 0, 2).reshape(128, 4 * TBW).astype(BF16NP))

    g["d1_wq"] = np.ascontiguousarray(I["L1_attn_a_Wq"].T)
    g["d1_wk"] = np.ascontiguousarray(I["L1_attn_a_Wk"].T)
    g["d1_wv"] = np.ascontiguousarray(I["L1_attn_a_Wv"].T)
    wbp = np.zeros((1024, 32), dtype=np.float32)
    wbp[:, 0:4] = I["L1_attn_a_Wb"].T
    g["d1_wb"] = np.ascontiguousarray(wbp)
    for nm in ("qconv_w", "kconv_w", "vconv_w"):
        w = np.asarray(I[f"L1_attn_a_{nm}"])
        g[f"d1_{nm}"] = np.ascontiguousarray(w.reshape(8, 128, 4).transpose(1, 0, 2).reshape(128, 32))
    fs = np.asarray(I["L1_attn_a_fir_short_w"]).reshape(1024, 3)
    fl = np.asarray(I["L1_attn_a_fir_long_w"]).reshape(1024, 63)
    g["d1_fir_s"] = np.ascontiguousarray(fs.reshape(8, 128, 3).transpose(1, 0, 2).reshape(128, 24))
    g["d1_fir_l"] = np.ascontiguousarray(fl.reshape(8, 128, 63).transpose(1, 0, 2).reshape(128, 8 * 63))
    g1p = np.zeros((1152, 2048), dtype=np.float32)
    w1t = I["L1_attn_a_gate_w1"].T
    g1p[0:1024] = w1t[0:1024]
    for mv in range(2):
        for path in range(4):
            for hh in range(4):
                g1p[1024 + mv * 16 + path * 4 + hh] = w1t[1024 + path * 8 + hh * 2 + mv]
    g["d1_g1"] = np.ascontiguousarray(g1p)
    g["d1_g1b"] = np.ascontiguousarray(I["L1_attn_a_gate_b1"].reshape(16, 128).T)
    g["d1_g2"] = np.ascontiguousarray(I["L1_attn_a_gate_w2"].T)
    g["d1_bb"] = np.ascontiguousarray(I["L1_attn_a_base_bias"].reshape(16, 1))
    g["d1_logt"] = np.ascontiguousarray(np.repeat(np.asarray(I["L1_attn_a_log_temp"]).reshape(4), 4).reshape(16, 1).astype(np.float32))
    g["d1_floor"] = np.ascontiguousarray(np.asarray(I["L1_attn_a_floor_raw"]).reshape(1, 16).astype(np.float32))
    g["d1_onw"] = np.ascontiguousarray(np.tile(I["L1_attn_a_onorm_w"].reshape(2, 128).T, (1, 4)))
    g["d1_wo"] = np.ascontiguousarray(I["L1_attn_a_Wo"].T)

    wg = np.zeros((1024, 32), dtype=np.float32)
    wg[:, 0:8] = I["L1_ffn_Wg"]
    g["m1_wg"] = np.ascontiguousarray(wg)
    g["m1_w1"] = np.ascontiguousarray(I["L1_ffn_W1"])
    g["m1_w2"] = np.ascontiguousarray(I["L1_ffn_W2"])
    g["m1_b1"] = np.ascontiguousarray(np.asarray(I["L1_ffn_b1"]).reshape(8, 32, 128).transpose(0, 2, 1).reshape(8, 128 * 32))
    g["m1_b2"] = np.ascontiguousarray(I["L1_ffn_b2"])

    g["s1_ns"] = np.ascontiguousarray(I["L1_attn_b_norm_scale"].reshape(8, 128).T)
    w_in = np.asarray(I["L1_attn_b_w_in"])
    DIN = 2048
    gate = w_in[:, 0:DIN]
    ssm = w_in[:, DIN:2 * DIN]
    dtr = w_in[:, 2 * DIN:2 * DIN + 32]
    Bc = w_in[:, 2 * DIN + 32:2 * DIN + 96]
    Cc = w_in[:, 2 * DIN + 96:2 * DIN + 160]
    zeros = np.zeros((1024, 96), dtype=np.float32)
    g["s1_win"] = np.ascontiguousarray(np.concatenate([Bc, Cc, dtr, zeros, gate, ssm], 1))
    cw = np.asarray(I["L1_attn_b_conv_weight"])
    g["s1_cw"] = np.ascontiguousarray(cw.T.reshape(16, 128, 4).transpose(1, 0, 2).reshape(128, 64))
    g["s1_cb"] = np.ascontiguousarray(I["L1_attn_b_conv_bias"].reshape(16, 128).T)
    g["s1_alog"] = np.ascontiguousarray(I["L1_attn_b_A_log"].reshape(32, 1))
    g["s1_dtb"] = np.ascontiguousarray(I["L1_attn_b_dt_bias"].reshape(32, 1))
    g["s1_wout"] = np.ascontiguousarray(I["L1_attn_b_w_out"])

    sel = np.zeros((8, 8, 128), dtype=BF16NP)
    for e_ in range(8):
        sel[e_, e_, :] = 1.0
    g["m1_sel"] = np.ascontiguousarray(sel.transpose(1, 0, 2).reshape(8, 1024))
    bgp = np.full((1, 32), -1e9, dtype=np.float32)
    bgp[0, 0:8] = I["L1_ffn_bg"]
    g["m1_bg"] = np.ascontiguousarray(bgp)
    g["c_ident"] = np.eye(128, dtype=BF16NP)
    g["c_ones"] = np.ones((128, 128), dtype=np.float32)
    i32 = np.arange(32)
    tril_s = (i32[:, None] > i32[None, :]).astype(np.float32)
    triu_i = (i32[:, None] <= i32[None, :]).astype(np.float32)
    bd = np.zeros((128, 128), dtype=np.float32)
    bdT = np.zeros((128, 128), dtype=np.float32)
    bdA = np.zeros((128, 128), dtype=np.float32)
    for h in range(4):
        bd[32 * h:32 * h + 32, 32 * h:32 * h + 32] = -tril_s
        bdT[32 * h:32 * h + 32, 32 * h:32 * h + 32] = -tril_s.T
        bdA[32 * h:32 * h + 32, 32 * h:32 * h + 32] = triu_i
    cd = np.zeros((128, 64, 32), dtype=np.float32)
    for path in range(4):
        for fc in range(8):
            cd[:, (path * 8 + fc) * 2 + 0, path * 4 + fc // 2] = 1.0 / 256
            cd[:, (path * 8 + fc) * 2 + 1, 16 + path * 4 + fc // 2] = 1.0 / 256
    g["c_dsel"] = np.ascontiguousarray(cd.reshape(128, 2048).astype(BF16NP))
    cdh = np.zeros((128, 8, 4), dtype=np.float32)
    for fc in range(8):
        cdh[:, fc, fc // 2] = 1.0 / 256
    g["c_dselh"] = np.ascontiguousarray(cdh.reshape(128, 32).astype(BF16NP))
    g["c_bd"] = bd
    g["c_bdT"] = bdT
    g["c_bdA"] = bdA
    st = (np.arange(128)[:, None] <= np.arange(128)[None, :])
    g["c_mst"] = np.tile(st.astype(BF16NP), (1, 4))
    g["c_utri"] = np.ascontiguousarray(np.triu(np.ones((128, 128), dtype=np.float32)).T)
    for k, v in list(g.items()):
        if HOSTBF16 and v.dtype == np.float32 and v.size > 65536:
            g[k] = np.ascontiguousarray(v.astype(BF16NP))
    return g, per_core


class Env:
    pass


def _mmgroup(E, ps_ap, pairs, deps, extra=None, tile_position=None, start_clear=True,
             extra_tile_position=None, stop_group=True):
    def emit(h):
        last = None
        n = len(pairs)
        for i, (l, r) in enumerate(pairs):
            last = h.matmul(ps_ap, l, r, start=(i == 0 and start_clear),
                            stop=(i == n - 1 and extra is None and stop_group),
                            tile_position=tile_position)
        if extra is not None:
            for j, (l, r) in enumerate(extra):
                last = h.matmul(ps_ap, l, r, start=False,
                                stop=(j == len(extra) - 1 and stop_group),
                                tile_position=extra_tile_position)
        return last
    return E.p.op("tensor", emit, deps)


def _mmgroup_tp(E, ps_fn, triples, deps, start_clear=True, stop_group=True, pairwise=False):
    def emit(h):
        last = None
        n = len(triples)
        for i, (ps_ap, l, r, tp) in enumerate(triples):
            if pairwise:
                st, sp = (i % 2 == 0), (i % 2 == 1)
            else:
                st, sp = (i == 0 and start_clear), (i == n - 1 and stop_group)
            last = h.matmul(ps_ap, l, r, start=st, stop=sp, tile_position=tp)
        return last
    return E.p.op("tensor", emit, deps)


def declare_input(E, name):
    if name in E.innames:
        return E.innames[name]
    arr = E.g[name]
    dt = BF16 if arr.dtype == BF16NP else F32
    ap = E.nc.dram_tensor(name, list(arr.shape), dt, kind="ExternalInput").ap()
    E.innames[name] = ap
    return ap


def barrier(E):
    snap = dict(E.p.cnt)
    toks = [(k, c) for k, c in snap.items() if c > 0]
    new = {}
    for eng in ENGS:
        new[eng] = E.p.op(eng, lambda h: h.nop(), deps=toks)
    return new


def load_weight_bf16(E, name, dst_ap, deps=(), cols=None):
    src = declare_input(E, name)
    R, C = E.g[name].shape
    nch = R // 128
    if cols is None:
        cols = C

    def emit(h):
        return h.dma_start(
            out=dst_ap.rearrange("p (c h) -> p c h", c=nch),
            in_=src.rearrange("(c p) h -> p c h", p=128)[:, :, 0:cols])
    return E.p.op("gpsimd", emit, deps, dma=True)


def emit_ln(E, xsrc=None, xsrc_tok=None, out=None, rms=False):
    p = E.p
    out = out if out is not None else E.xln
    if rms:
        nsc = E.pool.alloc_f32(8)
        ns_in = declare_input(E, "s1_ns")
        t_ns = p.op("gpsimd", (lambda h: h.dma_start(out=nsc, in_=ns_in)), dma=True)
    E.pool.push()
    E.lnbf = E.pool.alloc_bf16(NCH * TBW)
    E.lnsq = E.pool.alloc_bf16(NCH * TBW)
    E.lnt1 = E.lnsq
    E.lnbf_war = E.ln_scratch_war
    E.lnsq_war = E.ln_scratch_war
    E.lnt1_war = E.ln_scratch_war
    E.lntok = {}
    for tb in range(NTB):
        q0 = tb * TBW
        cps, sqs = [], []
        for c in range(NCH):
            if xsrc is None:
                xs = E.x[:, c * T + q0: c * T + q0 + TBW]
                d = [E.xtok.get((c, tb))]
            else:
                xs = xsrc[:, c * T + q0: c * T + q0 + TBW]
                d = [xsrc_tok]
            cps.append(p.op("vector", (lambda h, c=c, xs=xs, b_=E.lnbf: h.tensor_copy(
                b_[:, c * TBW:(c + 1) * TBW], xs)), d + [E.lnbf_war]))
            sqs.append(p.op("scalar", (lambda h, c=c, xs=xs, b_=E.lnsq: h.activation(
                b_[:, c * TBW:(c + 1) * TBW], xs, AF.Square)), d + [E.lnsq_war]))
        t_s1 = None
        if not rms:
            t_s1 = _mmgroup(E, E.ps[0][0:1, :],
                            [(E.ones_bf[:, 0:1], E.lnbf[:, c * TBW:(c + 1) * TBW]) for c in range(NCH)],
                            cps + [E.ps_war[0]])
        t_s2 = _mmgroup(E, E.ps[1][0:1, :],
                        [(E.ones_bf[:, 0:1], E.lnsq[:, c * TBW:(c + 1) * TBW]) for c in range(NCH)],
                        sqs + [E.ps_war[1]])
        rm, rv, rr, rs, rt = E.row_m, E.row_v, E.row_r, E.row_s, E.row_t
        if rms:
            t2 = p.op("vector", lambda h: h.tensor_scalar_mul(rv, E.ps[1][0:1, :], 1.0 / D),
                      [t_s2, E.row_war])
            t5 = p.op("vector", lambda h: h.tensor_scalar_add(rv, rv, 1e-6), [t2])
            t1 = t5
        else:
            t1 = p.op("vector", lambda h: h.tensor_scalar_mul(rm, E.ps[0][0:1, :], 1.0 / D), [t_s1, E.row_war])
            t2 = p.op("vector", lambda h: h.tensor_scalar_mul(rv, E.ps[1][0:1, :], 1.0 / D), [t_s2, E.row_war])
            t3 = p.op("vector", lambda h: h.tensor_tensor(rt, rm, rm, ALU.mult), [t1])
            t4 = p.op("vector", lambda h: h.tensor_sub(rv, rv, rt), [t2, t3])
            t5 = p.op("vector", lambda h: h.tensor_scalar_add(rv, rv, 1e-5), [t4])
        t6 = p.op("scalar", lambda h: h.activation(rt, rv, AF.Ln), [t5])
        t7 = p.op("scalar", lambda h: h.activation(rr, rt, AF.Exp, scale=-0.5), [t6])
        if rms:
            t8b = p.op("vector", lambda h: h.tensor_copy(E.row_b1, rr), [t7, E.row_b_war])
            t_rb = _mmgroup(E, E.ps[2][:, :], [(E.ones_bf[0:1, :], E.row_b1)], [t8b, E.ps_war[2]])
            tas = []
            t1w = E.lnt1_war
            for c in range(NCH):
                tas.append(p.op("vector", (lambda h, c=c, t_=E.lnt1, b_=E.lnbf: h.tensor_tensor(
                    t_[:, c * TBW:(c + 1) * TBW],
                    b_[:, c * TBW:(c + 1) * TBW], E.ps[2][:, :], ALU.mult)), [t_rb, cps[c], t1w]))
            lastv = None
            for c in range(NCH):
                lastv = p.op("vector", (lambda h, c=c, q0=q0, t_=E.lnt1: h.tensor_scalar(
                    out[:, c * T + q0: c * T + q0 + TBW],
                    t_[:, c * TBW:(c + 1) * TBW], nsc[:, c:c + 1], None, ALU.mult)),
                    [t_ns, tas[c]])
                E.lntok[(c, tb)] = lastv
            E.row_b_war = lastv
            E.ps_war[2] = lastv
        else:
            t8 = p.op("vector", lambda h: h.scalar_tensor_tensor(rs, rm, -1.0, rr, ALU.mult, ALU.mult), [t7])
            t8b = p.op("vector", lambda h: h.tensor_copy(E.row_b1, rr), [t7, E.row_b_war])
            t8c = p.op("vector", lambda h: h.tensor_copy(E.row_b2, rs), [t8, E.row_b_war])
            t_rb = _mmgroup(E, E.ps[2][:, :], [(E.ones_bf[0:1, :], E.row_b1)], [t8b, E.ps_war[2]])
            t_sb = _mmgroup(E, E.ps[3][:, :], [(E.ones_bf[0:1, :], E.row_b2)], [t8c, E.ps_war[3]])
            tas = []
            t1w = E.lnt1_war
            for c in range(NCH):
                tas.append(p.op("vector", (lambda h, c=c, t_=E.lnt1, b_=E.lnbf: h.tensor_tensor(
                    t_[:, c * TBW:(c + 1) * TBW],
                    b_[:, c * TBW:(c + 1) * TBW], E.ps[2][:, :], ALU.mult)), [t_rb, cps[c], t1w]))
            lastv = None
            for c in range(NCH):
                lastv = p.op("vector", (lambda h, c=c, q0=q0, t_=E.lnt1: h.tensor_tensor(
                    out[:, c * T + q0: c * T + q0 + TBW],
                    t_[:, c * TBW:(c + 1) * TBW], E.ps[3][:, :], ALU.add)), [t_sb, tas[c]])
                E.lntok[(c, tb)] = lastv
            E.row_b_war = lastv
        if not rms:
            E.ps_war[0] = t1
            E.ps_war[3] = lastv
        E.ps_war[1] = t2
        E.ps_war[2] = lastv
        E.row_war = lastv
        E.lnbf_war = lastv
        E.lnsq_war = lastv
        E.lnt1_war = lastv
    E.lntoks = list(E.lntok.values())
    E.ln_scratch_war = lastv
    E.pool.pop()


def emit_attn_a(E):
    p = E.p
    QB = 256
    NQB = T // QB
    E.pool.push()
    E.xln = E.pool.alloc_bf16(NCH * T)
    emit_ln(E)
    if DEBUG:
        E.dbg["xln"] = (E.xln, E.lntoks)
    KT = E.pool.alloc_bf16(NCH * T)
    VT = E.pool.alloc_bf16(16 * D)
    sqs = E.pool.alloc_bf16(2 * TBW)
    rowk = E.pool.alloc_bf16(T)
    kmax = E.pool.alloc_f32(8)
    E.pool.push()
    wbuf = E.pool.alloc_bf16(NCH * D)
    prev_sq = None

    t_w = load_weight_bf16(E, "a0_wk", wbuf, [E.wload_war])
    ktoks = []
    sq_mm_toks = []
    for ec in range(NCH):
        for tb in range(NTB):
            i = (ec * NTB + tb) % 2
            pairs = [(wbuf[:, c * D + ec * 128: c * D + ec * 128 + 128],
                      E.xln[:, c * T + tb * TBW: c * T + tb * TBW + TBW]) for c in range(NCH)]
            tm = _mmgroup(E, E.ps[i][:, :], pairs, E.lntoks + [t_w, E.ps_war[i]])
            tv = p.op("vector", (lambda h, ec=ec, tb=tb, i=i: h.tensor_copy(
                KT[:, ec * T + tb * TBW: ec * T + tb * TBW + TBW], E.ps[i][:, :])), [tm])
            E.ps_war[i] = tv
            ktoks.append(tv)
    prev_sq = None
    for tb in range(NTB):
        mmdeps = [E.ps_war[2]]
        for ec in range(NCH):
            half = ec % 2
            tsq = p.op("scalar", (lambda h, ec=ec, tb=tb, half=half: h.activation(
                sqs[:, half * TBW:(half + 1) * TBW],
                KT[:, ec * T + tb * TBW: ec * T + tb * TBW + TBW], AF.Square)),
                [ktoks[ec * NTB + tb], prev_sq] + sq_mm_toks[-2:])
            tm = _mmgroup(E, E.ps[2][0:1, :], [(E.ones_bf[:, 0:1], sqs[:, half * TBW:(half + 1) * TBW])],
                          [tsq] + mmdeps, start_clear=(ec == 0), stop_group=(ec == NCH - 1))
            sq_mm_toks.append(tm)
            prev_sq = tsq
        tv = p.op("vector", (lambda h, tb=tb: h.tensor_copy(
            rowk[0:1, tb * TBW:(tb + 1) * TBW], E.ps[2][0:1, :])), [sq_mm_toks[-1]])
        E.ps_war[2] = tv
        ktoks.append(tv)
    t_km = p.op("vector", lambda h: h.reduce_max(kmax[0:1, 0:1], rowk[0:1, :], axis=AX.X),
                [ktoks[-1], ktoks[-2], ktoks[-3], ktoks[-4]])
    t_km2 = p.op("scalar", lambda h: h.activation(kmax[0:1, 1:2], kmax[0:1, 0:1], AF.Sqrt), [t_km])
    t_km3 = p.op("scalar", lambda h: h.mul(kmax[0:1, 2:3], kmax[0:1, 1:2], -1.0), [t_km2])

    t_wv = load_weight_bf16(E, "a0_wv", wbuf, ktoks + sq_mm_toks)
    vtoks = []
    for tc in range(16):
        for eb in range(2):
            i = (tc * 2 + eb) % 2
            pairs = [(E.xln[:, c * T + tc * 128: c * T + tc * 128 + 128],
                      wbuf[:, c * D + eb * TBW: c * D + eb * TBW + TBW]) for c in range(NCH)]
            tm = _mmgroup(E, E.ps[i][:, :], pairs, E.lntoks + [t_wv, E.ps_war[i]])
            tv = p.op("vector", (lambda h, tc=tc, eb=eb, i=i: h.tensor_copy(
                VT[:, tc * D + eb * TBW: tc * D + eb * TBW + TBW], E.ps[i][:, :])), [tm])
            E.ps_war[i] = tv
            vtoks.append(tv)

    if DEBUG:
        E.dbg["KT"] = (KT, ktoks)
        E.dbg["VT"] = (VT, vtoks)
        E.dbg["rowk"] = (rowk, ktoks[-4:])
    E.pool.pop()
    QTb = E.pool.alloc_bf16(NCH * QB)
    PT = E.pool.alloc_bf16(QB)
    rowq = E.pool.alloc_f32(QB)
    nbound = E.pool.alloc_bf16(QB)
    RB = E.pool.alloc_bf16(QB)
    ot1 = E.pool.alloc_bf16(QB)
    wq2 = E.pool.alloc_bf16(2 * D)
    wqin = declare_input(E, "a0_wq")
    wq_consumed = {0: vtoks, 1: vtoks}

    for qb in range(NQB):
        q0 = qb * QB
        tb = q0 // TBW
        qtoks = []
        for ec in range(NCH):
            i = ec % 2
            slot = ec % 2
            wsl = wq2[:, slot * D:(slot + 1) * D]
            t_wq = E.p.op("gpsimd", (lambda h, ec=ec, wsl=wsl: h.dma_start(
                out=wsl.rearrange("p (c e) -> p c e", c=NCH),
                in_=wqin.rearrange("(c p) e -> p c e", p=128)[:, :, ec * 128:(ec + 1) * 128])),
                wq_consumed[slot], dma=True)
            pairs = [(wsl[:, c * 128:(c + 1) * 128],
                      E.xln[:, c * T + q0: c * T + q0 + QB]) for c in range(NCH)]
            tm = _mmgroup(E, E.ps[i][:, 0:QB], pairs, E.lntoks + [t_wq, E.ps_war[i]])
            wq_consumed[slot] = [tm]
            tv = p.op("vector", (lambda h, ec=ec, i=i: h.tensor_scalar_mul(
                QTb[:, ec * QB:(ec + 1) * QB], E.ps[i][:, 0:QB], 1.0 / 32.0)), [tm])
            E.ps_war[i] = tv
            qtoks.append(tv)
        sqq = []
        for ec in range(NCH):
            half = ec % 2
            tsq = p.op("scalar", (lambda h, ec=ec, half=half: h.activation(
                sqs[:, half * TBW: half * TBW + QB], QTb[:, ec * QB:(ec + 1) * QB], AF.Square)),
                [qtoks[ec], prev_sq] + sq_mm_toks[-2:])
            tm = _mmgroup(E, E.ps[2][0:1, 0:QB], [(E.ones_bf[:, 0:1], sqs[:, half * TBW: half * TBW + QB])],
                          [tsq, E.ps_war[2]], start_clear=(ec == 0), stop_group=(ec == NCH - 1))
            sq_mm_toks.append(tm)
            prev_sq = tsq
        t_qr = p.op("scalar", lambda h: h.activation(rowq[0:1, :], E.ps[2][0:1, 0:QB], AF.Sqrt),
                    [sq_mm_toks[-1]])
        E.ps_war[2] = t_qr
        t_nb = p.op("vector", lambda h: h.tensor_scalar(nbound[0:1, :], rowq[0:1, :],
                                                        kmax[0:1, 2:3], None, ALU.mult),
                    [t_qr, t_km3])

        o_mms = []
        exps = []
        for kc in range(16):
            i = kc % 2
            pairs = [(KT[:, ec * T + kc * 128: ec * T + kc * 128 + 128],
                      QTb[:, ec * QB:(ec + 1) * QB]) for ec in range(NCH)]
            tm = _mmgroup(E, E.ps[i][:, 0:QB], pairs, [t_nb, E.ps_war[i]],
                          extra=[(E.ones_bf[0:1, :], nbound[0:1, :])])
            te = p.op("scalar", (lambda h, i=i: h.activation(PT, E.ps[i][:, 0:QB], AF.Exp)),
                      [tm] + o_mms[-8:])
            E.ps_war[i] = te
            exps.append(te)
            tden = _mmgroup(E, E.ps[6][0:1, 0:QB], [(E.ones_bf[:, 0:1], PT)],
                            [te, E.ps_war[6]], start_clear=(kc == 0), stop_group=(kc == 15))
            if kc == 15:
                E.ps_war[6] = tden
            o_mms = []
            for ec in range(NCH):
                bank = 2 + ec // 2
                off = (ec % 2) * QB
                tm2 = _mmgroup(E, E.ps[bank][:, off:off + QB],
                               [(VT[:, kc * D + ec * 128: kc * D + ec * 128 + 128], PT)],
                               [te, E.ps_war[bank]], start_clear=(kc == 0), stop_group=(kc == 15))
                o_mms.append(tm2)
            tdl = tden
        t_e = p.op("vector", lambda h: h.tensor_copy(E.row_t[0:1, 0:QB], E.ps[6][0:1, 0:QB]),
                   [tdl, E.row_war])
        E.ps_war[6] = t_e
        t_rc = p.op("vector", lambda h: h.reciprocal(E.row_m[0:1, 0:QB], E.row_t[0:1, 0:QB]), [t_e])
        t_rb = _mmgroup(E, E.ps[0][:, 0:QB], [(E.ones_f32[0:1, :], E.row_m[0:1, 0:QB])],
                        [t_rc, exps[-1], E.ps_war[0]])
        t_rbs = p.op("vector", lambda h: h.tensor_copy(RB, E.ps[0][:, 0:QB]), [t_rb])
        E.ps_war[0] = t_rbs
        E.row_war = t_rbs
        ot_war = None
        for ec in range(NCH):
            bank = 2 + ec // 2
            off = (ec % 2) * QB
            t1 = p.op("vector", (lambda h, bank=bank, off=off: h.tensor_tensor(
                ot1, E.ps[bank][:, off:off + QB], RB, ALU.mult)), [o_mms[ec], t_rbs, ot_war])
            tx = p.op("vector", (lambda h, ec=ec, q0=q0: h.tensor_tensor(
                E.x[:, ec * T + q0: ec * T + q0 + QB],
                E.x[:, ec * T + q0: ec * T + q0 + QB], ot1, ALU.add)),
                [t1, E.xtok.get((ec, tb))])
            ot_war = tx
            E.xtok[(ec, tb)] = tx
            E.ps_war[bank] = tx
    E.pool.pop()


SUBLAYERS = ("attn_a", "ffn0a", "attn_b", "ffn0b", "delta", "moe0", "mamba", "moe1")


def build_program(subs):
    nc = bass.Bass("TRN2", target_bir_lowering=False, debug=False)
    E = Env()
    E.nc = nc
    E.g = build_program.g
    E.innames = {}
    E.p = Prog(nc)
    p = E.p
    xin = nc.dram_tensor("xin", [D, T], F32, kind="ExternalInput").ap()
    xout = nc.dram_tensor("xout", [D, T], F32, kind="ExternalOutput").ap()

    with contextlib.ExitStack() as stack:
        e = stack.enter_context
        xsb = e(nc.sbuf_tensor("x_sb", [128, NCH * T], F32))
        E.x = xsb.ap()
        poolt = e(nc.sbuf_tensor("pool", [128, 73400], BF16))
        E.pool = Pool(poolt, 73400)
        E.ps = [e(nc.psum_tensor(f"psum{i}", [128, 512], F32)).ap() for i in range(7)]
        E.psb = e(nc.psum_tensor("psumb", [128, 1024], BF16)).ap()
        E.ps_war = {i: None for i in range(7)}
        E.psb_war = None
        E.ln_scratch_war = None
        E.row_m = E.pool.alloc_f32(TBW)[0:1, :]
        E.row_v = E.pool.alloc_f32(TBW)[0:1, :]
        E.row_r = E.pool.alloc_f32(TBW)[0:1, :]
        E.row_s = E.pool.alloc_f32(TBW)[0:1, :]
        E.row_t = E.pool.alloc_f32(TBW)[0:1, :]
        E.ones_bf = E.pool.alloc_bf16(128)
        E.onesr = E.pool.alloc_bf16(TBW)
        E.row_b1 = E.pool.alloc_bf16(TBW)[0:1, :]
        E.row_b2 = E.pool.alloc_bf16(TBW)[0:1, :]
        E.row_b_war = None
        cident = E.pool.alloc_bf16(128)
        E.ident = cident
        E.ones_f32 = E.pool.alloc_f32(128)
        E.row_war = None
        E.wload_war = None
        E.hsc_war = {}
        E.qsw_war = None
        E.rp1_war = None
        E.nb_war = None
        E.obuf_war = None
        E.msc_war = None
        E.mwb_war = None
        E.ot1_war = None
        E.gsc_war = None
        E.gall_war = None
        E.b2h_war = None
        E.selam_war = None
        E.sstmp_war = None
        E.ystg_war = None
        E.yint_war = None
        E.cbm_war = None
        E.sq_war = None
        E.rr_war = None
        E.vtk_war = None
        E.bcsc_war = None
        E.pmb_war = None
        E.zz_war = None
        E.at_war = None
        E.dstg2_war = None
        E.xtok = {}
        E.lntok = {}

        idin = declare_input(E, "c_ident")
        onin = declare_input(E, "c_ones")
        t_c1 = p.op("gpsimd", lambda h: h.dma_start(out=E.ones_bf, in_=onin[:, 0:128]), dma=True)
        t_c2 = p.op("gpsimd", lambda h: h.dma_start(out=cident, in_=idin[:, :]), dma=True)
        t_c3 = p.op("gpsimd", lambda h: h.dma_start(out=E.ones_f32, in_=onin[:, :]), dma=True)
        t_c4 = p.op("gpsimd", lambda h: h.memset(E.onesr[0:1, :], 1.0))
        E.const_tok = [t_c1, t_c2, t_c3, t_c4]

        t_x = p.op("sync", lambda h: h.dma_start(
            out=E.x.rearrange("p (c t) -> p c t", c=NCH),
            in_=xin.rearrange("(c p) t -> p c t", p=128)), dma=True)
        for c in range(NCH):
            for tb in range(NTB):
                E.xtok[(c, tb)] = t_x
        barrier(E)

        E.dbg = {}
        first = True
        for s in subs:
            if not first:
                barrier(E)
            first = False
            if s == 0:
                emit_attn_a(E)
            elif s == 1 or s == 3:
                emit_ffn0(E, s)
            elif s == 2:
                emit_attn_b(E)
            elif s == 4:
                emit_delta(E)
            elif s in (5, 7):
                emit_moe(E, s)
            elif s == 6:
                emit_mamba(E)

        for nm, (ap, toks) in E.dbg.items():
            od = nc.dram_tensor("dbg_" + nm, list(ap.shape), ap.dtype, kind="ExternalOutput").ap()
            E.p.op("sync", (lambda h, od=od, ap=ap: h.dma_start(out=od, in_=ap)), toks, dma=True)
        xdeps = [E.xtok[(c, tb)] for c in range(NCH) for tb in range(NTB)]
        t_o = p.op("sync", lambda h: h.dma_start(
            out=xout.rearrange("(c p) t -> p c t", p=128),
            in_=E.x.rearrange("p (c t) -> p c t", c=NCH)), xdeps, dma=True)
        p.op("sync", lambda h: h.nop(), [t_o])

        block = e(nc.Block())
        p.finalize(stack, block)
    return nc, set(E.innames.keys())


GROUPS = [[0, 1, 2, 3, 4, 5, 6, 7]]


def kernel(**inputs):
    g, per_core = prep_inputs(inputs)
    build_program.g = g
    x_cur = [pc["xin"] for pc in per_core]
    for subs in GROUPS:
        nc, names = build_program(subs)
        in_maps = []
        for b in range(CORES):
            m = {"xin": np.ascontiguousarray(x_cur[b], dtype=np.float32)}
            for nm in names:
                if nm != "xin":
                    m[nm] = g[nm]
            in_maps.append(m)
        res = run_bass_kernel_spmd(nc, in_maps, core_ids=list(range(CORES)))
        x_cur = [res.results[b]["xout"] for b in range(CORES)]
    return np.stack([x.T for x in x_cur]).astype(np.float32)


def emit_ffn0(E, s):
    p = E.p
    nc = E.nc
    R = 256
    agin = nc.dram_tensor(f"ffn_agin{s}", [D, T], BF16).ap()
    agout = nc.dram_tensor(f"ffn_agout{s}", [8 * D, T], BF16).ap()
    zin = nc.dram_tensor(f"ffn_zin{s}", [D, R], F32).ap()
    zout = nc.dram_tensor(f"ffn_zout{s}", [8 * D, R], F32).ap()
    pid = nc.partition_id()

    E.pool.push()
    E.pool.push()
    E.xln = E.pool.alloc_bf16(NCH * T)
    emit_ln(E)
    t_w = p.op("sync", (lambda h, x_=E.xln: h.dma_start(
        out=agin.rearrange("(c p) t -> p c t", p=128),
        in_=x_.rearrange("p (c t) -> p c t", c=NCH))), E.lntoks, dma=True)
    t_ag = p.collective(lambda h: h.collective_compute(
        "AllGather", ALU.bypass, replica_groups=[list(range(CORES))],
        ins=[agin[:, :]], outs=[agout[:, :]]), [t_w])
    E.pool.pop()

    wbase = E.pool.off
    wbufs, tw = {}, {}
    for nm in ("wf", "wi", "wc", "uf", "ui", "uc"):
        wbufs[nm] = E.pool.alloc_bf16(NCH * D)
        tw[nm] = load_weight_bf16(E, f"f0_{nm}", wbufs[nm], [t_w])
    bias, tb_ = {}, {}
    for nm in ("bf", "bi", "bc", "in_c", "bo"):
        bias[nm] = E.pool.alloc_f32(8)
        bin_ = declare_input(E, f"f0_{nm}")
        tb_[nm] = p.op("sync", (lambda h, b=bias[nm], bin_=bin_: h.dma_start(out=b, in_=bin_)), [t_w], dma=True)
    cbase = E.pool.off
    cst = E.pool.alloc_f32(2 * NCH * R)
    cbf = E.pool.alloc_bf16(2 * NCH * R)
    xgp = E.pool.alloc_bf16(NCH * R)
    gsc = E.pool.alloc_f32(3 * R)
    t1s = E.pool.alloc_f32(2 * R)

    def cs(sl, mc):
        return cst[:, (sl * NCH + mc) * R:(sl * NCH + mc) * R + R]

    def cb(sl, mc):
        return cbf[:, (sl * NCH + mc) * R:(sl * NCH + mc) * R + R]

    t_z = p.op("gpsimd", lambda h: h.memset(cst[:, 0:NCH * R], 0.0))
    ctoks = {}
    cbtoks = {}
    for mc in range(NCH):
        t0 = p.op("vector", (lambda h, mc=mc: h.tensor_scalar(
            cs(0, mc), cs(0, mc), bias["in_c"][:, mc:mc + 1], None, ALU.add)),
            [tb_["in_c"], t_z])
        ctoks[mc] = t0
        cbtoks[mc] = p.op("vector", (lambda h, mc=mc: h.tensor_copy(cb(0, mc), cs(0, mc))), [t0])

    ag4 = agout.rearrange("(b c p) t -> b p c t", p=128, c=NCH)
    xg_consumed = {0: [], 1: []}
    GATES = (("wf", "uf", "bf", AF.Sigmoid), ("wi", "ui", "bi", AF.Sigmoid), ("wc", "uc", "bc", AF.Tanh))
    for t in range(8):
        slot = 0
        rd, wr = t % 2, (t + 1) % 2
        xg = xgp
        t_xg = p.op("gpsimd", (lambda h, t=t, xg=xg: h.dma_start(
            out=xg.rearrange("p (c r) -> p c r", c=NCH),
            in_=ag4[t][:, :, bass.ts(pid, R)])), [t_ag] + xg_consumed[slot], dma=True)
        xg_consumed[slot] = []
        new_ctoks, new_cbtoks = {}, {}
        for mc in range(NCH):
            gtoks = []
            for gi, (wn, un, bn, fn) in enumerate(GATES):
                bank = gi
                pairs = [(wbufs[wn][:, cc * D + mc * 128: cc * D + mc * 128 + 128],
                          xg[:, cc * R:(cc + 1) * R]) for cc in range(NCH)]
                pairs += [(wbufs[un][:, cc * D + mc * 128: cc * D + mc * 128 + 128],
                           cb(rd, cc)) for cc in range(NCH)]
                deps = [t_xg, tw[wn], tw[un], E.ps_war[bank]] + [cbtoks[cc] for cc in range(NCH)]
                tm = _mmgroup(E, E.ps[bank][:, 0:R], pairs, deps)
                xg_consumed[slot].append(tm)
                ta = p.op("scalar", (lambda h, gi=gi, bank=bank, bn=bn, fn=fn, mc=mc: h.activation(
                    gsc[:, gi * R:(gi + 1) * R], E.ps[bank][:, 0:R], fn,
                    bias=bias[bn][:, mc:mc + 1])), [tm, tb_[bn]])
                E.ps_war[bank] = ta
                gtoks.append(ta)
            tm1 = p.op("vector", (lambda h, rd=rd, mc=mc: h.tensor_tensor(
                t1s[:, 0:R], gsc[:, 0:R], cs(rd, mc), ALU.mult)), [gtoks[0], ctoks[mc]])
            tm2 = p.op("vector", lambda h: h.tensor_tensor(
                t1s[:, R:2 * R], gsc[:, R:2 * R], gsc[:, 2 * R:3 * R], ALU.mult),
                [gtoks[1], gtoks[2], tm1])
            tc = p.op("vector", (lambda h, wr=wr, mc=mc: h.tensor_tensor(
                cs(wr, mc), t1s[:, 0:R], t1s[:, R:2 * R], ALU.add)),
                [tm2, new_ctoks.get(("war", mc))])
            new_ctoks[mc] = tc
            new_cbtoks[mc] = p.op("vector", (lambda h, wr=wr, mc=mc: h.tensor_copy(
                cb(wr, mc), cs(wr, mc))), [tc])
        ctoks, cbtoks = new_ctoks, new_cbtoks

    fin = 8 % 2
    uo = wbufs["uf"]
    wo = wbufs["ui"]
    t_uo = load_weight_bf16(E, "f0_uo", uo, list(xg_consumed[0]) + list(xg_consumed[1]))
    t_wo = load_weight_bf16(E, "f0_wo", wo, list(xg_consumed[0]) + list(xg_consumed[1]))
    ztoks = []
    prev_ts = None
    for oc in range(NCH):
        bank = oc % 2
        sl = oc % 2
        pairs = [(uo[:, cc * D + oc * 128: cc * D + oc * 128 + 128], cb(fin, cc)) for cc in range(NCH)]
        tm = _mmgroup(E, E.ps[bank][:, 0:R], pairs,
                      [t_uo, E.ps_war[bank]] + list(cbtoks.values()))
        tv = p.op("vector", (lambda h, sl=sl, bank=bank: h.tensor_copy(
            t1s[:, sl * R:(sl + 1) * R], E.ps[bank][:, 0:R])), [tm] + ztoks[-2:])
        ts_ = p.op("sync", (lambda h, oc=oc, sl=sl: h.dma_start(
            out=zin.rearrange("(c p) r -> p c r", p=128)[:, oc], in_=t1s[:, sl * R:(sl + 1) * R])),
            [tv], dma=True)
        E.ps_war[bank] = tv
        ztoks.append(ts_)
    t_agz = p.collective(lambda h: h.collective_compute(
        "AllGather", ALU.bypass, replica_groups=[list(range(CORES))],
        ins=[zin[:, :]], outs=[zout[:, :]]), ztoks)

    allmm = list(xg_consumed[0]) + list(xg_consumed[1]) + ztoks + [t_agz]
    zbf = E.pool.at(wbase, NCH * T)
    zbf4 = zbf.rearrange("p (c r t) -> p c r t", c=NCH, r=CORES)
    zout4 = zout.rearrange("(r c p) t -> r p c t", p=128, c=NCH)
    t_zls = []
    for r in range(CORES):
        t_zls.append(p.op("gpsimd", (lambda h, r=r: h.dma_start(
            out=zbf4[:, :, r], in_=zout4[r])), allmm, dma=True))
    t_zl = t_zls[-1]
    xlb = E.pool.at(wbase + 2 * NCH * D, 2 * NCH * TBW)
    hsc = E.pool.at(cbase, 4 * TBW).bitcast(F32)
    xl_consumed = {0: [], 1: []}
    agin4 = agin.rearrange("(c p) t -> p c t", p=128)
    for tb in range(NTB):
        slot = tb % 2
        xl = xlb[:, slot * NCH * TBW:(slot + 1) * NCH * TBW]
        t_xl = p.op("gpsimd", (lambda h, tb=tb, xl=xl: h.dma_start(
            out=xl.rearrange("p (c t) -> p c t", c=NCH),
            in_=agin4[:, :, tb * TBW:(tb + 1) * TBW])), xl_consumed[slot] + allmm, dma=True)
        xl_consumed[slot] = []
        for ec in range(NCH):
            bank = 2 + ec % 2
            hslot = ec % 2
            pairs = [(wo[:, cc * D + ec * 128: cc * D + ec * 128 + 128],
                      xl[:, cc * TBW:(cc + 1) * TBW]) for cc in range(NCH)]
            tm = _mmgroup(E, E.ps[bank][:, :], pairs, [t_xl, t_wo, E.ps_war[bank]])
            xl_consumed[slot].append(tm)
            ta = p.op("vector", (lambda h, ec=ec, tb=tb, bank=bank, hslot=hslot: h.tensor_tensor(
                hsc[:, hslot * TBW:(hslot + 1) * TBW], E.ps[bank][:, :],
                zbf[:, ec * T + tb * TBW: ec * T + tb * TBW + TBW], ALU.add)),
                [tm] + t_zls + [E.hsc_war.get(hslot)])
            th = p.op("scalar", (lambda h, ec=ec, hslot=hslot: h.activation(
                hsc[:, hslot * TBW:(hslot + 1) * TBW],
                hsc[:, hslot * TBW:(hslot + 1) * TBW], AF.Tanh,
                bias=bias["bo"][:, ec:ec + 1])), [ta, tb_["bo"]])
            E.ps_war[bank] = th
            tx = p.op("vector", (lambda h, ec=ec, tb=tb, hslot=hslot: h.tensor_tensor(
                E.x[:, ec * T + tb * TBW: ec * T + tb * TBW + TBW],
                E.x[:, ec * T + tb * TBW: ec * T + tb * TBW + TBW],
                hsc[:, hslot * TBW:(hslot + 1) * TBW], ALU.add)),
                [th, E.xtok.get((ec, tb))])
            E.hsc_war[hslot] = tx
            E.xtok[(ec, tb)] = tx
    E.pool.pop()


def emit_attn_b(E):
    p = E.p
    nc = E.nc
    E.pool.push()
    t1b = E.pool.alloc_bf16(T)
    t2b = E.pool.alloc_bf16(T)
    t_ms1 = p.op("gpsimd", lambda h: h.memset(t1b[64:128, :], 0.0))
    t_ms2a = p.op("gpsimd", lambda h: h.memset(t2b[32:64, :], 0.0))
    t_ms2b = p.op("gpsimd", lambda h: h.memset(t2b[64:128, :], 0.0))
    t_ms2 = t_ms2b
    E.pool.push()
    E.xln = E.pool.alloc_bf16(NCH * T)
    emit_ln(E)
    if DEBUG:
        _od = nc.dram_tensor("dbg_xlnb", [128, NCH * T], BF16, kind="ExternalOutput").ap()
        p.op("sync", (lambda h, _od=_od: h.dma_start(out=_od, in_=E.xln)), E.lntoks, dma=True)
    wqa = E.pool.alloc_bf16(NCH * 64)
    wkva = E.pool.alloc_bf16(NCH * 32)
    t_wqa = load_weight_bf16(E, "b0_wqa", wqa, E.lntoks)
    t_wkva = load_weight_bf16(E, "b0_wkva", wkva, E.lntoks)
    t1toks, t2toks = [], []
    for (dst, wsrc, wtok, w_, toks) in ((t1b, wqa, t_wqa, 64, t1toks), (t2b, wkva, t_wkva, 32, t2toks)):
        for tb in range(NTB):
            i = tb % 2
            pairs = [(wsrc[:, c * w_:(c + 1) * w_],
                      E.xln[:, c * T + tb * TBW: c * T + tb * TBW + TBW]) for c in range(NCH)]
            tm = _mmgroup(E, E.ps[i][0:w_, :], pairs,
                          E.lntoks + [wtok, E.ps_war[i], t_ms1, t_ms2a, t_ms2b])
            tv = p.op("vector", (lambda h, dst=dst, tb=tb, i=i, w_=w_: h.tensor_copy(
                dst[0:w_, tb * TBW:(tb + 1) * TBW], E.ps[i][0:w_, :])), [tm])
            E.ps_war[i] = tv
            toks.append(tv)
    if DEBUG:
        E.dbg["t1b"] = (t1b, t1toks)
        E.dbg["t2b"] = (t2b, t2toks)
    E.pool.pop()

    kT = E.pool.alloc_bf16(2 * T)
    vtok = E.pool.alloc_bf16(16 * 256)
    qT = E.pool.alloc_bf16(NCH * T)
    SQF = E.pool.alloc_bf16(TBW)
    rp1 = E.pool.alloc_bf16(T)
    qsw = E.pool.alloc_bf16(T)
    ctab = E.pool.alloc_bf16(T)
    stab = E.pool.alloc_bf16(T)
    masks = E.pool.alloc_bf16(4 * TBW)
    wkvb = E.pool.alloc_bf16(512)
    wqb = E.pool.alloc_bf16(D)
    wout = E.pool.alloc_bf16(NCH * D)
    boutb = E.pool.alloc_bf16(D)
    obuf = E.pool.alloc_bf16(NCH * TBW)
    PT = E.pool.alloc_bf16(2 * TBW)
    rkb = E.pool.alloc_f32(T)
    rowq = E.pool.alloc_f32(TBW)
    kmax = E.pool.alloc_f32(16)
    RBb = E.pool.alloc_bf16(TBW)
    oadd = E.pool.alloc_bf16(TBW)
    E.oadd_war = None
    E.rbb_war = None
    E.sqf_war = None

    prj = t1toks + t2toks
    cin = declare_input(E, "b0_ropec")
    sin_ = declare_input(E, "b0_ropes")
    min_ = declare_input(E, "b0_masks")
    t_ct = p.op("gpsimd", lambda h: h.dma_start(out=ctab, in_=cin[:, :]), prj, dma=True)
    t_st = p.op("gpsimd", lambda h: h.dma_start(out=stab, in_=sin_[:, :]), prj, dma=True)
    t_mk = p.op("gpsimd", lambda h: h.dma_start(out=masks, in_=min_[:, :]), prj, dma=True)
    wkvb_in = declare_input(E, "b0_wkvb")
    t_wkvb = p.op("gpsimd", lambda h: h.dma_start(out=wkvb[0:32, :], in_=wkvb_in[:, :]), prj, dma=True)
    wqb_in = declare_input(E, "b0_wqb")
    t_wqb = p.op("gpsimd", lambda h: h.dma_start(out=wqb[0:64, :], in_=wqb_in[:, :]), prj, dma=True)
    t_wout = load_weight_bf16(E, "b0_wout", wout, prj)
    bout_in = declare_input(E, "b0_bout")
    t_bout = p.op("gpsimd", lambda h: h.dma_start(out=boutb[0:1, :], in_=bout_in[:, :]), prj, dma=True)

    ktoks, vtoks = [], []
    for mc in range(2):
        for tb in range(NTB):
            i = (mc * NTB + tb) % 2
            tm = _mmgroup(E, E.ps[i][:, :],
                          [(wkvb[0:32, mc * 128:(mc + 1) * 128],
                            t2b[0:32, tb * TBW:(tb + 1) * TBW])],
                          t2toks + [t_wkvb, E.ps_war[i], t_ms2a, t_ms2b])
            tv = p.op("vector", (lambda h, mc=mc, tb=tb, i=i: h.tensor_copy(
                kT[:, mc * T + tb * TBW: mc * T + tb * TBW + TBW], E.ps[i][:, :])), [tm])
            E.ps_war[i] = tv
            ktoks.append(tv)
    for tc in range(16):
        i = tc % 2
        tm = _mmgroup(E, E.ps[i][:, 0:256],
                      [(t2b[0:32, tc * 128:(tc + 1) * 128], wkvb[0:32, 256:512])],
                      t2toks + [t_wkvb, E.ps_war[i], t_ms2a, t_ms2b])
        tv = p.op("vector", (lambda h, tc=tc, i=i: h.tensor_copy(
            vtok[:, tc * 256:(tc + 1) * 256], E.ps[i][:, 0:256])), [tm])
        E.ps_war[i] = tv
        vtoks.append(tv)
    qtoks = []
    for ec in range(NCH):
        for tb in range(NTB):
            i = (ec * NTB + tb) % 2
            tm = _mmgroup(E, E.ps[i][:, :],
                          [(wqb[0:64, ec * 128:(ec + 1) * 128],
                            t1b[0:64, tb * TBW:(tb + 1) * TBW])],
                          t1toks + [t_wqb, E.ps_war[i], t_ms1])
            tv = p.op("vector", (lambda h, ec=ec, tb=tb, i=i: h.tensor_scalar_mul(
                qT[:, ec * T + tb * TBW: ec * T + tb * TBW + TBW], E.ps[i][:, :], 0.125)), [tm])
            E.ps_war[i] = tv
            qtoks.append(tv)

    def rope_chunk(buf, c, deps):
        base = buf[:, c * T:(c + 1) * T]
        sw = []
        for (dst, srow) in ((0, 32), (32, 0), (64, 96), (96, 64)):
            sw.append(p.op("sync", (lambda h, dst=dst, srow=srow, base=base: h.dma_start(
                out=qsw[dst:dst + 32, :], in_=base[srow:srow + 32, :])), deps + [E.qsw_war], dma=True))
        tr1 = p.op("vector", (lambda h, base=base: h.tensor_tensor(rp1, base, ctab, ALU.mult)),
                   deps + [t_ct, E.rp1_war])
        tr2 = p.op("vector", lambda h: h.tensor_tensor(qsw, qsw, stab, ALU.mult), sw + [t_st])
        tr3 = p.op("vector", (lambda h, base=base: h.tensor_tensor(base, rp1, qsw, ALU.add)),
                   [tr1, tr2])
        E.qsw_war = tr3
        E.rp1_war = tr3
        return tr3
    E.qsw_war = None
    E.rp1_war = None
    rtoks = []
    for c in range(NCH):
        rtoks.append(rope_chunk(qT, c, qtoks))
    krtoks = []
    for c in range(2):
        krtoks.append(rope_chunk(kT, c, ktoks))
    if DEBUG:
        E.dbg["qT"] = (qT, rtoks)
        E.dbg["kT"] = (kT, krtoks)
        E.dbg["vtok"] = (vtok, vtoks)
    if PHASE <= 1:
        E.pool.pop()
        return

    sqk = []
    for c in range(2):
        sqk.append(p.op("scalar", (lambda h, c=c: h.activation(
            rp1 if c == 0 else qsw, kT[:, c * T:(c + 1) * T], AF.Square)),
            [krtoks[c], E.rp1_war]))
    kmtoks = []
    for g in range(4):
        b = (g % 2) * 64
        src_ = rp1 if g // 2 == 0 else qsw
        for tb in range(NTB):
            tm = _mmgroup(E, E.ps[5][0:1, :],
                          [(E.ones_bf[b:b + 64, 0:1], src_[b:b + 64, tb * TBW:(tb + 1) * TBW])],
                          [sqk[g // 2], E.ps_war[5]], tile_position=(b, 0), start_clear=True)
            tv = p.op("vector", (lambda h, tb=tb: h.tensor_copy(
                rkb[0:1, tb * TBW:(tb + 1) * TBW], E.ps[5][0:1, :])), [tm])
            E.ps_war[5] = tv
            kmtoks.append(tv)
        t_m = p.op("vector", (lambda h, g=g: h.reduce_max(
            kmax[0:1, g:g + 1], rkb[0:1, :], axis=AX.X)), kmtoks[-4:])
        kmtoks.append(p.op("scalar", (lambda h, g=g: h.activation(
            kmax[0:1, 4 + g:5 + g], kmax[0:1, g:g + 1], AF.Sqrt)), [t_m]))
        kmtoks.append(p.op("scalar", (lambda h, g=g: h.mul(
            kmax[0:1, 8 + g:9 + g], kmax[0:1, 4 + g:5 + g], -1.0)), [kmtoks[-1]]))

    if DEBUG:
        E.dbg["kmax"] = (kmax, kmtoks[-2:])
    if PHASE <= 2:
        E.pool.pop()
        return
    for qb in range(NTB):
        oproj_deps = []
        for hh in range(16):
            ch = (hh % 4) + 4 * (hh // 8)
            half = (hh // 4) % 2
            b = half * 64
            g = hh // 4
            tsq = p.op("scalar", (lambda h, ch=ch, qb=qb, b=b: h.activation(
                SQF[b:b + 64, :], qT[:, ch * T + qb * TBW: ch * T + qb * TBW + TBW][b:b + 64, :],
                AF.Square)), [rtoks[ch], E.sqf_war])
            tqm = _mmgroup(E, E.ps[5][0:1, :],
                           [(E.ones_bf[b:b + 64, 0:1], SQF[b:b + 64, :])],
                           [tsq, E.ps_war[5]], tile_position=(b, 0))
            E.sqf_war = tqm
            tqr = p.op("scalar", lambda h: h.activation(rowq[0:1, :], E.ps[5][0:1, :], AF.Sqrt), [tqm])
            E.ps_war[5] = tqr
            tnb = p.op("vector", (lambda h, g=g: h.tensor_scalar(
                PT[0:1, TBW:TBW + TBW], rowq[0:1, :], kmax[0:1, 8 + g:9 + g], None, ALU.mult)),
                [tqr, kmtoks[-1], E.nb_war])
            nbrow = PT[0:1, TBW:2 * TBW]
            nkc = 4 * qb + 4
            omms = []
            for kc in range(nkc):
                i = kc % 2
                pairs = [(kT[:, g * T // 2 + kc * 128: g * T // 2 + kc * 128 + 128][b:b + 64, :]
                          if False else
                          kT[:, (g // 2) * T + kc * 128:(g // 2) * T + kc * 128 + 128][b:b + 64, :],
                          qT[:, ch * T + qb * TBW: ch * T + qb * TBW + TBW][b:b + 64, :])]
                tm = _mmgroup(E, E.ps[i][:, :], pairs, [rtoks[ch], krtoks[g // 2], tnb, E.ps_war[i]],
                              extra=[(E.ones_bf[0:1, :], nbrow)], tile_position=(b, 0))
                if kc >= 4 * qb:
                    r = kc - 4 * qb
                    tm = p.op("vector", (lambda h, i=i, r=r: h.tensor_tensor(
                        E.ps[i][:, :], E.ps[i][:, :], masks[:, r * TBW:(r + 1) * TBW], ALU.add)),
                        [tm, t_mk])
                te = p.op("scalar", (lambda h, i=i, kc=kc: h.activation(
                    PT[:, (kc % 2) * TBW:(kc % 2) * TBW + TBW] if False else PT[:, 0:TBW],
                    E.ps[i][:, :], AF.Exp)), [tm] + omms)
                E.ps_war[i] = te
                tden = _mmgroup(E, E.ps[6][0:1, :], [(E.ones_bf[:, 0:1], PT[:, 0:TBW])],
                                [te, E.ps_war[6]], start_clear=(kc == 0), stop_group=(kc == nkc - 1))
                tO = _mmgroup(E, E.ps[2][0:64, :],
                              [(vtok[:, kc * 256 + g * 64: kc * 256 + g * 64 + 64], PT[:, 0:TBW])],
                              [te, E.ps_war[2]], start_clear=(kc == 0), stop_group=(kc == nkc - 1))
                omms = [tO, tden]
            E.ps_war[6] = omms[1]
            t_e = p.op("vector", lambda h: h.tensor_copy(E.row_t, E.ps[6][0:1, :]),
                       [omms[1], E.row_war])
            E.ps_war[6] = t_e
            t_rc = p.op("vector", lambda h: h.reciprocal(E.row_m, E.row_t), [t_e])
            t_rb = _mmgroup(E, E.ps[4][0:64, :], [(E.ones_f32[0:1, 0:64], E.row_m)],
                            [t_rc, E.ps_war[4]])
            t_rbs = p.op("vector", lambda h: h.tensor_copy(RBb[0:64, :], E.ps[4][0:64, :]),
                         [t_rb, E.rbb_war])
            t_o = p.op("vector", (lambda h, ch=ch, b=b: h.tensor_tensor(
                obuf[:, ch * TBW:(ch + 1) * TBW][b:b + 64, :],
                E.ps[2][0:64, :], RBb[0:64, :], ALU.mult)), [t_rbs, omms[0], E.obuf_war])
            E.rbb_war = t_o
            E.ps_war[4] = t_o
            E.ps_war[2] = t_o
            E.row_war = t_o
            E.nb_war = t_o
            oproj_deps.append(t_o)
        if PHASE in (21, 22, 23):
            break
        for ec in range(NCH):
            bank = 3 + ec % 2
            pairs = [(wout[:, cc * D + ec * 128: cc * D + ec * 128 + 128],
                      obuf[:, cc * TBW:(cc + 1) * TBW]) for cc in range(NCH)]
            tm = _mmgroup(E, E.ps[bank][:, :], pairs,
                          oproj_deps + [t_wout, t_bout, E.const_tok[-1], E.ps_war[bank]],
                          extra=[(boutb[0:1, ec * 128:(ec + 1) * 128], E.onesr[0:1, :])])
            tcp = p.op("vector", (lambda h, bank=bank: h.tensor_copy(oadd, E.ps[bank][:, :])),
                       [tm, E.oadd_war])
            tx = p.op("vector", (lambda h, ec=ec, qb=qb: h.tensor_tensor(
                E.x[:, ec * T + qb * TBW: ec * T + qb * TBW + TBW],
                E.x[:, ec * T + qb * TBW: ec * T + qb * TBW + TBW],
                oadd, ALU.add)), [tcp, E.xtok.get((ec, qb))])
            E.oadd_war = tx
            E.ps_war[bank] = tcp
            E.xtok[(ec, qb)] = tx
            E.obuf_war = tx
    E.pool.pop()


def emit_moe(E, s):
    p = E.p
    nc = E.nc
    xld = nc.dram_tensor(f"moe_xln{s}", [D, T], BF16).ap()
    E.pool.push()
    mw8 = E.pool.alloc_bf16(T)
    E.pool.push()
    E.xln = E.pool.alloc_bf16(NCH * T)
    emit_ln(E)
    wgb = E.pool.alloc_bf16(NCH * 32)
    lg = E.pool.alloc_bf16(T)
    ltok = E.pool.alloc_bf16(16 * 32)
    msc = E.pool.alloc_f32(4 * 16)
    prb = E.pool.alloc_f32(16 * 32)
    prbb = E.pool.alloc_bf16(16 * 32)
    bgr = E.pool.alloc_bf16(32)
    t_wg = load_weight_bf16(E, "m1_wg", wgb, E.lntoks)
    bg_in = declare_input(E, "m1_bg")
    t_bg = p.op("gpsimd", lambda h: h.dma_start(out=bgr[0:1, 0:32], in_=bg_in[:, :]), E.lntoks, dma=True)
    lgt = []
    for tb in range(NTB):
        i = tb % 2
        pairs = [(wgb[:, c * 32:(c + 1) * 32],
                  E.xln[:, c * T + tb * TBW: c * T + tb * TBW + TBW]) for c in range(NCH)]
        tm = _mmgroup(E, E.ps[i][0:32, :], pairs, E.lntoks + [t_wg, t_bg, E.ps_war[i]],
                      extra=[(bgr[0:1, 0:32], E.onesr[0:1, :])])
        tv = p.op("vector", (lambda h, tb=tb, i=i: h.tensor_copy(
            lg[0:32, tb * TBW:(tb + 1) * TBW], E.ps[i][0:32, :])), [tm])
        E.ps_war[i] = tv
        lgt.append(tv)
    if PHASE == 411:
        E.pool.pop()
        E.pool.pop()
        return
    def all_tr(h):
        last = None
        for tc in range(16):
            last = h.transpose(E.psb[:, tc * 32:(tc + 1) * 32],
                               lg[0:32, tc * 128:(tc + 1) * 128], E.ident[0:32, 0:32])
        return last
    ttr = p.op("tensor", all_tr, lgt + [E.const_tok[1], E.psb_war])
    tte = p.op("vector", lambda h: h.tensor_copy(ltok[:, 0:512], E.psb[:, 0:512]), [ttr])
    ttk = [tte] * 16
    E.psb_war = tte
    if PHASE == 412:
        E.pool.pop()
        E.pool.pop()
        return
    mw_toks = []
    for tc in range(16):
        sl = ltok[:, tc * 32:(tc + 1) * 32]
        po = prb[:, tc * 32:(tc + 1) * 32]
        pb = prbb[:, tc * 32:(tc + 1) * 32]
        m1_ = msc[:, tc:tc + 1]
        t0 = p.op("vector", (lambda h, sl=sl, m1_=m1_: h.reduce_max(m1_, sl, axis=AX.X, negate=True)),
                  [ttk[tc], E.msc_war])
        t1 = p.op("scalar", (lambda h, sl=sl, po=po, m1_=m1_: h.activation(
            po, sl, AF.Exp, bias=m1_)), [t0])
        t2 = p.op("vector", (lambda h, po=po, tc=tc: h.reduce_sum(
            msc[:, 16 + tc:17 + tc], po, axis=AX.X)), [t1])
        t3 = p.op("vector", (lambda h, tc=tc: h.reciprocal(
            msc[:, 32 + tc:33 + tc], msc[:, 16 + tc:17 + tc])), [t2])
        t6 = p.op("vector", (lambda h, po=po, m1_=m1_: h.reduce_max(m1_, po, axis=AX.X)), [t1, t0])
        t7 = p.op("scalar", (lambda h, tc=tc, m1_=m1_: h.copy(msc[:, 48 + tc:49 + tc], m1_)), [t6])
        t8 = p.op("vector", (lambda h, po=po, tc=tc: h.tensor_scalar(
            po, po, msc[:, 48 + tc:49 + tc], None, ALU.is_ge)), [t7, t3])
        t9 = p.op("scalar", (lambda h, tc=tc: h.mul(msc[:, tc:tc + 1],
                                                    msc[:, 32 + tc:33 + tc], 1.0)), [t3, t6, t7])
        ta_ = p.op("vector", (lambda h, tc=tc: h.tensor_tensor(
            msc[:, 16 + tc:17 + tc], msc[:, tc:tc + 1], msc[:, 48 + tc:49 + tc], ALU.mult)),
            [t9, t7])
        tb2 = p.op("scalar", (lambda h, tc=tc: h.copy(msc[:, 32 + tc:33 + tc],
                                                      msc[:, 16 + tc:17 + tc])), [ta_])
        tc_ = p.op("vector", (lambda h, po=po, pb=pb, tc=tc: h.tensor_scalar(
            pb, po, msc[:, 32 + tc:33 + tc], None, ALU.mult)), [tb2, t8])
        mw_toks.append(tc_)
    E.msc_war = mw_toks[-1]
    if PHASE == 413:
        E.pool.pop()
        E.pool.pop()
        return
    mw8t = []
    for rc in range(2):
        def all_tr2(h, rc=rc):
            last = None
            for j in range(8):
                tc = rc * 8 + j
                last = h.transpose(E.psb[0:32, j * 128:(j + 1) * 128],
                                   prbb[:, tc * 32:(tc + 1) * 32], E.ident)
            return last
        ttr = p.op("tensor", all_tr2, mw_toks + [E.const_tok[1], E.psb_war])
        tv = p.op("vector", (lambda h, rc=rc: h.tensor_copy(
            mw8[0:32, rc * 1024:(rc + 1) * 1024], E.psb[0:32, 0:1024])), [ttr])
        E.psb_war = tv
        mw8t.append(tv)
    t_xd = p.op("sync", (lambda h, x_=E.xln: h.dma_start(
        out=xld.rearrange("(c p) t -> p c t", p=128),
        in_=x_.rearrange("p (c t) -> p c t", c=NCH))), E.lntoks, dma=True)
    if DEBUG:
        E.dbg["mw8"] = (mw8[0:32, :], mw8t)
    E.pool.pop()
    if PHASE == 41:
        E.pool.pop()
        return

    xlr = E.pool.alloc_bf16(NCH * TBW)
    hbuf = E.pool.alloc_bf16(32 * TBW)
    w1b = E.pool.alloc_bf16(2 * NCH * 2048)
    w2b = E.pool.alloc_bf16(2 * NCH * 256)
    mwbs = [E.pool.alloc_bf16(TBW) for _ in range(2)]
    ot1s = [E.pool.alloc_bf16(TBW) for _ in range(4)]
    mwbw = [[E.mwb_war], [E.mwb_war]]
    ot1w = [E.ot1_war] * 4
    oix = [0]
    mix_ = [0]
    selb = E.pool.alloc_bf16(1024)
    b1b = E.pool.alloc_f32(8 * 32)
    b2b = E.pool.alloc_bf16(1024)
    sel_in = declare_input(E, "m1_sel")
    b1_in = declare_input(E, "m1_b1")
    b2_in = declare_input(E, "m1_b2")
    gdeps = mw8t + [t_xd]
    t_sel = p.op("gpsimd", lambda h: h.dma_start(out=selb[0:8, :], in_=sel_in[:, :]), gdeps, dma=True)
    t_b1 = p.op("gpsimd", lambda h: h.dma_start(
        out=b1b.rearrange("p (e c) -> p e c", e=8), in_=b1_in.rearrange("e (p c) -> p e c", p=128)),
        gdeps, dma=True)
    t_b2 = p.op("gpsimd", lambda h: h.dma_start(out=b2b[0:8, :], in_=b2_in[:, :]), gdeps, dma=True)
    w1_in = declare_input(E, "m1_w1")
    w2_in = declare_input(E, "m1_w2")
    w1_4 = w1_in.rearrange("e (c p) m -> e p c m", p=128)
    w2_4 = w2_in.rearrange("e (c p) m -> e p c m", p=128)
    xld4 = xld.rearrange("(c p) t -> p c t", p=128)

    w1tok = {0: gdeps, 1: gdeps}
    w2tok = {0: gdeps, 1: gdeps}
    xltok = {0: gdeps, 1: gdeps}
    for e in range(8):
        t_w1 = {}
        for hf in range(2):
            t_w1[hf] = p.op("gpsimd", (lambda h, e=e, hf=hf: h.dma_start(
                out=w1b[:, hf * NCH * 2048:(hf + 1) * NCH * 2048].rearrange("p (c m) -> p c m", c=NCH),
                in_=w1_4[e][:, :, hf * 2048:(hf + 1) * 2048])), w1tok[hf], dma=True)
        w1reads = {0: [], 1: []}
        for tb in range(NTB):
            slot = 0
            xl = xlr
            t_xl = p.op("sync", (lambda h, tb=tb, xl=xl: h.dma_start(
                out=xl.rearrange("p (c t) -> p c t", c=NCH),
                in_=xld4[:, :, tb * TBW:(tb + 1) * TBW])), xltok[slot] + [t_xd], dma=True)
            xltok[slot] = []
            htoks = []
            for hc in range(32):
                hf = hc // 16
                i = hc % 2
                wsl = w1b[:, hf * NCH * 2048:(hf + 1) * NCH * 2048]
                hc_ = hc % 16
                pairs = [(wsl[:, c * 2048 + hc_ * 128: c * 2048 + hc_ * 128 + 128],
                          xl[:, c * TBW:(c + 1) * TBW]) for c in range(NCH)]
                tm = _mmgroup(E, E.ps[i][:, :], pairs, [t_xl, t_w1[hf], t_b1, E.ps_war[i]])
                xltok[slot].append(tm)
                w1reads[hf].append(tm)
                th = p.op("scalar", (lambda h, hc=hc, i=i, e=e: h.activation(
                    hbuf[:, hc * TBW:(hc + 1) * TBW], E.ps[i][:, :], AF.Silu,
                    bias=b1b[:, e * 32 + hc: e * 32 + hc + 1])), [tm])
                E.ps_war[i] = th
                htoks.append(th)
            mb = _mmgroup(E, E.ps[6][:, :], [(selb[0:8, e * 128:(e + 1) * 128],
                                              mw8[0:8, tb * TBW:(tb + 1) * TBW])],
                          [t_sel, E.ps_war[6]] + mw8t)
            ms = mix_[0] % 2
            mix_[0] += 1
            mwb = mwbs[ms]
            mvb = p.op("vector", (lambda h, mwb=mwb: h.tensor_copy(mwb, E.ps[6][:, :])),
                       [mb] + mwbw[ms])
            mwbw[ms] = []
            E.ps_war[6] = mvb
            for grp in range(4):
                otoks = []
                for Kq in range(4):
                    wslot = (grp * 4 + Kq) % 2
                    t_w2 = p.op("gpsimd", (lambda h, e=e, grp=grp, Kq=Kq, wslot=wslot: h.dma_start(
                        out=w2b[:, wslot * NCH * 256:(wslot + 1) * NCH * 256].rearrange(
                            "p (c m) -> p c m", c=NCH),
                        in_=w2_4[e][:, Kq * 8:(Kq + 1) * 8, grp * 256:(grp + 1) * 256])),
                        w2tok[wslot], dma=True)
                    w2tok[wslot] = []
                    for dh in range(2):
                        dc = grp * 2 + dh
                        bank = 2 + dh
                        wsl = w2b[:, wslot * NCH * 256:(wslot + 1) * NCH * 256]
                        pairs = [(wsl[:, kc * 256 + dh * 128: kc * 256 + dh * 128 + 128],
                                  hbuf[:, (Kq * 8 + kc) * TBW:(Kq * 8 + kc + 1) * TBW])
                                 for kc in range(8)]
                        tm = _mmgroup(E, E.ps[bank][:, :], pairs,
                                      htoks + [t_w2, E.ps_war[bank]], start_clear=(Kq == 0),
                                      stop_group=(Kq == 3))
                        w2tok[wslot].append(tm)
                        otoks.append(tm)
                for dh in range(2):
                    dc = grp * 2 + dh
                    bank = 2 + dh
                    osl = oix[0] % 4
                    oix[0] += 1
                    ot1 = ot1s[osl]
                    t1 = p.op("vector", (lambda h, bank=bank, ot1=ot1, mwb=mwb: h.tensor_tensor(
                        ot1, E.ps[bank][:, :], mwb, ALU.mult)),
                        [otoks[-2 + dh], mvb, ot1w[osl]])
                    mwbw[ms].append(t1)
                    tx = p.op("vector", (lambda h, dc=dc, tb=tb, ot1=ot1: h.tensor_tensor(
                        E.x[:, dc * T + tb * TBW: dc * T + tb * TBW + TBW],
                        E.x[:, dc * T + tb * TBW: dc * T + tb * TBW + TBW], ot1, ALU.add)),
                        [t1, E.xtok.get((dc, tb))])
                    ot1w[osl] = tx
                    E.ps_war[bank] = tx
                    E.xtok[(dc, tb)] = tx
        w1tok = {0: w1reads[0], 1: w1reads[1]}
        if PHASE == 42:
            break
    for tb in range(NTB):
        for dc in range(NCH):
            bank = 2 + dc % 2
            tm = _mmgroup(E, E.ps[bank][:, :],
                          [(b2b[0:8, dc * 128:(dc + 1) * 128], mw8[0:8, tb * TBW:(tb + 1) * TBW])],
                          [t_b2, E.ps_war[bank]] + mw8t)
            osl = oix[0] % 4
            oix[0] += 1
            ot1 = ot1s[osl]
            t1 = p.op("vector", (lambda h, bank=bank, ot1=ot1: h.tensor_copy(ot1, E.ps[bank][:, :])),
                      [tm, ot1w[osl]])
            tx = p.op("vector", (lambda h, dc=dc, tb=tb, ot1=ot1: h.tensor_tensor(
                E.x[:, dc * T + tb * TBW: dc * T + tb * TBW + TBW],
                E.x[:, dc * T + tb * TBW: dc * T + tb * TBW + TBW], ot1, ALU.add)),
                [t1, E.xtok.get((dc, tb))])
            ot1w[osl] = tx
            E.ot1_war = tx
            E.ps_war[bank] = tx
            E.xtok[(dc, tb)] = tx
    E.pool.pop()


def emit_mamba(E):
    p = E.p
    nc = E.nc
    NH, PP, NN = 32, 64, 64
    DIN = 2048
    gated = nc.dram_tensor("mb_gate", [DIN, T], BF16).ap()
    ssmd = nc.dram_tensor("mb_ssm", [DIN, T], BF16).ap()
    htd = nc.dram_tensor("mb_htok", [T, DIN], BF16).ap()
    ytd = nc.dram_tensor("mb_yfm", [DIN, T], BF16).ap()
    ymd = nc.dram_tensor("mb_ymix", [DIN, T], BF16).ap()

    E.pool.push()
    bcT = E.pool.alloc_bf16(T)
    dtb = E.pool.alloc_bf16(T)
    lcb = E.pool.alloc_bf16(T)
    mlcb = E.pool.alloc_bf16(T)
    elamb = E.pool.alloc_bf16(16 * 32)
    lamr = E.pool.alloc_f32(16 * 32)
    lamb16 = E.pool.alloc_bf16(16 * 32)
    E.pool.push()
    xln1 = E.pool.alloc_bf16(NCH * T)
    E.xln = xln1
    emit_ln(E)
    ln1toks = list(E.lntoks)
    xnd = nc.dram_tensor("mb_ln1", [D, T], BF16).ap()
    t_x1 = p.op("sync", (lambda h, x_=xln1: h.dma_start(
        out=xnd.rearrange("(c p) t -> p c t", p=128),
        in_=x_.rearrange("p (c t) -> p c t", c=NCH))), ln1toks, dma=True)
    xn = E.pool.alloc_bf16(NCH * T)
    E.xln = xn
    emit_ln(E, xsrc=xln1, xsrc_tok=ln1toks[-1], out=xn, rms=True)
    dtrF = E.pool.alloc_f32(T)
    wrng = E.pool.alloc_bf16(2 * NCH * TBW)
    stg = E.pool.alloc_bf16(TBW)
    win_in = declare_input(E, "s1_win")
    win4 = win_in.rearrange("(c p) m -> p c m", p=128)
    wtok = {0: E.lntoks, 1: E.lntoks}
    bct = []
    dtt = []
    stg_war = None
    for blk in range(9):
        ncols = 512 if blk < 8 else 256
        slot = blk % 2
        wsl = wrng[:, slot * NCH * TBW: slot * NCH * TBW + NCH * ncols]
        t_w = p.op("gpsimd", (lambda h, blk=blk, wsl=wsl, ncols=ncols: h.dma_start(
            out=wsl.rearrange("p (c m) -> p c m", c=NCH),
            in_=win4[:, :, blk * 512: blk * 512 + ncols])), wtok[slot], dma=True)
        wtok[slot] = []
        for j in range(ncols // 128):
            oc = blk * 4 + j
            for tb in range(NTB):
                i = tb % 2
                pairs = [(wsl[:, c * ncols + j * 128: c * ncols + j * 128 + 128],
                          xn[:, c * T + tb * TBW: c * T + tb * TBW + TBW]) for c in range(NCH)]
                tm = _mmgroup(E, E.ps[i][:, :], pairs, E.lntoks + [t_w, E.ps_war[i]])
                wtok[slot].append(tm)
                if oc == 0:
                    tv = p.op("vector", (lambda h, tb=tb, i=i: h.tensor_copy(
                        bcT[:, tb * TBW:(tb + 1) * TBW], E.ps[i][:, :])), [tm])
                    bct.append(tv)
                elif oc == 1:
                    tv = p.op("vector", (lambda h, tb=tb, i=i: h.tensor_copy(
                        dtrF[0:32, tb * TBW:(tb + 1) * TBW], E.ps[i][0:32, :])), [tm])
                    dtt.append(tv)
                else:
                    dst = gated if oc < 18 else ssmd
                    fc = oc - 2 if oc < 18 else oc - 18
                    tv = p.op("vector", (lambda h, i=i: h.tensor_copy(stg, E.ps[i][:, :])),
                              [tm, stg_war])
                    ts_ = p.op("sync", (lambda h, dst=dst, fc=fc, tb=tb: h.dma_start(
                        out=dst.rearrange("(c p) t -> p c t", p=128)[:, fc, tb * TBW:(tb + 1) * TBW],
                        in_=stg)), [tv], dma=True)
                    stg_war = ts_
                E.ps_war[i] = tv
    sc32 = E.pool.alloc_f32(T)
    sc32b = E.pool.alloc_f32(T)
    dtb_in = declare_input(E, "s1_dtb")
    al_in = declare_input(E, "s1_alog")
    cols = E.pool.alloc_f32(8)
    t_c = p.op("gpsimd", lambda h: h.dma_start(out=cols[0:32, 0:1], in_=dtb_in[:, :]), dma=True)
    t_a = p.op("gpsimd", lambda h: h.dma_start(out=cols[0:32, 1:2], in_=al_in[:, :]), dma=True)
    dtv = dtrF[0:32, :]
    s_a = sc32[0:32, :]
    s_b = sc32b[0:32, :]
    t1 = p.op("vector", lambda h: h.tensor_scalar(dtv, dtv, cols[0:32, 0:1], None, ALU.add),
              dtt + [t_c])
    t2 = p.op("scalar", lambda h: h.activation(s_a, dtv, AF.Abs), [t1])
    t3 = p.op("scalar", lambda h: h.activation(s_b, s_a, AF.Exp, scale=-1.0), [t2])
    t4 = p.op("vector", lambda h: h.tensor_scalar_add(s_b, s_b, 1.0), [t3])
    t5 = p.op("scalar", lambda h: h.activation(s_b, s_b, AF.Ln), [t4])
    t6 = p.op("vector", lambda h: h.tensor_relu(s_a, dtv), [t2, t5])
    t7 = p.op("vector", lambda h: h.tensor_tensor(dtv, s_a, s_b, ALU.add), [t6, t5])
    t15 = p.op("vector", lambda h: h.tensor_copy(dtb[0:32, :], dtv), [t7])
    t9 = p.op("scalar", lambda h: h.activation(cols[0:32, 2:3], cols[0:32, 1:2], AF.Exp), [t_a])
    t10 = p.op("scalar", lambda h: h.mul(cols[0:32, 3:4], cols[0:32, 2:3], -1.0), [t9])
    t11 = p.op("vector", lambda h: h.tensor_scalar(s_a, dtv, cols[0:32, 3:4], None, ALU.mult),
               [t10, t6, t15])
    lct = []
    lam_t = []
    for c_ in range(16):
        lct.append(p.op("vector", (lambda h, c_=c_: h.tensor_tensor_scan(
            s_b[:, c_ * 128:(c_ + 1) * 128], E.ones_f32[0:32, :],
            s_a[:, c_ * 128:(c_ + 1) * 128], 0.0, ALU.mult, ALU.add)), [t11, E.const_tok[2], t5]))
        lam_t.append(p.op("vector", (lambda h, c_=c_: h.reduce_sum(
            dtrF[32:64, c_:c_ + 1], s_a[:, c_ * 128:(c_ + 1) * 128], axis=AX.X)), [t11]))
    t13 = p.op("vector", lambda h: h.tensor_copy(lcb[0:32, :], s_b[0:32, :]), lct)
    t14 = p.op("vector", lambda h: h.tensor_scalar_mul(mlcb[0:32, :], s_b[0:32, :], -1.0), lct)
    tds = []
    for c_ in range(16):
        tds.append(p.op("sync", (lambda h, c_=c_: h.dma_start(
            out=lamr[0:1, c_ * 32:(c_ + 1) * 32], in_=dtrF[32:64, c_:c_ + 1])), lam_t, dma=True))
    te_ = p.op("scalar", lambda h: h.activation(elamb[0:1, 0:512], lamr[0:1, 0:512], AF.Exp), tds)
    tlb16 = p.op("vector", lambda h: h.tensor_copy(lamb16[0:1, 0:512], lamr[0:1, 0:512]), tds)
    E.mbtok = dict(rows=[t13, t14, t15, te_], proj=list(wtok[0]) + list(wtok[1]) + [stg_war],
                   bct=bct, xnd=t_x1, lamb16=tlb16)
    E.mbtok["xnd_ap"] = xnd
    E.mb = (bcT, dtb, lcb, mlcb, elamb, lamb16)
    E.pool.pop()

    barrier(E)
    E.pool.push()
    cwcol = E.pool.alloc_f32(64)
    cbcol = E.pool.alloc_f32(16)
    cw_in = declare_input(E, "s1_cw")
    cb_in = declare_input(E, "s1_cb")
    prj = E.mbtok["proj"]
    t_cw = p.op("gpsimd", lambda h: h.dma_start(out=cwcol, in_=cw_in), prj, dma=True)
    t_cb = p.op("gpsimd", lambda h: h.dma_start(out=cbcol, in_=cb_in), prj, dma=True)
    spad = E.pool.alloc_bf16(T + 16)
    diag = E.pool.alloc_bf16(4 * 128)
    hTc = E.pool.alloc_bf16(T)
    hstg = E.pool.alloc_bf16(1024)
    sp_war, dg_war, hs_war = None, None, None
    ssm4 = ssmd.rearrange("(c p) t -> p c t", p=128)
    ht3 = htd.rearrange("(tc p) f -> p tc f", p=128)
    ht_done = []
    for fc in range(16):
        t_z = p.op("gpsimd", (lambda h: h.memset(spad[:, 0:16], 0.0)), [sp_war])
        t_l = p.op("sync", (lambda h, fc=fc: h.dma_start(
            out=spad[:, 13:13 + T], in_=ssm4[:, fc])), [t_z, sp_war] + prj, dma=True)
        dts = []
        for tap in range(4):
            dts.append(p.op("vector", (lambda h, fc=fc, tap=tap: h.tensor_scalar(
                diag[:, tap * 128:(tap + 1) * 128], E.ident,
                cwcol[:, fc * 4 + tap: fc * 4 + tap + 1], None, ALU.mult)),
                [t_cw, E.const_tok[1], dg_war]))
        httoks = []
        tm = None
        for tb in range(NTB):
            i = tb % 2
            pairs = [(diag[:, tap * 128:(tap + 1) * 128],
                      spad[:, 10 + tap + tb * TBW: 10 + tap + tb * TBW + TBW]) for tap in range(4)]
            tm = _mmgroup(E, E.ps[i][:, :], pairs, dts + [t_l, E.ps_war[i], t_cb])
            th = p.op("scalar", (lambda h, tb=tb, i=i, fc=fc: h.activation(
                hTc[:, tb * TBW:(tb + 1) * TBW], E.ps[i][:, :], AF.Silu,
                bias=cbcol[:, fc:fc + 1])), [tm, hs_war])
            E.ps_war[i] = th
            httoks.append(th)
        sp_war = tm
        dg_war = tm
        for rnd in range(2):
            def trs(h, rnd=rnd):
                last = None
                for j in range(8):
                    tc = rnd * 8 + j
                    last = h.transpose(E.psb[:, j * 128:(j + 1) * 128],
                                       hTc[:, tc * 128:(tc + 1) * 128], E.ident)
                return last
            ttr = p.op("tensor", trs, httoks + [E.psb_war, E.const_tok[1]])
            tev = p.op("vector", (lambda h: h.tensor_copy(hstg, E.psb[:, 0:1024])), [ttr, hs_war])
            E.psb_war = tev
            tdm = p.op("sync", (lambda h, fc=fc, rnd=rnd: h.dma_start(
                out=ht3[:, rnd * 8:(rnd + 1) * 8, fc * 128:(fc + 1) * 128],
                in_=hstg.rearrange("p (tc f) -> p tc f", tc=8))), [tev], dma=True)
            hs_war = tdm
            ht_done.append(tdm)
    E.pool.pop()

    barrier(E)
    E.pool.push()
    (bcT, dtb, lcb, mlcb, elamb, lamb16) = E.mb
    btok = E.pool.alloc_bf16(16 * 64)
    xhb = E.pool.alloc_bf16(2 * DIN)
    Gall = E.pool.alloc_bf16(2 * DIN)
    Gsc = E.pool.alloc_f32(DIN)
    lrow = E.pool.alloc_bf16(4096)
    mrow = E.pool.alloc_bf16(4096)
    dtk = E.pool.alloc_bf16(32)
    ltk = E.pool.alloc_f32(32)
    exl = E.pool.alloc_f32(32)
    e2s = E.pool.alloc_f32(64)
    ftk = E.pool.alloc_bf16(32)
    xht = E.pool.alloc_bf16(DIN)
    scat = E.pool.alloc_f32(DIN)
    scbf = E.pool.alloc_bf16(DIN)
    selam = E.pool.alloc_bf16(DIN)
    b2h = E.pool.alloc_bf16(DIN)
    yint = E.pool.alloc_bf16(DIN)
    ytok = E.pool.alloc_bf16(DIN)
    sstmp = E.pool.alloc_f32(TBW)
    ystg = E.pool.alloc_bf16(1024)
    umask = E.pool.alloc_bf16(128)
    rowst = E.mbtok["rows"]
    bct = E.mbtok["bct"]
    um_in = declare_input(E, "c_mst")
    t_um = p.op("gpsimd", lambda h: h.dma_start(out=umask, in_=um_in[:, 0:128]), prj, dma=True)
    bt_war = None
    btt = []
    bt3 = btok.rearrange("p (tc n) -> p tc n", tc=16)
    for j in range(16):
        tb_j = p.op("sync", (lambda h, j=j: h.dma_start_transpose(
            bt3[:, j], bcT[0:64, j * 128:(j + 1) * 128])), bct + [bt_war], dma=True)
        bt_war = tb_j
        btt.append(tb_j)
    cb0 = E.pool.alloc_bf16(T)
    cbm = E.pool.alloc_bf16(128)
    t_cb0 = p.op("sync", (lambda h: h.dma_start(out=cb0[0:64, :], in_=bcT[64:128, :])),
                 bct, dma=True)
    t_s0a = p.op("gpsimd", (lambda h: h.memset(scat[64:128, :], 0.0)), prj)
    t_s0b = p.op("gpsimd", (lambda h: h.memset(scbf[64:128, :], 0.0)), prj)
    sbf_tok = [t_s0a, t_s0b]
    yt4 = ytd.rearrange("(fc p) t -> p fc t", p=128)
    E.yint_war = None
    ring_war = {0: [], 1: []}
    row_war = []
    y_done = []
    ytok_war = [None] * 4
    for c in range(16):
        sl = c % 2
        xh = xhb[:, sl * DIN:(sl + 1) * DIN]
        t_xh = p.op("sync", (lambda h, c=c, xh=xh: h.dma_start(
            out=xh, in_=htd[c * 128:(c + 1) * 128, :])), ring_war[sl] + ht_done, dma=True)
        ring_war[sl] = []
        t_lr = p.op("gpsimd", (lambda h, c=c: h.dma_start(
            out=lrow[0:1, :], in_=lcb[0:32, c * 128:(c + 1) * 128])), rowst + row_war, dma=True)
        t_mr = p.op("gpsimd", (lambda h, c=c: h.dma_start(
            out=mrow[0:1, :], in_=mlcb[0:32, c * 128:(c + 1) * 128])), rowst + row_war, dma=True)

        def trdt(h, c=c):
            h.transpose(E.psb[:, 0:32], dtb[0:32, c * 128:(c + 1) * 128], E.ident[0:32, 0:32])
            return h.transpose(E.psb[:, 32:64], lcb[0:32, c * 128:(c + 1) * 128],
                               E.ident[0:32, 0:32])
        t_tr = p.op("tensor", trdt, rowst + [E.psb_war, E.const_tok[1]] + row_war)
        t_ev1 = p.op("vector", lambda h: h.tensor_copy(dtk[:, 0:32], E.psb[:, 0:32]),
                     [t_tr] + row_war)
        t_ev2 = p.op("vector", lambda h: h.tensor_copy(ltk[:, 0:32], E.psb[:, 32:64]),
                     [t_tr] + row_war)
        E.psb_war = t_ev2
        t_ex = p.op("scalar", lambda h: h.activation(exl[:, 0:32], ltk[:, 0:32], AF.Exp),
                    [t_ev2] + row_war)
        t_cbm = _mmgroup(E, E.ps[6][:, 0:128],
                         [(bcT[0:64, c * 128:(c + 1) * 128], cb0[0:64, c * 128:(c + 1) * 128])],
                         bct + [t_cb0, E.ps_war[6]])
        t_cbe = p.op("vector", (lambda h: h.tensor_tensor(cbm, E.ps[6][:, 0:128], umask,
                                                          ALU.mult)), [t_cbm, t_um, E.cbm_war])
        E.ps_war[6] = t_cbe
        E.cbm_war = t_cbe
        t_xhat = p.op("vector", (lambda h, xh=xh: h.tensor_tensor(
            xht.rearrange("p (hh pp) -> p hh pp", hh=32),
            xh.rearrange("p (hh pp) -> p hh pp", hh=32),
            dtk[:, 0:32].unsqueeze(2).broadcast_to([128, 32, 64]), ALU.mult)),
            [t_xh, t_ev1] + row_war)
        w1t = []
        for blk in range(4):
            w1t.append(_mmgroup(E, E.ps[blk][:, :],
                                [(bcT[64:128, c * 128:(c + 1) * 128],
                                  scbf[64:128, blk * TBW:(blk + 1) * TBW])],
                                bct + sbf_tok + [E.ps_war[blk]], tile_position=(64, 0)))
        yintt = []
        for blk in range(4):
            yintt.append(p.op("vector", (lambda h, blk=blk: h.tensor_tensor(
                yint[:, blk * TBW:(blk + 1) * TBW].rearrange("p (hh pp) -> p hh pp", hh=8),
                E.ps[blk][:, :].rearrange("p (hh pp) -> p hh pp", hh=8),
                exl[:, blk * 8:(blk + 1) * 8].unsqueeze(2).broadcast_to([128, 8, 64]),
                ALU.mult)), [w1t[blk], t_ex, E.yint_war]))
        E.yint_war = yintt[-1]
        gtoks = []
        for hf in range(2):
            emms = []
            for j in range(16):
                hh = hf * 16 + j
                bank = j // 4
                off = (j % 4) * 128
                emms.append(_mmgroup(
                    E, E.ps[bank][:, off:off + 128],
                    [(E.ones_bf[0:1, :], lrow[0:1, hh * 128:(hh + 1) * 128])],
                    [t_lr, t_mr, E.ps_war[bank]] + yintt,
                    extra=[(mrow[0:1, hh * 128:(hh + 1) * 128], E.onesr[0:1, 0:128])]))
                E.ps_war[bank] = emms[-1]
            tss = []
            for bank in range(4):
                tss.append(p.op("vector", (lambda h, hf=hf, bank=bank: h.tensor_scalar(
                    Gsc[:, bank * TBW:(bank + 1) * TBW], E.ps[bank][:, :], 20.0, None, ALU.min)),
                    emms + [E.gsc_war]))
                E.ps_war[bank] = tss[-1]
            t_ge = p.op("scalar", (lambda h, hf=hf: h.activation(
                Gsc[:, 0:DIN], Gsc[:, 0:DIN], AF.Exp)), tss)
            t_gm = p.op("vector", (lambda h, hf=hf: h.tensor_tensor(
                Gall[:, hf * DIN:(hf + 1) * DIN].rearrange("p (hh t) -> p hh t", hh=16),
                Gsc[:, 0:DIN].rearrange("p (hh t) -> p hh t", hh=16),
                cbm.unsqueeze(1).broadcast_to([128, 16, 128]), ALU.mult)),
                [t_ge, t_cbe, E.gall_war])
            E.gsc_war = t_gm
            E.gall_war = t_gm
            gtoks.append(t_gm)
        yev = []
        for w in range(4):
            bank = 4 + w % 2
            imms = []
            for j in range(8):
                hh = w * 8 + j
                imms.append(_mmgroup(E, E.ps[bank][:, j * 64:(j + 1) * 64],
                                     [(Gall[:, hh * 128:(hh + 1) * 128],
                                       xht[:, hh * 64:(hh + 1) * 64])],
                                     [gtoks[hh // 16], t_xhat, E.ps_war[bank]]))
                E.ps_war[bank] = imms[-1]
            tye = p.op("vector", (lambda h, w=w, bank=bank: h.tensor_tensor(
                ytok[:, w * TBW:(w + 1) * TBW], E.ps[bank][:, :],
                yint[:, w * TBW:(w + 1) * TBW], ALU.add)), imms + yintt + [ytok_war[w]])
            E.ps_war[bank] = tye
            yev.append(tye)
        t_f1 = _mmgroup(E, E.ps[6][:, 0:32], [(E.ones_bf[0:1, :], lamb16[0:1, c * 32:(c + 1) * 32])],
                        [E.ps_war[6], E.mbtok["lamb16"]])
        t_f2 = p.op("vector", lambda h: h.tensor_tensor(e2s[:, 0:32], E.ps[6][:, 0:32],
                                                        ltk[:, 0:32], ALU.subtract), [t_f1, t_ev2])
        E.ps_war[6] = t_f2
        t_f3 = p.op("scalar", lambda h: h.activation(e2s[:, 32:64], e2s[:, 0:32], AF.Exp), [t_f2])
        t_f4 = p.op("vector", lambda h: h.tensor_copy(ftk[:, 0:32], e2s[:, 32:64]), [t_f3])
        t_bh = p.op("vector", (lambda h, c=c: h.tensor_tensor(
            b2h.rearrange("p (hh n) -> p hh n", hh=32),
            bt3[:, c].unsqueeze(1).broadcast_to([128, 32, 64]),
            ftk[:, 0:32].unsqueeze(2).broadcast_to([128, 32, 64]), ALU.mult)),
            btt + [t_f4, E.b2h_war])
        E.b2h_war = t_bh
        t_se = p.op("vector", (lambda h, c=c: h.tensor_copy(
            selam[0:1, :].rearrange("q (hh pp) -> q hh pp", hh=32),
            elamb[0:1, c * 32:(c + 1) * 32].unsqueeze(2).broadcast_to([1, 32, 64]))),
            [rowst[3], E.selam_war])
        for w in range(4):
            dmms = []
            for j in range(8):
                hh = w * 8 + j
                dmms.append(_mmgroup(E, E.ps[4][64:128, j * 64:(j + 1) * 64],
                                     [(b2h[:, hh * 64:(hh + 1) * 64], xht[:, hh * 64:(hh + 1) * 64])],
                                     [t_bh, t_xhat, E.ps_war[4]], tile_position=(0, 64)))
                E.ps_war[4] = dmms[-1]
            t_lb = _mmgroup(E, E.ps[5][64:128, :], [(E.ones_bf[0:1, 0:64],
                                                     selam[0:1, w * TBW:(w + 1) * TBW])],
                            [t_se, E.ps_war[5]], tile_position=(0, 64))
            t_u1 = p.op("vector", (lambda h, w=w: h.tensor_tensor(
                sstmp[64:128, :], scat[64:128, w * TBW:(w + 1) * TBW], E.ps[5][64:128, :],
                ALU.mult)), [t_lb, E.sstmp_war] + sbf_tok)
            E.ps_war[5] = t_u1
            t_u2 = p.op("vector", (lambda h, w=w: h.tensor_tensor(
                scat[64:128, w * TBW:(w + 1) * TBW], sstmp[64:128, :], E.ps[4][64:128, :],
                ALU.add)), dmms + [t_u1])
            E.ps_war[4] = t_u2
            E.sstmp_war = t_u2
            t_u3 = p.op("vector", (lambda h, w=w: h.tensor_copy(
                scbf[64:128, w * TBW:(w + 1) * TBW], scat[64:128, w * TBW:(w + 1) * TBW])), [t_u2])
            sbf_tok = [t_u3]
            ring_war[sl].append(t_u2)
        for rnd in range(2):
            def trs2(h, rnd=rnd):
                last = None
                for j in range(8):
                    fco = rnd * 8 + j
                    last = h.transpose(E.psb[:, j * 128:(j + 1) * 128],
                                      ytok[:, fco * 128:(fco + 1) * 128], E.ident)
                return last
            ttr = p.op("tensor", trs2, yev + [E.psb_war, E.const_tok[1]])
            tev = p.op("vector", (lambda h: h.tensor_copy(ystg, E.psb[:, 0:1024])),
                       [ttr, E.ystg_war])
            E.psb_war = tev
            tdm = p.op("sync", (lambda h, c=c, rnd=rnd: h.dma_start(
                out=yt4[:, rnd * 8:(rnd + 1) * 8, c * 128:(c + 1) * 128],
                in_=ystg.rearrange("p (fc t) -> p fc t", fc=8))), [tev], dma=True)
            E.ystg_war = tdm
            y_done.append(tdm)
        for w in range(4):
            ytok_war[w] = y_done[-1]
        row_war = gtoks + [t_f2, t_bh, t_xhat, t_ev1, t_ev2, t_ex]
    E.pool.pop()

    barrier(E)
    ychk = E.pool.alloc_bf16(DIN)
    gchk = E.pool.alloc_bf16(DIN)
    mchk = E.pool.alloc_bf16(DIN)
    ym_done = []
    yg_war = None
    gt4 = gated.rearrange("(c p) t -> p c t", p=128)
    ym4 = ymd.rearrange("(c p) t -> p c t", p=128)
    for fc in range(16):
        t_y = p.op("sync", (lambda h, fc=fc: h.dma_start(out=ychk, in_=yt4[:, fc])),
                   y_done + [yg_war], dma=True)
        t_g = p.op("gpsimd", (lambda h, fc=fc: h.dma_start(out=gchk, in_=gt4[:, fc])),
                   prj + [yg_war], dma=True)
        t_sg = p.op("scalar", (lambda h: h.activation(gchk, gchk, AF.Silu)), [t_g])
        t_m = p.op("vector", (lambda h: h.tensor_tensor(mchk, ychk, gchk, ALU.mult)),
                   [t_y, t_sg, yg_war])
        t_o = p.op("sync", (lambda h, fc=fc: h.dma_start(out=ym4[:, fc], in_=mchk)), [t_m], dma=True)
        yg_war = t_o
        ym_done.append(t_o)
    barrier(E)
    wob = E.pool.alloc_bf16(16 * D)
    t_wo = load_weight_bf16(E, "s1_wout", wob, prj + ym_done)
    ymr = E.pool.alloc_bf16(2 * TBW)
    xnr = E.pool.alloc_bf16(2 * NCH * TBW)
    oadd = E.pool.alloc_bf16(TBW)
    xnd_ap = E.mbtok["xnd_ap"]
    xn4 = xnd_ap.rearrange("(c p) t -> p c t", p=128)
    ymr_war = {0: [], 1: []}
    xnr_war = {0: [], 1: []}
    oadd_war = None
    for tb in range(NTB):
        xsl = tb % 2
        xnt = xnr[:, xsl * NCH * TBW:(xsl + 1) * NCH * TBW]
        t_xn = p.op("gpsimd", (lambda h, tb=tb, xnt=xnt: h.dma_start(
            out=xnt.rearrange("p (c t) -> p c t", c=NCH),
            in_=xn4[:, :, tb * TBW:(tb + 1) * TBW])), xnr_war[xsl] + [E.mbtok["xnd"]], dma=True)
        xnr_war[xsl] = []
        for g in range(2):
            mms = {}
            for fc in range(16):
                ysl = fc % 2
                ym_ = ymr[:, ysl * TBW:(ysl + 1) * TBW]
                t_ym = p.op("sync", (lambda h, tb=tb, fc=fc, ym_=ym_: h.dma_start(
                    out=ym_, in_=ym4[:, fc, tb * TBW:(tb + 1) * TBW])),
                    ymr_war[ysl] + ym_done, dma=True)
                ymr_war[ysl] = []
                for ei in range(4):
                    ec = g * 4 + ei
                    bank = 2 + ei
                    tm = _mmgroup(E, E.ps[bank][:, :],
                                  [(wob[:, fc * D + ec * 128: fc * D + ec * 128 + 128], ym_)],
                                  [t_ym, t_wo, E.ps_war[bank]] if fc == 0 else [t_ym, t_wo],
                                  start_clear=(fc == 0), stop_group=(fc == 15))
                    ymr_war[ysl].append(tm)
                    mms[ei] = tm
            for ei in range(4):
                ec = g * 4 + ei
                bank = 2 + ei
                t1 = p.op("vector", (lambda h, bank=bank, ec=ec, xnt=xnt: h.tensor_tensor(
                    oadd, E.ps[bank][:, :], xnt[:, ec * TBW:(ec + 1) * TBW], ALU.add)),
                    [mms[ei], t_xn, oadd_war])
                tx = p.op("vector", (lambda h, ec=ec, tb=tb: h.tensor_tensor(
                    E.x[:, ec * T + tb * TBW: ec * T + tb * TBW + TBW],
                    E.x[:, ec * T + tb * TBW: ec * T + tb * TBW + TBW], oadd, ALU.add)),
                    [t1, E.xtok.get((ec, tb))])
                oadd_war = tx
                E.ps_war[bank] = tx
                E.xtok[(ec, tb)] = tx
                xnr_war[xsl].append(t1)
    E.pool.pop()


def emit_delta(E):
    p = E.p
    nc = E.nc
    H, DV, CH = 4, 256, 32
    NC = T // CH
    vd = nc.dram_tensor("dl_v", [D, T], BF16).ap()
    xlnd = nc.dram_tensor("dl_xln", [D, T], BF16).ap()
    vstkd = nc.dram_tensor("dl_vstk", [128, NC * DV], BF16).ap()
    kstkd = nc.dram_tensor("dl_kstk", [128, NC * DV], BF16).ap()
    dd = nc.dram_tensor("dl_delta", [D, T], BF16).ap()
    mixd = nc.dram_tensor("dl_mix", [D, T], BF16).ap()

    qd = nc.dram_tensor("dl_q", [D, T], BF16).ap()
    kd = nc.dram_tensor("dl_k", [D, T], BF16).ap()
    vbd = nc.dram_tensor("dl_vb", [D, T], BF16).ap()
    betd = nc.dram_tensor("dl_bet", [4, T], BF16).ap()

    E.pool.push()
    E.pool.push()
    E.xln = E.pool.alloc_bf16(NCH * T)
    emit_ln(E)
    t_xs = p.op("sync", (lambda h, x_=E.xln: h.dma_start(
        out=xlnd.rearrange("(c p) t -> p c t", p=128),
        in_=x_.rearrange("p (c t) -> p c t", c=NCH))), E.lntoks, dma=True)
    wbuf = E.pool.alloc_bf16(NCH * D)
    wbb = E.pool.alloc_bf16(NCH * 32)
    bet = E.pool.alloc_bf16(T)
    betr = E.pool.alloc_bf16(4 * T)
    ccol = E.pool.alloc_f32(32)
    diag = [E.pool.alloc_bf16(4 * 128) for _ in range(2)]
    pp = [E.pool.alloc_bf16(T + 16) for _ in range(2)]
    stg = [E.pool.alloc_bf16(TBW) for _ in range(2)]
    stg2 = [E.pool.alloc_bf16(TBW) for _ in range(2)]
    t_wb = load_weight_bf16(E, "d1_wb", wbb, E.lntoks)
    btoks = []
    for tb in range(NTB):
        i = tb % 2
        pairs = [(wbb[:, c * 32:(c + 1) * 32],
                  E.xln[:, c * T + tb * TBW: c * T + tb * TBW + TBW]) for c in range(NCH)]
        tm = _mmgroup(E, E.ps[i][0:32, :], pairs, E.lntoks + [t_wb, E.ps_war[i]])
        ts_ = p.op("scalar", (lambda h, tb=tb, i=i: h.activation(
            bet[0:32, tb * TBW:(tb + 1) * TBW], E.ps[i][0:32, :], AF.Sigmoid)), [tm])
        E.ps_war[i] = ts_
        btoks.append(ts_)
    brt = []
    for hh in range(4):
        brt.append(p.op("sync", (lambda h, hh=hh: h.dma_start(
            out=betd[hh:hh + 1, :], in_=bet[hh:hh + 1, :])), btoks, dma=True))
    t_br = p.op("sync", lambda h: h.dma_start(
        out=betr[0:1, :], in_=betd.rearrange("hh t -> (hh t)").unsqueeze(0)), brt, dma=True)
    vtoks = []
    qtoks = []
    ktoks = []
    pmms = list(E.lntoks)
    ccw = []
    ppw = {0: [], 1: []}
    dgw = {0: [], 1: []}
    sgw = {0: [], 1: []}
    sg2w = {0: None, 1: None}
    for pi, (wn, cn) in enumerate((("d1_wv", "d1_vconv_w"), ("d1_wk", "d1_kconv_w"),
                                   ("d1_wq", "d1_qconv_w"))):
        outd = (vd, kd, qd)[pi]
        otk = (vtoks, ktoks, qtoks)[pi]
        t_w = load_weight_bf16(E, wn, wbuf, pmms)
        cin = declare_input(E, cn)
        t_cc = p.op("gpsimd", (lambda h, cin=cin: h.dma_start(out=ccol, in_=cin)),
                    pmms + ccw, dma=True)
        pmms = []
        ccw = []
        if pi == 0:
            t_pz = [p.op("gpsimd", (lambda h, s=s: h.memset(pp[s][:, 0:16], 0.0)))
                    for s in (0, 1)]
            ppw = {0: t_pz[0:1], 1: t_pz[1:2]}
        for ec in range(NCH):
            sl = ec % 2
            pptoks = []
            for tb in range(NTB):
                i = tb % 2
                pairs = [(wbuf[:, c * D + ec * 128: c * D + ec * 128 + 128],
                          E.xln[:, c * T + tb * TBW: c * T + tb * TBW + TBW]) for c in range(NCH)]
                tm = _mmgroup(E, E.ps[i][:, :], pairs, E.lntoks + [t_w, E.ps_war[i]])
                tv = p.op("vector", (lambda h, sl=sl, tb=tb, i=i: h.tensor_copy(
                    pp[sl][:, 16 + tb * TBW: 16 + tb * TBW + TBW],
                    E.ps[i][:, :])), [tm] + ppw[sl])
                E.ps_war[i] = tv
                pptoks.append(tv)
                pmms.append(tm)
            dts = []
            for tap in range(4):
                dts.append(p.op("vector", (lambda h, sl=sl, ec=ec, tap=tap: h.tensor_scalar(
                    diag[sl][:, tap * 128:(tap + 1) * 128], E.ident,
                    ccol[:, ec * 4 + tap: ec * 4 + tap + 1], None, ALU.mult)),
                    [t_cc] + dgw[sl] + [E.const_tok[1]]))
            ccw += dts
            dgw[sl] = []
            ppw[sl] = []
            for tb in range(NTB):
                i = 4 + tb % 2
                pairs = [(diag[sl][:, tap * 128:(tap + 1) * 128],
                          pp[sl][:, 13 + tap + tb * TBW: 13 + tap + tb * TBW + TBW])
                         for tap in range(4)]
                tm = _mmgroup(E, E.ps[i][:, :], pairs, dts + pptoks + [E.ps_war[i]])
                dgw[sl].append(tm)
                ppw[sl].append(tm)
                th = p.op("scalar", (lambda h, sl=sl, tb=tb, i=i: h.activation(
                    stg[tb % 2], E.ps[i][:, :], AF.Silu)), [tm] + sgw[tb % 2])
                E.ps_war[i] = th
                td = p.op("sync", (lambda h, outd=outd, ec=ec, tb=tb, s=tb % 2: h.dma_start(
                    out=outd[ec * 128:(ec + 1) * 128, tb * TBW:(tb + 1) * TBW], in_=stg[s])),
                    [th], dma=True)
                sgw[tb % 2] = [td]
                otk.append(td)
                if pi == 0:
                    hh = ec // 2
                    t_rb = _mmgroup(E, E.ps[6][:, :],
                                    [(E.ones_bf[0:1, 0:128],
                                      betr[0:1, hh * T + tb * TBW: hh * T + tb * TBW + TBW])],
                                    [t_br, E.ps_war[6]])
                    t_sc = p.op("vector", (lambda h, s=tb % 2: h.tensor_tensor(
                        stg2[s], stg[s], E.ps[6][:, :], ALU.mult)),
                        [th, t_rb, sg2w[tb % 2]])
                    E.ps_war[6] = t_sc
                    sgw[tb % 2].append(t_sc)
                    td2 = p.op("sync", (lambda h, ec=ec, tb=tb, s=tb % 2: h.dma_start(
                        out=vbd[ec * 128:(ec + 1) * 128, tb * TBW:(tb + 1) * TBW], in_=stg2[s])),
                        [t_sc], dma=True)
                    sg2w[tb % 2] = td2
                    vtoks.append(td2)
    E.pool.pop()

    qnT = E.pool.alloc_bf16(NCH * T)
    knT = E.pool.alloc_bf16(NCH * T)
    kbT = E.pool.alloc_bf16(NCH * T)
    E.pool.push()
    betr2 = E.pool.alloc_bf16(4 * T)
    sqsc = E.pool.alloc_bf16(TBW)
    bcsc = E.pool.alloc_bf16(TBW)
    rrow = E.pool.alloc_f32(TBW)
    t_br2 = p.op("sync", lambda h: h.dma_start(
        out=betr2[0:1, :], in_=betd.rearrange("hh t -> (hh t)").unsqueeze(0)), brt, dma=True)
    t_lq = p.op("sync", lambda h: h.dma_start(
        out=qnT.rearrange("p (c t) -> p c t", c=NCH),
        in_=qd.rearrange("(c p) t -> p c t", p=128)), qtoks[-1:] + pmms, dma=True)
    t_lk = p.op("sync", lambda h: h.dma_start(
        out=knT.rearrange("p (c t) -> p c t", c=NCH),
        in_=kd.rearrange("(c p) t -> p c t", p=128)), ktoks[-1:] + pmms, dma=True)
    l2toks = []
    for (buf, tld) in ((qnT, t_lq), (knT, t_lk)):
        for hh in range(H):
            for tb in range(NTB):
                sq2 = []
                for cc in range(2):
                    c = 2 * hh + cc
                    tq = p.op("scalar", (lambda h, buf=buf, c=c, tb=tb: h.activation(
                        sqsc, buf[:, c * T + tb * TBW: c * T + tb * TBW + TBW], AF.Square)),
                        [tld, E.sq_war] + l2toks[-2:])
                    tmm = _mmgroup(E, E.ps[6][0:1, :], [(E.ones_bf[:, 0:1], sqsc)],
                                   [tq, E.ps_war[6]], start_clear=(cc == 0), stop_group=(cc == 1))
                    E.ps_war[6] = tmm
                    E.sq_war = tmm
                    sq2.append(tmm)
                t1 = p.op("vector", lambda h: h.tensor_scalar_add(rrow[0:1, :], E.ps[6][0:1, :],
                                                                 1e-6), [sq2[-1], E.rr_war])
                E.ps_war[6] = t1
                t2 = p.op("scalar", lambda h: h.activation(rrow[0:1, :], rrow[0:1, :], AF.Ln), [t1])
                t3 = p.op("scalar", lambda h: h.activation(rrow[0:1, :], rrow[0:1, :], AF.Exp,
                                                           scale=-0.5), [t2])
                t4 = p.op("vector", lambda h: h.tensor_copy(E.row_b1, rrow[0:1, :]), [t3, E.row_b_war])
                t_rb = _mmgroup(E, E.ps[5][:, :], [(E.ones_bf[0:1, :], E.row_b1)],
                                [t4, E.ps_war[5]])
                t_rv = p.op("vector", lambda h: h.tensor_copy(bcsc, E.ps[5][:, :]),
                            [t_rb, E.bcsc_war])
                E.ps_war[5] = t_rv
                ap2 = []
                for cc in range(2):
                    c = 2 * hh + cc
                    ap2.append(p.op("vector", (lambda h, buf=buf, c=c, tb=tb: h.tensor_tensor(
                        buf[:, c * T + tb * TBW: c * T + tb * TBW + TBW],
                        buf[:, c * T + tb * TBW: c * T + tb * TBW + TBW], bcsc,
                        ALU.mult)), [t_rv]))
                E.bcsc_war = ap2[-1]
                E.row_b_war = ap2[-1]
                E.rr_war = t1
                l2toks += ap2
    kbtoks = []
    for hh in range(H):
        for tb in range(NTB):
            t_rb = _mmgroup(E, E.ps[5][:, :],
                            [(E.ones_bf[0:1, :], betr2[0:1, hh * T + tb * TBW: hh * T + tb * TBW + TBW])],
                            [t_br2, E.ps_war[5]])
            for cc in range(2):
                c = 2 * hh + cc
                tv = p.op("vector", (lambda h, c=c, tb=tb: h.tensor_tensor(
                    kbT[:, c * T + tb * TBW: c * T + tb * TBW + TBW],
                    knT[:, c * T + tb * TBW: c * T + tb * TBW + TBW], E.ps[5][:, :], ALU.mult)),
                    [t_rb] + l2toks)
                kbtoks.append(tv)
            E.ps_war[5] = kbtoks[-1]
    E.pool.pop()

    barrier(E)
    E.pool.push()
    vtmp = [E.pool.alloc_bf16(NCH * 128) for _ in range(2)]
    vtk1 = [E.pool.alloc_bf16(D) for _ in range(2)]
    sdt = []
    vbw = {0: [], 1: []}
    vkw = {0: [], 1: []}
    for src_is_v in (True, False):
        dstd = vstkd if src_is_v else kstkd
        for tc in range(16):
            sl = tc % 2
            if src_is_v:
                t_vt = p.op("sync", (lambda h, tc=tc, sl=sl: h.dma_start(
                    out=vtmp[sl].rearrange("p (c w) -> p c w", c=NCH),
                    in_=vbd.rearrange("(c p) t -> p c t", p=128)[:, :, tc * 128:(tc + 1) * 128])),
                    vbw[sl] + vtoks[-1:], dma=True)
                vbw[sl] = []
                srcb, sdep = vtmp[sl], [t_vt]
            else:
                srcb, sdep = knT, l2toks[-1:] + kbtoks[-1:]
            evs = []
            for rnd in range(2):
                def trs(h, tc=tc, rnd=rnd, srcb=srcb, src_is_v=src_is_v):
                    last = None
                    for j in range(4):
                        c = rnd * 4 + j
                        off = c * 128 if src_is_v else c * T + tc * 128
                        last = h.transpose(E.psb[:, j * 128:(j + 1) * 128],
                                           srcb[:, off:off + 128], E.ident)
                    return last
                ttr = p.op("tensor", trs, sdep + [E.psb_war, E.const_tok[1]])
                if src_is_v:
                    vbw[sl].append(ttr)
                tev = p.op("vector", (lambda h, sl=sl, rnd=rnd: h.tensor_copy(
                    vtk1[sl][:, rnd * 512:(rnd + 1) * 512], E.psb[:, 0:512])),
                    [ttr] + vkw[sl])
                E.psb_war = tev
                evs.append(tev)
            vkw[sl] = []
            for s in range(4):
                c = tc * 4 + s
                tsd = p.op("sync", (lambda h, sl=sl, s=s, c=c, dstd=dstd: h.dma_start(
                    out=dstd.rearrange("(hh i) (cc dv) -> i hh cc dv", hh=4, cc=NC)[:, :, c],
                    in_=vtk1[sl].rearrange("p (hh dv) -> p hh dv", hh=4)[32 * s:32 * s + 32])),
                    evs, dma=True)
                vkw[sl].append(tsd)
                sdt.append(tsd)
    E.delta_sdt = sdt[-1:]
    E.pool.pop()
    barrier(E)
    PM = E.pool.alloc_bf16(5 * 256)
    zz = E.pool.alloc_bf16(DV)
    at = E.pool.alloc_bf16(128)
    Sbf = E.pool.alloc_bf16(8 * DV)
    scat = [E.pool.alloc_f32(8 * DV) for _ in range(2)]
    dstg = E.pool.alloc_bf16(DV)
    vkr = E.pool.alloc_bf16(4 * DV)
    cbd_in = declare_input(E, "c_bd")
    cbdT_in = declare_input(E, "c_bdT")
    cbdA_in = declare_input(E, "c_bdA")
    cmsk = E.pool.alloc_bf16(3 * 256)
    t_m1 = p.op("gpsimd", lambda h: h.dma_start(out=cmsk[:, 0:256].bitcast(F32), in_=cbd_in), dma=True)
    t_m2 = p.op("gpsimd", lambda h: h.dma_start(out=cmsk[:, 256:512].bitcast(F32), in_=cbdT_in), dma=True)
    t_m3 = p.op("gpsimd", lambda h: h.dma_start(out=cmsk[:, 512:768].bitcast(F32), in_=cbdA_in), dma=True)
    mbd = cmsk[:, 0:256].bitcast(F32)
    mbdT = cmsk[:, 256:512].bitcast(F32)
    mbdA = cmsk[:, 512:768].bitcast(F32)
    t_sz = p.op("gpsimd", lambda h: h.memset(scat[1][:, :], 0.0), E.delta_sdt)
    t_pz0 = p.op("vector", lambda h: h.memset(E.ps[0][:, 0:128], 0.0), [E.ps_war[0]])
    t_pz3 = p.op("vector", lambda h: h.memset(E.ps[3][:, 0:128], 0.0), [E.ps_war[3]])
    E.ps_war[0] = t_pz0
    E.ps_war[3] = t_pz3
    t_szb = p.op("vector", lambda h: h.tensor_copy(Sbf[:, :], scat[1][:, :]), [t_sz])
    stoks = [t_szb]
    vk_war = {0: [], 1: []}
    prevst = [t_szb] * 8
    dstg_war = None
    ddone = []
    for c in range(NC):
        sl = c % 2
        vs = vkr[:, sl * DV:(sl + 1) * DV]
        ks = vkr[:, (2 + sl) * DV:(3 + sl) * DV]
        t_vl = p.op("sync", (lambda h, c=c, vs=vs: h.dma_start(
            out=vs, in_=vstkd[:, c * DV:(c + 1) * DV])), vk_war[sl] + E.delta_sdt, dma=True)
        t_kl = p.op("sync", (lambda h, c=c, ks=ks: h.dma_start(
            out=ks, in_=kstkd[:, c * DV:(c + 1) * DV])), vk_war[sl] + E.delta_sdt, dma=True)
        vk_war[sl] = []
        amms = []
        for rev in range(2):
            dst = PM[:, 0:128] if rev == 0 else PM[:, 128:256]
            msk = mbd if rev == 0 else mbdT
            triples = []
            for hh in range(H):
                for cc in range(2):
                    ch_ = 2 * hh + cc
                    a_ = kbT[:, ch_ * T + c * CH: ch_ * T + c * CH + CH]
                    b_ = knT[:, ch_ * T + c * CH: ch_ * T + c * CH + CH]
                    l_, r_ = (a_, b_) if rev == 0 else (b_, a_)
                    triples.append((E.ps[0][32 * hh:32 * hh + 32, 32 * hh:32 * hh + 32],
                                    l_, r_, (0, 32 * hh)))
            tmm = _mmgroup_tp(E, None, triples, [E.ps_war[0]] + stoks[-1:] + kbtoks[-1:] + l2toks[-1:],
                              pairwise=True)
            tev = p.op("vector", (lambda h, dst=dst, msk=msk: h.tensor_tensor(
                dst, E.ps[0][:, 0:128], msk, ALU.mult)), [tmm, t_m1, t_m2, E.pmb_war])
            E.ps_war[0] = tev
            E.pmb_war = tev
            amms.append(tev)
        utt = list(amms)
        for j in range(1, 5):
            def dbl(h, j=j):
                pp_ = PM[:, (j - 1) * 256:(j - 1) * 256 + 128]
                mm_ = PM[:, (j - 1) * 256 + 128: j * 256]
                last = None
                if j < 4:
                    last = h.matmul(E.ps[1][:, 0:128], mm_, pp_, start=True, stop=True)
                last = h.matmul(E.ps[1][:, 128:256], pp_, mm_, start=True, stop=True)
                return last
            t_pm = p.op("tensor", dbl, utt + [E.ps_war[1]])
            if j < 4:
                t_pe = p.op("vector", (lambda h, j=j: h.tensor_copy(
                    PM[:, j * 256:(j + 1) * 256], E.ps[1][:, 0:256])), [t_pm])
            else:
                t_pe = p.op("vector", (lambda h, j=j: h.tensor_copy(
                    PM[:, j * 256 + 128:(j + 1) * 256], E.ps[1][:, 128:256])), [t_pm])
            E.ps_war[1] = t_pe
            utt.append(t_pe)
        triples = []
        for hh in range(H):
            for cc in range(2):
                triples.append((E.ps[2][32 * hh:32 * hh + 32, 0:DV],
                                kbT[:, (2 * hh + cc) * T + c * CH: (2 * hh + cc) * T + c * CH + CH],
                                Sbf[:, (hh * 2 + cc) * DV:(hh * 2 + cc + 1) * DV],
                                (0, 32 * hh)))
        rmms = [_mmgroup_tp(E, None, triples, stoks + [E.ps_war[2]] + kbtoks[-1:], pairwise=True)]
        E.ps_war[2] = rmms[-1]
        t_r = p.op("vector", (lambda h, vs=vs: h.tensor_tensor(zz, vs, E.ps[2][:, 0:DV],
                                                               ALU.subtract)),
                   rmms + [t_vl, E.zz_war])
        E.ps_war[2] = t_r
        zt = t_r
        for j in range(4, -1, -1):
            t_a = _mmgroup(E, E.ps[2][:, 0:DV],
                           [(PM[:, j * 256 + 128:(j + 1) * 256], zz)],
                           utt + [zt, E.ps_war[2]])
            zt = p.op("vector", lambda h: h.tensor_tensor(zz, zz, E.ps[2][:, 0:DV], ALU.add),
                      [t_a])
            E.ps_war[2] = zt
        E.zz_war = zt
        triples = []
        for hh in range(H):
            for cc in range(2):
                triples.append((E.ps[3][32 * hh:32 * hh + 32, 32 * hh:32 * hh + 32],
                                knT[:, (2 * hh + cc) * T + c * CH: (2 * hh + cc) * T + c * CH + CH],
                                qnT[:, (2 * hh + cc) * T + c * CH: (2 * hh + cc) * T + c * CH + CH],
                                (0, 32 * hh)))
        prev = _mmgroup_tp(E, None, triples, [E.ps_war[3]] + l2toks[-1:], pairwise=True)
        t_at = p.op("vector", lambda h: h.tensor_tensor(at, E.ps[3][:, 0:128], mbdA, ALU.mult),
                    [prev, t_m3, E.at_war])
        E.ps_war[3] = t_at
        E.at_war = t_at
        omms = []
        first = {}
        for hh in range(H):
            for half in range(2):
                fo = (2 * hh + half) * 32
                pairs = [(Sbf[:, (hh * 2 + cc) * DV + half * 128:(hh * 2 + cc) * DV + half * 128 + 128],
                          qnT[:, (2 * hh + cc) * T + c * CH: (2 * hh + cc) * T + c * CH + CH])
                         for cc in range(2)]
                t_o1 = _mmgroup(E, E.ps[4][:, fo:fo + 32], pairs,
                                stoks + [E.ps_war[4], zt], stop_group=False)
                t_o2 = _mmgroup(E, E.ps[4][:, fo:fo + 32],
                                [(zz[32 * hh:32 * hh + 32, half * 128: half * 128 + 128],
                                  at[32 * hh:32 * hh + 32, 32 * hh:32 * hh + 32])],
                                [t_o1, zt, t_at], tile_position=(32 * hh, 0), start_clear=False)
                E.ps_war[4] = t_o2
                omms.append(t_o2)
        t_oe = p.op("vector", lambda h: h.tensor_copy(dstg, E.ps[4][:, 0:DV]),
                    omms + [dstg_war])
        E.ps_war[4] = t_oe
        t_od = p.op("sync", (lambda h, c=c: h.dma_start(
            out=dd.rearrange("(fc p) t -> p fc t", p=128)[:, :, c * CH:(c + 1) * CH],
            in_=dstg.rearrange("p (fc i) -> p fc i", fc=8))), [t_oe], dma=True)
        dstg_war = t_od
        ddone.append(t_od)
        so, si = scat[c % 2], scat[1 - c % 2]
        newst = []
        for hh in range(H):
            for cc in range(2):
                idx = hh * 2 + cc
                bank = 5 + idx % 2
                t_d = _mmgroup(E, E.ps[bank][:, 0:DV],
                               [(ks[32 * hh:32 * hh + 32, cc * 128: cc * 128 + 128],
                                 zz[32 * hh:32 * hh + 32, :])],
                               [t_kl, zt, E.ps_war[bank]], tile_position=(32 * hh, 0))
                t_su = p.op("vector", (lambda h, so=so, si=si, idx=idx, bank=bank: h.tensor_tensor(
                    so[:, idx * DV:(idx + 1) * DV], si[:, idx * DV:(idx + 1) * DV],
                    E.ps[bank][:, 0:DV], ALU.add)),
                    [t_d, prevst[idx]] + stoks[-1:])
                E.ps_war[bank] = t_su
                newst.append(t_su)
                vk_war[sl].append(t_d)
        prevst = newst
        t_sb2 = p.op("vector", (lambda h, so=so: h.tensor_copy(Sbf[:, :], so[:, :])), newst)
        stoks = [t_sb2]
        vk_war[sl].append(t_r)
    E.pool.pop()

    firs_d = nc.dram_tensor("dl_firs", [D, T], BF16).ap()
    firl_d = nc.dram_tensor("dl_firl", [D, T], BF16).ap()
    prd = nc.dram_tensor("dl_prd", [16, TBW], BF16).ap()
    ord_ = nc.dram_tensor("dl_ord", [4, TBW], F32).ap()

    barrier(E)
    E.pool.push()
    PWF = 64 + T
    vch = [E.pool.alloc_bf16(PWF) for _ in range(2)]
    diagl = E.pool.alloc_bf16(63 * 128)
    diags = E.pool.alloc_bf16(3 * 128)
    fstg = [E.pool.alloc_bf16(TBW) for _ in range(4)]
    flb = E.pool.alloc_f32(8 * 63)
    fsb = E.pool.alloc_f32(24)
    fl_in = declare_input(E, "d1_fir_l")
    fs_in = declare_input(E, "d1_fir_s")
    t_fl = p.op("gpsimd", lambda h: h.dma_start(out=flb, in_=fl_in), dma=True)
    t_fs = p.op("gpsimd", lambda h: h.dma_start(out=fsb, in_=fs_in), dma=True)
    t_pz = [p.op("gpsimd", (lambda h, s=s: h.memset(vch[s][:, 0:64], 0.0))) for s in (0, 1)]
    vw = {0: [], 1: []}
    dw = []
    fw = [None] * 4
    fir_done = []
    for fc in range(8):
        sl = fc % 2
        t_v = p.op("sync", (lambda h, fc=fc, sl=sl: h.dma_start(
            out=vch[sl][:, 64:PWF], in_=vd[fc * 128:(fc + 1) * 128, :])),
            vw[sl] + [t_pz[sl]] + vtoks[-1:], dma=True)
        vw[sl] = []
        dts = []
        for k in range(63):
            dts.append(p.op("vector", (lambda h, fc=fc, k=k: h.tensor_scalar(
                diagl[:, k * 128:(k + 1) * 128], E.ident, flb[:, fc * 63 + k: fc * 63 + k + 1],
                None, ALU.mult)), [t_fl, E.const_tok[1]] + dw))
        for k in range(3):
            dts.append(p.op("vector", (lambda h, fc=fc, k=k: h.tensor_scalar(
                diags[:, k * 128:(k + 1) * 128], E.ident, fsb[:, fc * 3 + k: fc * 3 + k + 1],
                None, ALU.mult)), [t_fs, E.const_tok[1]] + dw))
        dw = []
        for tb in range(NTB):
            si = (tb % 2) * 2
            pairs = [(diagl[:, k * 128:(k + 1) * 128],
                      vch[sl][:, 2 + tb * TBW + k: 2 + tb * TBW + k + TBW]) for k in range(63)]
            tm = _mmgroup(E, E.ps[6][:, :], pairs, dts + [t_v, E.ps_war[6]])
            te = p.op("vector", (lambda h, si=si: h.tensor_copy(fstg[si], E.ps[6][:, :])),
                      [tm, fw[si]])
            E.ps_war[6] = te
            td = p.op("sync", (lambda h, fc=fc, tb=tb, si=si: h.dma_start(
                out=firl_d[fc * 128:(fc + 1) * 128, tb * TBW:(tb + 1) * TBW], in_=fstg[si])),
                [te], dma=True)
            fw[si] = td
            fir_done.append(td)
            pairs = [(diags[:, k * 128:(k + 1) * 128],
                      vch[sl][:, 62 + tb * TBW + k: 62 + tb * TBW + k + TBW]) for k in range(3)]
            tm2 = _mmgroup(E, E.ps[5][:, :], pairs, dts + [t_v, E.ps_war[5]])
            te2 = p.op("vector", (lambda h, si=si: h.tensor_copy(fstg[si + 1], E.ps[5][:, :])),
                       [tm2, fw[si + 1]])
            E.ps_war[5] = te2
            td2 = p.op("sync", (lambda h, fc=fc, tb=tb, si=si: h.dma_start(
                out=firs_d[fc * 128:(fc + 1) * 128, tb * TBW:(tb + 1) * TBW], in_=fstg[si + 1])),
                [te2], dma=True)
            fw[si + 1] = td2
            fir_done.append(td2)
            vw[sl] += [tm, tm2]
            dw += [tm, tm2]
    E.pool.pop()

    barrier(E)
    E.pool.push()
    vt_b = E.pool.alloc_bf16(NCH * TBW)
    st_b = E.pool.alloc_bf16(NCH * TBW)
    lt_b = E.pool.alloc_bf16(NCH * TBW)
    dt_b = E.pool.alloc_bf16(NCH * TBW)
    xl_b = E.pool.alloc_bf16(NCH * TBW)
    pbuf = (st_b, lt_b, dt_b, vt_b)
    mixf = E.pool.alloc_bf16(NCH * TBW)
    mix3 = E.pool.alloc_bf16(NCH * TBW)
    hmsl = [E.pool.alloc_bf16(TBW) for _ in range(2)]
    g1db = [E.pool.alloc_bf16(1152) for _ in range(2)]
    wobuf = E.pool.alloc_bf16(8 * 1024)
    g2buf = E.pool.alloc_bf16(256)
    g1bb = E.pool.alloc_f32(16)
    gstat = E.pool.alloc_bf16(TBW)
    sqp = [E.pool.alloc_bf16(TBW) for _ in range(2)]
    s32 = E.pool.alloc_bf16(TBW)
    stT = E.pool.alloc_f32(128)
    msq = E.pool.alloc_f32(16)
    stTb = E.pool.alloc_bf16(128)
    lgf = E.pool.alloc_f32(TBW)
    lgb = E.pool.alloc_bf16(TBW)
    lgT = E.pool.alloc_f32(64)
    prf = E.pool.alloc_f32(64)
    prb_ = E.pool.alloc_bf16(64)
    sx2 = E.pool.alloc_f32(32)
    sx1 = E.pool.alloc_f32(16)
    prT = E.pool.alloc_bf16(TBW)
    prrow = E.pool.alloc_bf16(8 * TBW)
    prbcf = E.pool.alloc_f32(TBW)
    msc = [E.pool.alloc_f32(TBW) for _ in range(2)]
    ost = E.pool.alloc_f32(TBW)
    orow = E.pool.alloc_f32(4 * TBW)
    orb = E.pool.alloc_bf16(4 * TBW)
    oadd = E.pool.alloc_f32(TBW)
    flbb = E.pool.alloc_f32(16)
    r16 = E.pool.alloc_f32(8)
    flr = E.pool.alloc_f32(32)
    frb = E.pool.alloc_bf16(16)
    onwb = E.pool.alloc_f32(8)
    cdbuf = E.pool.alloc_bf16(2048)
    cdh = E.pool.alloc_bf16(32)

    cdsel_in = declare_input(E, "c_dsel")
    cdh_in = declare_input(E, "c_dselh")
    t_cd = p.op("gpsimd", lambda h: h.dma_start(out=cdbuf, in_=cdsel_in), dma=True)
    t_ch = p.op("gpsimd", lambda h: h.dma_start(out=cdh, in_=cdh_in), dma=True)
    t_wo = load_weight_bf16(E, "d1_wo", wobuf)
    t_g2 = load_weight_bf16(E, "d1_g2", g2buf)
    g1b_in = declare_input(E, "d1_g1b")
    t_g1b = p.op("sync", lambda h: h.dma_start(out=g1bb, in_=g1b_in), dma=True)
    onw_in = declare_input(E, "d1_onw")
    t_onw = p.op("sync", lambda h: h.dma_start(out=onwb, in_=onw_in), dma=True)
    bb_in = declare_input(E, "d1_bb")
    t_bb = p.op("sync", lambda h: h.dma_start(out=r16[0:16, 0:1], in_=bb_in), dma=True)
    lt_in = declare_input(E, "d1_logt")
    t_lt = p.op("sync", lambda h: h.dma_start(out=r16[0:16, 1:2], in_=lt_in), dma=True)
    fl2_in = declare_input(E, "d1_floor")
    t_fr = p.op("sync", lambda h: h.dma_start(out=flr[0:1, 0:16], in_=fl2_in), dma=True)
    g1_in = declare_input(E, "d1_g1")

    ta_ = p.op("scalar", lambda h: h.activation(r16[0:16, 2:3], r16[0:16, 1:2], AF.Relu), [t_lt])
    tb_ = p.op("scalar", lambda h: h.activation(r16[0:16, 3:4], r16[0:16, 1:2], AF.Abs), [t_lt])
    tc_ = p.op("scalar", lambda h: h.activation(r16[0:16, 3:4], r16[0:16, 3:4], AF.Exp, scale=-1.0), [tb_])
    td_ = p.op("vector", lambda h: h.tensor_scalar_add(r16[0:16, 3:4], r16[0:16, 3:4], 1.0), [tc_])
    te_ = p.op("scalar", lambda h: h.activation(r16[0:16, 3:4], r16[0:16, 3:4], AF.Ln), [td_])
    tf_ = p.op("vector", lambda h: h.tensor_tensor(r16[0:16, 4:5], r16[0:16, 2:3], r16[0:16, 3:4], ALU.add), [ta_, te_])
    tg_ = p.op("vector", lambda h: h.tensor_scalar_add(r16[0:16, 4:5], r16[0:16, 4:5], 1e-4), [tf_])
    th_ = p.op("vector", lambda h: h.reciprocal(r16[0:16, 5:6], r16[0:16, 4:5]), [tg_])
    rtmp = r16[0:16, 5:6]
    tfs = p.op("scalar", lambda h: h.activation(flr[0:1, 16:32], flr[0:1, 0:16], AF.Sigmoid), [t_fr])
    tfm = p.op("vector", lambda h: h.tensor_scalar_mul(flr[0:1, 16:32], flr[0:1, 16:32], 0.05), [tfs])
    tfc = p.op("vector", lambda h: h.tensor_copy(frb[0:1, :], flr[0:1, 16:32]), [tfm])
    tfb = _mmgroup(E, E.ps[4][:, 0:16], [(E.ones_bf[0:1, 0:128], frb[0:1, 0:16])], [tfc, E.ps_war[4]])
    t_flb = p.op("vector", lambda h: h.tensor_copy(flbb[:, 0:16], E.ps[4][:, 0:16]), [tfb])
    E.ps_war[4] = t_flb
    t_gz = p.op("gpsimd", lambda h: h.memset(gstat, 0.0))

    v3 = lgT.rearrange("p (g q) -> p g q", q=4)
    p3 = prf.rearrange("p (g q) -> p g q", q=4)
    p4 = prf.rearrange("p (q l) -> p q l", l=16)
    m2 = sx2.rearrange("p (g q) -> p g q", q=2)
    m1 = sx1.rearrange("p (g q) -> p g q", q=1)

    tbw = []
    gstw = [t_gz]
    sqw = [None, None]
    psbw = [E.psb_war]
    s32w = stTw = msqw = stTbw = lgfw = lgbw = lgTw = sxw = prfw = prbw = None
    prTw = prroww = ostw = ordw = oroww = None
    orbw = []
    prbcw = []
    mscw = {0: None, 1: None}
    oaddw = None
    mixtok = {}
    mix3tok = {}
    mfw = {fc: [] for fc in range(NCH)}
    mix3w = {fc: [] for fc in range(NCH)}
    hmw = [None, None]
    g1slw = {0: [], 1: []}
    prdw = []
    for tb in range(NTB):
        t0 = tb * TBW
        ld = {}
        for nm, buf, srcd, dps in (("v", vt_b, vd, vtoks[-1:]),
                                   ("s", st_b, firs_d, fir_done[-1:]),
                                   ("l", lt_b, firl_d, fir_done[-1:]),
                                   ("d", dt_b, dd, ddone[-1:]),
                                   ("x", xl_b, xlnd, [t_xs])):
            ld[nm] = p.op("sync", (lambda h, buf=buf, srcd=srcd, t0=t0: h.dma_start(
                out=buf.rearrange("p (c w) -> p c w", c=NCH),
                in_=srcd.rearrange("(c p) t -> p c t", p=128)[:, :, t0:t0 + TBW])),
                dps + tbw, dma=True)
        ldp = [ld["s"], ld["l"], ld["d"], ld["v"]]
        tbw = []
        pairs = []
        for path in range(4):
            for fc in range(NCH):
                b = (path * 8 + fc) * 2
                pairs.append((cdbuf[:, b * 32:(b + 1) * 32],
                              pbuf[path][:, fc * TBW:(fc + 1) * TBW]))
        t_raw = _mmgroup(E, E.ps[0][0:32, :], pairs, ldp + [t_cd, E.ps_war[0]], stop_group=False)
        prevmm = t_raw
        sq_mm = list(sqw)
        i = 0
        for path in range(4):
            for fc in range(NCH):
                b = (path * 8 + fc) * 2 + 1
                t_sq = p.op("scalar", (lambda h, path=path, fc=fc, i=i: h.activation(
                    sqp[i % 2], pbuf[path][:, fc * TBW:(fc + 1) * TBW], AF.Square)),
                    [ldp[path], sq_mm[i % 2]])
                t_mm = _mmgroup(E, E.ps[0][0:32, :], [(cdbuf[:, b * 32:(b + 1) * 32], sqp[i % 2])],
                                [t_sq, t_cd], start_clear=False, stop_group=(i == 31))
                sq_mm[i % 2] = t_mm
                prevmm = t_mm
                i += 1
        t_se = p.op("vector", (lambda h: h.tensor_copy(s32[0:32, :], E.ps[0][0:32, :])),
                    [prevmm, s32w])
        E.ps_war[0] = t_se

        def trf(h, t_se=t_se):
            last = None
            for q in range(4):
                last = h.transpose(E.psb[:, q * 32:(q + 1) * 32],
                                   s32[0:32, q * 128:(q + 1) * 128], E.ident[0:32, 0:32])
            return last
        t_tf = p.op("tensor", trf, [t_se, E.const_tok[1]] + psbw)
        s32w = t_tf
        t_st = p.op("vector", lambda h: h.tensor_copy(stT, E.psb[:, 0:128]), [t_tf, stTw])
        psbw = [t_st]
        vfix = []
        tprev = t_st
        for q in range(4):
            t_m2 = p.op("scalar", (lambda h, q=q: h.activation(
                msq[:, 0:16], stT[:, q * 32:q * 32 + 16], AF.Square)), [t_st, msqw, tprev])
            t_vf = p.op("vector", (lambda h, q=q: h.tensor_tensor(
                stT[:, q * 32 + 16:q * 32 + 32], stT[:, q * 32 + 16:q * 32 + 32],
                msq[:, 0:16], ALU.subtract)), [t_m2])
            msqw = t_vf
            tprev = t_vf
            vfix.append(t_vf)
        t_sb = p.op("vector", lambda h: h.tensor_copy(stTb, stT), vfix + [stTbw])
        stTw = t_sb

        def trb(h, t_sb=t_sb):
            last = None
            for q in range(4):
                last = h.transpose(E.psb[0:32, q * 128:(q + 1) * 128],
                                   stTb[:, q * 32:(q + 1) * 32], E.ident)
            return last
        t_tb2 = p.op("tensor", trb, [t_sb, E.const_tok[1]] + psbw)
        stTbw = t_tb2
        t_gs = p.op("vector", (lambda h: h.tensor_copy(gstat[0:32, :], E.psb[0:32, 0:512])),
                    [t_tb2] + gstw)
        psbw = [t_gs]
        gw = []
        t_lgmm = None
        for hbc in range(16):
            slt = hbc % 2
            bank = 2 + slt
            t_g = p.op("gpsimd", (lambda h, hbc=hbc, slt=slt: h.dma_start(
                out=g1db[slt].rearrange("p (c m) -> p c m", c=9),
                in_=g1_in[:, hbc * 128:(hbc + 1) * 128].rearrange("(c p) m -> p c m", p=128))),
                g1slw[slt], dma=True)
            pairs = [(g1db[slt][:, c * 128:(c + 1) * 128],
                      xl_b[:, c * TBW:(c + 1) * TBW]) for c in range(NCH)] +                     [(g1db[slt][:, 8 * 128:9 * 128], gstat)]
            t_mm = _mmgroup(E, E.ps[bank][:, :], pairs, [t_g, ld["x"], t_gs, E.ps_war[bank]])
            t_gel = p.op("scalar", (lambda h, slt=slt, bank=bank, hbc=hbc: h.activation(
                hmsl[slt], E.ps[bank][:, :], AF.Gelu, bias=g1bb[:, hbc:hbc + 1])),
                [t_mm, t_g1b, hmw[slt]])
            E.ps_war[bank] = t_gel
            t_lgmm = _mmgroup(E, E.ps[1][0:16, :], [(g2buf[:, hbc * 16:(hbc + 1) * 16], hmsl[slt])],
                              [t_gel, t_g2] + ([E.ps_war[1]] if hbc == 0 else []),
                              start_clear=(hbc == 0), stop_group=(hbc == 15))
            hmw[slt] = t_lgmm
            g1slw[slt] = [t_mm]
            gw.append(t_mm)
        gstw = gw
        t_l1 = p.op("vector", lambda h: h.tensor_scalar(lgf[0:16, :], E.ps[1][0:16, :],
                                                        r16[0:16, 0:1], None, ALU.add),
                    [t_lgmm, t_bb, lgfw])
        E.ps_war[1] = t_l1
        t_l2 = p.op("vector", lambda h: h.tensor_scalar(lgf[0:16, :], lgf[0:16, :], rtmp,
                                                        None, ALU.mult), [t_l1, th_])
        t_l3 = p.op("vector", lambda h: h.tensor_copy(lgb[0:16, :], lgf[0:16, :]), [t_l2, lgbw])
        lgfw = t_l3

        def trl(h, t_l3=t_l3):
            last = None
            for q in range(4):
                last = h.transpose(E.psb[:, q * 16:(q + 1) * 16],
                                   lgb[0:16, q * 128:(q + 1) * 128], E.ident[0:16, 0:16])
            return last
        t_tl = p.op("tensor", trl, [t_l3, E.const_tok[1]] + psbw)
        lgbw = t_tl
        t_lt2 = p.op("vector", lambda h: h.tensor_copy(lgT, E.psb[:, 0:64]), [t_tl, lgTw])
        psbw = [t_lt2]
        t_x1 = p.op("vector", lambda h: h.tensor_tensor(m2, v3[:, :, 0:2], v3[:, :, 2:4], ALU.max),
                    [t_lt2, sxw])
        t_x2 = p.op("vector", lambda h: h.tensor_tensor(m1, m2[:, :, 0:1], m2[:, :, 1:2], ALU.max),
                    [t_x1])
        t_x3 = p.op("vector", lambda h: h.tensor_tensor(
            v3, v3, m1.broadcast_to([128, 16, 4]), ALU.subtract), [t_x2])
        lgTw = t_x3
        t_x4 = p.op("scalar", lambda h: h.activation(prf, lgT, AF.Exp), [t_x3, prfw])
        t_s1_ = p.op("vector", lambda h: h.tensor_tensor(m2, p3[:, :, 0:2], p3[:, :, 2:4], ALU.add),
                     [t_x4])
        t_s2_ = p.op("vector", lambda h: h.tensor_tensor(m1, m2[:, :, 0:1], m2[:, :, 1:2], ALU.add),
                     [t_s1_])
        t_s3 = p.op("vector", lambda h: h.reciprocal(sx1, sx1), [t_s2_])
        t_s4 = p.op("vector", lambda h: h.tensor_tensor(
            p3, p3, m1.broadcast_to([128, 16, 4]), ALU.mult), [t_s3])
        t_f1 = p.op("vector", lambda h: h.tensor_tensor(
            p4, p4, flbb[:, 0:16].unsqueeze(1).broadcast_to([128, 4, 16]), ALU.max),
            [t_s4, t_flb])
        t_s5 = p.op("vector", lambda h: h.tensor_tensor(m2, p3[:, :, 0:2], p3[:, :, 2:4], ALU.add),
                    [t_f1])
        t_s6 = p.op("vector", lambda h: h.tensor_tensor(m1, m2[:, :, 0:1], m2[:, :, 1:2], ALU.add),
                    [t_s5])
        t_s7 = p.op("vector", lambda h: h.reciprocal(sx1, sx1), [t_s6])
        t_s8 = p.op("vector", lambda h: h.tensor_tensor(
            p3, p3, m1.broadcast_to([128, 16, 4]), ALU.mult), [t_s7])
        sxw = t_s8
        t_pc = p.op("vector", lambda h: h.tensor_copy(prb_, prf), [t_s8, prbw])
        prfw = t_pc

        def trp(h, t_pc=t_pc):
            last = None
            for q in range(4):
                last = h.transpose(E.psb[0:16, q * 128:(q + 1) * 128],
                                   prb_[:, q * 16:(q + 1) * 16], E.ident)
            return last
        t_tp = p.op("tensor", trp, [t_pc, E.const_tok[1]] + psbw)
        prbw = t_tp
        t_pe = p.op("vector", lambda h: h.tensor_copy(prT[0:16, :], E.psb[0:16, 0:512]),
                    [t_tp, prTw])
        psbw = [t_pe]
        t_d1 = p.op("sync", lambda h: h.dma_start(out=prd, in_=prT[0:16, :]),
                    [t_pe] + ([prdw] if prdw else []), dma=True)
        prTw = t_d1
        t_d2 = p.op("sync", lambda h: h.dma_start(
            out=prrow[0:1, :], in_=prd[0:8, :].rearrange("l w -> (l w)").unsqueeze(0)),
            [t_d1] + prbcw, dma=True)
        prdw = t_d2
        prroww = []
        prbcw = []
        for hh in range(4):
            if hh == 2:
                t_d2 = p.op("sync", lambda h: h.dma_start(
                    out=prrow[0:1, :], in_=prd[8:16, :].rearrange("l w -> (l w)").unsqueeze(0)),
                    [t_d1] + prroww, dma=True)
                prdw = t_d2
            for path in range(4):
                l = hh * 4 + path
                t_b1 = _mmgroup(E, E.ps[4][:, :],
                                [(E.ones_bf[0:1, 0:128], prrow[0:1, (l % 8) * TBW:(l % 8 + 1) * TBW])],
                                [t_d2, E.ps_war[4]])
                prroww.append(t_b1)
                t_be = p.op("vector", lambda h: h.tensor_copy(prbcf, E.ps[4][:, :]),
                            [t_b1] + prbcw)
                E.ps_war[4] = t_be
                prbcw = []
                for cc in range(2):
                    fc = 2 * hh + cc
                    if path == 0:
                        tt = p.op("vector", (lambda h, fc=fc, path=path: h.tensor_tensor(
                            mixf[:, fc * TBW:(fc + 1) * TBW],
                            pbuf[path][:, fc * TBW:(fc + 1) * TBW], prbcf, ALU.mult)),
                            [t_be, ldp[path]] + mfw[fc])
                    else:
                        t_t1 = p.op("vector", (lambda h, fc=fc, path=path, mi=cc: h.tensor_tensor(
                            msc[mi], pbuf[path][:, fc * TBW:(fc + 1) * TBW], prbcf, ALU.mult)),
                            [t_be, ldp[path], mscw[cc]])
                        tt = p.op("vector", (lambda h, fc=fc, mi=cc: h.tensor_tensor(
                            mixf[:, fc * TBW:(fc + 1) * TBW], mixf[:, fc * TBW:(fc + 1) * TBW],
                            msc[mi], ALU.add)), [t_t1, mixtok[fc]])
                        mscw[cc] = tt
                    mixtok[fc] = tt
                    prbcw.append(tt)
        sq_mm2 = list(sq_mm)
        rmms = None
        for fc in range(NCH):
            t_sq = p.op("scalar", (lambda h, fc=fc: h.activation(
                sqp[fc % 2], mixf[:, fc * TBW:(fc + 1) * TBW], AF.Square)),
                [mixtok[fc], sq_mm2[fc % 2]])
            rmms = _mmgroup(E, E.ps[5][0:4, :], [(cdh[:, fc * 4:(fc + 1) * 4], sqp[fc % 2])],
                            [t_sq, t_ch] + ([E.ps_war[5]] if fc == 0 else []),
                            start_clear=(fc == 0), stop_group=(fc == 7))
            sq_mm2[fc % 2] = rmms
        sqw = list(sq_mm2)
        t_oe2 = p.op("vector", lambda h: h.tensor_copy(ost[0:4, :], E.ps[5][0:4, :]),
                     [rmms, ostw])
        E.ps_war[5] = t_oe2
        t_od1 = p.op("sync", lambda h: h.dma_start(out=ord_, in_=ost[0:4, :]),
                     [t_oe2] + ([ordw] if ordw else []), dma=True)
        ostw = t_od1
        t_od2 = p.op("sync", lambda h: h.dma_start(
            out=orow[0:1, :], in_=ord_.rearrange("l w -> (l w)").unsqueeze(0)),
            [t_od1, oroww], dma=True)
        ordw = t_od2
        t_o1 = p.op("vector", lambda h: h.tensor_scalar_add(orow[0:1, :], orow[0:1, :], 1e-5),
                    [t_od2])
        t_o2 = p.op("scalar", lambda h: h.activation(orow[0:1, :], orow[0:1, :], AF.Ln), [t_o1])
        t_o3 = p.op("scalar", lambda h: h.activation(orow[0:1, :], orow[0:1, :], AF.Exp,
                                                     scale=-0.5), [t_o2])
        t_o4 = p.op("vector", lambda h: h.tensor_copy(orb[0:1, :], orow[0:1, :]),
                    [t_o3] + orbw)
        oroww = t_o4
        orbw = []
        for hh in range(4):
            t_b2 = _mmgroup(E, E.ps[4][:, :],
                            [(E.ones_bf[0:1, 0:128], orb[0:1, hh * TBW:(hh + 1) * TBW])],
                            [t_o4, E.ps_war[4]])
            orbw.append(t_b2)
            t_be2 = p.op("vector", lambda h: h.tensor_copy(prbcf, E.ps[4][:, :]),
                         [t_b2] + prbcw)
            E.ps_war[4] = t_be2
            prbcw = []
            for cc in range(2):
                fc = 2 * hh + cc
                t_mn = p.op("vector", (lambda h, fc=fc, mi=cc: h.tensor_tensor(
                    msc[mi], mixf[:, fc * TBW:(fc + 1) * TBW], prbcf, ALU.mult)),
                    [t_be2, mscw[cc]] + ([sq_mm2[fc % 2]] if True else []))
                t_m3 = p.op("vector", (lambda h, fc=fc, mi=cc: h.tensor_scalar(
                    mix3[:, fc * TBW:(fc + 1) * TBW], msc[mi], onwb[:, fc:fc + 1],
                    None, ALU.mult)), [t_mn, t_onw] + mix3w[fc])
                mscw[cc] = t_m3
                mix3tok[fc] = t_m3
                mfw[fc] = [t_mn]
                prbcw.append(t_mn)
        wol = []
        for oc in range(NCH):
            bank = 2 + oc % 2
            pairs = [(wobuf[:, fc * 1024 + oc * 128: fc * 1024 + oc * 128 + 128],
                      mix3[:, fc * TBW:(fc + 1) * TBW]) for fc in range(NCH)]
            t_wm = _mmgroup(E, E.ps[bank][:, :], pairs,
                            [t_wo, E.ps_war[bank]] + [mix3tok[fc] for fc in range(NCH)])
            t_ev = p.op("vector", (lambda h, bank=bank: h.tensor_copy(oadd, E.ps[bank][:, :])),
                        [t_wm, oaddw])
            t_x = p.op("vector", (lambda h, oc=oc, t0=t0: h.tensor_tensor(
                E.x[:, oc * T + t0: oc * T + t0 + TBW], E.x[:, oc * T + t0: oc * T + t0 + TBW],
                oadd, ALU.add)), [t_ev, E.xtok.get((oc, tb))])
            E.ps_war[bank] = t_x
            oaddw = t_x
            E.xtok[(oc, tb)] = t_x
            wol.append(t_wm)
        for fc in range(NCH):
            mix3w[fc] = wol
        tbw = [t_raw, t_lgmm, prevmm] + wol + [mixtok[fc] for fc in range(NCH)] + gw[-1:]
    E.pool.pop()


# version 07
# speedup vs baseline: 1.1080x; 1.0036x over previous
import contextlib
import numpy as np
import ml_dtypes
import concourse.bass as bass
import concourse.mybir as mybir
from concourse.bass_utils import run_bass_kernel_spmd

F32 = mybir.dt.float32
BF16 = mybir.dt.bfloat16
AF = mybir.ActivationFunctionType
ALU = mybir.AluOpType
AX = mybir.AxisListType
BF16NP = np.dtype(ml_dtypes.bfloat16)

D = 1024
T = 2048
NCH = 8
TBW = 512
NTB = T // TBW
CORES = 8

ENGS = ("tensor", "vector", "scalar", "gpsimd", "sync")
HOSTBF16 = True
DEBUG = False
PHASE = 99


class Prog:
    def __init__(self, nc):
        self.nc = nc
        self.ops = {e: [] for e in ENGS}
        self.cnt = {}
        self.waited = {e: {} for e in ENGS}
        self.chkeys = []
        self.n_cc = 0

    def _key(self, eng, dma):
        k = ("d" if dma else "c", eng)
        if k not in self.cnt:
            self.cnt[k] = 0
            self.chkeys.append(k)
        return k

    def op(self, eng, emit, deps=(), dma=False):
        key = self._key(eng, dma)
        deps = list(deps)
        if dma and self.cnt[key] > 0:
            deps.append((key, self.cnt[key]))
        wl = {}
        for tok in deps:
            if tok is None:
                continue
            k, c = tok
            if self.waited[eng].get(k, -1) >= c:
                continue
            if wl.get(k, -1) < c:
                wl[k] = c
        for k, c in wl.items():
            self.waited[eng][k] = c
        amt = 16 if dma else 1
        self.cnt[key] += amt
        self.ops[eng].append((wl, emit, key, amt))
        return (key, self.cnt[key])

    def collective(self, emit, deps=()):
        self.n_cc += 1
        key = ("cc", self.n_cc)
        self.cnt[key] = 1
        self.chkeys.append(key)
        eng = "gpsimd"
        wl = {}
        for tok in deps:
            if tok is None:
                continue
            k, c = tok
            if self.waited[eng].get(k, -1) >= c:
                continue
            if wl.get(k, -1) < c:
                wl[k] = c
        for k, c in wl.items():
            self.waited[eng][k] = c
        self.ops[eng].append((wl, emit, key, 1))
        return (key, 1)

    def finalize(self, stack, block):
        sems = {}
        for k in self.chkeys:
            sems[k] = stack.enter_context(self.nc.semaphore(f"s_{k[0]}_{k[1]}"))
        for eng in ENGS:
            ops = self.ops[eng]
            if not ops:
                continue

            def make(eng, ops):
                def body(h):
                    for wl, emit, inckey, amt in ops:
                        for k, c in wl.items():
                            h.wait_ge(sems[k], c)
                        inst = emit(h)
                        inst.then_inc(sems[inckey], amt)
                return body
            getattr(block, eng)(make(eng, ops))


class Pool:
    def __init__(self, tensor, nelem):
        self.t = tensor
        self.n = nelem
        self.off = 0
        self.marks = []

    def alloc_bf16(self, cols):
        cols = (cols + 15) // 16 * 16
        assert self.off + cols <= self.n, f"pool overflow {self.off}+{cols}>{self.n}"
        ap = self.t.ap()[:, self.off:self.off + cols]
        self.off += cols
        return ap

    def alloc_f32(self, cols):
        return self.alloc_bf16(2 * cols).bitcast(F32)

    def at(self, off, cols):
        return self.t.ap()[:, off:off + cols]

    def push(self):
        self.marks.append(self.off)

    def pop(self):
        self.off = self.marks.pop()


def _rope_tables():
    d = 64
    inv = 1.0 / (10000.0 ** (np.arange(0, d, 2, dtype=np.float64) / d))
    t = np.arange(T, dtype=np.float64)
    fr = np.outer(t, inv)
    cos = np.cos(fr).T.astype(np.float32)
    sin = np.sin(fr).T.astype(np.float32)
    C = np.concatenate([cos, cos, cos, cos], 0)
    S = np.concatenate([-sin, sin, -sin, sin], 0)
    return C.astype(BF16NP), S.astype(BF16NP)


def _perm_rope_cols():
    return np.concatenate([np.arange(0, 64, 2), np.arange(1, 64, 2)])


def prep_inputs(I):
    g = {}
    X = np.asarray(I["L0_attn_a_X"])
    per_core = [{"xin": np.ascontiguousarray(X[b].T)} for b in range(CORES)]

    g["a0_wq"] = np.ascontiguousarray(I["L0_attn_a_W_Q"])
    g["a0_wk"] = np.ascontiguousarray(I["L0_attn_a_W_K"])
    g["a0_wv"] = np.ascontiguousarray(I["L0_attn_a_W_V"])

    for nm in ("wf", "wi", "wc", "wo"):
        g[f"f0_{nm}"] = np.ascontiguousarray(I[f"L0_ffn_{nm}"].T)
    for nm in ("uf", "ui", "uc", "uo"):
        g[f"f0_{nm}"] = np.ascontiguousarray(I[f"L0_ffn_{nm}"].T)
    for nm in ("bf", "bi", "bc", "bo", "in_c"):
        g[f"f0_{nm}"] = np.ascontiguousarray(I[f"L0_ffn_{nm}"].reshape(8, 128).T)

    perm = _perm_rope_cols()
    Wqb = np.asarray(I["L0_attn_b_Wqb"])
    Wqb_p = Wqb.reshape(64, 16, 64)[:, :, perm].reshape(64, 16, 64)
    hperm = [0] * 16
    for h in range(16):
        slot = 2 * ((h % 4) + 4 * (h // 8)) + ((h // 4) % 2)
        hperm[slot] = h
    Wqb_p = Wqb_p[:, hperm].reshape(64, 1024)
    g["b0_hperm"] = np.asarray(hperm)
    g["b0_wqa"] = np.ascontiguousarray(I["L0_attn_b_Wqa"])
    g["b0_wqb"] = np.ascontiguousarray(Wqb_p)
    Wkvb = np.asarray(I["L0_attn_b_Wkvb"])
    Wk_part = Wkvb.reshape(32, 4, 64, 2)[:, :, perm, 0].reshape(32, 256)
    Wv_part = Wkvb.reshape(32, 4, 64, 2)[:, :, :, 1].reshape(32, 256)
    g["b0_wkva"] = np.ascontiguousarray(I["L0_attn_b_Wkva"])
    g["b0_wkvb"] = np.ascontiguousarray(np.concatenate([Wk_part, Wv_part], 1))
    Wout = np.asarray(I["L0_attn_b_Wout"]).reshape(16, 64, 1024)[hperm].reshape(1024, 1024)
    g["b0_wout"] = np.ascontiguousarray(Wout)
    g["b0_bout"] = np.ascontiguousarray(I["L0_attn_b_bout"].reshape(1, 1024))
    C, S = _rope_tables()
    g["b0_ropec"] = C
    g["b0_ropes"] = S
    masks = np.zeros((4, 128, TBW), dtype=np.float32)
    kk = np.arange(128)[:, None]
    qq = np.arange(TBW)[None, :]
    for r in range(4):
        masks[r] = np.where(128 * r + kk <= qq, 0.0, -1e9)
    g["b0_masks"] = np.ascontiguousarray(masks.transpose(1, 0, 2).reshape(128, 4 * TBW).astype(BF16NP))

    g["d1_wq"] = np.ascontiguousarray(I["L1_attn_a_Wq"].T)
    g["d1_wk"] = np.ascontiguousarray(I["L1_attn_a_Wk"].T)
    g["d1_wv"] = np.ascontiguousarray(I["L1_attn_a_Wv"].T)
    wbp = np.zeros((1024, 32), dtype=np.float32)
    wbp[:, 0:4] = I["L1_attn_a_Wb"].T
    g["d1_wb"] = np.ascontiguousarray(wbp)
    for nm in ("qconv_w", "kconv_w", "vconv_w"):
        w = np.asarray(I[f"L1_attn_a_{nm}"])
        g[f"d1_{nm}"] = np.ascontiguousarray(w.reshape(8, 128, 4).transpose(1, 0, 2).reshape(128, 32))
    fs = np.asarray(I["L1_attn_a_fir_short_w"]).reshape(1024, 3)
    fl = np.asarray(I["L1_attn_a_fir_long_w"]).reshape(1024, 63)
    g["d1_fir_s"] = np.ascontiguousarray(fs.reshape(8, 128, 3).transpose(1, 0, 2).reshape(128, 24))
    g["d1_fir_l"] = np.ascontiguousarray(fl.reshape(8, 128, 63).transpose(1, 0, 2).reshape(128, 8 * 63))
    g1p = np.zeros((1152, 2048), dtype=np.float32)
    w1t = I["L1_attn_a_gate_w1"].T
    g1p[0:1024] = w1t[0:1024]
    for mv in range(2):
        for path in range(4):
            for hh in range(4):
                g1p[1024 + mv * 16 + path * 4 + hh] = w1t[1024 + path * 8 + hh * 2 + mv]
    g["d1_g1"] = np.ascontiguousarray(g1p)
    g["d1_g1b"] = np.ascontiguousarray(I["L1_attn_a_gate_b1"].reshape(16, 128).T)
    g["d1_g2"] = np.ascontiguousarray(I["L1_attn_a_gate_w2"].T)
    g["d1_bb"] = np.ascontiguousarray(I["L1_attn_a_base_bias"].reshape(16, 1))
    g["d1_logt"] = np.ascontiguousarray(np.repeat(np.asarray(I["L1_attn_a_log_temp"]).reshape(4), 4).reshape(16, 1).astype(np.float32))
    g["d1_floor"] = np.ascontiguousarray(np.asarray(I["L1_attn_a_floor_raw"]).reshape(1, 16).astype(np.float32))
    g["d1_onw"] = np.ascontiguousarray(np.tile(I["L1_attn_a_onorm_w"].reshape(2, 128).T, (1, 4)))
    g["d1_wo"] = np.ascontiguousarray(I["L1_attn_a_Wo"].T)

    wg = np.zeros((1024, 32), dtype=np.float32)
    wg[:, 0:8] = I["L1_ffn_Wg"]
    g["m1_wg"] = np.ascontiguousarray(wg)
    g["m1_w1"] = np.ascontiguousarray(I["L1_ffn_W1"])
    g["m1_w2"] = np.ascontiguousarray(I["L1_ffn_W2"])
    g["m1_b1"] = np.ascontiguousarray(np.asarray(I["L1_ffn_b1"]).reshape(8, 32, 128).transpose(0, 2, 1).reshape(8, 128 * 32))
    g["m1_b2"] = np.ascontiguousarray(I["L1_ffn_b2"])

    g["s1_ns"] = np.ascontiguousarray(I["L1_attn_b_norm_scale"].reshape(8, 128).T)
    w_in = np.asarray(I["L1_attn_b_w_in"])
    DIN = 2048
    gate = w_in[:, 0:DIN]
    ssm = w_in[:, DIN:2 * DIN]
    dtr = w_in[:, 2 * DIN:2 * DIN + 32]
    Bc = w_in[:, 2 * DIN + 32:2 * DIN + 96]
    Cc = w_in[:, 2 * DIN + 96:2 * DIN + 160]
    zeros = np.zeros((1024, 96), dtype=np.float32)
    g["s1_win"] = np.ascontiguousarray(np.concatenate([Bc, Cc, dtr, zeros, gate, ssm], 1))
    cw = np.asarray(I["L1_attn_b_conv_weight"])
    g["s1_cw"] = np.ascontiguousarray(cw.T.reshape(16, 128, 4).transpose(1, 0, 2).reshape(128, 64))
    g["s1_cb"] = np.ascontiguousarray(I["L1_attn_b_conv_bias"].reshape(16, 128).T)
    g["s1_alog"] = np.ascontiguousarray(I["L1_attn_b_A_log"].reshape(32, 1))
    g["s1_dtb"] = np.ascontiguousarray(I["L1_attn_b_dt_bias"].reshape(32, 1))
    g["s1_wout"] = np.ascontiguousarray(I["L1_attn_b_w_out"])

    sel = np.zeros((8, 8, 128), dtype=BF16NP)
    for e_ in range(8):
        sel[e_, e_, :] = 1.0
    g["m1_sel"] = np.ascontiguousarray(sel.transpose(1, 0, 2).reshape(8, 1024))
    bgp = np.full((1, 32), -1e9, dtype=np.float32)
    bgp[0, 0:8] = I["L1_ffn_bg"]
    g["m1_bg"] = np.ascontiguousarray(bgp)
    g["c_ident"] = np.eye(128, dtype=BF16NP)
    g["c_ones"] = np.ones((128, 128), dtype=np.float32)
    i32 = np.arange(32)
    tril_s = (i32[:, None] > i32[None, :]).astype(np.float32)
    triu_i = (i32[:, None] <= i32[None, :]).astype(np.float32)
    bd = np.zeros((128, 128), dtype=np.float32)
    bdT = np.zeros((128, 128), dtype=np.float32)
    bdA = np.zeros((128, 128), dtype=np.float32)
    for h in range(4):
        bd[32 * h:32 * h + 32, 32 * h:32 * h + 32] = -tril_s
        bdT[32 * h:32 * h + 32, 32 * h:32 * h + 32] = -tril_s.T
        bdA[32 * h:32 * h + 32, 32 * h:32 * h + 32] = triu_i
    cd = np.zeros((128, 64, 32), dtype=np.float32)
    for path in range(4):
        for fc in range(8):
            cd[:, (path * 8 + fc) * 2 + 0, path * 4 + fc // 2] = 1.0 / 256
            cd[:, (path * 8 + fc) * 2 + 1, 16 + path * 4 + fc // 2] = 1.0 / 256
    g["c_dsel"] = np.ascontiguousarray(cd.reshape(128, 2048).astype(BF16NP))
    cdh = np.zeros((128, 8, 4), dtype=np.float32)
    for fc in range(8):
        cdh[:, fc, fc // 2] = 1.0 / 256
    g["c_dselh"] = np.ascontiguousarray(cdh.reshape(128, 32).astype(BF16NP))
    g["c_bd"] = bd
    g["c_bdT"] = bdT
    g["c_bdA"] = bdA
    st = (np.arange(128)[:, None] <= np.arange(128)[None, :])
    g["c_mst"] = np.tile(st.astype(BF16NP), (1, 4))
    g["c_utri"] = np.ascontiguousarray(np.triu(np.ones((128, 128), dtype=np.float32)).T)
    for k, v in list(g.items()):
        if HOSTBF16 and v.dtype == np.float32 and v.size > 65536:
            g[k] = np.ascontiguousarray(v.astype(BF16NP))
    return g, per_core


class Env:
    pass


def _mmgroup(E, ps_ap, pairs, deps, extra=None, tile_position=None, start_clear=True,
             extra_tile_position=None, stop_group=True):
    def emit(h):
        last = None
        n = len(pairs)
        for i, (l, r) in enumerate(pairs):
            last = h.matmul(ps_ap, l, r, start=(i == 0 and start_clear),
                            stop=(i == n - 1 and extra is None and stop_group),
                            tile_position=tile_position)
        if extra is not None:
            for j, (l, r) in enumerate(extra):
                last = h.matmul(ps_ap, l, r, start=False,
                                stop=(j == len(extra) - 1 and stop_group),
                                tile_position=extra_tile_position)
        return last
    return E.p.op("tensor", emit, deps)


def _mmgroup_tp(E, ps_fn, triples, deps, start_clear=True, stop_group=True, pairwise=False):
    def emit(h):
        last = None
        n = len(triples)
        for i, (ps_ap, l, r, tp) in enumerate(triples):
            if pairwise:
                st, sp = (i % 2 == 0), (i % 2 == 1)
            else:
                st, sp = (i == 0 and start_clear), (i == n - 1 and stop_group)
            last = h.matmul(ps_ap, l, r, start=st, stop=sp, tile_position=tp)
        return last
    return E.p.op("tensor", emit, deps)


def declare_input(E, name):
    if name in E.innames:
        return E.innames[name]
    arr = E.g[name]
    dt = BF16 if arr.dtype == BF16NP else F32
    ap = E.nc.dram_tensor(name, list(arr.shape), dt, kind="ExternalInput").ap()
    E.innames[name] = ap
    return ap


def barrier(E):
    snap = dict(E.p.cnt)
    toks = [(k, c) for k, c in snap.items() if c > 0]
    new = {}
    for eng in ENGS:
        new[eng] = E.p.op(eng, lambda h: h.nop(), deps=toks)
    return new


def load_weight_bf16(E, name, dst_ap, deps=(), cols=None):
    src = declare_input(E, name)
    R, C = E.g[name].shape
    nch = R // 128
    if cols is None:
        cols = C

    def emit(h):
        return h.dma_start(
            out=dst_ap.rearrange("p (c h) -> p c h", c=nch),
            in_=src.rearrange("(c p) h -> p c h", p=128)[:, :, 0:cols])
    return E.p.op("gpsimd", emit, deps, dma=True)


def emit_ln(E, xsrc=None, xsrc_tok=None, out=None, rms=False):
    p = E.p
    out = out if out is not None else E.xln
    if rms:
        nsc = E.pool.alloc_f32(8)
        ns_in = declare_input(E, "s1_ns")
        t_ns = p.op("gpsimd", (lambda h: h.dma_start(out=nsc, in_=ns_in)), dma=True)
    E.pool.push()
    E.lnbf = E.pool.alloc_bf16(NCH * TBW)
    E.lnsq = E.pool.alloc_bf16(NCH * TBW)
    E.lnt1 = E.lnsq
    E.lnbf_war = E.ln_scratch_war
    E.lnsq_war = E.ln_scratch_war
    E.lnt1_war = E.ln_scratch_war
    E.lntok = {}
    for tb in range(NTB):
        q0 = tb * TBW
        cps, sqs = [], []
        for c in range(NCH):
            if xsrc is None:
                xs = E.x[:, c * T + q0: c * T + q0 + TBW]
                d = [E.xtok.get((c, tb))]
            else:
                xs = xsrc[:, c * T + q0: c * T + q0 + TBW]
                d = [xsrc_tok]
            cps.append(p.op("vector", (lambda h, c=c, xs=xs, b_=E.lnbf: h.tensor_copy(
                b_[:, c * TBW:(c + 1) * TBW], xs)), d + [E.lnbf_war]))
            sqs.append(p.op("scalar", (lambda h, c=c, xs=xs, b_=E.lnsq: h.activation(
                b_[:, c * TBW:(c + 1) * TBW], xs, AF.Square)), d + [E.lnsq_war]))
        t_s1 = None
        if not rms:
            t_s1 = _mmgroup(E, E.ps[0][0:1, :],
                            [(E.ones_bf[:, 0:1], E.lnbf[:, c * TBW:(c + 1) * TBW]) for c in range(NCH)],
                            cps + [E.ps_war[0]])
        t_s2 = _mmgroup(E, E.ps[1][0:1, :],
                        [(E.ones_bf[:, 0:1], E.lnsq[:, c * TBW:(c + 1) * TBW]) for c in range(NCH)],
                        sqs + [E.ps_war[1]])
        rm, rv, rr, rs, rt = E.row_m, E.row_v, E.row_r, E.row_s, E.row_t
        if rms:
            t2 = p.op("vector", lambda h: h.tensor_scalar_mul(rv, E.ps[1][0:1, :], 1.0 / D),
                      [t_s2, E.row_war])
            t5 = p.op("vector", lambda h: h.tensor_scalar_add(rv, rv, 1e-6), [t2])
            t1 = t5
        else:
            t1 = p.op("vector", lambda h: h.tensor_scalar_mul(rm, E.ps[0][0:1, :], 1.0 / D), [t_s1, E.row_war])
            t2 = p.op("vector", lambda h: h.tensor_scalar_mul(rv, E.ps[1][0:1, :], 1.0 / D), [t_s2, E.row_war])
            t3 = p.op("vector", lambda h: h.tensor_tensor(rt, rm, rm, ALU.mult), [t1])
            t4 = p.op("vector", lambda h: h.tensor_sub(rv, rv, rt), [t2, t3])
            t5 = p.op("vector", lambda h: h.tensor_scalar_add(rv, rv, 1e-5), [t4])
        t6 = p.op("scalar", lambda h: h.activation(rt, rv, AF.Ln), [t5])
        t7 = p.op("scalar", lambda h: h.activation(rr, rt, AF.Exp, scale=-0.5), [t6])
        if rms:
            t8b = p.op("vector", lambda h: h.tensor_copy(E.row_b1, rr), [t7, E.row_b_war])
            t_rb = _mmgroup(E, E.ps[2][:, :], [(E.ones_bf[0:1, :], E.row_b1)], [t8b, E.ps_war[2]])
            tas = []
            t1w = E.lnt1_war
            for c in range(NCH):
                tas.append(p.op("vector", (lambda h, c=c, t_=E.lnt1, b_=E.lnbf: h.tensor_tensor(
                    t_[:, c * TBW:(c + 1) * TBW],
                    b_[:, c * TBW:(c + 1) * TBW], E.ps[2][:, :], ALU.mult)), [t_rb, cps[c], t1w]))
            lastv = None
            for c in range(NCH):
                lastv = p.op("vector", (lambda h, c=c, q0=q0, t_=E.lnt1: h.tensor_scalar(
                    out[:, c * T + q0: c * T + q0 + TBW],
                    t_[:, c * TBW:(c + 1) * TBW], nsc[:, c:c + 1], None, ALU.mult)),
                    [t_ns, tas[c]])
                E.lntok[(c, tb)] = lastv
            E.row_b_war = lastv
            E.ps_war[2] = lastv
        else:
            t8 = p.op("vector", lambda h: h.scalar_tensor_tensor(rs, rm, -1.0, rr, ALU.mult, ALU.mult), [t7])
            t8b = p.op("vector", lambda h: h.tensor_copy(E.row_b1, rr), [t7, E.row_b_war])
            t8c = p.op("vector", lambda h: h.tensor_copy(E.row_b2, rs), [t8, E.row_b_war])
            t_rb = _mmgroup(E, E.ps[2][:, :], [(E.ones_bf[0:1, :], E.row_b1)], [t8b, E.ps_war[2]])
            t_sb = _mmgroup(E, E.ps[3][:, :], [(E.ones_bf[0:1, :], E.row_b2)], [t8c, E.ps_war[3]])
            tas = []
            t1w = E.lnt1_war
            for c in range(NCH):
                tas.append(p.op("vector", (lambda h, c=c, t_=E.lnt1, b_=E.lnbf: h.tensor_tensor(
                    t_[:, c * TBW:(c + 1) * TBW],
                    b_[:, c * TBW:(c + 1) * TBW], E.ps[2][:, :], ALU.mult)), [t_rb, cps[c], t1w]))
            lastv = None
            for c in range(NCH):
                lastv = p.op("vector", (lambda h, c=c, q0=q0, t_=E.lnt1: h.tensor_tensor(
                    out[:, c * T + q0: c * T + q0 + TBW],
                    t_[:, c * TBW:(c + 1) * TBW], E.ps[3][:, :], ALU.add)), [t_sb, tas[c]])
                E.lntok[(c, tb)] = lastv
            E.row_b_war = lastv
        if not rms:
            E.ps_war[0] = t1
            E.ps_war[3] = lastv
        E.ps_war[1] = t2
        E.ps_war[2] = lastv
        E.row_war = lastv
        E.lnbf_war = lastv
        E.lnsq_war = lastv
        E.lnt1_war = lastv
    E.lntoks = list(E.lntok.values())
    E.ln_scratch_war = lastv
    E.pool.pop()


def emit_attn_a(E):
    p = E.p
    QB = 256
    NQB = T // QB
    E.pool.push()
    E.xln = E.pool.alloc_bf16(NCH * T)
    emit_ln(E)
    if DEBUG:
        E.dbg["xln"] = (E.xln, E.lntoks)
    KT = E.pool.alloc_bf16(NCH * T)
    VT = E.pool.alloc_bf16(16 * D)
    sqs = E.pool.alloc_bf16(2 * TBW)
    rowk = E.pool.alloc_bf16(T)
    kmax = E.pool.alloc_f32(8)
    E.pool.push()
    wbuf = E.pool.alloc_bf16(NCH * D)
    prev_sq = None

    t_w = load_weight_bf16(E, "a0_wk", wbuf, [E.wload_war])
    ktoks = []
    sq_mm_toks = []
    for ec in range(NCH):
        for tb in range(NTB):
            i = (ec * NTB + tb) % 2
            pairs = [(wbuf[:, c * D + ec * 128: c * D + ec * 128 + 128],
                      E.xln[:, c * T + tb * TBW: c * T + tb * TBW + TBW]) for c in range(NCH)]
            tm = _mmgroup(E, E.ps[i][:, :], pairs, E.lntoks + [t_w, E.ps_war[i]])
            tv = p.op("vector", (lambda h, ec=ec, tb=tb, i=i: h.tensor_copy(
                KT[:, ec * T + tb * TBW: ec * T + tb * TBW + TBW], E.ps[i][:, :])), [tm])
            E.ps_war[i] = tv
            ktoks.append(tv)
    prev_sq = None
    for tb in range(NTB):
        mmdeps = [E.ps_war[2]]
        for ec in range(NCH):
            half = ec % 2
            tsq = p.op("scalar", (lambda h, ec=ec, tb=tb, half=half: h.activation(
                sqs[:, half * TBW:(half + 1) * TBW],
                KT[:, ec * T + tb * TBW: ec * T + tb * TBW + TBW], AF.Square)),
                [ktoks[ec * NTB + tb], prev_sq] + sq_mm_toks[-2:])
            tm = _mmgroup(E, E.ps[2][0:1, :], [(E.ones_bf[:, 0:1], sqs[:, half * TBW:(half + 1) * TBW])],
                          [tsq] + mmdeps, start_clear=(ec == 0), stop_group=(ec == NCH - 1))
            sq_mm_toks.append(tm)
            prev_sq = tsq
        tv = p.op("vector", (lambda h, tb=tb: h.tensor_copy(
            rowk[0:1, tb * TBW:(tb + 1) * TBW], E.ps[2][0:1, :])), [sq_mm_toks[-1]])
        E.ps_war[2] = tv
        ktoks.append(tv)
    t_km = p.op("vector", lambda h: h.reduce_max(kmax[0:1, 0:1], rowk[0:1, :], axis=AX.X),
                [ktoks[-1], ktoks[-2], ktoks[-3], ktoks[-4]])
    t_km2 = p.op("scalar", lambda h: h.activation(kmax[0:1, 1:2], kmax[0:1, 0:1], AF.Sqrt), [t_km])
    t_km3 = p.op("scalar", lambda h: h.mul(kmax[0:1, 2:3], kmax[0:1, 1:2], -1.0), [t_km2])

    t_wv = load_weight_bf16(E, "a0_wv", wbuf, ktoks + sq_mm_toks)
    vtoks = []
    for tc in range(16):
        for eb in range(2):
            i = (tc * 2 + eb) % 2
            pairs = [(E.xln[:, c * T + tc * 128: c * T + tc * 128 + 128],
                      wbuf[:, c * D + eb * TBW: c * D + eb * TBW + TBW]) for c in range(NCH)]
            tm = _mmgroup(E, E.ps[i][:, :], pairs, E.lntoks + [t_wv, E.ps_war[i]])
            tv = p.op("vector", (lambda h, tc=tc, eb=eb, i=i: h.tensor_copy(
                VT[:, tc * D + eb * TBW: tc * D + eb * TBW + TBW], E.ps[i][:, :])), [tm])
            E.ps_war[i] = tv
            vtoks.append(tv)

    if DEBUG:
        E.dbg["KT"] = (KT, ktoks)
        E.dbg["VT"] = (VT, vtoks)
        E.dbg["rowk"] = (rowk, ktoks[-4:])
    E.pool.pop()
    QTb = E.pool.alloc_bf16(NCH * QB)
    PT = E.pool.alloc_bf16(QB)
    rowq = E.pool.alloc_f32(QB)
    nbound = E.pool.alloc_bf16(QB)
    RB = E.pool.alloc_bf16(QB)
    ot1 = E.pool.alloc_bf16(QB)
    wq2 = E.pool.alloc_bf16(2 * D)
    wqin = declare_input(E, "a0_wq")
    wq_consumed = {0: vtoks, 1: vtoks}

    for qb in range(NQB):
        q0 = qb * QB
        tb = q0 // TBW
        qtoks = []
        for ec in range(NCH):
            i = ec % 2
            slot = ec % 2
            wsl = wq2[:, slot * D:(slot + 1) * D]
            t_wq = E.p.op("gpsimd", (lambda h, ec=ec, wsl=wsl: h.dma_start(
                out=wsl.rearrange("p (c e) -> p c e", c=NCH),
                in_=wqin.rearrange("(c p) e -> p c e", p=128)[:, :, ec * 128:(ec + 1) * 128])),
                wq_consumed[slot], dma=True)
            pairs = [(wsl[:, c * 128:(c + 1) * 128],
                      E.xln[:, c * T + q0: c * T + q0 + QB]) for c in range(NCH)]
            tm = _mmgroup(E, E.ps[i][:, 0:QB], pairs, E.lntoks + [t_wq, E.ps_war[i]])
            wq_consumed[slot] = [tm]
            tv = p.op("vector", (lambda h, ec=ec, i=i: h.tensor_scalar_mul(
                QTb[:, ec * QB:(ec + 1) * QB], E.ps[i][:, 0:QB], 1.0 / 32.0)), [tm])
            E.ps_war[i] = tv
            qtoks.append(tv)
        sqq = []
        for ec in range(NCH):
            half = ec % 2
            tsq = p.op("scalar", (lambda h, ec=ec, half=half: h.activation(
                sqs[:, half * TBW: half * TBW + QB], QTb[:, ec * QB:(ec + 1) * QB], AF.Square)),
                [qtoks[ec], prev_sq] + sq_mm_toks[-2:])
            tm = _mmgroup(E, E.ps[2][0:1, 0:QB], [(E.ones_bf[:, 0:1], sqs[:, half * TBW: half * TBW + QB])],
                          [tsq, E.ps_war[2]], start_clear=(ec == 0), stop_group=(ec == NCH - 1))
            sq_mm_toks.append(tm)
            prev_sq = tsq
        t_qr = p.op("scalar", lambda h: h.activation(rowq[0:1, :], E.ps[2][0:1, 0:QB], AF.Sqrt),
                    [sq_mm_toks[-1]])
        E.ps_war[2] = t_qr
        t_nb = p.op("vector", lambda h: h.tensor_scalar(nbound[0:1, :], rowq[0:1, :],
                                                        kmax[0:1, 2:3], None, ALU.mult),
                    [t_qr, t_km3])

        o_mms = []
        exps = []
        for kc in range(16):
            i = kc % 2
            pairs = [(KT[:, ec * T + kc * 128: ec * T + kc * 128 + 128],
                      QTb[:, ec * QB:(ec + 1) * QB]) for ec in range(NCH)]
            tm = _mmgroup(E, E.ps[i][:, 0:QB], pairs, [t_nb, E.ps_war[i]],
                          extra=[(E.ones_bf[0:1, :], nbound[0:1, :])])
            te = p.op("scalar", (lambda h, i=i: h.activation(PT, E.ps[i][:, 0:QB], AF.Exp)),
                      [tm] + o_mms[-8:])
            E.ps_war[i] = te
            exps.append(te)
            tden = _mmgroup(E, E.ps[6][0:1, 0:QB], [(E.ones_bf[:, 0:1], PT)],
                            [te, E.ps_war[6]], start_clear=(kc == 0), stop_group=(kc == 15))
            if kc == 15:
                E.ps_war[6] = tden
            o_mms = []
            for ec in range(NCH):
                bank = 2 + ec // 2
                off = (ec % 2) * QB
                tm2 = _mmgroup(E, E.ps[bank][:, off:off + QB],
                               [(VT[:, kc * D + ec * 128: kc * D + ec * 128 + 128], PT)],
                               [te, E.ps_war[bank]], start_clear=(kc == 0), stop_group=(kc == 15))
                o_mms.append(tm2)
            tdl = tden
        t_e = p.op("vector", lambda h: h.tensor_copy(E.row_t[0:1, 0:QB], E.ps[6][0:1, 0:QB]),
                   [tdl, E.row_war])
        E.ps_war[6] = t_e
        t_rc = p.op("vector", lambda h: h.reciprocal(E.row_m[0:1, 0:QB], E.row_t[0:1, 0:QB]), [t_e])
        t_rb = _mmgroup(E, E.ps[0][:, 0:QB], [(E.ones_f32[0:1, :], E.row_m[0:1, 0:QB])],
                        [t_rc, exps[-1], E.ps_war[0]])
        t_rbs = p.op("vector", lambda h: h.tensor_copy(RB, E.ps[0][:, 0:QB]), [t_rb])
        E.ps_war[0] = t_rbs
        E.row_war = t_rbs
        ot_war = None
        for ec in range(NCH):
            bank = 2 + ec // 2
            off = (ec % 2) * QB
            t1 = p.op("vector", (lambda h, bank=bank, off=off: h.tensor_tensor(
                ot1, E.ps[bank][:, off:off + QB], RB, ALU.mult)), [o_mms[ec], t_rbs, ot_war])
            tx = p.op("vector", (lambda h, ec=ec, q0=q0: h.tensor_tensor(
                E.x[:, ec * T + q0: ec * T + q0 + QB],
                E.x[:, ec * T + q0: ec * T + q0 + QB], ot1, ALU.add)),
                [t1, E.xtok.get((ec, tb))])
            ot_war = tx
            E.xtok[(ec, tb)] = tx
            E.ps_war[bank] = tx
    E.pool.pop()


SUBLAYERS = ("attn_a", "ffn0a", "attn_b", "ffn0b", "delta", "moe0", "mamba", "moe1")


def build_program(subs):
    nc = bass.Bass("TRN2", target_bir_lowering=False, debug=False)
    E = Env()
    E.nc = nc
    E.g = build_program.g
    E.innames = {}
    E.p = Prog(nc)
    p = E.p
    xin = nc.dram_tensor("xin", [D, T], F32, kind="ExternalInput").ap()
    xout = nc.dram_tensor("xout", [D, T], F32, kind="ExternalOutput").ap()

    with contextlib.ExitStack() as stack:
        e = stack.enter_context
        xsb = e(nc.sbuf_tensor("x_sb", [128, NCH * T], F32))
        E.x = xsb.ap()
        poolt = e(nc.sbuf_tensor("pool", [128, 73400], BF16))
        E.pool = Pool(poolt, 73400)
        E.ps = [e(nc.psum_tensor(f"psum{i}", [128, 512], F32)).ap() for i in range(7)]
        E.psb = e(nc.psum_tensor("psumb", [128, 1024], BF16)).ap()
        E.ps_war = {i: None for i in range(7)}
        E.psb_war = None
        E.ln_scratch_war = None
        E.row_m = E.pool.alloc_f32(TBW)[0:1, :]
        E.row_v = E.pool.alloc_f32(TBW)[0:1, :]
        E.row_r = E.pool.alloc_f32(TBW)[0:1, :]
        E.row_s = E.pool.alloc_f32(TBW)[0:1, :]
        E.row_t = E.pool.alloc_f32(TBW)[0:1, :]
        E.ones_bf = E.pool.alloc_bf16(128)
        E.onesr = E.pool.alloc_bf16(TBW)
        E.row_b1 = E.pool.alloc_bf16(TBW)[0:1, :]
        E.row_b2 = E.pool.alloc_bf16(TBW)[0:1, :]
        E.row_b_war = None
        cident = E.pool.alloc_bf16(128)
        E.ident = cident
        E.ones_f32 = E.pool.alloc_f32(128)
        E.row_war = None
        E.wload_war = None
        E.hsc_war = {}
        E.qsw_war = None
        E.rp1_war = None
        E.nb_war = None
        E.obuf_war = None
        E.msc_war = None
        E.mwb_war = None
        E.ot1_war = None
        E.gsc_war = None
        E.gall_war = None
        E.b2h_war = None
        E.selam_war = None
        E.sstmp_war = None
        E.ystg_war = None
        E.yint_war = None
        E.cbm_war = None
        E.sq_war = None
        E.rr_war = None
        E.vtk_war = None
        E.bcsc_war = None
        E.pmb_war = None
        E.zz_war = None
        E.at_war = None
        E.dstg2_war = None
        E.xtok = {}
        E.lntok = {}

        idin = declare_input(E, "c_ident")
        onin = declare_input(E, "c_ones")
        t_c1 = p.op("gpsimd", lambda h: h.dma_start(out=E.ones_bf, in_=onin[:, 0:128]), dma=True)
        t_c2 = p.op("gpsimd", lambda h: h.dma_start(out=cident, in_=idin[:, :]), dma=True)
        t_c3 = p.op("gpsimd", lambda h: h.dma_start(out=E.ones_f32, in_=onin[:, :]), dma=True)
        t_c4 = p.op("gpsimd", lambda h: h.memset(E.onesr[0:1, :], 1.0))
        E.const_tok = [t_c1, t_c2, t_c3, t_c4]

        t_x = p.op("sync", lambda h: h.dma_start(
            out=E.x.rearrange("p (c t) -> p c t", c=NCH),
            in_=xin.rearrange("(c p) t -> p c t", p=128)), dma=True)
        for c in range(NCH):
            for tb in range(NTB):
                E.xtok[(c, tb)] = t_x
        barrier(E)

        E.dbg = {}
        first = True
        for s in subs:
            if not first:
                barrier(E)
            first = False
            if s == 0:
                emit_attn_a(E)
            elif s == 1 or s == 3:
                emit_ffn0(E, s)
            elif s == 2:
                emit_attn_b(E)
            elif s == 4:
                emit_delta(E)
            elif s in (5, 7):
                emit_moe(E, s)
            elif s == 6:
                emit_mamba(E)

        for nm, (ap, toks) in E.dbg.items():
            od = nc.dram_tensor("dbg_" + nm, list(ap.shape), ap.dtype, kind="ExternalOutput").ap()
            E.p.op("sync", (lambda h, od=od, ap=ap: h.dma_start(out=od, in_=ap)), toks, dma=True)
        xdeps = [E.xtok[(c, tb)] for c in range(NCH) for tb in range(NTB)]
        t_o = p.op("sync", lambda h: h.dma_start(
            out=xout.rearrange("(c p) t -> p c t", p=128),
            in_=E.x.rearrange("p (c t) -> p c t", c=NCH)), xdeps, dma=True)
        p.op("sync", lambda h: h.nop(), [t_o])

        block = e(nc.Block())
        p.finalize(stack, block)
    return nc, set(E.innames.keys())


GROUPS = [[0, 1, 2, 3, 4, 5, 6, 7]]


def kernel(**inputs):
    g, per_core = prep_inputs(inputs)
    build_program.g = g
    x_cur = [pc["xin"] for pc in per_core]
    for subs in GROUPS:
        nc, names = build_program(subs)
        in_maps = []
        for b in range(CORES):
            m = {"xin": np.ascontiguousarray(x_cur[b], dtype=np.float32)}
            for nm in names:
                if nm != "xin":
                    m[nm] = g[nm]
            in_maps.append(m)
        res = run_bass_kernel_spmd(nc, in_maps, core_ids=list(range(CORES)))
        x_cur = [res.results[b]["xout"] for b in range(CORES)]
    return np.stack([x.T for x in x_cur]).astype(np.float32)


def emit_ffn0(E, s):
    p = E.p
    nc = E.nc
    R = 256
    agin = nc.dram_tensor(f"ffn_agin{s}", [D, T], BF16).ap()
    agout = nc.dram_tensor(f"ffn_agout{s}", [8 * D, T], BF16).ap()
    zin = nc.dram_tensor(f"ffn_zin{s}", [D, R], F32).ap()
    zout = nc.dram_tensor(f"ffn_zout{s}", [8 * D, R], F32).ap()
    pid = nc.partition_id()

    E.pool.push()
    E.pool.push()
    E.xln = E.pool.alloc_bf16(NCH * T)
    emit_ln(E)
    t_w = p.op("sync", (lambda h, x_=E.xln: h.dma_start(
        out=agin.rearrange("(c p) t -> p c t", p=128),
        in_=x_.rearrange("p (c t) -> p c t", c=NCH))), E.lntoks, dma=True)
    t_ag = p.collective(lambda h: h.collective_compute(
        "AllGather", ALU.bypass, replica_groups=[list(range(CORES))],
        ins=[agin[:, :]], outs=[agout[:, :]]), [t_w])
    E.pool.pop()

    wbase = E.pool.off
    wbufs, tw = {}, {}
    for nm in ("wf", "wi", "wc", "uf", "ui", "uc"):
        wbufs[nm] = E.pool.alloc_bf16(NCH * D)
        tw[nm] = load_weight_bf16(E, f"f0_{nm}", wbufs[nm], [t_w])
    bias, tb_ = {}, {}
    for nm in ("bf", "bi", "bc", "in_c", "bo"):
        bias[nm] = E.pool.alloc_f32(8)
        bin_ = declare_input(E, f"f0_{nm}")
        tb_[nm] = p.op("sync", (lambda h, b=bias[nm], bin_=bin_: h.dma_start(out=b, in_=bin_)), [t_w], dma=True)
    cbase = E.pool.off
    cst = E.pool.alloc_f32(2 * NCH * R)
    cbf = E.pool.alloc_bf16(2 * NCH * R)
    xgp = E.pool.alloc_bf16(NCH * R)
    gsc = E.pool.alloc_f32(3 * R)
    t1s = E.pool.alloc_f32(2 * R)

    def cs(sl, mc):
        return cst[:, (sl * NCH + mc) * R:(sl * NCH + mc) * R + R]

    def cb(sl, mc):
        return cbf[:, (sl * NCH + mc) * R:(sl * NCH + mc) * R + R]

    t_z = p.op("gpsimd", lambda h: h.memset(cst[:, 0:NCH * R], 0.0))
    ctoks = {}
    cbtoks = {}
    for mc in range(NCH):
        t0 = p.op("vector", (lambda h, mc=mc: h.tensor_scalar(
            cs(0, mc), cs(0, mc), bias["in_c"][:, mc:mc + 1], None, ALU.add)),
            [tb_["in_c"], t_z])
        ctoks[mc] = t0
        cbtoks[mc] = p.op("vector", (lambda h, mc=mc: h.tensor_copy(cb(0, mc), cs(0, mc))), [t0])

    ag4 = agout.rearrange("(b c p) t -> b p c t", p=128, c=NCH)
    xg_consumed = {0: [], 1: []}
    GATES = (("wf", "uf", "bf", AF.Sigmoid), ("wi", "ui", "bi", AF.Sigmoid), ("wc", "uc", "bc", AF.Tanh))
    for t in range(8):
        slot = 0
        rd, wr = t % 2, (t + 1) % 2
        xg = xgp
        t_xg = p.op("gpsimd", (lambda h, t=t, xg=xg: h.dma_start(
            out=xg.rearrange("p (c r) -> p c r", c=NCH),
            in_=ag4[t][:, :, bass.ts(pid, R)])), [t_ag] + xg_consumed[slot], dma=True)
        xg_consumed[slot] = []
        new_ctoks, new_cbtoks = {}, {}
        for mc in range(NCH):
            gtoks = []
            for gi, (wn, un, bn, fn) in enumerate(GATES):
                bank = gi
                pairs = [(wbufs[wn][:, cc * D + mc * 128: cc * D + mc * 128 + 128],
                          xg[:, cc * R:(cc + 1) * R]) for cc in range(NCH)]
                pairs += [(wbufs[un][:, cc * D + mc * 128: cc * D + mc * 128 + 128],
                           cb(rd, cc)) for cc in range(NCH)]
                deps = [t_xg, tw[wn], tw[un], E.ps_war[bank]] + [cbtoks[cc] for cc in range(NCH)]
                tm = _mmgroup(E, E.ps[bank][:, 0:R], pairs, deps)
                xg_consumed[slot].append(tm)
                ta = p.op("scalar", (lambda h, gi=gi, bank=bank, bn=bn, fn=fn, mc=mc: h.activation(
                    gsc[:, gi * R:(gi + 1) * R], E.ps[bank][:, 0:R], fn,
                    bias=bias[bn][:, mc:mc + 1])), [tm, tb_[bn]])
                E.ps_war[bank] = ta
                gtoks.append(ta)
            tm1 = p.op("vector", (lambda h, rd=rd, mc=mc: h.tensor_tensor(
                t1s[:, 0:R], gsc[:, 0:R], cs(rd, mc), ALU.mult)), [gtoks[0], ctoks[mc]])
            tm2 = p.op("vector", lambda h: h.tensor_tensor(
                t1s[:, R:2 * R], gsc[:, R:2 * R], gsc[:, 2 * R:3 * R], ALU.mult),
                [gtoks[1], gtoks[2], tm1])
            tc = p.op("vector", (lambda h, wr=wr, mc=mc: h.tensor_tensor(
                cs(wr, mc), t1s[:, 0:R], t1s[:, R:2 * R], ALU.add)),
                [tm2, new_ctoks.get(("war", mc))])
            new_ctoks[mc] = tc
            new_cbtoks[mc] = p.op("vector", (lambda h, wr=wr, mc=mc: h.tensor_copy(
                cb(wr, mc), cs(wr, mc))), [tc])
        ctoks, cbtoks = new_ctoks, new_cbtoks

    fin = 8 % 2
    uo = wbufs["uf"]
    wo = wbufs["ui"]
    t_uo = load_weight_bf16(E, "f0_uo", uo, list(xg_consumed[0]) + list(xg_consumed[1]))
    t_wo = load_weight_bf16(E, "f0_wo", wo, list(xg_consumed[0]) + list(xg_consumed[1]))
    ztoks = []
    prev_ts = None
    for oc in range(NCH):
        bank = oc % 2
        sl = oc % 2
        pairs = [(uo[:, cc * D + oc * 128: cc * D + oc * 128 + 128], cb(fin, cc)) for cc in range(NCH)]
        tm = _mmgroup(E, E.ps[bank][:, 0:R], pairs,
                      [t_uo, E.ps_war[bank]] + list(cbtoks.values()))
        tv = p.op("vector", (lambda h, sl=sl, bank=bank: h.tensor_copy(
            t1s[:, sl * R:(sl + 1) * R], E.ps[bank][:, 0:R])), [tm] + ztoks[-2:])
        ts_ = p.op("sync", (lambda h, oc=oc, sl=sl: h.dma_start(
            out=zin.rearrange("(c p) r -> p c r", p=128)[:, oc], in_=t1s[:, sl * R:(sl + 1) * R])),
            [tv], dma=True)
        E.ps_war[bank] = tv
        ztoks.append(ts_)
    t_agz = p.collective(lambda h: h.collective_compute(
        "AllGather", ALU.bypass, replica_groups=[list(range(CORES))],
        ins=[zin[:, :]], outs=[zout[:, :]]), ztoks)

    allmm = list(xg_consumed[0]) + list(xg_consumed[1]) + ztoks + [t_agz]
    zbf = E.pool.at(wbase, NCH * T)
    zbf4 = zbf.rearrange("p (c r t) -> p c r t", c=NCH, r=CORES)
    zout4 = zout.rearrange("(r c p) t -> r p c t", p=128, c=NCH)
    t_zls = []
    for r in range(CORES):
        t_zls.append(p.op("gpsimd", (lambda h, r=r: h.dma_start(
            out=zbf4[:, :, r], in_=zout4[r])), allmm, dma=True))
    t_zl = t_zls[-1]
    xlb = E.pool.at(wbase + 2 * NCH * D, 2 * NCH * TBW)
    hsc = E.pool.at(cbase, 4 * TBW).bitcast(F32)
    xl_consumed = {0: [], 1: []}
    agin4 = agin.rearrange("(c p) t -> p c t", p=128)
    for tb in range(NTB):
        slot = tb % 2
        xl = xlb[:, slot * NCH * TBW:(slot + 1) * NCH * TBW]
        t_xl = p.op("gpsimd", (lambda h, tb=tb, xl=xl: h.dma_start(
            out=xl.rearrange("p (c t) -> p c t", c=NCH),
            in_=agin4[:, :, tb * TBW:(tb + 1) * TBW])), xl_consumed[slot] + allmm, dma=True)
        xl_consumed[slot] = []
        for ec in range(NCH):
            bank = 2 + ec % 2
            hslot = ec % 2
            pairs = [(wo[:, cc * D + ec * 128: cc * D + ec * 128 + 128],
                      xl[:, cc * TBW:(cc + 1) * TBW]) for cc in range(NCH)]
            tm = _mmgroup(E, E.ps[bank][:, :], pairs, [t_xl, t_wo, E.ps_war[bank]])
            xl_consumed[slot].append(tm)
            ta = p.op("vector", (lambda h, ec=ec, tb=tb, bank=bank, hslot=hslot: h.tensor_tensor(
                hsc[:, hslot * TBW:(hslot + 1) * TBW], E.ps[bank][:, :],
                zbf[:, ec * T + tb * TBW: ec * T + tb * TBW + TBW], ALU.add)),
                [tm] + t_zls + [E.hsc_war.get(hslot)])
            th = p.op("scalar", (lambda h, ec=ec, hslot=hslot: h.activation(
                hsc[:, hslot * TBW:(hslot + 1) * TBW],
                hsc[:, hslot * TBW:(hslot + 1) * TBW], AF.Tanh,
                bias=bias["bo"][:, ec:ec + 1])), [ta, tb_["bo"]])
            E.ps_war[bank] = th
            tx = p.op("vector", (lambda h, ec=ec, tb=tb, hslot=hslot: h.tensor_tensor(
                E.x[:, ec * T + tb * TBW: ec * T + tb * TBW + TBW],
                E.x[:, ec * T + tb * TBW: ec * T + tb * TBW + TBW],
                hsc[:, hslot * TBW:(hslot + 1) * TBW], ALU.add)),
                [th, E.xtok.get((ec, tb))])
            E.hsc_war[hslot] = tx
            E.xtok[(ec, tb)] = tx
    E.pool.pop()


def emit_attn_b(E):
    p = E.p
    nc = E.nc
    E.pool.push()
    t1b = E.pool.alloc_bf16(T)
    t2b = E.pool.alloc_bf16(T)
    t_ms1 = p.op("gpsimd", lambda h: h.memset(t1b[64:128, :], 0.0))
    t_ms2a = p.op("gpsimd", lambda h: h.memset(t2b[32:64, :], 0.0))
    t_ms2b = p.op("gpsimd", lambda h: h.memset(t2b[64:128, :], 0.0))
    t_ms2 = t_ms2b
    E.pool.push()
    E.xln = E.pool.alloc_bf16(NCH * T)
    emit_ln(E)
    if DEBUG:
        _od = nc.dram_tensor("dbg_xlnb", [128, NCH * T], BF16, kind="ExternalOutput").ap()
        p.op("sync", (lambda h, _od=_od: h.dma_start(out=_od, in_=E.xln)), E.lntoks, dma=True)
    wqa = E.pool.alloc_bf16(NCH * 64)
    wkva = E.pool.alloc_bf16(NCH * 32)
    t_wqa = load_weight_bf16(E, "b0_wqa", wqa, E.lntoks)
    t_wkva = load_weight_bf16(E, "b0_wkva", wkva, E.lntoks)
    t1toks, t2toks = [], []
    for (dst, wsrc, wtok, w_, toks) in ((t1b, wqa, t_wqa, 64, t1toks), (t2b, wkva, t_wkva, 32, t2toks)):
        for tb in range(NTB):
            i = tb % 2
            pairs = [(wsrc[:, c * w_:(c + 1) * w_],
                      E.xln[:, c * T + tb * TBW: c * T + tb * TBW + TBW]) for c in range(NCH)]
            tm = _mmgroup(E, E.ps[i][0:w_, :], pairs,
                          E.lntoks + [wtok, E.ps_war[i], t_ms1, t_ms2a, t_ms2b])
            tv = p.op("vector", (lambda h, dst=dst, tb=tb, i=i, w_=w_: h.tensor_copy(
                dst[0:w_, tb * TBW:(tb + 1) * TBW], E.ps[i][0:w_, :])), [tm])
            E.ps_war[i] = tv
            toks.append(tv)
    if DEBUG:
        E.dbg["t1b"] = (t1b, t1toks)
        E.dbg["t2b"] = (t2b, t2toks)
    E.pool.pop()

    kT = E.pool.alloc_bf16(2 * T)
    vtok = E.pool.alloc_bf16(16 * 256)
    qT = E.pool.alloc_bf16(NCH * T)
    SQF = E.pool.alloc_bf16(TBW)
    rp1 = E.pool.alloc_bf16(T)
    qsw = E.pool.alloc_bf16(T)
    ctab = E.pool.alloc_bf16(T)
    stab = E.pool.alloc_bf16(T)
    masks = E.pool.alloc_bf16(4 * TBW)
    wkvb = E.pool.alloc_bf16(512)
    wqb = E.pool.alloc_bf16(D)
    wout = E.pool.alloc_bf16(NCH * D)
    boutb = E.pool.alloc_bf16(D)
    obuf = E.pool.alloc_bf16(NCH * TBW)
    PT = E.pool.alloc_bf16(2 * TBW)
    rkb = E.pool.alloc_f32(T)
    rowq = E.pool.alloc_f32(TBW)
    kmax = E.pool.alloc_f32(16)
    RBb = E.pool.alloc_bf16(TBW)
    oadds = [E.pool.alloc_bf16(TBW) for _ in range(4)]
    oaddw = [None] * 4
    oai = [0]
    E.rbb_war = None
    E.sqf_war = None

    prj = t1toks + t2toks
    cin = declare_input(E, "b0_ropec")
    sin_ = declare_input(E, "b0_ropes")
    min_ = declare_input(E, "b0_masks")
    t_ct = p.op("gpsimd", lambda h: h.dma_start(out=ctab, in_=cin[:, :]), prj, dma=True)
    t_st = p.op("gpsimd", lambda h: h.dma_start(out=stab, in_=sin_[:, :]), prj, dma=True)
    t_mk = p.op("gpsimd", lambda h: h.dma_start(out=masks, in_=min_[:, :]), prj, dma=True)
    wkvb_in = declare_input(E, "b0_wkvb")
    t_wkvb = p.op("gpsimd", lambda h: h.dma_start(out=wkvb[0:32, :], in_=wkvb_in[:, :]), prj, dma=True)
    wqb_in = declare_input(E, "b0_wqb")
    t_wqb = p.op("gpsimd", lambda h: h.dma_start(out=wqb[0:64, :], in_=wqb_in[:, :]), prj, dma=True)
    t_wout = load_weight_bf16(E, "b0_wout", wout, prj)
    bout_in = declare_input(E, "b0_bout")
    t_bout = p.op("gpsimd", lambda h: h.dma_start(out=boutb[0:1, :], in_=bout_in[:, :]), prj, dma=True)

    ktoks, vtoks = [], []
    for mc in range(2):
        for tb in range(NTB):
            i = (mc * NTB + tb) % 2
            tm = _mmgroup(E, E.ps[i][:, :],
                          [(wkvb[0:32, mc * 128:(mc + 1) * 128],
                            t2b[0:32, tb * TBW:(tb + 1) * TBW])],
                          t2toks + [t_wkvb, E.ps_war[i], t_ms2a, t_ms2b])
            tv = p.op("vector", (lambda h, mc=mc, tb=tb, i=i: h.tensor_copy(
                kT[:, mc * T + tb * TBW: mc * T + tb * TBW + TBW], E.ps[i][:, :])), [tm])
            E.ps_war[i] = tv
            ktoks.append(tv)
    for tc in range(16):
        i = tc % 2
        tm = _mmgroup(E, E.ps[i][:, 0:256],
                      [(t2b[0:32, tc * 128:(tc + 1) * 128], wkvb[0:32, 256:512])],
                      t2toks + [t_wkvb, E.ps_war[i], t_ms2a, t_ms2b])
        tv = p.op("vector", (lambda h, tc=tc, i=i: h.tensor_copy(
            vtok[:, tc * 256:(tc + 1) * 256], E.ps[i][:, 0:256])), [tm])
        E.ps_war[i] = tv
        vtoks.append(tv)
    qtoks = []
    for ec in range(NCH):
        for tb in range(NTB):
            i = (ec * NTB + tb) % 2
            tm = _mmgroup(E, E.ps[i][:, :],
                          [(wqb[0:64, ec * 128:(ec + 1) * 128],
                            t1b[0:64, tb * TBW:(tb + 1) * TBW])],
                          t1toks + [t_wqb, E.ps_war[i], t_ms1])
            tv = p.op("vector", (lambda h, ec=ec, tb=tb, i=i: h.tensor_scalar_mul(
                qT[:, ec * T + tb * TBW: ec * T + tb * TBW + TBW], E.ps[i][:, :], 0.125)), [tm])
            E.ps_war[i] = tv
            qtoks.append(tv)

    def rope_chunk(buf, c, deps):
        base = buf[:, c * T:(c + 1) * T]
        sw = []
        for (dst, srow) in ((0, 32), (32, 0), (64, 96), (96, 64)):
            sw.append(p.op("sync", (lambda h, dst=dst, srow=srow, base=base: h.dma_start(
                out=qsw[dst:dst + 32, :], in_=base[srow:srow + 32, :])), deps + [E.qsw_war], dma=True))
        tr1 = p.op("vector", (lambda h, base=base: h.tensor_tensor(rp1, base, ctab, ALU.mult)),
                   deps + [t_ct, E.rp1_war])
        tr2 = p.op("vector", lambda h: h.tensor_tensor(qsw, qsw, stab, ALU.mult), sw + [t_st])
        tr3 = p.op("vector", (lambda h, base=base: h.tensor_tensor(base, rp1, qsw, ALU.add)),
                   [tr1, tr2])
        E.qsw_war = tr3
        E.rp1_war = tr3
        return tr3
    E.qsw_war = None
    E.rp1_war = None
    rtoks = []
    for c in range(NCH):
        rtoks.append(rope_chunk(qT, c, qtoks))
    krtoks = []
    for c in range(2):
        krtoks.append(rope_chunk(kT, c, ktoks))
    if DEBUG:
        E.dbg["qT"] = (qT, rtoks)
        E.dbg["kT"] = (kT, krtoks)
        E.dbg["vtok"] = (vtok, vtoks)
    if PHASE <= 1:
        E.pool.pop()
        return

    sqk = []
    for c in range(2):
        sqk.append(p.op("scalar", (lambda h, c=c: h.activation(
            rp1 if c == 0 else qsw, kT[:, c * T:(c + 1) * T], AF.Square)),
            [krtoks[c], E.rp1_war]))
    kmtoks = []
    for g in range(4):
        b = (g % 2) * 64
        src_ = rp1 if g // 2 == 0 else qsw
        for tb in range(NTB):
            tm = _mmgroup(E, E.ps[5][0:1, :],
                          [(E.ones_bf[b:b + 64, 0:1], src_[b:b + 64, tb * TBW:(tb + 1) * TBW])],
                          [sqk[g // 2], E.ps_war[5]], tile_position=(b, 0), start_clear=True)
            tv = p.op("vector", (lambda h, tb=tb: h.tensor_copy(
                rkb[0:1, tb * TBW:(tb + 1) * TBW], E.ps[5][0:1, :])), [tm])
            E.ps_war[5] = tv
            kmtoks.append(tv)
        t_m = p.op("vector", (lambda h, g=g: h.reduce_max(
            kmax[0:1, g:g + 1], rkb[0:1, :], axis=AX.X)), kmtoks[-4:])
        kmtoks.append(p.op("scalar", (lambda h, g=g: h.activation(
            kmax[0:1, 4 + g:5 + g], kmax[0:1, g:g + 1], AF.Sqrt)), [t_m]))
        kmtoks.append(p.op("scalar", (lambda h, g=g: h.mul(
            kmax[0:1, 8 + g:9 + g], kmax[0:1, 4 + g:5 + g], -1.0)), [kmtoks[-1]]))

    if DEBUG:
        E.dbg["kmax"] = (kmax, kmtoks[-2:])
    if PHASE <= 2:
        E.pool.pop()
        return
    for qb in range(NTB):
        oproj_deps = []
        for hh in range(16):
            ch = (hh % 4) + 4 * (hh // 8)
            half = (hh // 4) % 2
            b = half * 64
            g = hh // 4
            tsq = p.op("scalar", (lambda h, ch=ch, qb=qb, b=b: h.activation(
                SQF[b:b + 64, :], qT[:, ch * T + qb * TBW: ch * T + qb * TBW + TBW][b:b + 64, :],
                AF.Square)), [rtoks[ch], E.sqf_war])
            tqm = _mmgroup(E, E.ps[5][0:1, :],
                           [(E.ones_bf[b:b + 64, 0:1], SQF[b:b + 64, :])],
                           [tsq, E.ps_war[5]], tile_position=(b, 0))
            E.sqf_war = tqm
            tqr = p.op("scalar", lambda h: h.activation(rowq[0:1, :], E.ps[5][0:1, :], AF.Sqrt), [tqm])
            E.ps_war[5] = tqr
            tnb = p.op("vector", (lambda h, g=g: h.tensor_scalar(
                PT[0:1, TBW:TBW + TBW], rowq[0:1, :], kmax[0:1, 8 + g:9 + g], None, ALU.mult)),
                [tqr, kmtoks[-1], E.nb_war])
            nbrow = PT[0:1, TBW:2 * TBW]
            nkc = 4 * qb + 4
            omms = []
            for kc in range(nkc):
                i = kc % 2
                pairs = [(kT[:, g * T // 2 + kc * 128: g * T // 2 + kc * 128 + 128][b:b + 64, :]
                          if False else
                          kT[:, (g // 2) * T + kc * 128:(g // 2) * T + kc * 128 + 128][b:b + 64, :],
                          qT[:, ch * T + qb * TBW: ch * T + qb * TBW + TBW][b:b + 64, :])]
                tm = _mmgroup(E, E.ps[i][:, :], pairs, [rtoks[ch], krtoks[g // 2], tnb, E.ps_war[i]],
                              extra=[(E.ones_bf[0:1, :], nbrow)], tile_position=(b, 0))
                if kc >= 4 * qb:
                    r = kc - 4 * qb
                    tm = p.op("vector", (lambda h, i=i, r=r: h.tensor_tensor(
                        E.ps[i][:, :], E.ps[i][:, :], masks[:, r * TBW:(r + 1) * TBW], ALU.add)),
                        [tm, t_mk])
                te = p.op("scalar", (lambda h, i=i, kc=kc: h.activation(
                    PT[:, (kc % 2) * TBW:(kc % 2) * TBW + TBW] if False else PT[:, 0:TBW],
                    E.ps[i][:, :], AF.Exp)), [tm] + omms)
                E.ps_war[i] = te
                tden = _mmgroup(E, E.ps[6][0:1, :], [(E.ones_bf[:, 0:1], PT[:, 0:TBW])],
                                [te, E.ps_war[6]], start_clear=(kc == 0), stop_group=(kc == nkc - 1))
                tO = _mmgroup(E, E.ps[2][0:64, :],
                              [(vtok[:, kc * 256 + g * 64: kc * 256 + g * 64 + 64], PT[:, 0:TBW])],
                              [te, E.ps_war[2]], start_clear=(kc == 0), stop_group=(kc == nkc - 1))
                omms = [tO, tden]
            E.ps_war[6] = omms[1]
            t_e = p.op("vector", lambda h: h.tensor_copy(E.row_t, E.ps[6][0:1, :]),
                       [omms[1], E.row_war])
            E.ps_war[6] = t_e
            t_rc = p.op("vector", lambda h: h.reciprocal(E.row_m, E.row_t), [t_e])
            t_rb = _mmgroup(E, E.ps[4][0:64, :], [(E.ones_f32[0:1, 0:64], E.row_m)],
                            [t_rc, E.ps_war[4]])
            t_rbs = p.op("vector", lambda h: h.tensor_copy(RBb[0:64, :], E.ps[4][0:64, :]),
                         [t_rb, E.rbb_war])
            t_o = p.op("vector", (lambda h, ch=ch, b=b: h.tensor_tensor(
                obuf[:, ch * TBW:(ch + 1) * TBW][b:b + 64, :],
                E.ps[2][0:64, :], RBb[0:64, :], ALU.mult)), [t_rbs, omms[0], E.obuf_war])
            E.rbb_war = t_o
            E.ps_war[4] = t_o
            E.ps_war[2] = t_o
            E.row_war = t_o
            E.nb_war = t_o
            oproj_deps.append(t_o)
        if PHASE in (21, 22, 23):
            break
        for ec in range(NCH):
            bank = 3 + ec % 2
            pairs = [(wout[:, cc * D + ec * 128: cc * D + ec * 128 + 128],
                      obuf[:, cc * TBW:(cc + 1) * TBW]) for cc in range(NCH)]
            tm = _mmgroup(E, E.ps[bank][:, :], pairs,
                          oproj_deps + [t_wout, t_bout, E.const_tok[-1], E.ps_war[bank]],
                          extra=[(boutb[0:1, ec * 128:(ec + 1) * 128], E.onesr[0:1, :])])
            osl = oai[0] % 4
            oai[0] += 1
            oadd = oadds[osl]
            tcp = p.op("vector", (lambda h, bank=bank, oadd=oadd: h.tensor_copy(
                oadd, E.ps[bank][:, :])), [tm, oaddw[osl]])
            tx = p.op("vector", (lambda h, ec=ec, qb=qb, oadd=oadd: h.tensor_tensor(
                E.x[:, ec * T + qb * TBW: ec * T + qb * TBW + TBW],
                E.x[:, ec * T + qb * TBW: ec * T + qb * TBW + TBW],
                oadd, ALU.add)), [tcp, E.xtok.get((ec, qb))])
            oaddw[osl] = tx
            E.ps_war[bank] = tcp
            E.xtok[(ec, qb)] = tx
            E.obuf_war = tx
    E.pool.pop()


def emit_moe(E, s):
    p = E.p
    nc = E.nc
    xld = nc.dram_tensor(f"moe_xln{s}", [D, T], BF16).ap()
    E.pool.push()
    mw8 = E.pool.alloc_bf16(T)
    E.pool.push()
    E.xln = E.pool.alloc_bf16(NCH * T)
    emit_ln(E)
    wgb = E.pool.alloc_bf16(NCH * 32)
    lg = E.pool.alloc_bf16(T)
    ltok = E.pool.alloc_bf16(16 * 32)
    msc = E.pool.alloc_f32(4 * 16)
    prb = E.pool.alloc_f32(16 * 32)
    prbb = E.pool.alloc_bf16(16 * 32)
    bgr = E.pool.alloc_bf16(32)
    t_wg = load_weight_bf16(E, "m1_wg", wgb, E.lntoks)
    bg_in = declare_input(E, "m1_bg")
    t_bg = p.op("gpsimd", lambda h: h.dma_start(out=bgr[0:1, 0:32], in_=bg_in[:, :]), E.lntoks, dma=True)
    lgt = []
    for tb in range(NTB):
        i = tb % 2
        pairs = [(wgb[:, c * 32:(c + 1) * 32],
                  E.xln[:, c * T + tb * TBW: c * T + tb * TBW + TBW]) for c in range(NCH)]
        tm = _mmgroup(E, E.ps[i][0:32, :], pairs, E.lntoks + [t_wg, t_bg, E.ps_war[i]],
                      extra=[(bgr[0:1, 0:32], E.onesr[0:1, :])])
        tv = p.op("vector", (lambda h, tb=tb, i=i: h.tensor_copy(
            lg[0:32, tb * TBW:(tb + 1) * TBW], E.ps[i][0:32, :])), [tm])
        E.ps_war[i] = tv
        lgt.append(tv)
    if PHASE == 411:
        E.pool.pop()
        E.pool.pop()
        return
    def all_tr(h):
        last = None
        for tc in range(16):
            last = h.transpose(E.psb[:, tc * 32:(tc + 1) * 32],
                               lg[0:32, tc * 128:(tc + 1) * 128], E.ident[0:32, 0:32])
        return last
    ttr = p.op("tensor", all_tr, lgt + [E.const_tok[1], E.psb_war])
    tte = p.op("vector", lambda h: h.tensor_copy(ltok[:, 0:512], E.psb[:, 0:512]), [ttr])
    ttk = [tte] * 16
    E.psb_war = tte
    if PHASE == 412:
        E.pool.pop()
        E.pool.pop()
        return
    mw_toks = []
    for tc in range(16):
        sl = ltok[:, tc * 32:(tc + 1) * 32]
        po = prb[:, tc * 32:(tc + 1) * 32]
        pb = prbb[:, tc * 32:(tc + 1) * 32]
        m1_ = msc[:, tc:tc + 1]
        t0 = p.op("vector", (lambda h, sl=sl, m1_=m1_: h.reduce_max(m1_, sl, axis=AX.X, negate=True)),
                  [ttk[tc], E.msc_war])
        t1 = p.op("scalar", (lambda h, sl=sl, po=po, m1_=m1_: h.activation(
            po, sl, AF.Exp, bias=m1_)), [t0])
        t2 = p.op("vector", (lambda h, po=po, tc=tc: h.reduce_sum(
            msc[:, 16 + tc:17 + tc], po, axis=AX.X)), [t1])
        t3 = p.op("vector", (lambda h, tc=tc: h.reciprocal(
            msc[:, 32 + tc:33 + tc], msc[:, 16 + tc:17 + tc])), [t2])
        t6 = p.op("vector", (lambda h, po=po, m1_=m1_: h.reduce_max(m1_, po, axis=AX.X)), [t1, t0])
        t7 = p.op("scalar", (lambda h, tc=tc, m1_=m1_: h.copy(msc[:, 48 + tc:49 + tc], m1_)), [t6])
        t8 = p.op("vector", (lambda h, po=po, tc=tc: h.tensor_scalar(
            po, po, msc[:, 48 + tc:49 + tc], None, ALU.is_ge)), [t7, t3])
        t9 = p.op("scalar", (lambda h, tc=tc: h.mul(msc[:, tc:tc + 1],
                                                    msc[:, 32 + tc:33 + tc], 1.0)), [t3, t6, t7])
        ta_ = p.op("vector", (lambda h, tc=tc: h.tensor_tensor(
            msc[:, 16 + tc:17 + tc], msc[:, tc:tc + 1], msc[:, 48 + tc:49 + tc], ALU.mult)),
            [t9, t7])
        tb2 = p.op("scalar", (lambda h, tc=tc: h.copy(msc[:, 32 + tc:33 + tc],
                                                      msc[:, 16 + tc:17 + tc])), [ta_])
        tc_ = p.op("vector", (lambda h, po=po, pb=pb, tc=tc: h.tensor_scalar(
            pb, po, msc[:, 32 + tc:33 + tc], None, ALU.mult)), [tb2, t8])
        mw_toks.append(tc_)
    E.msc_war = mw_toks[-1]
    if PHASE == 413:
        E.pool.pop()
        E.pool.pop()
        return
    mw8t = []
    for rc in range(2):
        def all_tr2(h, rc=rc):
            last = None
            for j in range(8):
                tc = rc * 8 + j
                last = h.transpose(E.psb[0:32, j * 128:(j + 1) * 128],
                                   prbb[:, tc * 32:(tc + 1) * 32], E.ident)
            return last
        ttr = p.op("tensor", all_tr2, mw_toks + [E.const_tok[1], E.psb_war])
        tv = p.op("vector", (lambda h, rc=rc: h.tensor_copy(
            mw8[0:32, rc * 1024:(rc + 1) * 1024], E.psb[0:32, 0:1024])), [ttr])
        E.psb_war = tv
        mw8t.append(tv)
    t_xd = p.op("sync", (lambda h, x_=E.xln: h.dma_start(
        out=xld.rearrange("(c p) t -> p c t", p=128),
        in_=x_.rearrange("p (c t) -> p c t", c=NCH))), E.lntoks, dma=True)
    if DEBUG:
        E.dbg["mw8"] = (mw8[0:32, :], mw8t)
    E.pool.pop()
    if PHASE == 41:
        E.pool.pop()
        return

    xlr = E.pool.alloc_bf16(NCH * TBW)
    hbuf = E.pool.alloc_bf16(32 * TBW)
    w1b = E.pool.alloc_bf16(2 * NCH * 2048)
    w2b = E.pool.alloc_bf16(2 * NCH * 256)
    mwbs = [E.pool.alloc_bf16(TBW) for _ in range(2)]
    ot1s = [E.pool.alloc_bf16(TBW) for _ in range(4)]
    mwbw = [[E.mwb_war], [E.mwb_war]]
    ot1w = [E.ot1_war] * 4
    oix = [0]
    mix_ = [0]
    selb = E.pool.alloc_bf16(1024)
    b1b = E.pool.alloc_f32(8 * 32)
    b2b = E.pool.alloc_bf16(1024)
    sel_in = declare_input(E, "m1_sel")
    b1_in = declare_input(E, "m1_b1")
    b2_in = declare_input(E, "m1_b2")
    gdeps = mw8t + [t_xd]
    t_sel = p.op("gpsimd", lambda h: h.dma_start(out=selb[0:8, :], in_=sel_in[:, :]), gdeps, dma=True)
    t_b1 = p.op("gpsimd", lambda h: h.dma_start(
        out=b1b.rearrange("p (e c) -> p e c", e=8), in_=b1_in.rearrange("e (p c) -> p e c", p=128)),
        gdeps, dma=True)
    t_b2 = p.op("gpsimd", lambda h: h.dma_start(out=b2b[0:8, :], in_=b2_in[:, :]), gdeps, dma=True)
    w1_in = declare_input(E, "m1_w1")
    w2_in = declare_input(E, "m1_w2")
    w1_4 = w1_in.rearrange("e (c p) m -> e p c m", p=128)
    w2_4 = w2_in.rearrange("e (c p) m -> e p c m", p=128)
    xld4 = xld.rearrange("(c p) t -> p c t", p=128)

    w1tok = {0: gdeps, 1: gdeps}
    w2tok = {0: gdeps, 1: gdeps}
    xltok = {0: gdeps, 1: gdeps}
    for e in range(8):
        t_w1 = {}
        for hf in range(2):
            t_w1[hf] = p.op("gpsimd", (lambda h, e=e, hf=hf: h.dma_start(
                out=w1b[:, hf * NCH * 2048:(hf + 1) * NCH * 2048].rearrange("p (c m) -> p c m", c=NCH),
                in_=w1_4[e][:, :, hf * 2048:(hf + 1) * 2048])), w1tok[hf], dma=True)
        w1reads = {0: [], 1: []}
        for tb in range(NTB):
            slot = 0
            xl = xlr
            t_xl = p.op("sync", (lambda h, tb=tb, xl=xl: h.dma_start(
                out=xl.rearrange("p (c t) -> p c t", c=NCH),
                in_=xld4[:, :, tb * TBW:(tb + 1) * TBW])), xltok[slot] + [t_xd], dma=True)
            xltok[slot] = []
            htoks = []
            for hc in range(32):
                hf = hc // 16
                i = hc % 2
                wsl = w1b[:, hf * NCH * 2048:(hf + 1) * NCH * 2048]
                hc_ = hc % 16
                pairs = [(wsl[:, c * 2048 + hc_ * 128: c * 2048 + hc_ * 128 + 128],
                          xl[:, c * TBW:(c + 1) * TBW]) for c in range(NCH)]
                tm = _mmgroup(E, E.ps[i][:, :], pairs, [t_xl, t_w1[hf], t_b1, E.ps_war[i]])
                xltok[slot].append(tm)
                w1reads[hf].append(tm)
                th = p.op("scalar", (lambda h, hc=hc, i=i, e=e: h.activation(
                    hbuf[:, hc * TBW:(hc + 1) * TBW], E.ps[i][:, :], AF.Silu,
                    bias=b1b[:, e * 32 + hc: e * 32 + hc + 1])), [tm])
                E.ps_war[i] = th
                htoks.append(th)
            mb = _mmgroup(E, E.ps[6][:, :], [(selb[0:8, e * 128:(e + 1) * 128],
                                              mw8[0:8, tb * TBW:(tb + 1) * TBW])],
                          [t_sel, E.ps_war[6]] + mw8t)
            ms = mix_[0] % 2
            mix_[0] += 1
            mwb = mwbs[ms]
            mvb = p.op("vector", (lambda h, mwb=mwb: h.tensor_copy(mwb, E.ps[6][:, :])),
                       [mb] + mwbw[ms])
            mwbw[ms] = []
            E.ps_war[6] = mvb
            for grp in range(4):
                otoks = []
                for Kq in range(4):
                    wslot = (grp * 4 + Kq) % 2
                    t_w2 = p.op("gpsimd", (lambda h, e=e, grp=grp, Kq=Kq, wslot=wslot: h.dma_start(
                        out=w2b[:, wslot * NCH * 256:(wslot + 1) * NCH * 256].rearrange(
                            "p (c m) -> p c m", c=NCH),
                        in_=w2_4[e][:, Kq * 8:(Kq + 1) * 8, grp * 256:(grp + 1) * 256])),
                        w2tok[wslot], dma=True)
                    w2tok[wslot] = []
                    for dh in range(2):
                        dc = grp * 2 + dh
                        bank = 2 + dh
                        wsl = w2b[:, wslot * NCH * 256:(wslot + 1) * NCH * 256]
                        pairs = [(wsl[:, kc * 256 + dh * 128: kc * 256 + dh * 128 + 128],
                                  hbuf[:, (Kq * 8 + kc) * TBW:(Kq * 8 + kc + 1) * TBW])
                                 for kc in range(8)]
                        tm = _mmgroup(E, E.ps[bank][:, :], pairs,
                                      htoks + [t_w2, E.ps_war[bank]], start_clear=(Kq == 0),
                                      stop_group=(Kq == 3))
                        w2tok[wslot].append(tm)
                        otoks.append(tm)
                for dh in range(2):
                    dc = grp * 2 + dh
                    bank = 2 + dh
                    osl = oix[0] % 4
                    oix[0] += 1
                    ot1 = ot1s[osl]
                    t1 = p.op("vector", (lambda h, bank=bank, ot1=ot1, mwb=mwb: h.tensor_tensor(
                        ot1, E.ps[bank][:, :], mwb, ALU.mult)),
                        [otoks[-2 + dh], mvb, ot1w[osl]])
                    mwbw[ms].append(t1)
                    tx = p.op("vector", (lambda h, dc=dc, tb=tb, ot1=ot1: h.tensor_tensor(
                        E.x[:, dc * T + tb * TBW: dc * T + tb * TBW + TBW],
                        E.x[:, dc * T + tb * TBW: dc * T + tb * TBW + TBW], ot1, ALU.add)),
                        [t1, E.xtok.get((dc, tb))])
                    ot1w[osl] = tx
                    E.ps_war[bank] = tx
                    E.xtok[(dc, tb)] = tx
        w1tok = {0: w1reads[0], 1: w1reads[1]}
        if PHASE == 42:
            break
    for tb in range(NTB):
        for dc in range(NCH):
            bank = 2 + dc % 2
            tm = _mmgroup(E, E.ps[bank][:, :],
                          [(b2b[0:8, dc * 128:(dc + 1) * 128], mw8[0:8, tb * TBW:(tb + 1) * TBW])],
                          [t_b2, E.ps_war[bank]] + mw8t)
            osl = oix[0] % 4
            oix[0] += 1
            ot1 = ot1s[osl]
            t1 = p.op("vector", (lambda h, bank=bank, ot1=ot1: h.tensor_copy(ot1, E.ps[bank][:, :])),
                      [tm, ot1w[osl]])
            tx = p.op("vector", (lambda h, dc=dc, tb=tb, ot1=ot1: h.tensor_tensor(
                E.x[:, dc * T + tb * TBW: dc * T + tb * TBW + TBW],
                E.x[:, dc * T + tb * TBW: dc * T + tb * TBW + TBW], ot1, ALU.add)),
                [t1, E.xtok.get((dc, tb))])
            ot1w[osl] = tx
            E.ot1_war = tx
            E.ps_war[bank] = tx
            E.xtok[(dc, tb)] = tx
    E.pool.pop()


def emit_mamba(E):
    p = E.p
    nc = E.nc
    NH, PP, NN = 32, 64, 64
    DIN = 2048
    gated = nc.dram_tensor("mb_gate", [DIN, T], BF16).ap()
    ssmd = nc.dram_tensor("mb_ssm", [DIN, T], BF16).ap()
    htd = nc.dram_tensor("mb_htok", [T, DIN], BF16).ap()
    ytd = nc.dram_tensor("mb_yfm", [DIN, T], BF16).ap()
    ymd = nc.dram_tensor("mb_ymix", [DIN, T], BF16).ap()

    E.pool.push()
    bcT = E.pool.alloc_bf16(T)
    dtb = E.pool.alloc_bf16(T)
    lcb = E.pool.alloc_bf16(T)
    mlcb = E.pool.alloc_bf16(T)
    elamb = E.pool.alloc_bf16(16 * 32)
    lamr = E.pool.alloc_f32(16 * 32)
    lamb16 = E.pool.alloc_bf16(16 * 32)
    E.pool.push()
    xln1 = E.pool.alloc_bf16(NCH * T)
    E.xln = xln1
    emit_ln(E)
    ln1toks = list(E.lntoks)
    xnd = nc.dram_tensor("mb_ln1", [D, T], BF16).ap()
    t_x1 = p.op("sync", (lambda h, x_=xln1: h.dma_start(
        out=xnd.rearrange("(c p) t -> p c t", p=128),
        in_=x_.rearrange("p (c t) -> p c t", c=NCH))), ln1toks, dma=True)
    xn = E.pool.alloc_bf16(NCH * T)
    E.xln = xn
    emit_ln(E, xsrc=xln1, xsrc_tok=ln1toks[-1], out=xn, rms=True)
    dtrF = E.pool.alloc_f32(T)
    wrng = E.pool.alloc_bf16(2 * NCH * TBW)
    stg = E.pool.alloc_bf16(TBW)
    win_in = declare_input(E, "s1_win")
    win4 = win_in.rearrange("(c p) m -> p c m", p=128)
    wtok = {0: E.lntoks, 1: E.lntoks}
    bct = []
    dtt = []
    stg_war = None
    for blk in range(9):
        ncols = 512 if blk < 8 else 256
        slot = blk % 2
        wsl = wrng[:, slot * NCH * TBW: slot * NCH * TBW + NCH * ncols]
        t_w = p.op("gpsimd", (lambda h, blk=blk, wsl=wsl, ncols=ncols: h.dma_start(
            out=wsl.rearrange("p (c m) -> p c m", c=NCH),
            in_=win4[:, :, blk * 512: blk * 512 + ncols])), wtok[slot], dma=True)
        wtok[slot] = []
        for j in range(ncols // 128):
            oc = blk * 4 + j
            for tb in range(NTB):
                i = tb % 2
                pairs = [(wsl[:, c * ncols + j * 128: c * ncols + j * 128 + 128],
                          xn[:, c * T + tb * TBW: c * T + tb * TBW + TBW]) for c in range(NCH)]
                tm = _mmgroup(E, E.ps[i][:, :], pairs, E.lntoks + [t_w, E.ps_war[i]])
                wtok[slot].append(tm)
                if oc == 0:
                    tv = p.op("vector", (lambda h, tb=tb, i=i: h.tensor_copy(
                        bcT[:, tb * TBW:(tb + 1) * TBW], E.ps[i][:, :])), [tm])
                    bct.append(tv)
                elif oc == 1:
                    tv = p.op("vector", (lambda h, tb=tb, i=i: h.tensor_copy(
                        dtrF[0:32, tb * TBW:(tb + 1) * TBW], E.ps[i][0:32, :])), [tm])
                    dtt.append(tv)
                else:
                    dst = gated if oc < 18 else ssmd
                    fc = oc - 2 if oc < 18 else oc - 18
                    tv = p.op("vector", (lambda h, i=i: h.tensor_copy(stg, E.ps[i][:, :])),
                              [tm, stg_war])
                    ts_ = p.op("sync", (lambda h, dst=dst, fc=fc, tb=tb: h.dma_start(
                        out=dst.rearrange("(c p) t -> p c t", p=128)[:, fc, tb * TBW:(tb + 1) * TBW],
                        in_=stg)), [tv], dma=True)
                    stg_war = ts_
                E.ps_war[i] = tv
    sc32 = E.pool.alloc_f32(T)
    sc32b = E.pool.alloc_f32(T)
    dtb_in = declare_input(E, "s1_dtb")
    al_in = declare_input(E, "s1_alog")
    cols = E.pool.alloc_f32(8)
    t_c = p.op("gpsimd", lambda h: h.dma_start(out=cols[0:32, 0:1], in_=dtb_in[:, :]), dma=True)
    t_a = p.op("gpsimd", lambda h: h.dma_start(out=cols[0:32, 1:2], in_=al_in[:, :]), dma=True)
    dtv = dtrF[0:32, :]
    s_a = sc32[0:32, :]
    s_b = sc32b[0:32, :]
    t1 = p.op("vector", lambda h: h.tensor_scalar(dtv, dtv, cols[0:32, 0:1], None, ALU.add),
              dtt + [t_c])
    t2 = p.op("scalar", lambda h: h.activation(s_a, dtv, AF.Abs), [t1])
    t3 = p.op("scalar", lambda h: h.activation(s_b, s_a, AF.Exp, scale=-1.0), [t2])
    t4 = p.op("vector", lambda h: h.tensor_scalar_add(s_b, s_b, 1.0), [t3])
    t5 = p.op("scalar", lambda h: h.activation(s_b, s_b, AF.Ln), [t4])
    t6 = p.op("vector", lambda h: h.tensor_relu(s_a, dtv), [t2, t5])
    t7 = p.op("vector", lambda h: h.tensor_tensor(dtv, s_a, s_b, ALU.add), [t6, t5])
    t15 = p.op("vector", lambda h: h.tensor_copy(dtb[0:32, :], dtv), [t7])
    t9 = p.op("scalar", lambda h: h.activation(cols[0:32, 2:3], cols[0:32, 1:2], AF.Exp), [t_a])
    t10 = p.op("scalar", lambda h: h.mul(cols[0:32, 3:4], cols[0:32, 2:3], -1.0), [t9])
    t11 = p.op("vector", lambda h: h.tensor_scalar(s_a, dtv, cols[0:32, 3:4], None, ALU.mult),
               [t10, t6, t15])
    lct = []
    lam_t = []
    for c_ in range(16):
        lct.append(p.op("vector", (lambda h, c_=c_: h.tensor_tensor_scan(
            s_b[:, c_ * 128:(c_ + 1) * 128], E.ones_f32[0:32, :],
            s_a[:, c_ * 128:(c_ + 1) * 128], 0.0, ALU.mult, ALU.add)), [t11, E.const_tok[2], t5]))
        lam_t.append(p.op("vector", (lambda h, c_=c_: h.reduce_sum(
            dtrF[32:64, c_:c_ + 1], s_a[:, c_ * 128:(c_ + 1) * 128], axis=AX.X)), [t11]))
    t13 = p.op("vector", lambda h: h.tensor_copy(lcb[0:32, :], s_b[0:32, :]), lct)
    t14 = p.op("vector", lambda h: h.tensor_scalar_mul(mlcb[0:32, :], s_b[0:32, :], -1.0), lct)
    tds = []
    for c_ in range(16):
        tds.append(p.op("sync", (lambda h, c_=c_: h.dma_start(
            out=lamr[0:1, c_ * 32:(c_ + 1) * 32], in_=dtrF[32:64, c_:c_ + 1])), lam_t, dma=True))
    te_ = p.op("scalar", lambda h: h.activation(elamb[0:1, 0:512], lamr[0:1, 0:512], AF.Exp), tds)
    tlb16 = p.op("vector", lambda h: h.tensor_copy(lamb16[0:1, 0:512], lamr[0:1, 0:512]), tds)
    E.mbtok = dict(rows=[t13, t14, t15, te_], proj=list(wtok[0]) + list(wtok[1]) + [stg_war],
                   bct=bct, xnd=t_x1, lamb16=tlb16)
    E.mbtok["xnd_ap"] = xnd
    E.mb = (bcT, dtb, lcb, mlcb, elamb, lamb16)
    E.pool.pop()

    barrier(E)
    E.pool.push()
    cwcol = E.pool.alloc_f32(64)
    cbcol = E.pool.alloc_f32(16)
    cw_in = declare_input(E, "s1_cw")
    cb_in = declare_input(E, "s1_cb")
    prj = E.mbtok["proj"]
    t_cw = p.op("gpsimd", lambda h: h.dma_start(out=cwcol, in_=cw_in), prj, dma=True)
    t_cb = p.op("gpsimd", lambda h: h.dma_start(out=cbcol, in_=cb_in), prj, dma=True)
    spad = E.pool.alloc_bf16(T + 16)
    diag = E.pool.alloc_bf16(4 * 128)
    hTc = E.pool.alloc_bf16(T)
    hstg = E.pool.alloc_bf16(1024)
    sp_war, dg_war, hs_war = None, None, None
    ssm4 = ssmd.rearrange("(c p) t -> p c t", p=128)
    ht3 = htd.rearrange("(tc p) f -> p tc f", p=128)
    ht_done = []
    for fc in range(16):
        t_z = p.op("gpsimd", (lambda h: h.memset(spad[:, 0:16], 0.0)), [sp_war])
        t_l = p.op("sync", (lambda h, fc=fc: h.dma_start(
            out=spad[:, 13:13 + T], in_=ssm4[:, fc])), [t_z, sp_war] + prj, dma=True)
        dts = []
        for tap in range(4):
            dts.append(p.op("vector", (lambda h, fc=fc, tap=tap: h.tensor_scalar(
                diag[:, tap * 128:(tap + 1) * 128], E.ident,
                cwcol[:, fc * 4 + tap: fc * 4 + tap + 1], None, ALU.mult)),
                [t_cw, E.const_tok[1], dg_war]))
        httoks = []
        tm = None
        for tb in range(NTB):
            i = tb % 2
            pairs = [(diag[:, tap * 128:(tap + 1) * 128],
                      spad[:, 10 + tap + tb * TBW: 10 + tap + tb * TBW + TBW]) for tap in range(4)]
            tm = _mmgroup(E, E.ps[i][:, :], pairs, dts + [t_l, E.ps_war[i], t_cb])
            th = p.op("scalar", (lambda h, tb=tb, i=i, fc=fc: h.activation(
                hTc[:, tb * TBW:(tb + 1) * TBW], E.ps[i][:, :], AF.Silu,
                bias=cbcol[:, fc:fc + 1])), [tm, hs_war])
            E.ps_war[i] = th
            httoks.append(th)
        sp_war = tm
        dg_war = tm
        for rnd in range(2):
            def trs(h, rnd=rnd):
                last = None
                for j in range(8):
                    tc = rnd * 8 + j
                    last = h.transpose(E.psb[:, j * 128:(j + 1) * 128],
                                       hTc[:, tc * 128:(tc + 1) * 128], E.ident)
                return last
            ttr = p.op("tensor", trs, httoks + [E.psb_war, E.const_tok[1]])
            tev = p.op("vector", (lambda h: h.tensor_copy(hstg, E.psb[:, 0:1024])), [ttr, hs_war])
            E.psb_war = tev
            tdm = p.op("sync", (lambda h, fc=fc, rnd=rnd: h.dma_start(
                out=ht3[:, rnd * 8:(rnd + 1) * 8, fc * 128:(fc + 1) * 128],
                in_=hstg.rearrange("p (tc f) -> p tc f", tc=8))), [tev], dma=True)
            hs_war = tdm
            ht_done.append(tdm)
    E.pool.pop()

    barrier(E)
    E.pool.push()
    (bcT, dtb, lcb, mlcb, elamb, lamb16) = E.mb
    btok = E.pool.alloc_bf16(16 * 64)
    xhb = E.pool.alloc_bf16(2 * DIN)
    Gall = E.pool.alloc_bf16(2 * DIN)
    Gsc = E.pool.alloc_f32(DIN)
    lrow = E.pool.alloc_bf16(4096)
    mrow = E.pool.alloc_bf16(4096)
    dtk = E.pool.alloc_bf16(32)
    ltk = E.pool.alloc_f32(32)
    exl = E.pool.alloc_f32(32)
    e2s = E.pool.alloc_f32(64)
    ftk = E.pool.alloc_bf16(32)
    xht = E.pool.alloc_bf16(DIN)
    scat = E.pool.alloc_f32(DIN)
    scbf = E.pool.alloc_bf16(DIN)
    selam = E.pool.alloc_bf16(DIN)
    b2h = E.pool.alloc_bf16(DIN)
    yint = E.pool.alloc_bf16(DIN)
    ytok = E.pool.alloc_bf16(DIN)
    sstmp = E.pool.alloc_f32(TBW)
    ystg = E.pool.alloc_bf16(1024)
    umask = E.pool.alloc_bf16(128)
    rowst = E.mbtok["rows"]
    bct = E.mbtok["bct"]
    um_in = declare_input(E, "c_mst")
    t_um = p.op("gpsimd", lambda h: h.dma_start(out=umask, in_=um_in[:, 0:128]), prj, dma=True)
    bt_war = None
    btt = []
    bt3 = btok.rearrange("p (tc n) -> p tc n", tc=16)
    for j in range(16):
        tb_j = p.op("sync", (lambda h, j=j: h.dma_start_transpose(
            bt3[:, j], bcT[0:64, j * 128:(j + 1) * 128])), bct + [bt_war], dma=True)
        bt_war = tb_j
        btt.append(tb_j)
    cb0 = E.pool.alloc_bf16(T)
    cbm = E.pool.alloc_bf16(128)
    t_cb0 = p.op("sync", (lambda h: h.dma_start(out=cb0[0:64, :], in_=bcT[64:128, :])),
                 bct, dma=True)
    t_s0a = p.op("gpsimd", (lambda h: h.memset(scat[64:128, :], 0.0)), prj)
    t_s0b = p.op("gpsimd", (lambda h: h.memset(scbf[64:128, :], 0.0)), prj)
    sbf_tok = [t_s0a, t_s0b]
    yt4 = ytd.rearrange("(fc p) t -> p fc t", p=128)
    E.yint_war = None
    ring_war = {0: [], 1: []}
    row_war = []
    y_done = []
    ytok_war = [None] * 4
    for c in range(16):
        sl = c % 2
        xh = xhb[:, sl * DIN:(sl + 1) * DIN]
        t_xh = p.op("sync", (lambda h, c=c, xh=xh: h.dma_start(
            out=xh, in_=htd[c * 128:(c + 1) * 128, :])), ring_war[sl] + ht_done, dma=True)
        ring_war[sl] = []
        t_lr = p.op("gpsimd", (lambda h, c=c: h.dma_start(
            out=lrow[0:1, :], in_=lcb[0:32, c * 128:(c + 1) * 128])), rowst + row_war, dma=True)
        t_mr = p.op("gpsimd", (lambda h, c=c: h.dma_start(
            out=mrow[0:1, :], in_=mlcb[0:32, c * 128:(c + 1) * 128])), rowst + row_war, dma=True)

        def trdt(h, c=c):
            h.transpose(E.psb[:, 0:32], dtb[0:32, c * 128:(c + 1) * 128], E.ident[0:32, 0:32])
            return h.transpose(E.psb[:, 32:64], lcb[0:32, c * 128:(c + 1) * 128],
                               E.ident[0:32, 0:32])
        t_tr = p.op("tensor", trdt, rowst + [E.psb_war, E.const_tok[1]] + row_war)
        t_ev1 = p.op("vector", lambda h: h.tensor_copy(dtk[:, 0:32], E.psb[:, 0:32]),
                     [t_tr] + row_war)
        t_ev2 = p.op("vector", lambda h: h.tensor_copy(ltk[:, 0:32], E.psb[:, 32:64]),
                     [t_tr] + row_war)
        E.psb_war = t_ev2
        t_ex = p.op("scalar", lambda h: h.activation(exl[:, 0:32], ltk[:, 0:32], AF.Exp),
                    [t_ev2] + row_war)
        t_cbm = _mmgroup(E, E.ps[6][:, 0:128],
                         [(bcT[0:64, c * 128:(c + 1) * 128], cb0[0:64, c * 128:(c + 1) * 128])],
                         bct + [t_cb0, E.ps_war[6]])
        t_cbe = p.op("vector", (lambda h: h.tensor_tensor(cbm, E.ps[6][:, 0:128], umask,
                                                          ALU.mult)), [t_cbm, t_um, E.cbm_war])
        E.ps_war[6] = t_cbe
        E.cbm_war = t_cbe
        t_xhat = p.op("vector", (lambda h, xh=xh: h.tensor_tensor(
            xht.rearrange("p (hh pp) -> p hh pp", hh=32),
            xh.rearrange("p (hh pp) -> p hh pp", hh=32),
            dtk[:, 0:32].unsqueeze(2).broadcast_to([128, 32, 64]), ALU.mult)),
            [t_xh, t_ev1] + row_war)
        w1t = []
        for blk in range(4):
            w1t.append(_mmgroup(E, E.ps[blk][:, :],
                                [(bcT[64:128, c * 128:(c + 1) * 128],
                                  scbf[64:128, blk * TBW:(blk + 1) * TBW])],
                                bct + sbf_tok + [E.ps_war[blk]], tile_position=(64, 0)))
        yintt = []
        for blk in range(4):
            yintt.append(p.op("vector", (lambda h, blk=blk: h.tensor_tensor(
                yint[:, blk * TBW:(blk + 1) * TBW].rearrange("p (hh pp) -> p hh pp", hh=8),
                E.ps[blk][:, :].rearrange("p (hh pp) -> p hh pp", hh=8),
                exl[:, blk * 8:(blk + 1) * 8].unsqueeze(2).broadcast_to([128, 8, 64]),
                ALU.mult)), [w1t[blk], t_ex, E.yint_war]))
        E.yint_war = yintt[-1]
        gtoks = []
        for hf in range(2):
            emms = []
            for j in range(16):
                hh = hf * 16 + j
                bank = j // 4
                off = (j % 4) * 128
                emms.append(_mmgroup(
                    E, E.ps[bank][:, off:off + 128],
                    [(E.ones_bf[0:1, :], lrow[0:1, hh * 128:(hh + 1) * 128])],
                    [t_lr, t_mr, E.ps_war[bank]] + yintt,
                    extra=[(mrow[0:1, hh * 128:(hh + 1) * 128], E.onesr[0:1, 0:128])]))
                E.ps_war[bank] = emms[-1]
            tss = []
            for bank in range(4):
                tss.append(p.op("vector", (lambda h, hf=hf, bank=bank: h.tensor_scalar(
                    Gsc[:, bank * TBW:(bank + 1) * TBW], E.ps[bank][:, :], 20.0, None, ALU.min)),
                    emms + [E.gsc_war]))
                E.ps_war[bank] = tss[-1]
            t_ge = p.op("scalar", (lambda h, hf=hf: h.activation(
                Gsc[:, 0:DIN], Gsc[:, 0:DIN], AF.Exp)), tss)
            t_gm = p.op("vector", (lambda h, hf=hf: h.tensor_tensor(
                Gall[:, hf * DIN:(hf + 1) * DIN].rearrange("p (hh t) -> p hh t", hh=16),
                Gsc[:, 0:DIN].rearrange("p (hh t) -> p hh t", hh=16),
                cbm.unsqueeze(1).broadcast_to([128, 16, 128]), ALU.mult)),
                [t_ge, t_cbe, E.gall_war])
            E.gsc_war = t_gm
            E.gall_war = t_gm
            gtoks.append(t_gm)
        yev = []
        for w in range(4):
            bank = 4 + w % 2
            imms = []
            for j in range(8):
                hh = w * 8 + j
                imms.append(_mmgroup(E, E.ps[bank][:, j * 64:(j + 1) * 64],
                                     [(Gall[:, hh * 128:(hh + 1) * 128],
                                       xht[:, hh * 64:(hh + 1) * 64])],
                                     [gtoks[hh // 16], t_xhat, E.ps_war[bank]]))
                E.ps_war[bank] = imms[-1]
            tye = p.op("vector", (lambda h, w=w, bank=bank: h.tensor_tensor(
                ytok[:, w * TBW:(w + 1) * TBW], E.ps[bank][:, :],
                yint[:, w * TBW:(w + 1) * TBW], ALU.add)), imms + yintt + [ytok_war[w]])
            E.ps_war[bank] = tye
            yev.append(tye)
        t_f1 = _mmgroup(E, E.ps[6][:, 0:32], [(E.ones_bf[0:1, :], lamb16[0:1, c * 32:(c + 1) * 32])],
                        [E.ps_war[6], E.mbtok["lamb16"]])
        t_f2 = p.op("vector", lambda h: h.tensor_tensor(e2s[:, 0:32], E.ps[6][:, 0:32],
                                                        ltk[:, 0:32], ALU.subtract), [t_f1, t_ev2])
        E.ps_war[6] = t_f2
        t_f3 = p.op("scalar", lambda h: h.activation(e2s[:, 32:64], e2s[:, 0:32], AF.Exp), [t_f2])
        t_f4 = p.op("vector", lambda h: h.tensor_copy(ftk[:, 0:32], e2s[:, 32:64]), [t_f3])
        t_bh = p.op("vector", (lambda h, c=c: h.tensor_tensor(
            b2h.rearrange("p (hh n) -> p hh n", hh=32),
            bt3[:, c].unsqueeze(1).broadcast_to([128, 32, 64]),
            ftk[:, 0:32].unsqueeze(2).broadcast_to([128, 32, 64]), ALU.mult)),
            btt + [t_f4, E.b2h_war])
        E.b2h_war = t_bh
        t_se = p.op("vector", (lambda h, c=c: h.tensor_copy(
            selam[0:1, :].rearrange("q (hh pp) -> q hh pp", hh=32),
            elamb[0:1, c * 32:(c + 1) * 32].unsqueeze(2).broadcast_to([1, 32, 64]))),
            [rowst[3], E.selam_war])
        for w in range(4):
            dmms = []
            for j in range(8):
                hh = w * 8 + j
                dmms.append(_mmgroup(E, E.ps[4][64:128, j * 64:(j + 1) * 64],
                                     [(b2h[:, hh * 64:(hh + 1) * 64], xht[:, hh * 64:(hh + 1) * 64])],
                                     [t_bh, t_xhat, E.ps_war[4]], tile_position=(0, 64)))
                E.ps_war[4] = dmms[-1]
            t_lb = _mmgroup(E, E.ps[5][64:128, :], [(E.ones_bf[0:1, 0:64],
                                                     selam[0:1, w * TBW:(w + 1) * TBW])],
                            [t_se, E.ps_war[5]], tile_position=(0, 64))
            t_u1 = p.op("vector", (lambda h, w=w: h.tensor_tensor(
                sstmp[64:128, :], scat[64:128, w * TBW:(w + 1) * TBW], E.ps[5][64:128, :],
                ALU.mult)), [t_lb, E.sstmp_war] + sbf_tok)
            E.ps_war[5] = t_u1
            t_u2 = p.op("vector", (lambda h, w=w: h.tensor_tensor(
                scat[64:128, w * TBW:(w + 1) * TBW], sstmp[64:128, :], E.ps[4][64:128, :],
                ALU.add)), dmms + [t_u1])
            E.ps_war[4] = t_u2
            E.sstmp_war = t_u2
            t_u3 = p.op("vector", (lambda h, w=w: h.tensor_copy(
                scbf[64:128, w * TBW:(w + 1) * TBW], scat[64:128, w * TBW:(w + 1) * TBW])), [t_u2])
            sbf_tok = [t_u3]
            ring_war[sl].append(t_u2)
        for rnd in range(2):
            def trs2(h, rnd=rnd):
                last = None
                for j in range(8):
                    fco = rnd * 8 + j
                    last = h.transpose(E.psb[:, j * 128:(j + 1) * 128],
                                      ytok[:, fco * 128:(fco + 1) * 128], E.ident)
                return last
            ttr = p.op("tensor", trs2, yev + [E.psb_war, E.const_tok[1]])
            tev = p.op("vector", (lambda h: h.tensor_copy(ystg, E.psb[:, 0:1024])),
                       [ttr, E.ystg_war])
            E.psb_war = tev
            tdm = p.op("sync", (lambda h, c=c, rnd=rnd: h.dma_start(
                out=yt4[:, rnd * 8:(rnd + 1) * 8, c * 128:(c + 1) * 128],
                in_=ystg.rearrange("p (fc t) -> p fc t", fc=8))), [tev], dma=True)
            E.ystg_war = tdm
            y_done.append(tdm)
        for w in range(4):
            ytok_war[w] = y_done[-1]
        row_war = gtoks + [t_f2, t_bh, t_xhat, t_ev1, t_ev2, t_ex]
    E.pool.pop()

    barrier(E)
    ychk = E.pool.alloc_bf16(DIN)
    gchk = E.pool.alloc_bf16(DIN)
    mchk = E.pool.alloc_bf16(DIN)
    ym_done = []
    yg_war = None
    gt4 = gated.rearrange("(c p) t -> p c t", p=128)
    ym4 = ymd.rearrange("(c p) t -> p c t", p=128)
    for fc in range(16):
        t_y = p.op("sync", (lambda h, fc=fc: h.dma_start(out=ychk, in_=yt4[:, fc])),
                   y_done + [yg_war], dma=True)
        t_g = p.op("gpsimd", (lambda h, fc=fc: h.dma_start(out=gchk, in_=gt4[:, fc])),
                   prj + [yg_war], dma=True)
        t_sg = p.op("scalar", (lambda h: h.activation(gchk, gchk, AF.Silu)), [t_g])
        t_m = p.op("vector", (lambda h: h.tensor_tensor(mchk, ychk, gchk, ALU.mult)),
                   [t_y, t_sg, yg_war])
        t_o = p.op("sync", (lambda h, fc=fc: h.dma_start(out=ym4[:, fc], in_=mchk)), [t_m], dma=True)
        yg_war = t_o
        ym_done.append(t_o)
    barrier(E)
    wob = E.pool.alloc_bf16(16 * D)
    t_wo = load_weight_bf16(E, "s1_wout", wob, prj + ym_done)
    ymr = E.pool.alloc_bf16(2 * TBW)
    xnr = E.pool.alloc_bf16(2 * NCH * TBW)
    oadds = [E.pool.alloc_bf16(TBW) for _ in range(4)]
    oai = [0]
    xnd_ap = E.mbtok["xnd_ap"]
    xn4 = xnd_ap.rearrange("(c p) t -> p c t", p=128)
    ymr_war = {0: [], 1: []}
    xnr_war = {0: [], 1: []}
    oadd_war = [None] * 4
    for tb in range(NTB):
        xsl = tb % 2
        xnt = xnr[:, xsl * NCH * TBW:(xsl + 1) * NCH * TBW]
        t_xn = p.op("gpsimd", (lambda h, tb=tb, xnt=xnt: h.dma_start(
            out=xnt.rearrange("p (c t) -> p c t", c=NCH),
            in_=xn4[:, :, tb * TBW:(tb + 1) * TBW])), xnr_war[xsl] + [E.mbtok["xnd"]], dma=True)
        xnr_war[xsl] = []
        for g in range(2):
            mms = {}
            for fc in range(16):
                ysl = fc % 2
                ym_ = ymr[:, ysl * TBW:(ysl + 1) * TBW]
                t_ym = p.op("sync", (lambda h, tb=tb, fc=fc, ym_=ym_: h.dma_start(
                    out=ym_, in_=ym4[:, fc, tb * TBW:(tb + 1) * TBW])),
                    ymr_war[ysl] + ym_done, dma=True)
                ymr_war[ysl] = []
                for ei in range(4):
                    ec = g * 4 + ei
                    bank = 2 + ei
                    tm = _mmgroup(E, E.ps[bank][:, :],
                                  [(wob[:, fc * D + ec * 128: fc * D + ec * 128 + 128], ym_)],
                                  [t_ym, t_wo, E.ps_war[bank]] if fc == 0 else [t_ym, t_wo],
                                  start_clear=(fc == 0), stop_group=(fc == 15))
                    ymr_war[ysl].append(tm)
                    mms[ei] = tm
            for ei in range(4):
                ec = g * 4 + ei
                bank = 2 + ei
                osl = oai[0] % 4
                oai[0] += 1
                oadd = oadds[osl]
                t1 = p.op("vector", (lambda h, bank=bank, ec=ec, xnt=xnt, oadd=oadd: h.tensor_tensor(
                    oadd, E.ps[bank][:, :], xnt[:, ec * TBW:(ec + 1) * TBW], ALU.add)),
                    [mms[ei], t_xn, oadd_war[osl]])
                tx = p.op("vector", (lambda h, ec=ec, tb=tb, oadd=oadd: h.tensor_tensor(
                    E.x[:, ec * T + tb * TBW: ec * T + tb * TBW + TBW],
                    E.x[:, ec * T + tb * TBW: ec * T + tb * TBW + TBW], oadd, ALU.add)),
                    [t1, E.xtok.get((ec, tb))])
                oadd_war[osl] = tx
                E.ps_war[bank] = tx
                E.xtok[(ec, tb)] = tx
                xnr_war[xsl].append(t1)
    E.pool.pop()


def emit_delta(E):
    p = E.p
    nc = E.nc
    H, DV, CH = 4, 256, 32
    NC = T // CH
    vd = nc.dram_tensor("dl_v", [D, T], BF16).ap()
    xlnd = nc.dram_tensor("dl_xln", [D, T], BF16).ap()
    vstkd = nc.dram_tensor("dl_vstk", [128, NC * DV], BF16).ap()
    kstkd = nc.dram_tensor("dl_kstk", [128, NC * DV], BF16).ap()
    dd = nc.dram_tensor("dl_delta", [D, T], BF16).ap()
    mixd = nc.dram_tensor("dl_mix", [D, T], BF16).ap()

    qd = nc.dram_tensor("dl_q", [D, T], BF16).ap()
    kd = nc.dram_tensor("dl_k", [D, T], BF16).ap()
    vbd = nc.dram_tensor("dl_vb", [D, T], BF16).ap()
    betd = nc.dram_tensor("dl_bet", [4, T], BF16).ap()

    E.pool.push()
    E.pool.push()
    E.xln = E.pool.alloc_bf16(NCH * T)
    emit_ln(E)
    t_xs = p.op("sync", (lambda h, x_=E.xln: h.dma_start(
        out=xlnd.rearrange("(c p) t -> p c t", p=128),
        in_=x_.rearrange("p (c t) -> p c t", c=NCH))), E.lntoks, dma=True)
    wbuf = E.pool.alloc_bf16(NCH * D)
    wbb = E.pool.alloc_bf16(NCH * 32)
    bet = E.pool.alloc_bf16(T)
    betr = E.pool.alloc_bf16(4 * T)
    ccol = E.pool.alloc_f32(32)
    diag = [E.pool.alloc_bf16(4 * 128) for _ in range(2)]
    pp = [E.pool.alloc_bf16(T + 16) for _ in range(2)]
    stg = [E.pool.alloc_bf16(TBW) for _ in range(2)]
    stg2 = [E.pool.alloc_bf16(TBW) for _ in range(2)]
    t_wb = load_weight_bf16(E, "d1_wb", wbb, E.lntoks)
    btoks = []
    for tb in range(NTB):
        i = tb % 2
        pairs = [(wbb[:, c * 32:(c + 1) * 32],
                  E.xln[:, c * T + tb * TBW: c * T + tb * TBW + TBW]) for c in range(NCH)]
        tm = _mmgroup(E, E.ps[i][0:32, :], pairs, E.lntoks + [t_wb, E.ps_war[i]])
        ts_ = p.op("scalar", (lambda h, tb=tb, i=i: h.activation(
            bet[0:32, tb * TBW:(tb + 1) * TBW], E.ps[i][0:32, :], AF.Sigmoid)), [tm])
        E.ps_war[i] = ts_
        btoks.append(ts_)
    brt = []
    for hh in range(4):
        brt.append(p.op("sync", (lambda h, hh=hh: h.dma_start(
            out=betd[hh:hh + 1, :], in_=bet[hh:hh + 1, :])), btoks, dma=True))
    t_br = p.op("sync", lambda h: h.dma_start(
        out=betr[0:1, :], in_=betd.rearrange("hh t -> (hh t)").unsqueeze(0)), brt, dma=True)
    vtoks = []
    qtoks = []
    ktoks = []
    pmms = list(E.lntoks)
    ccw = []
    ppw = {0: [], 1: []}
    dgw = {0: [], 1: []}
    sgw = {0: [], 1: []}
    sg2w = {0: None, 1: None}
    for pi, (wn, cn) in enumerate((("d1_wv", "d1_vconv_w"), ("d1_wk", "d1_kconv_w"),
                                   ("d1_wq", "d1_qconv_w"))):
        outd = (vd, kd, qd)[pi]
        otk = (vtoks, ktoks, qtoks)[pi]
        t_w = load_weight_bf16(E, wn, wbuf, pmms)
        cin = declare_input(E, cn)
        t_cc = p.op("gpsimd", (lambda h, cin=cin: h.dma_start(out=ccol, in_=cin)),
                    pmms + ccw, dma=True)
        pmms = []
        ccw = []
        if pi == 0:
            t_pz = [p.op("gpsimd", (lambda h, s=s: h.memset(pp[s][:, 0:16], 0.0)))
                    for s in (0, 1)]
            ppw = {0: t_pz[0:1], 1: t_pz[1:2]}
        for ec in range(NCH):
            sl = ec % 2
            pptoks = []
            for tb in range(NTB):
                i = tb % 2
                pairs = [(wbuf[:, c * D + ec * 128: c * D + ec * 128 + 128],
                          E.xln[:, c * T + tb * TBW: c * T + tb * TBW + TBW]) for c in range(NCH)]
                tm = _mmgroup(E, E.ps[i][:, :], pairs, E.lntoks + [t_w, E.ps_war[i]])
                tv = p.op("vector", (lambda h, sl=sl, tb=tb, i=i: h.tensor_copy(
                    pp[sl][:, 16 + tb * TBW: 16 + tb * TBW + TBW],
                    E.ps[i][:, :])), [tm] + ppw[sl])
                E.ps_war[i] = tv
                pptoks.append(tv)
                pmms.append(tm)
            dts = []
            for tap in range(4):
                dts.append(p.op("vector", (lambda h, sl=sl, ec=ec, tap=tap: h.tensor_scalar(
                    diag[sl][:, tap * 128:(tap + 1) * 128], E.ident,
                    ccol[:, ec * 4 + tap: ec * 4 + tap + 1], None, ALU.mult)),
                    [t_cc] + dgw[sl] + [E.const_tok[1]]))
            ccw += dts
            dgw[sl] = []
            ppw[sl] = []
            for tb in range(NTB):
                i = 4 + tb % 2
                pairs = [(diag[sl][:, tap * 128:(tap + 1) * 128],
                          pp[sl][:, 13 + tap + tb * TBW: 13 + tap + tb * TBW + TBW])
                         for tap in range(4)]
                tm = _mmgroup(E, E.ps[i][:, :], pairs, dts + pptoks + [E.ps_war[i]])
                dgw[sl].append(tm)
                ppw[sl].append(tm)
                th = p.op("scalar", (lambda h, sl=sl, tb=tb, i=i: h.activation(
                    stg[tb % 2], E.ps[i][:, :], AF.Silu)), [tm] + sgw[tb % 2])
                E.ps_war[i] = th
                td = p.op("sync", (lambda h, outd=outd, ec=ec, tb=tb, s=tb % 2: h.dma_start(
                    out=outd[ec * 128:(ec + 1) * 128, tb * TBW:(tb + 1) * TBW], in_=stg[s])),
                    [th], dma=True)
                sgw[tb % 2] = [td]
                otk.append(td)
                if pi == 0:
                    hh = ec // 2
                    t_rb = _mmgroup(E, E.ps[6][:, :],
                                    [(E.ones_bf[0:1, 0:128],
                                      betr[0:1, hh * T + tb * TBW: hh * T + tb * TBW + TBW])],
                                    [t_br, E.ps_war[6]])
                    t_sc = p.op("vector", (lambda h, s=tb % 2: h.tensor_tensor(
                        stg2[s], stg[s], E.ps[6][:, :], ALU.mult)),
                        [th, t_rb, sg2w[tb % 2]])
                    E.ps_war[6] = t_sc
                    sgw[tb % 2].append(t_sc)
                    td2 = p.op("sync", (lambda h, ec=ec, tb=tb, s=tb % 2: h.dma_start(
                        out=vbd[ec * 128:(ec + 1) * 128, tb * TBW:(tb + 1) * TBW], in_=stg2[s])),
                        [t_sc], dma=True)
                    sg2w[tb % 2] = td2
                    vtoks.append(td2)
    E.pool.pop()

    qnT = E.pool.alloc_bf16(NCH * T)
    knT = E.pool.alloc_bf16(NCH * T)
    kbT = E.pool.alloc_bf16(NCH * T)
    E.pool.push()
    betr2 = E.pool.alloc_bf16(4 * T)
    sqsc = E.pool.alloc_bf16(TBW)
    bcsc = E.pool.alloc_bf16(TBW)
    rrow = E.pool.alloc_f32(TBW)
    t_br2 = p.op("sync", lambda h: h.dma_start(
        out=betr2[0:1, :], in_=betd.rearrange("hh t -> (hh t)").unsqueeze(0)), brt, dma=True)
    t_lq = p.op("sync", lambda h: h.dma_start(
        out=qnT.rearrange("p (c t) -> p c t", c=NCH),
        in_=qd.rearrange("(c p) t -> p c t", p=128)), qtoks[-1:] + pmms, dma=True)
    t_lk = p.op("sync", lambda h: h.dma_start(
        out=knT.rearrange("p (c t) -> p c t", c=NCH),
        in_=kd.rearrange("(c p) t -> p c t", p=128)), ktoks[-1:] + pmms, dma=True)
    l2toks = []
    for (buf, tld) in ((qnT, t_lq), (knT, t_lk)):
        for hh in range(H):
            for tb in range(NTB):
                sq2 = []
                for cc in range(2):
                    c = 2 * hh + cc
                    tq = p.op("scalar", (lambda h, buf=buf, c=c, tb=tb: h.activation(
                        sqsc, buf[:, c * T + tb * TBW: c * T + tb * TBW + TBW], AF.Square)),
                        [tld, E.sq_war] + l2toks[-2:])
                    tmm = _mmgroup(E, E.ps[6][0:1, :], [(E.ones_bf[:, 0:1], sqsc)],
                                   [tq, E.ps_war[6]], start_clear=(cc == 0), stop_group=(cc == 1))
                    E.ps_war[6] = tmm
                    E.sq_war = tmm
                    sq2.append(tmm)
                t1 = p.op("vector", lambda h: h.tensor_scalar_add(rrow[0:1, :], E.ps[6][0:1, :],
                                                                 1e-6), [sq2[-1], E.rr_war])
                E.ps_war[6] = t1
                t2 = p.op("scalar", lambda h: h.activation(rrow[0:1, :], rrow[0:1, :], AF.Ln), [t1])
                t3 = p.op("scalar", lambda h: h.activation(rrow[0:1, :], rrow[0:1, :], AF.Exp,
                                                           scale=-0.5), [t2])
                t4 = p.op("vector", lambda h: h.tensor_copy(E.row_b1, rrow[0:1, :]), [t3, E.row_b_war])
                t_rb = _mmgroup(E, E.ps[5][:, :], [(E.ones_bf[0:1, :], E.row_b1)],
                                [t4, E.ps_war[5]])
                t_rv = p.op("vector", lambda h: h.tensor_copy(bcsc, E.ps[5][:, :]),
                            [t_rb, E.bcsc_war])
                E.ps_war[5] = t_rv
                ap2 = []
                for cc in range(2):
                    c = 2 * hh + cc
                    ap2.append(p.op("vector", (lambda h, buf=buf, c=c, tb=tb: h.tensor_tensor(
                        buf[:, c * T + tb * TBW: c * T + tb * TBW + TBW],
                        buf[:, c * T + tb * TBW: c * T + tb * TBW + TBW], bcsc,
                        ALU.mult)), [t_rv]))
                E.bcsc_war = ap2[-1]
                E.row_b_war = ap2[-1]
                E.rr_war = t1
                l2toks += ap2
    kbtoks = []
    for hh in range(H):
        for tb in range(NTB):
            t_rb = _mmgroup(E, E.ps[5][:, :],
                            [(E.ones_bf[0:1, :], betr2[0:1, hh * T + tb * TBW: hh * T + tb * TBW + TBW])],
                            [t_br2, E.ps_war[5]])
            for cc in range(2):
                c = 2 * hh + cc
                tv = p.op("vector", (lambda h, c=c, tb=tb: h.tensor_tensor(
                    kbT[:, c * T + tb * TBW: c * T + tb * TBW + TBW],
                    knT[:, c * T + tb * TBW: c * T + tb * TBW + TBW], E.ps[5][:, :], ALU.mult)),
                    [t_rb] + l2toks)
                kbtoks.append(tv)
            E.ps_war[5] = kbtoks[-1]
    E.pool.pop()

    barrier(E)
    E.pool.push()
    vtmp = [E.pool.alloc_bf16(NCH * 128) for _ in range(2)]
    vtk1 = [E.pool.alloc_bf16(D) for _ in range(2)]
    sdt = []
    vbw = {0: [], 1: []}
    vkw = {0: [], 1: []}
    for src_is_v in (True, False):
        dstd = vstkd if src_is_v else kstkd
        for tc in range(16):
            sl = tc % 2
            if src_is_v:
                t_vt = p.op("sync", (lambda h, tc=tc, sl=sl: h.dma_start(
                    out=vtmp[sl].rearrange("p (c w) -> p c w", c=NCH),
                    in_=vbd.rearrange("(c p) t -> p c t", p=128)[:, :, tc * 128:(tc + 1) * 128])),
                    vbw[sl] + vtoks[-1:], dma=True)
                vbw[sl] = []
                srcb, sdep = vtmp[sl], [t_vt]
            else:
                srcb, sdep = knT, l2toks[-1:] + kbtoks[-1:]
            evs = []
            for rnd in range(2):
                def trs(h, tc=tc, rnd=rnd, srcb=srcb, src_is_v=src_is_v):
                    last = None
                    for j in range(4):
                        c = rnd * 4 + j
                        off = c * 128 if src_is_v else c * T + tc * 128
                        last = h.transpose(E.psb[:, j * 128:(j + 1) * 128],
                                           srcb[:, off:off + 128], E.ident)
                    return last
                ttr = p.op("tensor", trs, sdep + [E.psb_war, E.const_tok[1]])
                if src_is_v:
                    vbw[sl].append(ttr)
                tev = p.op("vector", (lambda h, sl=sl, rnd=rnd: h.tensor_copy(
                    vtk1[sl][:, rnd * 512:(rnd + 1) * 512], E.psb[:, 0:512])),
                    [ttr] + vkw[sl])
                E.psb_war = tev
                evs.append(tev)
            vkw[sl] = []
            for s in range(4):
                c = tc * 4 + s
                tsd = p.op("sync", (lambda h, sl=sl, s=s, c=c, dstd=dstd: h.dma_start(
                    out=dstd.rearrange("(hh i) (cc dv) -> i hh cc dv", hh=4, cc=NC)[:, :, c],
                    in_=vtk1[sl].rearrange("p (hh dv) -> p hh dv", hh=4)[32 * s:32 * s + 32])),
                    evs, dma=True)
                vkw[sl].append(tsd)
                sdt.append(tsd)
    E.delta_sdt = sdt[-1:]
    E.pool.pop()
    barrier(E)
    PM = E.pool.alloc_bf16(5 * 256)
    zz = E.pool.alloc_bf16(DV)
    at = E.pool.alloc_bf16(128)
    Sbf = E.pool.alloc_bf16(8 * DV)
    scat = [E.pool.alloc_f32(8 * DV) for _ in range(2)]
    dstg = E.pool.alloc_bf16(DV)
    vkr = E.pool.alloc_bf16(4 * DV)
    cbd_in = declare_input(E, "c_bd")
    cbdT_in = declare_input(E, "c_bdT")
    cbdA_in = declare_input(E, "c_bdA")
    cmsk = E.pool.alloc_bf16(3 * 256)
    t_m1 = p.op("gpsimd", lambda h: h.dma_start(out=cmsk[:, 0:256].bitcast(F32), in_=cbd_in), dma=True)
    t_m2 = p.op("gpsimd", lambda h: h.dma_start(out=cmsk[:, 256:512].bitcast(F32), in_=cbdT_in), dma=True)
    t_m3 = p.op("gpsimd", lambda h: h.dma_start(out=cmsk[:, 512:768].bitcast(F32), in_=cbdA_in), dma=True)
    mbd = cmsk[:, 0:256].bitcast(F32)
    mbdT = cmsk[:, 256:512].bitcast(F32)
    mbdA = cmsk[:, 512:768].bitcast(F32)
    t_sz = p.op("gpsimd", lambda h: h.memset(scat[1][:, :], 0.0), E.delta_sdt)
    t_pz0 = p.op("vector", lambda h: h.memset(E.ps[0][:, 0:128], 0.0), [E.ps_war[0]])
    t_pz3 = p.op("vector", lambda h: h.memset(E.ps[3][:, 0:128], 0.0), [E.ps_war[3]])
    E.ps_war[0] = t_pz0
    E.ps_war[3] = t_pz3
    t_szb = p.op("vector", lambda h: h.tensor_copy(Sbf[:, :], scat[1][:, :]), [t_sz])
    stoks = [t_szb]
    vk_war = {0: [], 1: []}
    prevst = [t_szb] * 8
    dstg_war = None
    ddone = []
    for c in range(NC):
        sl = c % 2
        vs = vkr[:, sl * DV:(sl + 1) * DV]
        ks = vkr[:, (2 + sl) * DV:(3 + sl) * DV]
        t_vl = p.op("sync", (lambda h, c=c, vs=vs: h.dma_start(
            out=vs, in_=vstkd[:, c * DV:(c + 1) * DV])), vk_war[sl] + E.delta_sdt, dma=True)
        t_kl = p.op("sync", (lambda h, c=c, ks=ks: h.dma_start(
            out=ks, in_=kstkd[:, c * DV:(c + 1) * DV])), vk_war[sl] + E.delta_sdt, dma=True)
        vk_war[sl] = []
        amms = []
        for rev in range(2):
            dst = PM[:, 0:128] if rev == 0 else PM[:, 128:256]
            msk = mbd if rev == 0 else mbdT
            triples = []
            for hh in range(H):
                for cc in range(2):
                    ch_ = 2 * hh + cc
                    a_ = kbT[:, ch_ * T + c * CH: ch_ * T + c * CH + CH]
                    b_ = knT[:, ch_ * T + c * CH: ch_ * T + c * CH + CH]
                    l_, r_ = (a_, b_) if rev == 0 else (b_, a_)
                    triples.append((E.ps[0][32 * hh:32 * hh + 32, 32 * hh:32 * hh + 32],
                                    l_, r_, (0, 32 * hh)))
            tmm = _mmgroup_tp(E, None, triples, [E.ps_war[0]] + stoks[-1:] + kbtoks[-1:] + l2toks[-1:],
                              pairwise=True)
            tev = p.op("vector", (lambda h, dst=dst, msk=msk: h.tensor_tensor(
                dst, E.ps[0][:, 0:128], msk, ALU.mult)), [tmm, t_m1, t_m2, E.pmb_war])
            E.ps_war[0] = tev
            E.pmb_war = tev
            amms.append(tev)
        utt = list(amms)
        for j in range(1, 5):
            def dbl(h, j=j):
                pp_ = PM[:, (j - 1) * 256:(j - 1) * 256 + 128]
                mm_ = PM[:, (j - 1) * 256 + 128: j * 256]
                last = None
                if j < 4:
                    last = h.matmul(E.ps[1][:, 0:128], mm_, pp_, start=True, stop=True)
                last = h.matmul(E.ps[1][:, 128:256], pp_, mm_, start=True, stop=True)
                return last
            t_pm = p.op("tensor", dbl, utt + [E.ps_war[1]])
            if j < 4:
                t_pe = p.op("vector", (lambda h, j=j: h.tensor_copy(
                    PM[:, j * 256:(j + 1) * 256], E.ps[1][:, 0:256])), [t_pm])
            else:
                t_pe = p.op("vector", (lambda h, j=j: h.tensor_copy(
                    PM[:, j * 256 + 128:(j + 1) * 256], E.ps[1][:, 128:256])), [t_pm])
            E.ps_war[1] = t_pe
            utt.append(t_pe)
        triples = []
        for hh in range(H):
            for cc in range(2):
                triples.append((E.ps[2][32 * hh:32 * hh + 32, 0:DV],
                                kbT[:, (2 * hh + cc) * T + c * CH: (2 * hh + cc) * T + c * CH + CH],
                                Sbf[:, (hh * 2 + cc) * DV:(hh * 2 + cc + 1) * DV],
                                (0, 32 * hh)))
        rmms = [_mmgroup_tp(E, None, triples, stoks + [E.ps_war[2]] + kbtoks[-1:], pairwise=True)]
        E.ps_war[2] = rmms[-1]
        t_r = p.op("vector", (lambda h, vs=vs: h.tensor_tensor(zz, vs, E.ps[2][:, 0:DV],
                                                               ALU.subtract)),
                   rmms + [t_vl, E.zz_war])
        E.ps_war[2] = t_r
        zt = t_r
        for j in range(4, -1, -1):
            t_a = _mmgroup(E, E.ps[2][:, 0:DV],
                           [(PM[:, j * 256 + 128:(j + 1) * 256], zz)],
                           utt + [zt, E.ps_war[2]])
            zt = p.op("vector", lambda h: h.tensor_tensor(zz, zz, E.ps[2][:, 0:DV], ALU.add),
                      [t_a])
            E.ps_war[2] = zt
        E.zz_war = zt
        triples = []
        for hh in range(H):
            for cc in range(2):
                triples.append((E.ps[3][32 * hh:32 * hh + 32, 32 * hh:32 * hh + 32],
                                knT[:, (2 * hh + cc) * T + c * CH: (2 * hh + cc) * T + c * CH + CH],
                                qnT[:, (2 * hh + cc) * T + c * CH: (2 * hh + cc) * T + c * CH + CH],
                                (0, 32 * hh)))
        prev = _mmgroup_tp(E, None, triples, [E.ps_war[3]] + l2toks[-1:], pairwise=True)
        t_at = p.op("vector", lambda h: h.tensor_tensor(at, E.ps[3][:, 0:128], mbdA, ALU.mult),
                    [prev, t_m3, E.at_war])
        E.ps_war[3] = t_at
        E.at_war = t_at
        omms = []
        first = {}
        for hh in range(H):
            for half in range(2):
                fo = (2 * hh + half) * 32
                pairs = [(Sbf[:, (hh * 2 + cc) * DV + half * 128:(hh * 2 + cc) * DV + half * 128 + 128],
                          qnT[:, (2 * hh + cc) * T + c * CH: (2 * hh + cc) * T + c * CH + CH])
                         for cc in range(2)]
                t_o1 = _mmgroup(E, E.ps[4][:, fo:fo + 32], pairs,
                                stoks + [E.ps_war[4], zt], stop_group=False)
                t_o2 = _mmgroup(E, E.ps[4][:, fo:fo + 32],
                                [(zz[32 * hh:32 * hh + 32, half * 128: half * 128 + 128],
                                  at[32 * hh:32 * hh + 32, 32 * hh:32 * hh + 32])],
                                [t_o1, zt, t_at], tile_position=(32 * hh, 0), start_clear=False)
                E.ps_war[4] = t_o2
                omms.append(t_o2)
        t_oe = p.op("vector", lambda h: h.tensor_copy(dstg, E.ps[4][:, 0:DV]),
                    omms + [dstg_war])
        E.ps_war[4] = t_oe
        t_od = p.op("sync", (lambda h, c=c: h.dma_start(
            out=dd.rearrange("(fc p) t -> p fc t", p=128)[:, :, c * CH:(c + 1) * CH],
            in_=dstg.rearrange("p (fc i) -> p fc i", fc=8))), [t_oe], dma=True)
        dstg_war = t_od
        ddone.append(t_od)
        so, si = scat[c % 2], scat[1 - c % 2]
        newst = []
        for hh in range(H):
            for cc in range(2):
                idx = hh * 2 + cc
                bank = 5 + idx % 2
                t_d = _mmgroup(E, E.ps[bank][:, 0:DV],
                               [(ks[32 * hh:32 * hh + 32, cc * 128: cc * 128 + 128],
                                 zz[32 * hh:32 * hh + 32, :])],
                               [t_kl, zt, E.ps_war[bank]], tile_position=(32 * hh, 0))
                t_su = p.op("vector", (lambda h, so=so, si=si, idx=idx, bank=bank: h.tensor_tensor(
                    so[:, idx * DV:(idx + 1) * DV], si[:, idx * DV:(idx + 1) * DV],
                    E.ps[bank][:, 0:DV], ALU.add)),
                    [t_d, prevst[idx]] + stoks[-1:])
                E.ps_war[bank] = t_su
                newst.append(t_su)
                vk_war[sl].append(t_d)
        prevst = newst
        t_sb2 = p.op("vector", (lambda h, so=so: h.tensor_copy(Sbf[:, :], so[:, :])), newst)
        stoks = [t_sb2]
        vk_war[sl].append(t_r)
    E.pool.pop()

    firs_d = nc.dram_tensor("dl_firs", [D, T], BF16).ap()
    firl_d = nc.dram_tensor("dl_firl", [D, T], BF16).ap()
    prd = nc.dram_tensor("dl_prd", [16, TBW], BF16).ap()
    ord_ = nc.dram_tensor("dl_ord", [4, TBW], F32).ap()

    barrier(E)
    E.pool.push()
    PWF = 64 + T
    vch = [E.pool.alloc_bf16(PWF) for _ in range(2)]
    diagl = E.pool.alloc_bf16(63 * 128)
    diags = E.pool.alloc_bf16(3 * 128)
    fstg = [E.pool.alloc_bf16(TBW) for _ in range(4)]
    flb = E.pool.alloc_f32(8 * 63)
    fsb = E.pool.alloc_f32(24)
    fl_in = declare_input(E, "d1_fir_l")
    fs_in = declare_input(E, "d1_fir_s")
    t_fl = p.op("gpsimd", lambda h: h.dma_start(out=flb, in_=fl_in), dma=True)
    t_fs = p.op("gpsimd", lambda h: h.dma_start(out=fsb, in_=fs_in), dma=True)
    t_pz = [p.op("gpsimd", (lambda h, s=s: h.memset(vch[s][:, 0:64], 0.0))) for s in (0, 1)]
    vw = {0: [], 1: []}
    dw = []
    fw = [None] * 4
    fir_done = []
    for fc in range(8):
        sl = fc % 2
        t_v = p.op("sync", (lambda h, fc=fc, sl=sl: h.dma_start(
            out=vch[sl][:, 64:PWF], in_=vd[fc * 128:(fc + 1) * 128, :])),
            vw[sl] + [t_pz[sl]] + vtoks[-1:], dma=True)
        vw[sl] = []
        dts = []
        for k in range(63):
            dts.append(p.op("vector", (lambda h, fc=fc, k=k: h.tensor_scalar(
                diagl[:, k * 128:(k + 1) * 128], E.ident, flb[:, fc * 63 + k: fc * 63 + k + 1],
                None, ALU.mult)), [t_fl, E.const_tok[1]] + dw))
        for k in range(3):
            dts.append(p.op("vector", (lambda h, fc=fc, k=k: h.tensor_scalar(
                diags[:, k * 128:(k + 1) * 128], E.ident, fsb[:, fc * 3 + k: fc * 3 + k + 1],
                None, ALU.mult)), [t_fs, E.const_tok[1]] + dw))
        dw = []
        for tb in range(NTB):
            si = (tb % 2) * 2
            pairs = [(diagl[:, k * 128:(k + 1) * 128],
                      vch[sl][:, 2 + tb * TBW + k: 2 + tb * TBW + k + TBW]) for k in range(63)]
            tm = _mmgroup(E, E.ps[6][:, :], pairs, dts + [t_v, E.ps_war[6]])
            te = p.op("vector", (lambda h, si=si: h.tensor_copy(fstg[si], E.ps[6][:, :])),
                      [tm, fw[si]])
            E.ps_war[6] = te
            td = p.op("sync", (lambda h, fc=fc, tb=tb, si=si: h.dma_start(
                out=firl_d[fc * 128:(fc + 1) * 128, tb * TBW:(tb + 1) * TBW], in_=fstg[si])),
                [te], dma=True)
            fw[si] = td
            fir_done.append(td)
            pairs = [(diags[:, k * 128:(k + 1) * 128],
                      vch[sl][:, 62 + tb * TBW + k: 62 + tb * TBW + k + TBW]) for k in range(3)]
            tm2 = _mmgroup(E, E.ps[5][:, :], pairs, dts + [t_v, E.ps_war[5]])
            te2 = p.op("vector", (lambda h, si=si: h.tensor_copy(fstg[si + 1], E.ps[5][:, :])),
                       [tm2, fw[si + 1]])
            E.ps_war[5] = te2
            td2 = p.op("sync", (lambda h, fc=fc, tb=tb, si=si: h.dma_start(
                out=firs_d[fc * 128:(fc + 1) * 128, tb * TBW:(tb + 1) * TBW], in_=fstg[si + 1])),
                [te2], dma=True)
            fw[si + 1] = td2
            fir_done.append(td2)
            vw[sl] += [tm, tm2]
            dw += [tm, tm2]
    E.pool.pop()

    barrier(E)
    E.pool.push()
    vt_b = E.pool.alloc_bf16(NCH * TBW)
    st_b = E.pool.alloc_bf16(NCH * TBW)
    lt_b = E.pool.alloc_bf16(NCH * TBW)
    dt_b = E.pool.alloc_bf16(NCH * TBW)
    xl_b = E.pool.alloc_bf16(NCH * TBW)
    pbuf = (st_b, lt_b, dt_b, vt_b)
    mixf = E.pool.alloc_bf16(NCH * TBW)
    mix3 = E.pool.alloc_bf16(NCH * TBW)
    hmsl = [E.pool.alloc_bf16(TBW) for _ in range(2)]
    g1db = [E.pool.alloc_bf16(1152) for _ in range(2)]
    wobuf = E.pool.alloc_bf16(8 * 1024)
    g2buf = E.pool.alloc_bf16(256)
    g1bb = E.pool.alloc_f32(16)
    gstat = E.pool.alloc_bf16(TBW)
    sqp = [E.pool.alloc_bf16(TBW) for _ in range(2)]
    s32 = E.pool.alloc_bf16(TBW)
    stT = E.pool.alloc_f32(128)
    msq = E.pool.alloc_f32(16)
    stTb = E.pool.alloc_bf16(128)
    lgf = E.pool.alloc_f32(TBW)
    lgb = E.pool.alloc_bf16(TBW)
    lgT = E.pool.alloc_f32(64)
    prf = E.pool.alloc_f32(64)
    prb_ = E.pool.alloc_bf16(64)
    sx2 = E.pool.alloc_f32(32)
    sx1 = E.pool.alloc_f32(16)
    prT = E.pool.alloc_bf16(TBW)
    prrow = E.pool.alloc_bf16(8 * TBW)
    prbcf = E.pool.alloc_f32(TBW)
    msc = [E.pool.alloc_f32(TBW) for _ in range(2)]
    ost = E.pool.alloc_f32(TBW)
    orow = E.pool.alloc_f32(4 * TBW)
    orb = E.pool.alloc_bf16(4 * TBW)
    oadd = E.pool.alloc_f32(TBW)
    flbb = E.pool.alloc_f32(16)
    r16 = E.pool.alloc_f32(8)
    flr = E.pool.alloc_f32(32)
    frb = E.pool.alloc_bf16(16)
    onwb = E.pool.alloc_f32(8)
    cdbuf = E.pool.alloc_bf16(2048)
    cdh = E.pool.alloc_bf16(32)

    cdsel_in = declare_input(E, "c_dsel")
    cdh_in = declare_input(E, "c_dselh")
    t_cd = p.op("gpsimd", lambda h: h.dma_start(out=cdbuf, in_=cdsel_in), dma=True)
    t_ch = p.op("gpsimd", lambda h: h.dma_start(out=cdh, in_=cdh_in), dma=True)
    t_wo = load_weight_bf16(E, "d1_wo", wobuf)
    t_g2 = load_weight_bf16(E, "d1_g2", g2buf)
    g1b_in = declare_input(E, "d1_g1b")
    t_g1b = p.op("sync", lambda h: h.dma_start(out=g1bb, in_=g1b_in), dma=True)
    onw_in = declare_input(E, "d1_onw")
    t_onw = p.op("sync", lambda h: h.dma_start(out=onwb, in_=onw_in), dma=True)
    bb_in = declare_input(E, "d1_bb")
    t_bb = p.op("sync", lambda h: h.dma_start(out=r16[0:16, 0:1], in_=bb_in), dma=True)
    lt_in = declare_input(E, "d1_logt")
    t_lt = p.op("sync", lambda h: h.dma_start(out=r16[0:16, 1:2], in_=lt_in), dma=True)
    fl2_in = declare_input(E, "d1_floor")
    t_fr = p.op("sync", lambda h: h.dma_start(out=flr[0:1, 0:16], in_=fl2_in), dma=True)
    g1_in = declare_input(E, "d1_g1")

    ta_ = p.op("scalar", lambda h: h.activation(r16[0:16, 2:3], r16[0:16, 1:2], AF.Relu), [t_lt])
    tb_ = p.op("scalar", lambda h: h.activation(r16[0:16, 3:4], r16[0:16, 1:2], AF.Abs), [t_lt])
    tc_ = p.op("scalar", lambda h: h.activation(r16[0:16, 3:4], r16[0:16, 3:4], AF.Exp, scale=-1.0), [tb_])
    td_ = p.op("vector", lambda h: h.tensor_scalar_add(r16[0:16, 3:4], r16[0:16, 3:4], 1.0), [tc_])
    te_ = p.op("scalar", lambda h: h.activation(r16[0:16, 3:4], r16[0:16, 3:4], AF.Ln), [td_])
    tf_ = p.op("vector", lambda h: h.tensor_tensor(r16[0:16, 4:5], r16[0:16, 2:3], r16[0:16, 3:4], ALU.add), [ta_, te_])
    tg_ = p.op("vector", lambda h: h.tensor_scalar_add(r16[0:16, 4:5], r16[0:16, 4:5], 1e-4), [tf_])
    th_ = p.op("vector", lambda h: h.reciprocal(r16[0:16, 5:6], r16[0:16, 4:5]), [tg_])
    rtmp = r16[0:16, 5:6]
    tfs = p.op("scalar", lambda h: h.activation(flr[0:1, 16:32], flr[0:1, 0:16], AF.Sigmoid), [t_fr])
    tfm = p.op("vector", lambda h: h.tensor_scalar_mul(flr[0:1, 16:32], flr[0:1, 16:32], 0.05), [tfs])
    tfc = p.op("vector", lambda h: h.tensor_copy(frb[0:1, :], flr[0:1, 16:32]), [tfm])
    tfb = _mmgroup(E, E.ps[4][:, 0:16], [(E.ones_bf[0:1, 0:128], frb[0:1, 0:16])], [tfc, E.ps_war[4]])
    t_flb = p.op("vector", lambda h: h.tensor_copy(flbb[:, 0:16], E.ps[4][:, 0:16]), [tfb])
    E.ps_war[4] = t_flb
    t_gz = p.op("gpsimd", lambda h: h.memset(gstat, 0.0))

    v3 = lgT.rearrange("p (g q) -> p g q", q=4)
    p3 = prf.rearrange("p (g q) -> p g q", q=4)
    p4 = prf.rearrange("p (q l) -> p q l", l=16)
    m2 = sx2.rearrange("p (g q) -> p g q", q=2)
    m1 = sx1.rearrange("p (g q) -> p g q", q=1)

    tbw = []
    gstw = [t_gz]
    sqw = [None, None]
    psbw = [E.psb_war]
    s32w = stTw = msqw = stTbw = lgfw = lgbw = lgTw = sxw = prfw = prbw = None
    prTw = prroww = ostw = ordw = oroww = None
    orbw = []
    prbcw = []
    mscw = {0: None, 1: None}
    oaddw = None
    mixtok = {}
    mix3tok = {}
    mfw = {fc: [] for fc in range(NCH)}
    mix3w = {fc: [] for fc in range(NCH)}
    hmw = [None, None]
    g1slw = {0: [], 1: []}
    prdw = []
    for tb in range(NTB):
        t0 = tb * TBW
        ld = {}
        for nm, buf, srcd, dps in (("v", vt_b, vd, vtoks[-1:]),
                                   ("s", st_b, firs_d, fir_done[-1:]),
                                   ("l", lt_b, firl_d, fir_done[-1:]),
                                   ("d", dt_b, dd, ddone[-1:]),
                                   ("x", xl_b, xlnd, [t_xs])):
            ld[nm] = p.op("sync", (lambda h, buf=buf, srcd=srcd, t0=t0: h.dma_start(
                out=buf.rearrange("p (c w) -> p c w", c=NCH),
                in_=srcd.rearrange("(c p) t -> p c t", p=128)[:, :, t0:t0 + TBW])),
                dps + tbw, dma=True)
        ldp = [ld["s"], ld["l"], ld["d"], ld["v"]]
        tbw = []
        pairs = []
        for path in range(4):
            for fc in range(NCH):
                b = (path * 8 + fc) * 2
                pairs.append((cdbuf[:, b * 32:(b + 1) * 32],
                              pbuf[path][:, fc * TBW:(fc + 1) * TBW]))
        t_raw = _mmgroup(E, E.ps[0][0:32, :], pairs, ldp + [t_cd, E.ps_war[0]], stop_group=False)
        prevmm = t_raw
        sq_mm = list(sqw)
        i = 0
        for path in range(4):
            for fc in range(NCH):
                b = (path * 8 + fc) * 2 + 1
                t_sq = p.op("scalar", (lambda h, path=path, fc=fc, i=i: h.activation(
                    sqp[i % 2], pbuf[path][:, fc * TBW:(fc + 1) * TBW], AF.Square)),
                    [ldp[path], sq_mm[i % 2]])
                t_mm = _mmgroup(E, E.ps[0][0:32, :], [(cdbuf[:, b * 32:(b + 1) * 32], sqp[i % 2])],
                                [t_sq, t_cd], start_clear=False, stop_group=(i == 31))
                sq_mm[i % 2] = t_mm
                prevmm = t_mm
                i += 1
        t_se = p.op("vector", (lambda h: h.tensor_copy(s32[0:32, :], E.ps[0][0:32, :])),
                    [prevmm, s32w])
        E.ps_war[0] = t_se

        def trf(h, t_se=t_se):
            last = None
            for q in range(4):
                last = h.transpose(E.psb[:, q * 32:(q + 1) * 32],
                                   s32[0:32, q * 128:(q + 1) * 128], E.ident[0:32, 0:32])
            return last
        t_tf = p.op("tensor", trf, [t_se, E.const_tok[1]] + psbw)
        s32w = t_tf
        t_st = p.op("vector", lambda h: h.tensor_copy(stT, E.psb[:, 0:128]), [t_tf, stTw])
        psbw = [t_st]
        vfix = []
        tprev = t_st
        for q in range(4):
            t_m2 = p.op("scalar", (lambda h, q=q: h.activation(
                msq[:, 0:16], stT[:, q * 32:q * 32 + 16], AF.Square)), [t_st, msqw, tprev])
            t_vf = p.op("vector", (lambda h, q=q: h.tensor_tensor(
                stT[:, q * 32 + 16:q * 32 + 32], stT[:, q * 32 + 16:q * 32 + 32],
                msq[:, 0:16], ALU.subtract)), [t_m2])
            msqw = t_vf
            tprev = t_vf
            vfix.append(t_vf)
        t_sb = p.op("vector", lambda h: h.tensor_copy(stTb, stT), vfix + [stTbw])
        stTw = t_sb

        def trb(h, t_sb=t_sb):
            last = None
            for q in range(4):
                last = h.transpose(E.psb[0:32, q * 128:(q + 1) * 128],
                                   stTb[:, q * 32:(q + 1) * 32], E.ident)
            return last
        t_tb2 = p.op("tensor", trb, [t_sb, E.const_tok[1]] + psbw)
        stTbw = t_tb2
        t_gs = p.op("vector", (lambda h: h.tensor_copy(gstat[0:32, :], E.psb[0:32, 0:512])),
                    [t_tb2] + gstw)
        psbw = [t_gs]
        gw = []
        t_lgmm = None
        for hbc in range(16):
            slt = hbc % 2
            bank = 2 + slt
            t_g = p.op("gpsimd", (lambda h, hbc=hbc, slt=slt: h.dma_start(
                out=g1db[slt].rearrange("p (c m) -> p c m", c=9),
                in_=g1_in[:, hbc * 128:(hbc + 1) * 128].rearrange("(c p) m -> p c m", p=128))),
                g1slw[slt], dma=True)
            pairs = [(g1db[slt][:, c * 128:(c + 1) * 128],
                      xl_b[:, c * TBW:(c + 1) * TBW]) for c in range(NCH)] +                     [(g1db[slt][:, 8 * 128:9 * 128], gstat)]
            t_mm = _mmgroup(E, E.ps[bank][:, :], pairs, [t_g, ld["x"], t_gs, E.ps_war[bank]])
            t_gel = p.op("scalar", (lambda h, slt=slt, bank=bank, hbc=hbc: h.activation(
                hmsl[slt], E.ps[bank][:, :], AF.Gelu, bias=g1bb[:, hbc:hbc + 1])),
                [t_mm, t_g1b, hmw[slt]])
            E.ps_war[bank] = t_gel
            t_lgmm = _mmgroup(E, E.ps[1][0:16, :], [(g2buf[:, hbc * 16:(hbc + 1) * 16], hmsl[slt])],
                              [t_gel, t_g2] + ([E.ps_war[1]] if hbc == 0 else []),
                              start_clear=(hbc == 0), stop_group=(hbc == 15))
            hmw[slt] = t_lgmm
            g1slw[slt] = [t_mm]
            gw.append(t_mm)
        gstw = gw
        t_l1 = p.op("vector", lambda h: h.tensor_scalar(lgf[0:16, :], E.ps[1][0:16, :],
                                                        r16[0:16, 0:1], None, ALU.add),
                    [t_lgmm, t_bb, lgfw])
        E.ps_war[1] = t_l1
        t_l2 = p.op("vector", lambda h: h.tensor_scalar(lgf[0:16, :], lgf[0:16, :], rtmp,
                                                        None, ALU.mult), [t_l1, th_])
        t_l3 = p.op("vector", lambda h: h.tensor_copy(lgb[0:16, :], lgf[0:16, :]), [t_l2, lgbw])
        lgfw = t_l3

        def trl(h, t_l3=t_l3):
            last = None
            for q in range(4):
                last = h.transpose(E.psb[:, q * 16:(q + 1) * 16],
                                   lgb[0:16, q * 128:(q + 1) * 128], E.ident[0:16, 0:16])
            return last
        t_tl = p.op("tensor", trl, [t_l3, E.const_tok[1]] + psbw)
        lgbw = t_tl
        t_lt2 = p.op("vector", lambda h: h.tensor_copy(lgT, E.psb[:, 0:64]), [t_tl, lgTw])
        psbw = [t_lt2]
        t_x1 = p.op("vector", lambda h: h.tensor_tensor(m2, v3[:, :, 0:2], v3[:, :, 2:4], ALU.max),
                    [t_lt2, sxw])
        t_x2 = p.op("vector", lambda h: h.tensor_tensor(m1, m2[:, :, 0:1], m2[:, :, 1:2], ALU.max),
                    [t_x1])
        t_x3 = p.op("vector", lambda h: h.tensor_tensor(
            v3, v3, m1.broadcast_to([128, 16, 4]), ALU.subtract), [t_x2])
        lgTw = t_x3
        t_x4 = p.op("scalar", lambda h: h.activation(prf, lgT, AF.Exp), [t_x3, prfw])
        t_s1_ = p.op("vector", lambda h: h.tensor_tensor(m2, p3[:, :, 0:2], p3[:, :, 2:4], ALU.add),
                     [t_x4])
        t_s2_ = p.op("vector", lambda h: h.tensor_tensor(m1, m2[:, :, 0:1], m2[:, :, 1:2], ALU.add),
                     [t_s1_])
        t_s3 = p.op("vector", lambda h: h.reciprocal(sx1, sx1), [t_s2_])
        t_s4 = p.op("vector", lambda h: h.tensor_tensor(
            p3, p3, m1.broadcast_to([128, 16, 4]), ALU.mult), [t_s3])
        t_f1 = p.op("vector", lambda h: h.tensor_tensor(
            p4, p4, flbb[:, 0:16].unsqueeze(1).broadcast_to([128, 4, 16]), ALU.max),
            [t_s4, t_flb])
        t_s5 = p.op("vector", lambda h: h.tensor_tensor(m2, p3[:, :, 0:2], p3[:, :, 2:4], ALU.add),
                    [t_f1])
        t_s6 = p.op("vector", lambda h: h.tensor_tensor(m1, m2[:, :, 0:1], m2[:, :, 1:2], ALU.add),
                    [t_s5])
        t_s7 = p.op("vector", lambda h: h.reciprocal(sx1, sx1), [t_s6])
        t_s8 = p.op("vector", lambda h: h.tensor_tensor(
            p3, p3, m1.broadcast_to([128, 16, 4]), ALU.mult), [t_s7])
        sxw = t_s8
        t_pc = p.op("vector", lambda h: h.tensor_copy(prb_, prf), [t_s8, prbw])
        prfw = t_pc

        def trp(h, t_pc=t_pc):
            last = None
            for q in range(4):
                last = h.transpose(E.psb[0:16, q * 128:(q + 1) * 128],
                                   prb_[:, q * 16:(q + 1) * 16], E.ident)
            return last
        t_tp = p.op("tensor", trp, [t_pc, E.const_tok[1]] + psbw)
        prbw = t_tp
        t_pe = p.op("vector", lambda h: h.tensor_copy(prT[0:16, :], E.psb[0:16, 0:512]),
                    [t_tp, prTw])
        psbw = [t_pe]
        t_d1 = p.op("sync", lambda h: h.dma_start(out=prd, in_=prT[0:16, :]),
                    [t_pe] + ([prdw] if prdw else []), dma=True)
        prTw = t_d1
        t_d2 = p.op("sync", lambda h: h.dma_start(
            out=prrow[0:1, :], in_=prd[0:8, :].rearrange("l w -> (l w)").unsqueeze(0)),
            [t_d1] + prbcw, dma=True)
        prdw = t_d2
        prroww = []
        prbcw = []
        for hh in range(4):
            if hh == 2:
                t_d2 = p.op("sync", lambda h: h.dma_start(
                    out=prrow[0:1, :], in_=prd[8:16, :].rearrange("l w -> (l w)").unsqueeze(0)),
                    [t_d1] + prroww, dma=True)
                prdw = t_d2
            for path in range(4):
                l = hh * 4 + path
                t_b1 = _mmgroup(E, E.ps[4][:, :],
                                [(E.ones_bf[0:1, 0:128], prrow[0:1, (l % 8) * TBW:(l % 8 + 1) * TBW])],
                                [t_d2, E.ps_war[4]])
                prroww.append(t_b1)
                t_be = p.op("vector", lambda h: h.tensor_copy(prbcf, E.ps[4][:, :]),
                            [t_b1] + prbcw)
                E.ps_war[4] = t_be
                prbcw = []
                for cc in range(2):
                    fc = 2 * hh + cc
                    if path == 0:
                        tt = p.op("vector", (lambda h, fc=fc, path=path: h.tensor_tensor(
                            mixf[:, fc * TBW:(fc + 1) * TBW],
                            pbuf[path][:, fc * TBW:(fc + 1) * TBW], prbcf, ALU.mult)),
                            [t_be, ldp[path]] + mfw[fc])
                    else:
                        t_t1 = p.op("vector", (lambda h, fc=fc, path=path, mi=cc: h.tensor_tensor(
                            msc[mi], pbuf[path][:, fc * TBW:(fc + 1) * TBW], prbcf, ALU.mult)),
                            [t_be, ldp[path], mscw[cc]])
                        tt = p.op("vector", (lambda h, fc=fc, mi=cc: h.tensor_tensor(
                            mixf[:, fc * TBW:(fc + 1) * TBW], mixf[:, fc * TBW:(fc + 1) * TBW],
                            msc[mi], ALU.add)), [t_t1, mixtok[fc]])
                        mscw[cc] = tt
                    mixtok[fc] = tt
                    prbcw.append(tt)
        sq_mm2 = list(sq_mm)
        rmms = None
        for fc in range(NCH):
            t_sq = p.op("scalar", (lambda h, fc=fc: h.activation(
                sqp[fc % 2], mixf[:, fc * TBW:(fc + 1) * TBW], AF.Square)),
                [mixtok[fc], sq_mm2[fc % 2]])
            rmms = _mmgroup(E, E.ps[5][0:4, :], [(cdh[:, fc * 4:(fc + 1) * 4], sqp[fc % 2])],
                            [t_sq, t_ch] + ([E.ps_war[5]] if fc == 0 else []),
                            start_clear=(fc == 0), stop_group=(fc == 7))
            sq_mm2[fc % 2] = rmms
        sqw = list(sq_mm2)
        t_oe2 = p.op("vector", lambda h: h.tensor_copy(ost[0:4, :], E.ps[5][0:4, :]),
                     [rmms, ostw])
        E.ps_war[5] = t_oe2
        t_od1 = p.op("sync", lambda h: h.dma_start(out=ord_, in_=ost[0:4, :]),
                     [t_oe2] + ([ordw] if ordw else []), dma=True)
        ostw = t_od1
        t_od2 = p.op("sync", lambda h: h.dma_start(
            out=orow[0:1, :], in_=ord_.rearrange("l w -> (l w)").unsqueeze(0)),
            [t_od1, oroww], dma=True)
        ordw = t_od2
        t_o1 = p.op("vector", lambda h: h.tensor_scalar_add(orow[0:1, :], orow[0:1, :], 1e-5),
                    [t_od2])
        t_o2 = p.op("scalar", lambda h: h.activation(orow[0:1, :], orow[0:1, :], AF.Ln), [t_o1])
        t_o3 = p.op("scalar", lambda h: h.activation(orow[0:1, :], orow[0:1, :], AF.Exp,
                                                     scale=-0.5), [t_o2])
        t_o4 = p.op("vector", lambda h: h.tensor_copy(orb[0:1, :], orow[0:1, :]),
                    [t_o3] + orbw)
        oroww = t_o4
        orbw = []
        for hh in range(4):
            t_b2 = _mmgroup(E, E.ps[4][:, :],
                            [(E.ones_bf[0:1, 0:128], orb[0:1, hh * TBW:(hh + 1) * TBW])],
                            [t_o4, E.ps_war[4]])
            orbw.append(t_b2)
            t_be2 = p.op("vector", lambda h: h.tensor_copy(prbcf, E.ps[4][:, :]),
                         [t_b2] + prbcw)
            E.ps_war[4] = t_be2
            prbcw = []
            for cc in range(2):
                fc = 2 * hh + cc
                t_mn = p.op("vector", (lambda h, fc=fc, mi=cc: h.tensor_tensor(
                    msc[mi], mixf[:, fc * TBW:(fc + 1) * TBW], prbcf, ALU.mult)),
                    [t_be2, mscw[cc]] + ([sq_mm2[fc % 2]] if True else []))
                t_m3 = p.op("vector", (lambda h, fc=fc, mi=cc: h.tensor_scalar(
                    mix3[:, fc * TBW:(fc + 1) * TBW], msc[mi], onwb[:, fc:fc + 1],
                    None, ALU.mult)), [t_mn, t_onw] + mix3w[fc])
                mscw[cc] = t_m3
                mix3tok[fc] = t_m3
                mfw[fc] = [t_mn]
                prbcw.append(t_mn)
        wol = []
        for oc in range(NCH):
            bank = 2 + oc % 2
            pairs = [(wobuf[:, fc * 1024 + oc * 128: fc * 1024 + oc * 128 + 128],
                      mix3[:, fc * TBW:(fc + 1) * TBW]) for fc in range(NCH)]
            t_wm = _mmgroup(E, E.ps[bank][:, :], pairs,
                            [t_wo, E.ps_war[bank]] + [mix3tok[fc] for fc in range(NCH)])
            t_ev = p.op("vector", (lambda h, bank=bank: h.tensor_copy(oadd, E.ps[bank][:, :])),
                        [t_wm, oaddw])
            t_x = p.op("vector", (lambda h, oc=oc, t0=t0: h.tensor_tensor(
                E.x[:, oc * T + t0: oc * T + t0 + TBW], E.x[:, oc * T + t0: oc * T + t0 + TBW],
                oadd, ALU.add)), [t_ev, E.xtok.get((oc, tb))])
            E.ps_war[bank] = t_x
            oaddw = t_x
            E.xtok[(oc, tb)] = t_x
            wol.append(t_wm)
        for fc in range(NCH):
            mix3w[fc] = wol
        tbw = [t_raw, t_lgmm, prevmm] + wol + [mixtok[fc] for fc in range(NCH)] + gw[-1:]
    E.pool.pop()
